# version 02
# baseline (speedup 1.0000x reference)
import numpy as np
import concourse.bass as bass
import concourse.mybir as mybir

F32 = mybir.dt.float32
BF16 = mybir.dt.bfloat16
AF = mybir.ActivationFunctionType
OP = mybir.AluOpType

ENGINES = ("tensor", "vector", "scalar", "gpsimd", "sync")
AUTO = {"tensor": "aT", "vector": "aV", "scalar": "aS", "gpsimd": "aG", "sync": "aY"}


def vap(handle, offset, pattern):
    return bass.AP(handle, offset, pattern)


def _rect(ap):
    t = ap.tensor
    name = t.name
    cls = type(t).__name__
    space = "sbuf" if "SB" in cls else ("psum" if "PSum" in cls else "dram")
    shape = list(t.shape)
    whole = (name, space, 0, 1 << 20, 0, 1 << 40)
    W = shape[-1] if len(shape) >= 2 else shape[0]
    apl = [list(x) for x in ap.ap]
    if len(apl) == 2 and apl[1][0] == 1 and apl[0][0] == W:
        p0 = ap.offset // W
        c0 = ap.offset % W
        p1 = p0 + apl[0][1]
        c1 = c0 + apl[1][1]
    elif len(apl) == 1:
        p0, p1 = 0, 1
        c0, c1 = ap.offset, ap.offset + apl[0][1]
    else:
        return whole
    if space == "psum":
        bank = 512 if t.dtype == F32 else 1024
        p0, p1 = 0, 128
        c0 = (c0 // bank) * bank
        c1 = ((c1 + bank - 1) // bank) * bank
    return (name, space, p0, p1, c0, c1)


def _overlap(a, b):
    return a[0] == b[0] and a[2] < b[3] and b[2] < a[3] and a[4] < b[5] and b[4] < a[5]


class Prog:
    def __init__(self, nc):
        self.nc = nc
        self.segments = [{e: [] for e in ENGINES}]
        self.semctr = {}
        self.sems = {}
        self.ninstr = 0
        self.nwaits = 0
        self.log = {}
        self.clocks = {e: {} for e in ENGINES}
        self.slot_sems = {}
        self.ndma_sems = 0

    def sem(self, name):
        if name not in self.sems:
            h = self.nc.semaphore(name).__enter__()
            self.sems[name] = h
            self.semctr[name] = 0
        return self.sems[name]

    def val(self, name):
        self.sem(name)
        return self.semctr[name]

    def cut(self):
        if not any(self.segments[-1][e] for e in ENGINES):
            return
        self.segments.append({e: [] for e in ENGINES})
        cover = {AUTO[e]: self.semctr.get(AUTO[e], 0) for e in ENGINES}
        for e in ENGINES:
            for s, v in cover.items():
                if self.clocks[e].get(s, 0) < v:
                    self.clocks[e][s] = v
        for nm in list(self.log):
            keep = [ent for ent in self.log[nm]
                    if not all(self.clocks[e].get(ent[3], 0) >= ent[4] for e in ENGINES)]
            if keep:
                self.log[nm] = keep
            else:
                del self.log[nm]

    def _emit_wait(self, engine, semname, v, prod_clock):
        ck = self.clocks[engine]
        if ck.get(semname, 0) >= v:
            if prod_clock:
                for s, pv in prod_clock.items():
                    if ck.get(s, 0) < pv:
                        ck[s] = pv
            return
        self.sem(semname)

        def run(eng, h=self.sems[semname], v=v):
            eng.wait_ge(h, v)
        self.segments[-1][engine].append(run)
        self.nwaits += 1
        ck[semname] = v
        if prod_clock:
            for s, pv in prod_clock.items():
                if ck.get(s, 0) < pv:
                    ck[s] = pv

    def _deps(self, engine, reads, writes, is_pe):
        rrects = [_rect(a) for a in reads]
        wrects = [_rect(a) for a in writes]
        needed = {}
        for nm in {r[0] for r in rrects + wrects}:
            for ent in self.log.get(nm, ()):
                k, rect, peng, sem, v, ck = ent
                if k == "w":
                    conflict = any(_overlap(rect, r) for r in rrects + wrects)
                else:
                    conflict = any(_overlap(rect, r) for r in wrects)
                if not conflict:
                    continue
                if is_pe and peng == "tensor" and sem == AUTO["tensor"]:
                    continue
                if sem not in needed or needed[sem][0] < v:
                    needed[sem] = (v, ck)
        for sem, (v, ck) in needed.items():
            self._emit_wait(engine, sem, v, ck)
        return rrects, wrects

    def _record(self, rrects, wrects, engine, sem, v):
        ck = dict(self.clocks[engine])
        for r in rrects:
            self.log.setdefault(r[0], []).append(("r", r, engine, sem, v, ck))
        for r in wrects:
            self.log.setdefault(r[0], []).append(("w", r, engine, sem, v, ck))

    def op(self, engine, fn, reads=(), writes=(), inc=None, is_pe=False):
        rr, wr = self._deps(engine, list(reads), list(writes), is_pe and engine == "tensor")
        auto = AUTO[engine]
        self.sem(auto)
        self.semctr[auto] += 1
        v = self.semctr[auto]
        handles = [(self.sems[auto], 1)]
        ret = v
        if inc is not None:
            nm, amt = inc
            self.sem(nm)
            self.semctr[nm] += amt
            handles.append((self.sems[nm], amt))
            ret = self.semctr[nm]

        def run(eng, fn=fn, handles=handles):
            inst = fn(eng)
            for h, amt in handles:
                inst = inst.then_inc(h, amt)
        self.segments[-1][engine].append(run)
        self.ninstr += 1
        self._record(rr, wr, engine, auto, v)
        return ret

    def mm(self, fn, reads, writes):
        return self.op("tensor", fn, reads, writes, is_pe=True)

    def _slot_sem(self, key):
        if key not in self.slot_sems:
            nm = f"d{self.ndma_sems}"
            self.ndma_sems += 1
            self.slot_sems[key] = nm
        return self.slot_sems[key]

    def dma(self, engine, out, in_, **kw):
        orect = _rect(out)
        irect = _rect(in_)
        key = orect if orect[1] != "dram" else (irect if irect[1] != "dram" else orect)
        semname = self._slot_sem((key[0], key[2], key[4]))
        rr, wr = self._deps(engine, [in_], [out], False)
        self.sem(semname)
        self.semctr[semname] += 16
        v = self.semctr[semname]

        def run(eng, out=out, in_=in_, h=self.sems[semname], kw=kw):
            eng.dma_start(out=out, in_=in_, **kw).then_inc(h, 16)
        self.segments[-1][engine].append(run)
        self.ninstr += 1
        self._record(rr, wr, engine, semname, v)
        return semname, v

    def collective(self, kind, op, groups, ins, outs):
        rr, wr = self._deps("gpsimd", ins, outs, False)
        self.sem("cc")
        self.semctr["cc"] += 1
        v = self.semctr["cc"]

        def run(eng, ins=ins, outs=outs):
            eng.collective_compute(kind, op, replica_groups=groups,
                                   ins=[a.opt() for a in ins],
                                   outs=[a.opt() for a in outs]).then_inc(self.sems["cc"], 1)
        self.segments[-1]["gpsimd"].append(run)
        self.ninstr += 1
        self._record(rr, wr, "gpsimd", "cc", v)
        return v

    def wait(self, engine, semname, val=None):
        self.sem(semname)
        v = self.semctr[semname] if val is None else val
        self._emit_wait(engine, semname, v, None)
        return v

    def finish(self):
        for nm in sorted(set(self.slot_sems.values())):
            self.wait("sync", nm)

    def build(self):
        for seg in self.segments:
            if not any(seg[e] for e in ENGINES):
                continue
            block = self.nc.Block().__enter__()
            for e in ENGINES:
                if not seg[e]:
                    continue
                seq = seg[e]

                def section(eng, seq=seq):
                    for f in seq:
                        f(eng)
                getattr(block, e)(section)
            block.__exit__(None, None, None)
        return self.nc


class Ctx:
    def __init__(self, nc):
        self.nc = nc
        self.p = Prog(nc)

    def sb(self, name, shape, dtype):
        return self.nc.sbuf_tensor(name, shape, dtype).__enter__()

    def ps(self, name, shape, dtype=F32):
        return self.nc.psum_tensor(name, shape, dtype).__enter__()

    def din(self, name, shape, dtype=F32):
        return self.nc.dram_tensor(name, shape, dtype, kind="ExternalInput").ap()

    def dout(self, name, shape, dtype=F32):
        return self.nc.dram_tensor(name, shape, dtype, kind="ExternalOutput").ap()

    def dtmp(self, name, shape, dtype=F32):
        return self.nc.dram_tensor(name, shape, dtype).ap()

    def pools(self):
        self.LNX = self.sb("pool_lnx", [128, 32768], BF16)
        self.A = self.sb("pool_a", [128, 8192], F32)
        self.B = self.sb("pool_b", [128, 16384], BF16)
        self.C = self.sb("pool_c", [128, 16384], BF16)
        self.S = self.sb("pool_s", [128, 2048], F32)
        self.R = self.sb("pool_r", [1, 8192], F32)
        self.pA = self.ps("pool_pa", [128, 1024])
        self.pB = self.ps("pool_pb", [128, 1024])
        self.pC = self.ps("pool_pc", [128, 1024])
        self.pD = self.ps("pool_pd", [128, 512])
        self.pE = self.ps("pool_pe", [128, 1024], BF16)

    def constants(self):
        p = self.p
        self.iden_bf = self.sb("c_iden_bf", [128, 128], BF16)
        self.iden32 = self.sb("c_iden32", [128, 128], F32)
        self.ones_col_bf = self.sb("c_ones_col_bf", [128, 1], BF16)
        self.ones_row32 = self.sb("c_ones_row32", [1, 128], F32)
        self.ones_row_bf = self.sb("c_ones_row_bf", [1, 128], BF16)
        self.cst = self.sb("c_cst", [128, 4], F32)
        for t in (self.iden_bf, self.iden32):
            p.op("gpsimd", lambda g, t=t: g.memset(t[:, :], 0.0), writes=[t[:, :]])
            p.op("gpsimd", lambda g, t=t: g.affine_select(
                out=t[:, :], in_=t[:, :], compare_op=OP.not_equal, fill=1.0,
                base=0, pattern=[[-1, 128]], channel_multiplier=1),
                reads=[t[:, :]], writes=[t[:, :]])
        p.op("gpsimd", lambda g: g.memset(self.ones_col_bf[:, :], 1.0),
             writes=[self.ones_col_bf[:, :]])
        p.op("gpsimd", lambda g: g.memset(self.ones_row32[:, :], 1.0),
             writes=[self.ones_row32[:, :]])
        p.op("gpsimd", lambda g: g.memset(self.ones_row_bf[:, :], 1.0),
             writes=[self.ones_row_bf[:, :]])
        p.op("gpsimd", lambda g: g.memset(self.cst[:, 0:1], 1e-5), writes=[self.cst[:, 0:1]])
        p.op("gpsimd", lambda g: g.memset(self.cst[:, 1:2], 1.0), writes=[self.cst[:, 1:2]])
        p.op("gpsimd", lambda g: g.memset(self.cst[:, 2:3], 0.0), writes=[self.cst[:, 2:3]])
        p.op("gpsimd", lambda g: g.memset(self.cst[:, 3:4], 1e-6), writes=[self.cst[:, 3:4]])


def emit_ln(ctx, xd, tag, var_row=None):
    p = ctx.p
    nK = 32
    xc = ctx.A[:, 0:4096]
    tt_ = ctx.A[:, 4096:6144]
    mu_b = ctx.A[:, 6144:7168]
    rs_b = ctx.A[:, 7168:8192]
    xb = ctx.B[:, 0:2048]
    sq = ctx.B[:, 2048:4096]
    mu = ctx.R[:, 0:1024]
    rstd = ctx.R[:, 1024:2048]
    tmpr = ctx.R[:, 2048:3072]
    lnx = ctx.LNX
    for k in range(nK):
        xck = xc[:, (k % 4) * 1024:(k % 4 + 1) * 1024]
        xbk = xb[:, (k % 2) * 1024:(k % 2 + 1) * 1024]
        sqk = sq[:, (k % 2) * 1024:(k % 2 + 1) * 1024]
        p.dma("sync", xck, xd[k * 128:(k + 1) * 128, :])
        p.op("vector", lambda v, a=xbk, b=xck: v.tensor_copy(out=a, in_=b),
             reads=[xck], writes=[xbk])
        p.op("scalar", lambda s, a=sqk, b=xck: s.activation(out=a, in_=b, func=AF.Square),
             reads=[xck], writes=[sqk])
        st, sp = (k == 0), (k == nK - 1)
        for g in range(2):
            xbg = xbk[:, g * 512:(g + 1) * 512]
            sqg = sqk[:, g * 512:(g + 1) * 512]
            oa = ctx.pA[0:1, g * 512:(g + 1) * 512]
            ob = ctx.pB[0:1, g * 512:(g + 1) * 512]
            p.mm(lambda t, o=oa, r=xbg, st=st, sp=sp: t.matmul(o, ctx.ones_col_bf[:, :], r,
                 start=st, stop=sp), reads=[xbg, ctx.ones_col_bf[:, :]], writes=[oa])
            p.mm(lambda t, o=ob, r=sqg, st=st, sp=sp: t.matmul(o, ctx.ones_col_bf[:, :], r,
                 start=st, stop=sp), reads=[sqg, ctx.ones_col_bf[:, :]], writes=[ob])
    p.op("vector", lambda v: v.tensor_scalar(out=mu, in0=ctx.pA[0:1, :], scalar1=1.0 / 4096,
         scalar2=None, op0=OP.mult, op1=OP.bypass), reads=[ctx.pA[0:1, :]], writes=[mu])
    p.op("vector", lambda v: v.tensor_scalar(out=tmpr, in0=ctx.pB[0:1, :], scalar1=1.0 / 4096,
         scalar2=None, op0=OP.mult, op1=OP.bypass), reads=[ctx.pB[0:1, :]], writes=[tmpr])
    p.op("vector", lambda v: v.tensor_tensor(out=rstd, in0=mu, in1=mu, op=OP.mult),
         reads=[mu], writes=[rstd])
    p.op("vector", lambda v: v.tensor_tensor(out=tmpr, in0=tmpr, in1=rstd, op=OP.subtract),
         reads=[tmpr, rstd], writes=[tmpr])
    if var_row is not None:
        p.op("vector", lambda v: v.tensor_copy(out=var_row, in_=tmpr), reads=[tmpr],
             writes=[var_row])
    p.op("scalar", lambda s: s.activation(out=rstd, in_=tmpr, func=AF.Sqrt,
         bias=ctx.cst[0:1, 0:1], scale=1.0), reads=[tmpr, ctx.cst[0:1, 0:1]], writes=[rstd])
    p.op("vector", lambda v: v.reciprocal(out=rstd, in_=rstd), reads=[rstd], writes=[rstd])
    p.cut()
    for g in range(2):
        p.mm(lambda t, g=g: t.matmul(ctx.pA[:, g * 512:(g + 1) * 512], ctx.ones_row32[:, :],
             mu[0:1, g * 512:(g + 1) * 512], start=True, stop=True),
             reads=[mu[0:1, g * 512:(g + 1) * 512], ctx.ones_row32[:, :]],
             writes=[ctx.pA[:, g * 512:(g + 1) * 512]])
        p.mm(lambda t, g=g: t.matmul(ctx.pB[:, g * 512:(g + 1) * 512], ctx.ones_row32[:, :],
             rstd[0:1, g * 512:(g + 1) * 512], start=True, stop=True),
             reads=[rstd[0:1, g * 512:(g + 1) * 512], ctx.ones_row32[:, :]],
             writes=[ctx.pB[:, g * 512:(g + 1) * 512]])
    p.op("vector", lambda v: v.tensor_copy(out=mu_b, in_=ctx.pA[:, :]),
         reads=[ctx.pA[:, :]], writes=[mu_b])
    p.op("vector", lambda v: v.tensor_copy(out=rs_b, in_=ctx.pB[:, :]),
         reads=[ctx.pB[:, :]], writes=[rs_b])
    for k in range(nK):
        xck = xc[:, (k % 4) * 1024:(k % 4 + 1) * 1024]
        ttk = tt_[:, (k % 2) * 1024:(k % 2 + 1) * 1024]
        lnk = lnx[:, k * 1024:(k + 1) * 1024]
        p.dma("sync", xck, xd[k * 128:(k + 1) * 128, :])
        eng = "vector" if k % 2 == 0 else "gpsimd"
        p.op(eng, lambda e, a=ttk, b=xck: e.tensor_tensor(out=a, in0=b, in1=mu_b, op=OP.subtract),
             reads=[xck, mu_b], writes=[ttk])
        p.op(eng, lambda e, a=lnk, b=ttk: e.tensor_tensor(out=a, in0=b, in1=rs_b, op=OP.mult),
             reads=[ttk, rs_b], writes=[lnk])
    p.cut()


def emit_l0ffn(ctx, xd, w, tag):
    p = ctx.p
    lnx = ctx.LNX
    gw1t = ctx.C[:, 0:1024]
    aall = ctx.C[:, 1024:5120]
    bflat = ctx.C[:, 5120:9216]
    tmps = ctx.C[:, 9216:10240]
    wexp = ctx.C[:, 10240:11264]
    gw2t = ctx.C[0:32, 11264:11272]
    esel = ctx.C[0:8, 11272:11400]
    wTbf = ctx.C[0:8, 11400:12424]
    gb1 = ctx.S[0:32, 0:1]
    gb2 = ctx.S[0:8, 1:2]
    gamma = ctx.S[0:32, 2:3]
    beta = ctx.S[0:32, 3:4]
    alc = ctx.S[0:8, 4:6]
    g12 = ctx.S[:, 8:12]
    gtm = ctx.S[:, 16:24]
    m8 = ctx.S[:, 24:32]
    wtm = ctx.S[:, 32:48]
    h32 = ctx.S[0:32, 48:1072]
    hbf = ctx.B[0:32, 4096:5120]
    hsq = ctx.B[0:32, 5120:6144]
    hrelu = ctx.B[0:32, 6144:7168]
    ohs = ctx.B[:, 7168:7184]
    gfm = ctx.A[0:8, 0:1024]
    wT1 = ctx.A[0:8, 1024:2048]
    wT2 = ctx.A[0:8, 2048:3072]
    xc2 = ctx.A[:, 3072:4096]
    xnew = ctx.A[:, 4096:5120]
    hmu = ctx.R[:, 3088:4112]
    hrstd = ctx.R[:, 4112:5136]
    htmp = ctx.R[:, 5136:6160]
    crow = ctx.R[:, 3072:3088]
    for dst, src, eng in [(gw1t, w["gw1t_t"], "gpsimd"), (gw2t, w["gw2t"], "gpsimd"),
                          (esel, w["esel"], "gpsimd"), (aall, w["a_all_t"], "gpsimd"),
                          (bflat, w["b_flat"], "gpsimd"), (gb1, w["gb1"], "sync"),
                          (gb2, w["gb2"], "sync"), (gamma, w["lngamma"], "sync"),
                          (beta, w["lnbeta"], "sync")]:
        p.dma(eng, dst, src)
    for k in range(32):
        for g in range(2):
            rhs = lnx[:, k * 1024 + g * 512:k * 1024 + (g + 1) * 512]
            o = ctx.pA[0:32, g * 512:(g + 1) * 512]
            l = gw1t[:, k * 32:(k + 1) * 32]
            p.mm(lambda t, o=o, l=l, r=rhs, st=(k == 0), sp=(k == 31):
                 t.matmul(o, l, r, start=st, stop=sp), reads=[l, rhs], writes=[o])
    p.op("vector", lambda v: v.tensor_scalar(out=h32, in0=ctx.pA[0:32, :], scalar1=gb1,
         scalar2=None, op0=OP.add, op1=OP.bypass), reads=[ctx.pA[0:32, :], gb1], writes=[h32])
    p.op("vector", lambda v: v.tensor_copy(out=hbf, in_=h32), reads=[h32], writes=[hbf])
    p.op("scalar", lambda s: s.activation(out=hsq, in_=h32, func=AF.Square),
         reads=[h32], writes=[hsq])
    p.cut()
    for g in range(2):
        hb = hbf[:, g * 512:(g + 1) * 512]
        hq = hsq[:, g * 512:(g + 1) * 512]
        p.mm(lambda t, o=ctx.pB[0:1, g * 512:(g + 1) * 512], r=hb:
             t.matmul(o, ctx.ones_col_bf[0:32, :], r, start=True, stop=True),
             reads=[hb, ctx.ones_col_bf[0:32, :]], writes=[ctx.pB[0:1, g * 512:(g + 1) * 512]])
        p.mm(lambda t, o=ctx.pC[0:1, g * 512:(g + 1) * 512], r=hq:
             t.matmul(o, ctx.ones_col_bf[0:32, :], r, start=True, stop=True),
             reads=[hq, ctx.ones_col_bf[0:32, :]], writes=[ctx.pC[0:1, g * 512:(g + 1) * 512]])
    p.op("vector", lambda v: v.tensor_scalar(out=hmu, in0=ctx.pB[0:1, :], scalar1=1.0 / 32,
         scalar2=None, op0=OP.mult, op1=OP.bypass), reads=[ctx.pB[0:1, :]], writes=[hmu])
    p.op("vector", lambda v: v.tensor_scalar(out=htmp, in0=ctx.pC[0:1, :], scalar1=1.0 / 32,
         scalar2=None, op0=OP.mult, op1=OP.bypass), reads=[ctx.pC[0:1, :]], writes=[htmp])
    p.op("vector", lambda v: v.tensor_tensor(out=hrstd, in0=hmu, in1=hmu, op=OP.mult),
         reads=[hmu], writes=[hrstd])
    p.op("vector", lambda v: v.tensor_tensor(out=htmp, in0=htmp, in1=hrstd, op=OP.subtract),
         reads=[htmp, hrstd], writes=[htmp])
    p.op("scalar", lambda s: s.activation(out=hrstd, in_=htmp, func=AF.Sqrt,
         bias=ctx.cst[0:1, 0:1], scale=1.0), reads=[htmp, ctx.cst[0:1, 0:1]], writes=[hrstd])
    p.op("vector", lambda v: v.reciprocal(out=hrstd, in_=hrstd), reads=[hrstd], writes=[hrstd])
    p.cut()
    for g in range(2):
        p.mm(lambda t, g=g: t.matmul(ctx.pB[0:32, g * 512:(g + 1) * 512],
             ctx.ones_row32[0:1, 0:32], hmu[0:1, g * 512:(g + 1) * 512], start=True, stop=True),
             reads=[hmu[0:1, g * 512:(g + 1) * 512], ctx.ones_row32[0:1, 0:32]],
             writes=[ctx.pB[0:32, g * 512:(g + 1) * 512]])
        p.mm(lambda t, g=g: t.matmul(ctx.pC[0:32, g * 512:(g + 1) * 512],
             ctx.ones_row32[0:1, 0:32], hrstd[0:1, g * 512:(g + 1) * 512], start=True, stop=True),
             reads=[hrstd[0:1, g * 512:(g + 1) * 512], ctx.ones_row32[0:1, 0:32]],
             writes=[ctx.pC[0:32, g * 512:(g + 1) * 512]])
    p.op("vector", lambda v: v.tensor_tensor(out=h32, in0=h32, in1=ctx.pB[0:32, :],
         op=OP.subtract), reads=[h32, ctx.pB[0:32, :]], writes=[h32])
    p.op("vector", lambda v: v.tensor_tensor(out=h32, in0=h32, in1=ctx.pC[0:32, :],
         op=OP.mult), reads=[h32, ctx.pC[0:32, :]], writes=[h32])
    p.op("vector", lambda v: v.tensor_scalar(out=h32, in0=h32, scalar1=gamma, scalar2=beta,
         op0=OP.mult, op1=OP.add), reads=[h32, gamma, beta], writes=[h32])
    p.op("vector", lambda v: v.tensor_scalar(out=hrelu, in0=h32, scalar1=0.0, scalar2=None,
         op0=OP.max, op1=OP.bypass), reads=[h32], writes=[hrelu])
    for g in range(2):
        hr = hrelu[:, g * 512:(g + 1) * 512]
        p.mm(lambda t, o=ctx.pC[0:8, g * 512:(g + 1) * 512], r=hr:
             t.matmul(o, gw2t, r, start=True, stop=True),
             reads=[gw2t, hr], writes=[ctx.pC[0:8, g * 512:(g + 1) * 512]])
    p.op("vector", lambda v: v.tensor_scalar(out=gfm, in0=ctx.pC[0:8, :], scalar1=gb2,
         scalar2=None, op0=OP.add, op1=OP.bypass), reads=[ctx.pC[0:8, :], gb2], writes=[gfm])
    p.cut()
    for tt in range(8):
        gsl = gfm[:, tt * 128:(tt + 1) * 128]
        p.mm(lambda t, g=gsl: t.transpose(ctx.pA[:, 0:8], g, ctx.iden32[0:8, 0:8]),
             reads=[gsl, ctx.iden32[0:8, 0:8]], writes=[ctx.pA[:, 0:512]])
        p.op("vector", lambda v: v.tensor_copy(out=gtm, in_=ctx.pA[:, 0:8]),
             reads=[ctx.pA[:, 0:512]], writes=[gtm])
        p.op("vector", lambda v: v.max(out=m8, in_=gtm), reads=[gtm], writes=[m8])
        p.op("vector", lambda v: v.tensor_tensor(out=g12[:, 0:1], in0=m8[:, 0:1], in1=m8[:, 1:2],
             op=OP.subtract), reads=[m8], writes=[g12[:, 0:1]])
        p.op("scalar", lambda s: s.activation(out=g12[:, 1:2], in_=g12[:, 0:1], func=AF.Sigmoid),
             reads=[g12[:, 0:1]], writes=[g12[:, 1:2]])
        p.op("vector", lambda v: v.tensor_scalar(out=g12[:, 2:3], in0=g12[:, 1:2], scalar1=-1.0,
             scalar2=1.0, op0=OP.mult, op1=OP.add), reads=[g12[:, 1:2]], writes=[g12[:, 2:3]])
        p.op("vector", lambda v: v.tensor_scalar(out=ohs[:, 0:8], in0=gtm, scalar1=m8[:, 0:1],
             scalar2=None, op0=OP.is_equal, op1=OP.bypass), reads=[gtm, m8], writes=[ohs[:, 0:8]])
        p.op("vector", lambda v: v.tensor_scalar(out=ohs[:, 8:16], in0=gtm, scalar1=m8[:, 1:2],
             scalar2=None, op0=OP.is_equal, op1=OP.bypass), reads=[gtm, m8], writes=[ohs[:, 8:16]])
        p.mm(lambda t, st=(tt == 0), sp=(tt == 7): t.matmul(ctx.pB[0:1, 0:16],
             ctx.ones_col_bf[:, :], ohs, start=st, stop=sp),
             reads=[ohs, ctx.ones_col_bf[:, :]], writes=[ctx.pB[0:1, 0:512]])
        p.op("vector", lambda v: v.tensor_scalar(out=wtm[:, 0:8], in0=ohs[:, 0:8],
             scalar1=g12[:, 1:2], scalar2=None, op0=OP.mult, op1=OP.bypass),
             reads=[ohs[:, 0:8], g12[:, 1:2]], writes=[wtm[:, 0:8]])
        p.op("vector", lambda v: v.tensor_scalar(out=wtm[:, 8:16], in0=ohs[:, 8:16],
             scalar1=g12[:, 2:3], scalar2=None, op0=OP.mult, op1=OP.bypass),
             reads=[ohs[:, 8:16], g12[:, 2:3]], writes=[wtm[:, 8:16]])
        p.mm(lambda t: t.transpose(ctx.pC[0:8, 0:128], wtm[:, 0:8], ctx.iden32[:, :]),
             reads=[wtm[:, 0:8], ctx.iden32[:, :]], writes=[ctx.pC[0:8, 0:512]])
        p.mm(lambda t: t.transpose(ctx.pC[0:8, 512:640], wtm[:, 8:16], ctx.iden32[:, :]),
             reads=[wtm[:, 8:16], ctx.iden32[:, :]], writes=[ctx.pC[0:8, 512:1024]])
        p.op("vector", lambda v, tt=tt: v.tensor_copy(out=wT1[:, tt * 128:(tt + 1) * 128],
             in_=ctx.pC[0:8, 0:128]), reads=[ctx.pC[0:8, 0:512]],
             writes=[wT1[:, tt * 128:(tt + 1) * 128]])
        p.op("vector", lambda v, tt=tt: v.tensor_copy(out=wT2[:, tt * 128:(tt + 1) * 128],
             in_=ctx.pC[0:8, 512:640]), reads=[ctx.pC[0:8, 512:1024]],
             writes=[wT2[:, tt * 128:(tt + 1) * 128]])
    p.op("vector", lambda v: v.tensor_copy(out=crow, in_=ctx.pB[0:1, 0:16]),
         reads=[ctx.pB[0:1, 0:512]], writes=[crow])
    p.dma("gpsimd", w["cb_in"], crow)
    p.collective("AllReduce", OP.add, [list(range(8))], [w["cb_in"]], [w["cb_out"]])
    p.dma("gpsimd", alc[:, 0:1], w["cb_out"][0:1, 0:8].rearrange("a e -> e a"))
    p.dma("gpsimd", alc[:, 1:2], w["cb_out"][0:1, 8:16].rearrange("a e -> e a"))
    p.op("vector", lambda v: v.tensor_scalar(out=alc, in0=alc, scalar1=3072.0, scalar2=None,
         op0=OP.is_le, op1=OP.bypass), reads=[alc], writes=[alc])
    p.op("vector", lambda v: v.tensor_scalar(out=wT1, in0=wT1, scalar1=alc[:, 0:1],
         scalar2=None, op0=OP.mult, op1=OP.bypass), reads=[wT1, alc], writes=[wT1])
    p.op("vector", lambda v: v.scalar_tensor_tensor(out=wTbf, in0=wT2, scalar=alc[:, 1:2],
         in1=wT1, op0=OP.mult, op1=OP.add), reads=[wT2, alc, wT1], writes=[wTbf])
    p.cut()
    for g in range(2):
        wsl = wTbf[:, g * 512:(g + 1) * 512]
        o = (ctx.pA if g == 0 else ctx.pB)[:, 0:512]
        p.mm(lambda t, o=o, r=wsl: t.matmul(o, esel, r, start=True, stop=True),
             reads=[esel, wsl], writes=[o])
    p.op("vector", lambda v: v.tensor_copy(out=wexp[:, 0:512], in_=ctx.pA[:, 0:512]),
         reads=[ctx.pA[:, 0:512]], writes=[wexp[:, 0:512]])
    p.op("vector", lambda v: v.tensor_copy(out=wexp[:, 512:1024], in_=ctx.pB[:, 0:512]),
         reads=[ctx.pB[:, 0:512]], writes=[wexp[:, 512:1024]])
    for g in range(2):
        o = (ctx.pA if g == 0 else ctx.pB)[:, 512:1024]
        for k in range(32):
            rhs = lnx[:, k * 1024 + g * 512:k * 1024 + (g + 1) * 512]
            l = aall[:, k * 128:(k + 1) * 128]
            p.mm(lambda t, o=o, l=l, r=rhs, st=(k == 0), sp=(k == 31):
                 t.matmul(o, l, r, start=st, stop=sp), reads=[l, rhs], writes=[o])
    p.op("vector", lambda v: v.tensor_tensor(out=tmps[:, 0:512], in0=ctx.pA[:, 512:1024],
         in1=wexp[:, 0:512], op=OP.mult), reads=[ctx.pA[:, 512:1024], wexp[:, 0:512]],
         writes=[tmps[:, 0:512]])
    p.op("vector", lambda v: v.tensor_tensor(out=tmps[:, 512:1024], in0=ctx.pB[:, 512:1024],
         in1=wexp[:, 512:1024], op=OP.mult), reads=[ctx.pB[:, 512:1024], wexp[:, 512:1024]],
         writes=[tmps[:, 512:1024]])
    p.cut()
    wtsb = ctx.B[:, 8192:16384]
    for m in range(32):
        wsl = wtsb[:, (m % 2) * 4096:(m % 2 + 1) * 4096]
        p.dma("gpsimd", wsl, w["wt"][m])
        for g in range(2):
            idx = m * 2 + g
            xsl = xc2[:, (idx % 2) * 512:(idx % 2 + 1) * 512]
            nsl = xnew[:, (idx % 2) * 512:(idx % 2 + 1) * 512]
            xdr = xd[m * 128:(m + 1) * 128, g * 512:(g + 1) * 512]
            mp = ctx.pC[:, 0:512] if idx % 2 == 0 else ctx.pD[:, 0:512]
            p.dma("sync", xsl, xdr)
            for k in range(32):
                rhs = lnx[:, k * 1024 + g * 512:k * 1024 + (g + 1) * 512]
                lhs = wsl[:, k * 128:(k + 1) * 128]
                p.mm(lambda t, o=mp, l=lhs, r=rhs, st=(k == 0): t.matmul(o, l, r, start=st,
                     stop=False), reads=[lhs, rhs], writes=[mp])
            p.mm(lambda t, o=mp, l=bflat[:, m * 128:(m + 1) * 128],
                 r=tmps[:, g * 512:(g + 1) * 512]: t.matmul(o, l, r, start=False, stop=True),
                 reads=[bflat[:, m * 128:(m + 1) * 128], tmps[:, g * 512:(g + 1) * 512]],
                 writes=[mp])
            p.op("vector", lambda v, o=nsl, a=mp, b=xsl: v.scalar_tensor_tensor(
                 out=o, in0=a, scalar=0.5, in1=b, op0=OP.mult, op1=OP.add),
                 reads=[mp, xsl], writes=[nsl])
            p.dma("sync", xdr, nsl)
        if m % 8 == 7:
            p.cut()


def emit_stream_proj(ctx, xd, src, wt, scale, nM=32, extra_mm=None):
    p = ctx.p
    wtsb = ctx.B[:, 8192:16384]
    xc2 = ctx.A[:, 3072:4096]
    xnew = ctx.A[:, 4096:5120]
    for m in range(nM):
        wsl = wtsb[:, (m % 2) * 4096:(m % 2 + 1) * 4096]
        p.dma("gpsimd", wsl, wt[m])
        for g in range(2):
            idx = m * 2 + g
            xsl = xc2[:, (idx % 2) * 512:(idx % 2 + 1) * 512]
            nsl = xnew[:, (idx % 2) * 512:(idx % 2 + 1) * 512]
            xdr = xd[m * 128:(m + 1) * 128, g * 512:(g + 1) * 512]
            mp = ctx.pC[:, 0:512] if idx % 2 == 0 else ctx.pD[:, 0:512]
            p.dma("sync", xsl, xdr)
            for k in range(32):
                rhs = src[:, k * 1024 + g * 512:k * 1024 + (g + 1) * 512]
                lhs = wsl[:, k * 128:(k + 1) * 128]
                p.mm(lambda t, o=mp, l=lhs, r=rhs, st=(k == 0),
                     sp=(k == 31 and extra_mm is None):
                     t.matmul(o, l, r, start=st, stop=sp), reads=[lhs, rhs], writes=[mp])
            if extra_mm is not None:
                extra_mm(p, mp, m, g)
            p.op("vector", lambda v, o=nsl, a=mp, b=xsl: v.scalar_tensor_tensor(
                 out=o, in0=a, scalar=scale, in1=b, op0=OP.mult, op1=OP.add),
                 reads=[mp, xsl], writes=[nsl])
            p.dma("sync", xdr, nsl)
        if m % 8 == 7:
            p.cut()


def emit_l0attn(ctx, xd, w):
    p = ctx.p
    lnx = ctx.LNX
    S8 = 1.0 / float(np.sqrt(128.0))
    KMAX = 32.0
    cosf = ctx.A[:, 0:1024]
    sinf = ctx.A[:, 1024:2048]
    p.dma("sync", cosf, w["cos"])
    p.dma("sync", sinf, w["sin"])
    wqs = ctx.B[:, 0:8192]
    wks = ctx.B[:, 8192:16384]
    wvs = ctx.C[:, 0:8192]
    qro = ctx.C[:, 8192:9216]
    kro = ctx.C[:, 9216:10240]
    vbf = ctx.C[:, 10240:11264]
    vtm = ctx.C[:, 12288:13312]
    qc = ctx.A[:, 2048:3072]
    qs = ctx.A[:, 3072:4096]
    kc = ctx.A[:, 4096:5120]
    ks_ = ctx.A[:, 5120:6144]
    for m in range(32):
        wq_m = wqs[:, (m % 2) * 4096:(m % 2 + 1) * 4096]
        wk_m = wks[:, (m % 2) * 4096:(m % 2 + 1) * 4096]
        wv_m = wvs[:, (m % 2) * 4096:(m % 2 + 1) * 4096]
        p.dma("gpsimd", wq_m, w["wq_t"][m])
        p.dma("gpsimd", wk_m, w["wk_t"][m])
        p.dma("gpsimd", wv_m, w["wv_t"][m])
        for g in range(2):
            for k in range(32):
                rhs = lnx[:, k * 1024 + g * 512:k * 1024 + (g + 1) * 512]
                for ws_, pp in ((wq_m, ctx.pA), (wk_m, ctx.pB), (wv_m, ctx.pC)):
                    o = pp[:, g * 512:(g + 1) * 512]
                    p.mm(lambda t, o=o, l=ws_[:, k * 128:(k + 1) * 128], r=rhs, st=(k == 0),
                         sp=(k == 31): t.matmul(o, l, r, start=st, stop=sp),
                         reads=[ws_[:, k * 128:(k + 1) * 128], rhs], writes=[o])
        p.op("vector", lambda v: v.tensor_copy(out=vbf, in_=ctx.pC[:, :]),
             reads=[ctx.pC[:, :]], writes=[vbf])
        for j in range(8):
            p.mm(lambda t, j=j: t.transpose(ctx.pE[:, j * 128:(j + 1) * 128],
                 vbf[:, j * 128:(j + 1) * 128], ctx.iden_bf[:, :]),
                 reads=[vbf[:, j * 128:(j + 1) * 128], ctx.iden_bf[:, :]],
                 writes=[ctx.pE[:, j * 128:(j + 1) * 128]])
        p.op("vector", lambda v: v.tensor_copy(out=vtm, in_=ctx.pE[:, :]),
             reads=[ctx.pE[:, :]], writes=[vtm])
        p.dma("sync", w["v_loc"][:, m * 128:(m + 1) * 128].rearrange("(j p) c -> p j c", p=128),
              vtm[:, :].rearrange("p (j c) -> p j c", c=128))
        for (pp, cc, ssn, ro) in ((ctx.pA, qc, qs, qro), (ctx.pB, kc, ks_, kro)):
            p.op("vector", lambda v, o=cc, a=pp: v.tensor_tensor(out=o, in0=a[:, :], in1=cosf,
                 op=OP.mult), reads=[pp[:, :], cosf], writes=[cc])
            p.op("vector", lambda v, o=ssn, a=pp: v.tensor_tensor(
                 out=o[0:64, :], in0=a[64:128, :], in1=sinf[64:128, :], op=OP.mult),
                 reads=[pp[:, :], sinf], writes=[ssn[0:64, :]])
            p.op("vector", lambda v, o=ssn, a=pp: v.tensor_tensor(
                 out=o[64:128, :], in0=a[0:64, :], in1=sinf[0:64, :], op=OP.mult),
                 reads=[pp[:, :], sinf], writes=[ssn[64:128, :]])
            p.op("gpsimd", lambda e, o=ro, a=cc, b=ssn: e.tensor_tensor(
                 out=o[0:64, :], in0=a[0:64, :], in1=b[0:64, :], op=OP.subtract),
                 reads=[cc[0:64, :], ssn[0:64, :]], writes=[ro[0:64, :]])
            p.op("gpsimd", lambda e, o=ro, a=cc, b=ssn: e.tensor_tensor(
                 out=o[64:128, :], in0=a[64:128, :], in1=b[64:128, :], op=OP.add),
                 reads=[cc[64:128, :], ssn[64:128, :]], writes=[ro[64:128, :]])
        p.dma("sync", w["q_loc"][m * 128:(m + 1) * 128, :], qro)
        p.dma("sync", w["k_bounce"][m * 128:(m + 1) * 128, :], kro)
        if m % 4 == 3:
            p.cut()
    p.cut()
    grp = [[0, 1, 2, 3], [4, 5, 6, 7]]
    p.collective("AllGather", OP.bypass, grp, [w["k_bounce"]], [w["k_all"]])
    p.collective("AllGather", OP.bypass, grp, [w["v_loc"]], [w["v_all"]])
    p.cut()
    mtab = ctx.C[:, 4096:12288]
    p.dma("gpsimd", mtab, w["mtab"])
    qh = ctx.B[:, 0:1024]
    ktile = ctx.B[:, 1024:2048]
    vtile = ctx.B[:, 2048:3072]
    kown = ctx.B[:, 3072:4096]
    qkp = ctx.C[:, 1536:2560]
    qsq2 = ctx.C[:, 2560:3584]
    negb = ctx.C[0:1, 3072:4096]
    rden = ctx.S[0:1, 0:512]
    d_row = ctx.A[0:1, 6144:7168]
    n_row = ctx.A[0:1, 7168:8192]
    t_row = ctx.A[0:1, 5120:6144]
    rbc = ctx.A[:, 2048:2560]
    stslots = [ctx.pA[:, 0:512], ctx.pA[:, 512:1024], ctx.pB[:, 0:512]]
    ptslots = [ctx.C[:, i * 512:(i + 1) * 512] for i in range(3)]
    sti = 0
    myrank = None
    for h in range(32):
        p.dma("sync", qh, w["q_loc"][h * 128:(h + 1) * 128, :])
        p.dma("sync", kown, w["k_bounce"][h * 128:(h + 1) * 128, :])
        p.op("vector", lambda v: v.tensor_tensor(out=qkp, in0=qh, in1=kown, op=OP.mult),
             reads=[qh, kown], writes=[qkp])
        p.op("scalar", lambda s: s.activation(out=qsq2, in_=qh, func=AF.Square),
             reads=[qh], writes=[qsq2])
        for g in range(2):
            p.mm(lambda t, r=qkp[:, g * 512:(g + 1) * 512], o=ctx.pD[0:1, 0:512]:
                 t.matmul(o, ctx.ones_col_bf[:, :], r, start=True, stop=True),
                 reads=[qkp[:, g * 512:(g + 1) * 512], ctx.ones_col_bf[:, :]],
                 writes=[ctx.pD[0:1, 0:512]])
            p.op("vector", lambda v, o=d_row[0:1, g * 512:(g + 1) * 512]: v.tensor_copy(
                 out=o, in_=ctx.pD[0:1, 0:512]), reads=[ctx.pD[0:1, 0:512]],
                 writes=[d_row[0:1, g * 512:(g + 1) * 512]])
            p.mm(lambda t, r=qsq2[:, g * 512:(g + 1) * 512], o=ctx.pD[0:1, 0:512]:
                 t.matmul(o, ctx.ones_col_bf[:, :], r, start=True, stop=True),
                 reads=[qsq2[:, g * 512:(g + 1) * 512], ctx.ones_col_bf[:, :]],
                 writes=[ctx.pD[0:1, 0:512]])
            p.op("vector", lambda v, o=n_row[0:1, g * 512:(g + 1) * 512]: v.tensor_copy(
                 out=o, in_=ctx.pD[0:1, 0:512]), reads=[ctx.pD[0:1, 0:512]],
                 writes=[n_row[0:1, g * 512:(g + 1) * 512]])
        p.op("scalar", lambda s: s.activation(out=t_row, in_=n_row, func=AF.Sqrt,
             bias=ctx.cst[0:1, 0:1], scale=1.0), reads=[n_row, ctx.cst[0:1, 0:1]],
             writes=[t_row])
        p.op("vector", lambda v: v.tensor_scalar(out=t_row, in0=t_row, scalar1=KMAX,
             scalar2=None, op0=OP.mult, op1=OP.bypass), reads=[t_row], writes=[t_row])
        p.op("vector", lambda v: v.tensor_scalar(out=d_row, in0=d_row, scalar1=80.0 / S8,
             scalar2=None, op0=OP.add, op1=OP.bypass), reads=[d_row], writes=[d_row])
        p.op("vector", lambda v: v.tensor_tensor(out=t_row, in0=t_row, in1=d_row, op=OP.min),
             reads=[t_row, d_row], writes=[t_row])
        p.op("vector", lambda v: v.tensor_scalar(out=negb, in0=t_row, scalar1=-1.0,
             scalar2=None, op0=OP.mult, op1=OP.bypass), reads=[t_row], writes=[negb])
        for qg in range(2):
            nstrips = 4 if qg == 0 else 8
            qsl = qh[:, qg * 512:(qg + 1) * 512]
            nbs = negb[0:1, qg * 512:(qg + 1) * 512]
            O = ctx.pC[:, 0:512]
            den = ctx.pB[0:1, 512:1024]
            npairs = nstrips * 4
            pi = 0
            for s_ in range(nstrips):
                r = min(s_, 7 - s_)
                off = 0 if s_ < 4 else 512
                kts = ktile[:, (s_ % 2) * 512:(s_ % 2 + 1) * 512]
                vts = vtile[:, (s_ % 2) * 512:(s_ % 2 + 1) * 512]
                p.dma("sync", kts, w["k_all"][r * 4096 + h * 128:r * 4096 + (h + 1) * 128,
                                              off:off + 512])
                p.dma("sync", vts[:, :].rearrange("p (j c) -> p j c", c=128),
                      w["v_all"][r * 1024 + off:r * 1024 + off + 512,
                                 h * 128:(h + 1) * 128].rearrange("(j p) c -> p j c", p=128))
                use_mask = (qg == 0) or (s_ >= 4)
                for kt in range(4):
                    st = stslots[sti % 3]
                    pt = ptslots[sti % 3]
                    sti += 1
                    p.mm(lambda t, o=st, l=kts[:, kt * 128:(kt + 1) * 128], r_=qsl:
                         t.matmul(o, l, r_, start=True, stop=False),
                         reads=[kts[:, kt * 128:(kt + 1) * 128], qsl], writes=[st])
                    p.mm(lambda t, o=st, r_=nbs: t.matmul(o, ctx.ones_row_bf[:, :], r_,
                         start=False, stop=not use_mask),
                         reads=[nbs, ctx.ones_row_bf[:, :]], writes=[st])
                    if use_mask:
                        mcol = s_ * 1024
                        msl = mtab[:, mcol + 384 - 128 * kt:mcol + 896 - 128 * kt]
                        p.mm(lambda t, o=st, m_=msl: t.matmul(o, ctx.iden_bf[:, :], m_,
                             start=False, stop=True), reads=[ctx.iden_bf[:, :], msl],
                             writes=[st])
                    p.op("scalar", lambda s, o=pt, i_=st: s.activation(
                         out=o, in_=i_, func=AF.Exp, scale=S8), reads=[st], writes=[pt])
                    p.mm(lambda t, o=O, l=vts[:, kt * 128:(kt + 1) * 128], r_=pt,
                         st_=(pi == 0), sp=(pi == npairs - 1):
                         t.matmul(o, l, r_, start=st_, stop=sp),
                         reads=[vts[:, kt * 128:(kt + 1) * 128], pt], writes=[O])
                    p.mm(lambda t, o=den, r_=pt, st_=(pi == 0), sp=(pi == npairs - 1):
                         t.matmul(o, ctx.ones_col_bf[:, :], r_, start=st_, stop=sp),
                         reads=[pt, ctx.ones_col_bf[:, :]], writes=[den])
                    pi += 1
            p.op("vector", lambda v: v.tensor_copy(out=rden, in_=den), reads=[den], writes=[rden])
            p.op("vector", lambda v: v.reciprocal(out=rden, in_=rden), reads=[rden], writes=[rden])
            p.mm(lambda t: t.matmul(ctx.pD[:, 0:512], ctx.ones_row32[:, :], rden,
                 start=True, stop=True), reads=[rden, ctx.ones_row32[:, :]],
                 writes=[ctx.pD[:, 0:512]])
            p.op("vector", lambda v: v.tensor_copy(out=rbc, in_=ctx.pD[:, 0:512]),
                 reads=[ctx.pD[:, 0:512]], writes=[rbc])
            p.op("vector", lambda v, o=lnx[:, h * 1024 + qg * 512:h * 1024 + (qg + 1) * 512],
                 O_=O: v.tensor_tensor(out=o, in0=O_, in1=rbc, op=OP.mult),
                 reads=[O, rbc], writes=[lnx[:, h * 1024 + qg * 512:h * 1024 + (qg + 1) * 512]])
        if h % 2 == 1:
            p.cut()
    p.cut()
    emit_stream_proj(ctx, xd, lnx, w["wo_t"], 1.0)
    p.cut()


def emit_l1ffn(ctx, xd, w, var_row):
    p = ctx.p
    lnx = ctx.LNX
    SR48 = 1.0 / float(np.sqrt(48.0))
    xck = ctx.A[:, 0:2048]
    xnk = ctx.A[:, 2048:4096]
    for k in range(32):
        a = xck[:, (k % 2) * 1024:(k % 2 + 1) * 1024]
        b = xnk[:, (k % 2) * 1024:(k % 2 + 1) * 1024]
        xdr = xd[k * 128:(k + 1) * 128, :]
        p.dma("sync", a, xdr)
        p.op("vector", lambda v, o=b, l=lnx[:, k * 1024:(k + 1) * 1024], a=a:
             v.scalar_tensor_tensor(out=o, in0=l, scalar=0.5, in1=a, op0=OP.mult, op1=OP.add),
             reads=[lnx[:, k * 1024:(k + 1) * 1024], a], writes=[b])
        p.dma("sync", xdr, b)
    p.cut()
    scl3 = ctx.S[0:1, 74:77]
    p.dma("sync", scl3[0:1, 0:1], w["bu2"])
    p.dma("sync", scl3[0:1, 1:2], w["gate_w1"])
    p.dma("sync", scl3[0:1, 2:3], w["gate_bias"])
    vrec = ctx.A[0:1, 4096:5120]
    vhr = ctx.A[0:1, 5120:6144]
    logr = ctx.A[0:1, 6144:7168]
    srow = ctx.A[0:1, 7168:8192]
    p.op("vector", lambda v: v.tensor_scalar(out=vrec, in0=var_row, scalar1=1e-5, scalar2=None,
         op0=OP.add, op1=OP.bypass), reads=[var_row], writes=[vrec])
    p.op("vector", lambda v: v.reciprocal(out=vrec, in_=vrec), reads=[vrec], writes=[vrec])
    p.op("vector", lambda v: v.tensor_tensor(out=vhr, in0=var_row, in1=vrec, op=OP.mult),
         reads=[var_row, vrec], writes=[vhr])
    p.op("scalar", lambda s: s.activation(out=logr, in_=vhr, func=AF.Ln,
         bias=ctx.cst[0:1, 1:2], scale=1.0), reads=[vhr, ctx.cst[0:1, 1:2]], writes=[logr])
    p.op("vector", lambda v: v.reduce_sum(out=srow[0:1, 0:1], in_=logr,
         axis=mybir.AxisListType.X), reads=[logr], writes=[srow[0:1, 0:1]])
    p.dma("gpsimd", w["cb2_in"], srow[0:1, 0:1])
    p.collective("AllReduce", OP.add, [list(range(8))], [w["cb2_in"]], [w["cb2_out"]])
    p.dma("gpsimd", srow[0:1, 1:2], w["cb2_out"])
    p.op("vector", lambda v: v.tensor_scalar(out=srow[0:1, 2:3], in0=srow[0:1, 1:2],
         scalar1=1.0 / 8192.0, scalar2=1e-6, op0=OP.mult, op1=OP.add),
         reads=[srow[0:1, 1:2]], writes=[srow[0:1, 2:3]])
    p.op("vector", lambda v: v.reciprocal(out=srow[0:1, 3:4], in_=srow[0:1, 2:3]),
         reads=[srow[0:1, 2:3]], writes=[srow[0:1, 3:4]])
    nv = ctx.A[0:1, 4096:5120]
    p.op("vector", lambda v: v.tensor_scalar(out=nv, in0=logr, scalar1=srow[0:1, 3:4],
         scalar2=0.5, op0=OP.mult, op1=OP.mult), reads=[logr, srow[0:1, 3:4]], writes=[nv])
    wu1s = ctx.B[:, 0:8192]
    bu1c = ctx.S[:, 8:16]
    wu2c = ctx.C[:, 15360:15368]
    gu = ctx.C[:, 0:8192]
    u32 = ctx.A[:, 0:1024]
    t1 = ctx.A[:, 1024:2048]
    t2 = ctx.A[:, 2048:3072]
    p.dma("sync", bu1c, w["bu1_t"])
    p.dma("gpsimd", wu2c, w["wu2_t"])
    for m in range(8):
        ws = wu1s[:, (m % 2) * 4096:(m % 2 + 1) * 4096]
        p.dma("gpsimd", ws, w["wu1_t"][m])
        for g in range(2):
            o = ctx.pA[:, g * 512:(g + 1) * 512]
            for k in range(32):
                rhs = lnx[:, k * 1024 + g * 512:k * 1024 + (g + 1) * 512]
                p.mm(lambda t, o=o, l=ws[:, k * 128:(k + 1) * 128], r=rhs, st=(k == 0),
                     sp=(k == 31): t.matmul(o, l, r, start=st, stop=sp),
                     reads=[ws[:, k * 128:(k + 1) * 128], rhs], writes=[o])
        p.op("vector", lambda v, m=m: v.tensor_scalar(out=u32, in0=ctx.pA[:, :],
             scalar1=bu1c[:, m:m + 1], scalar2=None, op0=OP.add, op1=OP.bypass),
             reads=[ctx.pA[:, :], bu1c[:, m:m + 1]], writes=[u32])
        p.op("scalar", lambda s: s.activation(out=t1, in_=u32, func=AF.Square),
             reads=[u32], writes=[t1])
        p.op("vector", lambda v: v.tensor_scalar(out=t2, in0=t1, scalar1=0.044715,
             scalar2=1.0, op0=OP.mult, op1=OP.add), reads=[t1], writes=[t2])
        p.op("vector", lambda v: v.tensor_tensor(out=t2, in0=u32, in1=t2, op=OP.mult),
             reads=[u32, t2], writes=[t2])
        p.op("scalar", lambda s: s.activation(out=t2, in_=t2, func=AF.Tanh,
             scale=0.7978845608028654), reads=[t2], writes=[t2])
        p.op("vector", lambda v: v.tensor_scalar(out=t2, in0=t2, scalar1=1.0, scalar2=None,
             op0=OP.add, op1=OP.bypass), reads=[t2], writes=[t2])
        p.op("vector", lambda v, m=m: v.scalar_tensor_tensor(out=gu[:, m * 1024:(m + 1) * 1024],
             in0=u32, scalar=0.5, in1=t2, op0=OP.mult, op1=OP.mult),
             reads=[u32, t2], writes=[gu[:, m * 1024:(m + 1) * 1024]])
    for m in range(8):
        for g in range(2):
            o = ctx.pB[0:1, g * 512:(g + 1) * 512]
            p.mm(lambda t, o=o, l=wu2c[:, m:m + 1], r=gu[:, m * 1024 + g * 512:m * 1024 + (g + 1) * 512],
                 st=(m == 0), sp=(m == 7): t.matmul(o, l, r, start=st, stop=sp),
                 reads=[wu2c[:, m:m + 1], gu[:, m * 1024 + g * 512:m * 1024 + (g + 1) * 512]],
                 writes=[o])
    sig = ctx.A[0:1, 5120:6144]
    p.op("scalar", lambda s: s.activation(out=sig, in_=ctx.pB[0:1, 0:1024], func=AF.Sigmoid,
         bias=scl3[0:1, 0:1], scale=1.0), reads=[ctx.pB[0:1, 0:1024], scl3[0:1, 0:1]],
         writes=[sig])
    gfin = ctx.A[0:1, 6144:7168]
    p.op("vector", lambda v: v.tensor_scalar(out=sig, in0=sig, scalar1=2.5, scalar2=None,
         op0=OP.mult, op1=OP.bypass), reads=[sig], writes=[sig])
    p.op("vector", lambda v: v.tensor_tensor(out=sig, in0=nv, in1=sig, op=OP.add),
         reads=[nv, sig], writes=[sig])
    p.op("vector", lambda v: v.tensor_scalar(out=sig, in0=sig, scalar1=0.0, scalar2=5.0,
         op0=OP.max, op1=OP.min), reads=[sig], writes=[sig])
    p.op("vector", lambda v: v.tensor_scalar(out=sig, in0=sig, scalar1=-0.5, scalar2=1.0 / 1.5,
         op0=OP.add, op1=OP.mult), reads=[sig], writes=[sig])
    p.op("vector", lambda v: v.tensor_scalar(out=sig, in0=sig, scalar1=0.0, scalar2=1.0,
         op0=OP.max, op1=OP.min), reads=[sig], writes=[sig])
    p.op("vector", lambda v: v.tensor_scalar(out=sig, in0=sig, scalar1=scl3[0:1, 1:2],
         scalar2=scl3[0:1, 2:3], op0=OP.mult, op1=OP.add),
         reads=[sig, scl3[0:1, 1:2], scl3[0:1, 2:3]], writes=[sig])
    p.op("scalar", lambda s: s.activation(out=sig, in_=sig, func=AF.Sigmoid),
         reads=[sig], writes=[sig])
    p.op("vector", lambda v: v.tensor_scalar(out=gfin, in0=sig, scalar1=0.05, scalar2=None,
         op0=OP.is_ge, op1=OP.bypass), reads=[sig], writes=[gfin])
    p.op("vector", lambda v: v.tensor_tensor(out=gfin, in0=gfin, in1=sig, op=OP.mult),
         reads=[gfin, sig], writes=[gfin])
    p.op("vector", lambda v: v.tensor_scalar(out=gfin, in0=gfin, scalar1=0.5, scalar2=None,
         op0=OP.mult, op1=OP.bypass), reads=[gfin], writes=[gfin])
    p.cut()
    wqt = ctx.C[:, 13312:15360]
    p.dma("gpsimd", wqt, w["wq_t2"])
    qbf = ctx.C[0:64, 8192:9216]
    q8a = ctx.C[0:65, 9216:10240]
    rqa = ctx.C[0:49, 10240:11264]
    rka = ctx.C[0:49, 11264:12288]
    k2a = ctx.C[0:65, 12288:13312]
    wrt = ctx.C[0:64, 15368:15416]
    keysf = ctx.A[0:64, 3072:4096]
    p.dma("gpsimd", wrt, w["wr_t"])
    p.dma("sync", keysf, w["keys_f"])
    p.dma("gpsimd", k2a[64:65, :], w["log_rel"])
    p.dma("gpsimd", rka[48:49, :], w["log_rel"])
    for g in range(2):
        o = ctx.pA[0:64, g * 512:(g + 1) * 512]
        for k in range(32):
            rhs = lnx[:, k * 1024 + g * 512:k * 1024 + (g + 1) * 512]
            p.mm(lambda t, o=o, l=wqt[:, k * 64:(k + 1) * 64], r=rhs, st=(k == 0), sp=(k == 31):
                 t.matmul(o, l, r, start=st, stop=sp),
                 reads=[wqt[:, k * 64:(k + 1) * 64], rhs], writes=[o])
    p.op("gpsimd", lambda g: g.memset(q8a[:, :], 1.0), writes=[q8a[:, :]])
    p.op("gpsimd", lambda g: g.memset(rqa[:, :], 1.0), writes=[rqa[:, :]])
    p.op("vector", lambda v: v.tensor_copy(out=qbf, in_=ctx.pA[0:64, :]),
         reads=[ctx.pA[0:64, :]], writes=[qbf])
    p.op("vector", lambda v: v.tensor_scalar(out=q8a[0:64, :], in0=ctx.pA[0:64, :],
         scalar1=0.125, scalar2=None, op0=OP.mult, op1=OP.bypass),
         reads=[ctx.pA[0:64, :]], writes=[q8a[0:64, :]])
    p.op("vector", lambda v: v.tensor_copy(out=k2a[0:64, :], in_=keysf),
         reads=[keysf], writes=[k2a[0:64, :]])
    for g in range(2):
        o = ctx.pB[0:48, g * 512:(g + 1) * 512]
        p.mm(lambda t, o=o, r=qbf[:, g * 512:(g + 1) * 512]: t.matmul(o, wrt, r,
             start=True, stop=True), reads=[wrt, qbf[:, g * 512:(g + 1) * 512]], writes=[o])
    p.op("vector", lambda v: v.tensor_scalar(out=rqa[0:48, :], in0=ctx.pB[0:48, :],
         scalar1=SR48, scalar2=None, op0=OP.mult, op1=OP.bypass),
         reads=[ctx.pB[0:48, :]], writes=[rqa[0:48, :]])
    for g in range(2):
        o = ctx.pB[0:48, g * 512:(g + 1) * 512]
        p.mm(lambda t, o=o, r=k2a[0:64, g * 512:(g + 1) * 512]: t.matmul(o, wrt, r,
             start=True, stop=True), reads=[wrt, k2a[0:64, g * 512:(g + 1) * 512]], writes=[o])
    p.op("vector", lambda v: v.tensor_copy(out=rka[0:48, :], in_=ctx.pB[0:48, :]),
         reads=[ctx.pB[0:48, :]], writes=[rka[0:48, :]])
    for sc in range(8):
        p.dma("gpsimd", lnx[:, sc * 4096:(sc + 1) * 4096],
              w["values"][sc * 128:(sc + 1) * 128, :])
    w_all = ctx.B[:, 0:8192]
    expS = ctx.B[:, 8192:9216]
    m8 = ctx.S[:, 0:8]
    mx = ctx.S[:, 72:74]
    sums = ctx.S[:, 80:88]
    rs = ctx.S[:, 88:96]
    Mscr = ctx.A[:, 1024:2048]
    Mt = ctx.A[:, 2048:3072]
    for tt in range(8):
        for g in range(2):
            o = ctx.pC[:, g * 512:(g + 1) * 512]
            p.mm(lambda t, o=o, l=rqa[:, tt * 128:(tt + 1) * 128],
                 r=rka[:, g * 512:(g + 1) * 512]: t.matmul(o, l, r, start=True, stop=True),
                 reads=[rqa[:, tt * 128:(tt + 1) * 128], rka[:, g * 512:(g + 1) * 512]],
                 writes=[o])
        p.op("vector", lambda v: v.max(out=m8, in_=ctx.pC[:, :]), reads=[ctx.pC[:, :]],
             writes=[m8])
        p.op("vector", lambda v: v.match_replace(out=Mscr, in_to_replace=m8,
             in_values=ctx.pC[:, :], imm_value=-1e30), reads=[m8, ctx.pC[:, :]], writes=[Mscr])
        p.op("vector", lambda v: v.tensor_tensor(out=Mt, in0=ctx.pC[:, :], in1=Mscr,
             op=OP.subtract), reads=[ctx.pC[:, :], Mscr], writes=[Mt])
        p.op("vector", lambda v: v.tensor_scalar(out=Mt, in0=Mt, scalar1=1.0, scalar2=None,
             op0=OP.min, op1=OP.bypass), reads=[Mt], writes=[Mt])
        for g in range(2):
            o = ctx.pB[:, g * 512:(g + 1) * 512]
            p.mm(lambda t, o=o, l=q8a[:, tt * 128:(tt + 1) * 128],
                 r=k2a[:, g * 512:(g + 1) * 512]: t.matmul(o, l, r, start=True, stop=True),
                 reads=[q8a[:, tt * 128:(tt + 1) * 128], k2a[:, g * 512:(g + 1) * 512]],
                 writes=[o])
        p.op("vector", lambda v: v.reduce_max(out=mx[:, 0:1], in_=ctx.pB[:, :],
             axis=mybir.AxisListType.X), reads=[ctx.pB[:, :]], writes=[mx[:, 0:1]])
        p.op("vector", lambda v: v.tensor_scalar(out=mx[:, 1:2], in0=mx[:, 0:1], scalar1=-1.0,
             scalar2=None, op0=OP.mult, op1=OP.bypass), reads=[mx[:, 0:1]], writes=[mx[:, 1:2]])
        p.op("scalar", lambda s: s.activation(out=expS, in_=ctx.pB[:, :], func=AF.Exp,
             bias=mx[:, 1:2], scale=1.0), reads=[ctx.pB[:, :], mx[:, 1:2]], writes=[expS])
        p.op("vector", lambda v, tt=tt: v.scalar_tensor_tensor(
             out=w_all[:, tt * 1024:(tt + 1) * 1024], in0=expS, scalar=1.0, in1=Mt,
             op0=OP.mult, op1=OP.mult, accum_out=sums[:, tt:tt + 1]),
             reads=[expS, Mt], writes=[w_all[:, tt * 1024:(tt + 1) * 1024], sums[:, tt:tt + 1]])
    p.op("vector", lambda v: v.reciprocal(out=rs, in_=sums), reads=[sums], writes=[rs])
    for tt in range(8):
        p.op("vector", lambda v, tt=tt: v.tensor_scalar(
             out=w_all[:, tt * 1024:(tt + 1) * 1024], in0=w_all[:, tt * 1024:(tt + 1) * 1024],
             scalar1=rs[:, tt:tt + 1], scalar2=None, op0=OP.mult, op1=OP.bypass),
             reads=[w_all[:, tt * 1024:(tt + 1) * 1024], rs[:, tt:tt + 1]],
             writes=[w_all[:, tt * 1024:(tt + 1) * 1024]])
    p.cut()
    wT = ctx.B[:, 8192:16384]
    for tt in range(8):
        for sc in range(8):
            src = w_all[:, tt * 1024 + sc * 128:tt * 1024 + (sc + 1) * 128]
            p.mm(lambda t, s_=src, o=ctx.pE[:, (sc % 8) * 128:(sc % 8 + 1) * 128]:
                 t.transpose(o, s_, ctx.iden_bf[:, :]),
                 reads=[src, ctx.iden_bf[:, :]],
                 writes=[ctx.pE[:, (sc % 8) * 128:(sc % 8 + 1) * 128]])
        for sc in range(8):
            p.op("vector", lambda v, tt=tt, sc=sc: v.tensor_copy(
                 out=wT[:, sc * 1024 + tt * 128:sc * 1024 + (tt + 1) * 128],
                 in_=ctx.pE[:, sc * 128:(sc + 1) * 128]),
                 reads=[ctx.pE[:, sc * 128:(sc + 1) * 128]],
                 writes=[wT[:, sc * 1024 + tt * 128:sc * 1024 + (tt + 1) * 128]])
    p.cut()
    wvds = ctx.C[:, 0:8192]
    aux1c = ctx.C[:, 8192:9216]
    p.dma("gpsimd", wvds[:, 0:4096], w["wvd_t"][0])
    p.dma("gpsimd", wvds[:, 4096:8192], w["wvd_t"][1])
    for m in range(32):
        for g in range(2):
            o = ctx.pC[:, g * 512:(g + 1) * 512]
            for sc in range(8):
                l = lnx[:, sc * 4096 + m * 128:sc * 4096 + (m + 1) * 128]
                r = wT[:, sc * 1024 + g * 512:sc * 1024 + (g + 1) * 512]
                p.mm(lambda t, o=o, l=l, r=r, st=(sc == 0), sp=(sc == 7):
                     t.matmul(o, l, r, start=st, stop=sp), reads=[l, r], writes=[o])
        p.op("vector", lambda v: v.tensor_copy(out=aux1c, in_=ctx.pC[:, :]),
             reads=[ctx.pC[:, :]], writes=[aux1c])
        for mv in range(2):
            pp = ctx.pA if mv == 0 else ctx.pB
            for g in range(2):
                o = pp[:, g * 512:(g + 1) * 512]
                l = wvds[:, mv * 4096 + m * 128:mv * 4096 + (m + 1) * 128]
                p.mm(lambda t, o=o, l=l, r=aux1c[:, g * 512:(g + 1) * 512], st=(m == 0),
                     sp=(m == 31): t.matmul(o, l, r, start=st, stop=sp),
                     reads=[l, aux1c[:, g * 512:(g + 1) * 512]], writes=[o])
        if m % 8 == 7:
            p.cut()
    aux2 = ctx.C[:, 8192:10240]
    p.op("vector", lambda v: v.tensor_copy(out=aux2[:, 0:1024], in_=ctx.pA[:, :]),
         reads=[ctx.pA[:, :]], writes=[aux2[:, 0:1024]])
    p.op("vector", lambda v: v.tensor_copy(out=aux2[:, 1024:2048], in_=ctx.pB[:, :]),
         reads=[ctx.pB[:, :]], writes=[aux2[:, 1024:2048]])
    gq_bc = ctx.A[:, 2048:3072]
    for g in range(2):
        p.mm(lambda t, g=g: t.matmul(ctx.pD[:, 0:512], ctx.ones_row32[:, :],
             gfin[0:1, g * 512:(g + 1) * 512], start=True, stop=True),
             reads=[gfin[0:1, g * 512:(g + 1) * 512], ctx.ones_row32[:, :]],
             writes=[ctx.pD[:, 0:512]])
        p.op("vector", lambda v, g=g: v.tensor_copy(out=gq_bc[:, g * 512:(g + 1) * 512],
             in_=ctx.pD[:, 0:512]), reads=[ctx.pD[:, 0:512]],
             writes=[gq_bc[:, g * 512:(g + 1) * 512]])
    p.cut()
    wvus = ctx.C[:, 0:8192]
    xc3 = ctx.A[:, 0:1024]
    xn3 = ctx.A[:, 1024:2048]
    for m in range(32):
        ws = wvus[:, (m % 2) * 4096:(m % 2) * 4096 + 256]
        p.dma("gpsimd", ws, w["wvu_t"][m])
        for g in range(2):
            idx = m * 2 + g
            o = ctx.pC[:, 0:512] if idx % 2 == 0 else ctx.pD[:, 0:512]
            xsl = xc3[:, (idx % 2) * 512:(idx % 2 + 1) * 512]
            nsl = xn3[:, (idx % 2) * 512:(idx % 2 + 1) * 512]
            xdr = xd[m * 128:(m + 1) * 128, g * 512:(g + 1) * 512]
            p.dma("sync", xsl, xdr)
            for k in range(2):
                l = ws[:, k * 128:(k + 1) * 128]
                r = aux2[:, k * 1024 + g * 512:k * 1024 + (g + 1) * 512]
                p.mm(lambda t, o=o, l=l, r=r, st=(k == 0), sp=(k == 1):
                     t.matmul(o, l, r, start=st, stop=sp), reads=[l, r], writes=[o])
            p.op("vector", lambda v, o=nsl, a=o, b=gq_bc[:, g * 512:(g + 1) * 512]:
                 v.tensor_tensor(out=o, in0=a, in1=b, op=OP.mult),
                 reads=[o, gq_bc[:, g * 512:(g + 1) * 512]], writes=[nsl])
            p.op("vector", lambda v, o=nsl, b=xsl: v.tensor_tensor(out=o, in0=o, in1=b,
                 op=OP.add), reads=[nsl, xsl], writes=[nsl])
            p.dma("sync", xdr, nsl)
        if m % 8 == 7:
            p.cut()
    p.cut()


def emit_l1attn(ctx, xd, w):
    p = ctx.p
    lnx = ctx.LNX
    NEG = -1e30
    wstr = ctx.B[:, 0:8192]
    evc = ctx.C[:, 0:2048]
    halo_sb = ctx.C[:, 2048:2528]
    dtraw = ctx.C[:, 2528:3552]
    for m in range(145):
        ws = wstr[:, (m % 2) * 4096:(m % 2 + 1) * 4096]
        p.dma("gpsimd", ws, w["inproj_t"][m])
        for g in range(2):
            o = ctx.pA[:, 0:512] if (m * 2 + g) % 2 == 0 else ctx.pB[:, 0:512]
            for k in range(32):
                rhs = lnx[:, k * 1024 + g * 512:k * 1024 + (g + 1) * 512]
                p.mm(lambda t, o=o, l=ws[:, k * 128:(k + 1) * 128], r=rhs, st=(k == 0),
                     sp=(k == 31): t.matmul(o, l, r, start=st, stop=sp),
                     reads=[ws[:, k * 128:(k + 1) * 128], rhs], writes=[o])
            ev = evc[:, ((m * 2 + g) % 2) * 1024:((m * 2 + g) % 2) * 1024 + 512]
            if m == 144:
                ev = dtraw[:, g * 512:(g + 1) * 512]
            p.op("vector", lambda v, o=ev, a=o: v.tensor_copy(out=o, in_=a),
                 reads=[o], writes=[ev])
            if m < 144:
                if m < 64:
                    dst = w["gate_d"][m * 128:(m + 1) * 128, g * 512:(g + 1) * 512]
                else:
                    dst = w["xbc_d"][(m - 64) * 128:(m - 63) * 128, g * 512:(g + 1) * 512]
                p.dma("sync", dst, ev)
                if m >= 64:
                    mc = m - 64
                    hs = halo_sb[:, mc * 6 + (g * 3):mc * 6 + (g * 3) + 3]
                    p.op("gpsimd", lambda e, o=hs, a=ev: e.tensor_copy(out=o, in_=a[:, 509:512]),
                         reads=[ev], writes=[hs])
        if m % 8 == 7:
            p.cut()
    p.dma("sync", w["halo_snd"], halo_sb)
    p.cut()
    grp = [[0, 1, 2, 3], [4, 5, 6, 7]]
    p.collective("AllGather", OP.bypass, grp, [w["halo_snd"]], [w["halo_all"]])
    dt_f = ctx.A[:, 0:1024]
    dA_f = ctx.A[:, 1024:2048]
    cum = ctx.A[:, 2048:3072]
    sc1 = ctx.A[:, 3072:4096]
    sc2 = ctx.A[:, 4096:5120]
    dtb = ctx.S[:, 0:1]
    alog = ctx.S[:, 1:2]
    acol = ctx.S[:, 2:3]
    p.dma("sync", dtb, w["dtb"])
    p.dma("sync", alog, w["alog"])
    p.op("vector", lambda v: v.tensor_scalar(out=sc1, in0=dtraw, scalar1=dtb, scalar2=None,
         op0=OP.add, op1=OP.bypass), reads=[dtraw, dtb], writes=[sc1])
    p.op("scalar", lambda s: s.activation(out=sc2, in_=sc1, func=AF.Abs),
         reads=[sc1], writes=[sc2])
    p.op("scalar", lambda s: s.activation(out=sc2, in_=sc2, func=AF.Exp, scale=-1.0),
         reads=[sc2], writes=[sc2])
    p.op("scalar", lambda s: s.activation(out=sc2, in_=sc2, func=AF.Ln,
         bias=ctx.cst[:, 1:2], scale=1.0), reads=[sc2, ctx.cst[:, 1:2]], writes=[sc2])
    p.op("vector", lambda v: v.tensor_scalar(out=sc1, in0=sc1, scalar1=0.0, scalar2=None,
         op0=OP.max, op1=OP.bypass), reads=[sc1], writes=[sc1])
    p.op("vector", lambda v: v.tensor_tensor(out=dt_f, in0=sc1, in1=sc2, op=OP.add),
         reads=[sc1, sc2], writes=[dt_f])
    p.op("scalar", lambda s: s.activation(out=acol, in_=alog, func=AF.Exp), reads=[alog],
         writes=[acol])
    p.op("vector", lambda v: v.tensor_scalar(out=acol, in0=acol, scalar1=-1.0, scalar2=None,
         op0=OP.mult, op1=OP.bypass), reads=[acol], writes=[acol])
    p.op("vector", lambda v: v.tensor_scalar(out=dA_f, in0=dt_f, scalar1=acol, scalar2=None,
         op0=OP.mult, op1=OP.bypass), reads=[dt_f, acol], writes=[dA_f])
    for c in range(8):
        p.op("vector", lambda v, c=c: v.tensor_tensor_scan(
             out=cum[:, c * 128:(c + 1) * 128], data0=dA_f[:, c * 128:(c + 1) * 128],
             data1=dA_f[:, c * 128:(c + 1) * 128], initial=0.0, op0=OP.add, op1=OP.bypass),
             reads=[dA_f[:, c * 128:(c + 1) * 128]], writes=[cum[:, c * 128:(c + 1) * 128]])
    p.dma("sync", w["cum_d"], cum)
    cum_tm = ctx.A[:, 5120:6144]
    dt_tm = ctx.A[:, 6144:7168]
    for c in range(8):
        p.mm(lambda t, c=c: t.transpose(ctx.pD[:, 0:128], cum[:, c * 128:(c + 1) * 128],
             ctx.iden32[:, :]), reads=[cum[:, c * 128:(c + 1) * 128], ctx.iden32[:, :]],
             writes=[ctx.pD[:, 0:512]])
        p.op("vector", lambda v, c=c: v.tensor_copy(out=cum_tm[:, c * 128:(c + 1) * 128],
             in_=ctx.pD[:, 0:128]), reads=[ctx.pD[:, 0:512]],
             writes=[cum_tm[:, c * 128:(c + 1) * 128]])
        p.mm(lambda t, c=c: t.transpose(ctx.pD[:, 0:128], dt_f[:, c * 128:(c + 1) * 128],
             ctx.iden32[:, :]), reads=[dt_f[:, c * 128:(c + 1) * 128], ctx.iden32[:, :]],
             writes=[ctx.pD[:, 0:512]])
        p.op("vector", lambda v, c=c: v.tensor_copy(out=dt_tm[:, c * 128:(c + 1) * 128],
             in_=ctx.pD[:, 0:128]), reads=[ctx.pD[:, 0:512]],
             writes=[dt_tm[:, c * 128:(c + 1) * 128]])
    coef_tm = ctx.A[:, 7168:8192]
    coefs = ctx.S[:, 16:1040]
    for c in range(8):
        cl = ctx.S[:, 4 + c:5 + c]
        p.op("vector", lambda v, c=c, cl=cl: v.tensor_copy(out=cl,
             in_=cum[:, c * 128 + 127:c * 128 + 128]),
             reads=[cum[:, c * 128 + 127:c * 128 + 128]], writes=[cl])
        p.op("scalar", lambda s, c=c, cl=cl: s.activation(out=coefs[:, c * 128:(c + 1) * 128],
             in_=cum[:, c * 128:(c + 1) * 128], func=AF.Exp, bias=cl, scale=-1.0),
             reads=[cum[:, c * 128:(c + 1) * 128], cl], writes=[coefs[:, c * 128:(c + 1) * 128]])
    p.op("vector", lambda v: v.tensor_tensor(out=coefs, in0=dt_f, in1=coefs, op=OP.mult),
         reads=[dt_f, coefs], writes=[coefs])
    for c in range(8):
        p.mm(lambda t, c=c: t.transpose(ctx.pD[:, 0:128], coefs[:, c * 128:(c + 1) * 128],
             ctx.iden32[:, :]), reads=[coefs[:, c * 128:(c + 1) * 128], ctx.iden32[:, :]],
             writes=[ctx.pD[:, 0:512]])
        p.op("vector", lambda v, c=c: v.tensor_copy(out=coef_tm[:, c * 128:(c + 1) * 128],
             in_=ctx.pD[:, 0:128]), reads=[ctx.pD[:, 0:512]],
             writes=[coef_tm[:, c * 128:(c + 1) * 128]])
    p.cut()
    ext = ctx.C[:, 3552:4582]
    cdiag = ctx.C[:, 4608:5632]
    cb = ctx.S[:, 1040:1120]
    hsel = ctx.S[:, 1120:1136]
    p.dma("sync", cb, w["conv_b"])
    p.dma("sync", hsel, w["hsel"])
    for mc in range(80):
        cds = cdiag[:, (mc % 2) * 512:(mc % 2 + 1) * 512]
        p.dma("gpsimd", cds, w["convd_t"][mc])
        p.dma("sync", ext[:, 3:515], w["xbc_d"][mc * 128:(mc + 1) * 128, 0:512])
        p.dma("sync", ext[:, 518:1030], w["xbc_d"][mc * 128:(mc + 1) * 128, 512:1024])
        p.op("gpsimd", lambda e: e.memset(ext[:, 0:3], 0.0), writes=[ext[:, 0:3]])
        p.op("gpsimd", lambda e: e.memset(ext[:, 515:518], 0.0), writes=[ext[:, 515:518]])
        for cand in range(8):
            r, sl = (cand // 2), (cand % 2)
            hsrc = ctx.B[:, cand * 3:cand * 3 + 3]
            p.dma("sync", hsrc, w["halo_all"][r * 128:(r + 1) * 128,
                                              mc * 6 + sl * 3:mc * 6 + sl * 3 + 3])
            for dsti, dcol in ((0, 0), (1, 515)):
                p.op("vector", lambda v, h=hsrc, d=ext[:, dcol:dcol + 3],
                     sc=hsel[:, dsti * 8 + cand:dsti * 8 + cand + 1]:
                     v.scalar_tensor_tensor(out=d, in0=h, scalar=sc, in1=d,
                     op0=OP.mult, op1=OP.add),
                     reads=[hsrc, hsel[:, dsti * 8 + cand:dsti * 8 + cand + 1],
                            ext[:, dcol:dcol + 3]],
                     writes=[ext[:, dcol:dcol + 3]])
        for st_ in range(2):
            o = ctx.pC[:, st_ * 512:(st_ + 1) * 512]
            for j in range(4):
                r_ = ext[:, st_ * 515 + j:st_ * 515 + j + 512]
                p.mm(lambda t, o=o, l=cds[:, j * 128:(j + 1) * 128], rr=r_, stt=(j == 0),
                     sp=(j == 3): t.matmul(o, l, rr, start=stt, stop=sp),
                     reads=[cds[:, j * 128:(j + 1) * 128], r_], writes=[o])
        po = ctx.C[:, 5632:6656]
        p.op("scalar", lambda s, mc=mc: s.activation(out=po, in_=ctx.pC[:, :], func=AF.Silu,
             bias=cb[:, mc:mc + 1], scale=1.0), reads=[ctx.pC[:, :], cb[:, mc:mc + 1]],
             writes=[po])
        if mc < 64:
            p.dma("sync", w["xss_d"][mc * 128:(mc + 1) * 128, :], po)
        elif mc < 72:
            g_ = mc - 64
            p.dma("sync", w["bfc_d"][:, g_ * 1024:(g_ + 1) * 1024], po)
        else:
            g_ = mc - 72
            p.dma("sync", w["cfc_d"][:, g_ * 1024:(g_ + 1) * 1024], po)
        if mc % 8 == 7:
            p.cut()
    p.cut()

def emit_l1attn_state(ctx, xd, w):
    p = ctx.p
    dt_tm = ctx.A[:, 6144:7168]
    coef_tm = ctx.A[:, 7168:8192]
    prev = ctx.B[:, 0:8192]
    bct = ctx.B[:, 8192:9216]
    btm = ctx.B[:, 9216:10240]
    x_tm = ctx.C[:, 0:8192]
    cdecb = ctx.LNX[:, 0:8192]
    b4 = ctx.LNX[:, 8192:8704]
    stsb = ctx.LNX[:, 8704:9216]
    t1s = ctx.LNX[:, 9216:9728]
    cde = ctx.R[0:1, 0:8192]
    p.op("gpsimd", lambda g: g.memset(prev[:, :], 0.0), writes=[prev[:, :]])
    grp = [[0, 1, 2, 3], [4, 5, 6, 7]]
    for gc in range(8):
        ss, c = gc // 4, gc % 4
        for kk in range(8):
            p.dma("sync", x_tm[:, kk * 1024:(kk + 1) * 1024].rearrange("p (k t) -> p k t", t=128),
                  w["xss_d"][kk * 1024:(kk + 1) * 1024, gc * 128:(gc + 1) * 128]
                  .rearrange("(k p) t -> p k t", p=128))
        for k8 in range(8):
            for k in range(k8 * 8, k8 * 8 + 8):
                src = x_tm[:, k * 128:(k + 1) * 128]
                p.mm(lambda t, s_=src, k=k: t.transpose(
                     ctx.pE[:, (k % 8) * 128:(k % 8 + 1) * 128], s_, ctx.iden_bf[:, :]),
                     reads=[src, ctx.iden_bf[:, :]], writes=[ctx.pE[:, 0:1024]])
            p.op("vector", lambda v, k8=k8: v.tensor_copy(
                 out=x_tm[:, k8 * 1024:(k8 + 1) * 1024], in_=ctx.pE[:, :]),
                 reads=[ctx.pE[:, 0:1024]], writes=[x_tm[:, k8 * 1024:(k8 + 1) * 1024]])
        for g in range(8):
            p.dma("sync", bct[:, g * 128:(g + 1) * 128],
                  w["bfc_d"][:, g * 1024 + gc * 128:g * 1024 + (gc + 1) * 128])
            src = bct[:, g * 128:(g + 1) * 128]
            p.mm(lambda t, s_=src: t.transpose(ctx.pE[:, 0:128], s_, ctx.iden_bf[:, :]),
                 reads=[src, ctx.iden_bf[:, :]], writes=[ctx.pE[:, 0:1024]])
            p.op("vector", lambda v, g=g: v.tensor_copy(out=btm[:, g * 128:(g + 1) * 128],
                 in_=ctx.pE[:, 0:128]), reads=[ctx.pE[:, 0:1024]],
                 writes=[btm[:, g * 128:(g + 1) * 128]])
        cl = ctx.S[:, 4 + gc:5 + gc]
        p.mm(lambda t: t.transpose(ctx.pD[0:1, 0:128], cl, ctx.iden32[:, :]),
             reads=[cl, ctx.iden32[:, :]], writes=[ctx.pD[0:1, 0:512]])
        p.op("scalar", lambda s: s.activation(out=ctx.S[0:1, 1136:1264], in_=ctx.pD[0:1, 0:128],
             func=AF.Exp), reads=[ctx.pD[0:1, 0:512]], writes=[ctx.S[0:1, 1136:1264]])
        p.op("vector", lambda v: v.tensor_copy(
             out=vap(ctx.R, 0, [[8192, 1], [64, 128], [1, 64]]),
             in_=vap(ctx.S, 1136, [[2048, 1], [1, 128], [0, 64]])),
             reads=[ctx.S[0:1, 1136:1264]], writes=[cde])
        for bg in range(16):
            p.mm(lambda t, r_=cde[0:1, bg * 512:(bg + 1) * 512]: t.matmul(
                 ctx.pD[:, 0:512], ctx.ones_row32[:, :], r_, start=True, stop=True),
                 reads=[cde[0:1, bg * 512:(bg + 1) * 512], ctx.ones_row32[:, :]],
                 writes=[ctx.pD[:, 0:512]])
            p.op("vector", lambda v, bg=bg: v.tensor_copy(
                 out=cdecb[:, bg * 512:(bg + 1) * 512], in_=ctx.pD[:, 0:512]),
                 reads=[ctx.pD[:, 0:512]], writes=[cdecb[:, bg * 512:(bg + 1) * 512]])
        for g in range(8):
            for q in range(4):
                h0 = g * 16 + q * 4
                bsrc = vap(ctx.B, 9216 + g * 128, [[16384, 128], [0, 4], [1, 128]])
                cview = vap(ctx.A, 7168 + gc * 128 + h0, [[8192, 128], [1, 4], [0, 128]])
                b4o = vap(ctx.LNX, 8192, [[32768, 128], [128, 4], [1, 128]])
                p.op("vector", lambda v, bs=bsrc, cv=cview, bo=b4o: v.tensor_tensor(
                     out=bo, in0=bs, in1=cv, op=OP.mult),
                     reads=[btm[:, g * 128:(g + 1) * 128],
                            coef_tm[:, gc * 128 + h0:gc * 128 + h0 + 4]], writes=[b4])
                for hq in range(4):
                    h = h0 + hq
                    bank = ctx.pA if (h % 16) < 8 else ctx.pB
                    o = bank[:, (h % 8) * 64:(h % 8) * 64 + 64]
                    p.mm(lambda t, o=o, l=b4[:, hq * 128:(hq + 1) * 128],
                         r=x_tm[:, h * 64:(h + 1) * 64]: t.matmul(o, l, r, start=True,
                         stop=True), reads=[b4[:, hq * 128:(hq + 1) * 128],
                         x_tm[:, h * 64:(h + 1) * 64]], writes=[o])
            for half in range(2):
                bank = ctx.pA if half == 0 else ctx.pB
                lo = g * 1024 + half * 512
                p.op("vector", lambda v, b_=bank: v.tensor_copy(out=stsb, in_=b_[:, 0:512]),
                     reads=[bank[:, 0:512]], writes=[stsb])
                p.dma("sync", w["std_d"][gc * 128:(gc + 1) * 128, lo:lo + 512], stsb)
                p.op("vector", lambda v, lo=lo: v.tensor_tensor(out=t1s,
                     in0=prev[:, lo:lo + 512], in1=cdecb[:, lo:lo + 512], op=OP.mult),
                     reads=[prev[:, lo:lo + 512], cdecb[:, lo:lo + 512]], writes=[t1s])
                p.op("vector", lambda v, lo=lo, b_=bank: v.tensor_tensor(
                     out=prev[:, lo:lo + 512], in0=t1s, in1=b_[:, 0:512], op=OP.add),
                     reads=[t1s, bank[:, 0:512]], writes=[prev[:, lo:lo + 512]])
        if c == 3:
            p.dma("sync", w["st_snd"][ss * 128:(ss + 1) * 128, :], prev)
            dcol = ctx.S[:, 12:13]
            p.op("vector", lambda v, ss=ss: v.tensor_tensor(out=dcol,
                 in0=ctx.S[:, 4 + ss * 4:5 + ss * 4], in1=ctx.S[:, 5 + ss * 4:6 + ss * 4],
                 op=OP.add), reads=[ctx.S[:, 4 + ss * 4:6 + ss * 4]], writes=[dcol])
            for i_ in (2, 3):
                p.op("vector", lambda v, ss=ss, i_=i_: v.tensor_tensor(out=dcol, in0=dcol,
                     in1=ctx.S[:, 4 + ss * 4 + i_:5 + ss * 4 + i_], op=OP.add),
                     reads=[dcol, ctx.S[:, 4 + ss * 4 + i_:5 + ss * 4 + i_]], writes=[dcol])
            p.mm(lambda t: t.transpose(ctx.pD[0:1, 0:128], dcol, ctx.iden32[:, :]),
                 reads=[dcol, ctx.iden32[:, :]], writes=[ctx.pD[0:1, 0:512]])
            p.op("scalar", lambda s: s.activation(out=ctx.S[0:1, 1264:1392],
                 in_=ctx.pD[0:1, 0:128], func=AF.Exp), reads=[ctx.pD[0:1, 0:512]],
                 writes=[ctx.S[0:1, 1264:1392]])
            p.dma("sync", w["dp_snd"][ss:ss + 1, :], ctx.S[0:1, 1264:1392])
            p.op("gpsimd", lambda g: g.memset(prev[:, :], 0.0), writes=[prev[:, :]])
        p.cut()
    p.collective("AllGather", OP.bypass, grp, [w["st_snd"]], [w["st_all"]])
    p.collective("AllGather", OP.bypass, grp, [w["dp_snd"]], [w["dp_all"]])
    run = ctx.LNX[:, 0:8192]
    accA = ctx.LNX[:, 8192:16384]
    accB = ctx.LNX[:, 16384:24576]
    sout = ctx.B[:, 0:8192]
    selc = ctx.S[:, 1392:1408]
    dpr = ctx.R[0:1, 0:8192]
    p.dma("sync", selc, w["selc"])
    p.op("gpsimd", lambda g: g.memset(run[:, :1] if False else run, 0.0), writes=[run])
    p.op("gpsimd", lambda g: g.memset(accA, 0.0), writes=[accA])
    p.op("gpsimd", lambda g: g.memset(accB, 0.0), writes=[accB])
    for s_ in range(8):
        r = min(s_, 7 - s_)
        sl = 0 if s_ < 4 else 1
        p.op("vector", lambda v, s_=s_: v.scalar_tensor_tensor(out=accA, in0=run,
             scalar=selc[:, s_:s_ + 1], in1=accA, op0=OP.mult, op1=OP.add),
             reads=[run, selc[:, s_:s_ + 1], accA], writes=[accA])
        p.op("vector", lambda v, s_=s_: v.scalar_tensor_tensor(out=accB, in0=run,
             scalar=selc[:, 8 + s_:9 + s_], in1=accB, op0=OP.mult, op1=OP.add),
             reads=[run, selc[:, 8 + s_:9 + s_], accB], writes=[accB])
        p.dma("sync", sout, w["st_all"][r * 256 + sl * 128:r * 256 + (sl + 1) * 128, :])
        p.dma("sync", ctx.S[0:1, 1264:1392], w["dp_all"][r * 2 + sl:r * 2 + sl + 1, :])
        p.op("vector", lambda v: v.tensor_copy(
             out=vap(ctx.R, 0, [[8192, 1], [64, 128], [1, 64]]),
             in_=vap(ctx.S, 1264, [[2048, 1], [1, 128], [0, 64]])),
             reads=[ctx.S[0:1, 1264:1392]], writes=[dpr])
        for bg in range(16):
            p.mm(lambda t, r_=dpr[0:1, bg * 512:(bg + 1) * 512]: t.matmul(ctx.pD[:, 0:512],
                 ctx.ones_row32[:, :], r_, start=True, stop=True),
                 reads=[dpr[0:1, bg * 512:(bg + 1) * 512], ctx.ones_row32[:, :]],
                 writes=[ctx.pD[:, 0:512]])
            p.op("vector", lambda v, bg=bg: v.tensor_tensor(
                 out=run[:, bg * 512:(bg + 1) * 512], in0=run[:, bg * 512:(bg + 1) * 512],
                 in1=ctx.pD[:, 0:512], op=OP.mult),
                 reads=[run[:, bg * 512:(bg + 1) * 512], ctx.pD[:, 0:512]],
                 writes=[run[:, bg * 512:(bg + 1) * 512]])
        p.op("vector", lambda v: v.tensor_tensor(out=run, in0=run, in1=sout, op=OP.add),
             reads=[run, sout], writes=[run])
        p.cut()
    p.dma("sync", w["sin_d"][0:128, :], accA)
    p.dma("sync", w["sin_d"][128:256, :], accB)
    p.cut()

def emit_l1attn_y(ctx, xd, w):
    p = ctx.p
    dt_tm = ctx.A[:, 6144:7168]
    x_tm = ctx.C[:, 0:8192]
    stc = ctx.C[:, 8192:16384]
    prevg = ctx.B[:, 8192:16384]
    bct = ctx.B[:, 0:1024]
    cct = ctx.B[:, 1024:2048]
    s0sb = ctx.B[:, 3584:3712]
    d4 = ctx.B[:, 2048:2560]
    s1q = ctx.B[:, 2560:3072]
    c4 = ctx.B[:, 3072:3584]
    x8 = ctx.B[:, 4096:5120]
    g8 = ctx.B[:, 5120:6144]
    sg8 = ctx.B[:, 6144:7168]
    t1s = ctx.B[:, 7168:8192]
    negones = ctx.S[0:1, 1408:1536]
    trimask_sb = ctx.S[:, 16:528]
    dexp_sb = ctx.S[:, 1536:1600]
    nw_sb = ctx.S[:, 1600:1664]
    yn = ctx.LNX
    cde = ctx.R[0:1, 0:8192]
    p.op("gpsimd", lambda g: g.memset(negones, -1.0), writes=[negones])
    p.dma("sync", trimask_sb, w["trimask4"])
    p.dma("sync", dexp_sb, w["d_exp"])
    p.dma("sync", nw_sb, w["nw_t"])
    for ss in range(2):
        toff = ss * 512
        p.dma("sync", prevg, w["sin_d"][ss * 128:(ss + 1) * 128, :])
        for c in range(4):
            gc = ss * 4 + c
            for kk in range(8):
                p.dma("sync", x_tm[:, kk * 1024:(kk + 1) * 1024]
                      .rearrange("p (k t) -> p k t", t=128),
                      w["xss_d"][kk * 1024:(kk + 1) * 1024, gc * 128:(gc + 1) * 128]
                      .rearrange("(k p) t -> p k t", p=128))
            for k8 in range(8):
                for k in range(k8 * 8, k8 * 8 + 8):
                    src = x_tm[:, k * 128:(k + 1) * 128]
                    p.mm(lambda t, s_=src, k=k: t.transpose(
                         ctx.pE[:, (k % 8) * 128:(k % 8 + 1) * 128], s_, ctx.iden_bf[:, :]),
                         reads=[src, ctx.iden_bf[:, :]], writes=[ctx.pE[:, 0:1024]])
                p.op("vector", lambda v, k8=k8: v.tensor_copy(
                     out=x_tm[:, k8 * 1024:(k8 + 1) * 1024], in_=ctx.pE[:, :]),
                     reads=[ctx.pE[:, 0:1024]], writes=[x_tm[:, k8 * 1024:(k8 + 1) * 1024]])
            for g in range(8):
                p.dma("sync", bct[:, g * 128:(g + 1) * 128],
                      w["bfc_d"][:, g * 1024 + gc * 128:g * 1024 + (gc + 1) * 128])
                p.dma("sync", cct[:, g * 128:(g + 1) * 128],
                      w["cfc_d"][:, g * 1024 + gc * 128:g * 1024 + (gc + 1) * 128])
            for hh in range(4):
                p.dma("sync", vap(ctx.R, 0, [[8192, 1], [128, 32], [1, 128]]),
                      w["cum_d"][32 * hh:32 * hh + 32, gc * 128:(gc + 1) * 128]
                      .rearrange("(o h) t -> o h t", o=1))
                p.op("vector", lambda v: v.tensor_scalar(out=cde[0:1, 0:4096],
                     in0=cde[0:1, 0:4096], scalar1=-1.0, scalar2=None, op0=OP.mult,
                     op1=OP.bypass), reads=[cde[0:1, 0:4096]], writes=[cde[0:1, 0:4096]])
                p.op("scalar", lambda s: s.activation(out=cde[0:1, 4096:8192],
                     in_=cde[0:1, 0:4096], func=AF.Exp, scale=-1.0),
                     reads=[cde[0:1, 0:4096]], writes=[cde[0:1, 4096:8192]])
                for gg in range(2):
                    g = hh * 2 + gg
                    p.mm(lambda t, l=bct[:, g * 128:(g + 1) * 128],
                         r=cct[:, g * 128:(g + 1) * 128]: t.matmul(ctx.pC[:, 0:128], l, r,
                         start=True, stop=True),
                         reads=[bct[:, g * 128:(g + 1) * 128], cct[:, g * 128:(g + 1) * 128]],
                         writes=[ctx.pC[:, 0:512]])
                    p.op("vector", lambda v: v.tensor_copy(out=s0sb, in_=ctx.pC[:, 0:128]),
                         reads=[ctx.pC[:, 0:512]], writes=[s0sb])
                    for q in range(4):
                        h0 = g * 16 + q * 4
                        hrel = h0 - 32 * hh
                        p.mm(lambda t, r_=cde[0:1, hrel * 128:hrel * 128 + 512]:
                             t.matmul(ctx.pB[:, 0:512], negones, r_, start=True, stop=False),
                             reads=[cde[0:1, hrel * 128:hrel * 128 + 512], negones],
                             writes=[ctx.pB[:, 0:512]])
                        for hq in range(4):
                            p.mm(lambda t, l=cde[0:1, (hrel + hq) * 128:(hrel + hq + 1) * 128],
                                 hq=hq: t.matmul(ctx.pB[:, hq * 128:(hq + 1) * 128], l,
                                 ctx.ones_row32[:, :], start=False, stop=False),
                                 reads=[cde[0:1, (hrel + hq) * 128:(hrel + hq + 1) * 128],
                                        ctx.ones_row32[:, :]], writes=[ctx.pB[:, 0:512]])
                        p.mm(lambda t: t.matmul(ctx.pB[:, 0:512], ctx.iden32[:, :],
                             trimask_sb, start=False, stop=True),
                             reads=[ctx.iden32[:, :], trimask_sb],
                             writes=[ctx.pB[:, 0:512]])
                        p.op("scalar", lambda s: s.activation(out=d4, in_=ctx.pB[:, 0:512],
                             func=AF.Exp), reads=[ctx.pB[:, 0:512]], writes=[d4])
                        dt4 = vap(ctx.A, 6144 + gc * 128 + h0, [[8192, 128], [1, 4], [0, 128]])
                        s0b = vap(ctx.B, 3584, [[16384, 128], [0, 4], [1, 128]])
                        d4v = vap(ctx.B, 2048, [[16384, 128], [128, 4], [1, 128]])
                        s1v = vap(ctx.B, 2560, [[16384, 128], [128, 4], [1, 128]])
                        p.op("vector", lambda v, d4v=d4v, dt4=dt4, s1v=s1v: v.tensor_tensor(
                             out=s1v, in0=d4v, in1=dt4, op=OP.mult),
                             reads=[d4, dt_tm[:, gc * 128 + h0:gc * 128 + h0 + 4]],
                             writes=[s1q])
                        p.op("vector", lambda v, s1v=s1v, s0b=s0b: v.tensor_tensor(
                             out=s1v, in0=s1v, in1=s0b, op=OP.mult),
                             reads=[s1q, s0sb], writes=[s1q])
                        p.mm(lambda t, r_=cde[0:1, 4096 + hrel * 128:4096 + hrel * 128 + 512]:
                             t.matmul(ctx.pB[:, 512:1024], ctx.ones_row32[:, :], r_,
                             start=True, stop=True),
                             reads=[cde[0:1, 4096 + hrel * 128:4096 + hrel * 128 + 512],
                                    ctx.ones_row32[:, :]], writes=[ctx.pB[:, 512:1024]])
                        cbv = vap(ctx.B, 1024 + g * 128, [[16384, 128], [0, 4], [1, 128]])
                        c4v = vap(ctx.B, 3072, [[16384, 128], [128, 4], [1, 128]])
                        ebv = vap(ctx.pB, 512, [[1024, 128], [128, 4], [1, 128]])
                        p.op("vector", lambda v, cbv=cbv, c4v=c4v, ebv=ebv: v.tensor_tensor(
                             out=c4v, in0=cbv, in1=ebv, op=OP.mult),
                             reads=[cct[:, g * 128:(g + 1) * 128], ctx.pB[:, 512:1024]],
                             writes=[c4])
                        for hq in range(4):
                            h = h0 + hq
                            k = h // 2
                            ro = (h % 2) * 64
                            co = (k - 8 * g) * 128 if False else (k % 8) * 128
                            o = ctx.pA[ro:ro + 64, co:co + 128]
                            p.mm(lambda t, o=o, l=x_tm[:, h * 64:(h + 1) * 64],
                                 r=s1q[:, hq * 128:(hq + 1) * 128]:
                                 t.matmul(o, l, r, start=True, stop=False),
                                 reads=[x_tm[:, h * 64:(h + 1) * 64],
                                        s1q[:, hq * 128:(hq + 1) * 128]], writes=[ctx.pA[:, co:co + 128]])
                            p.mm(lambda t, o=o, l=prevg[:, h * 64:(h + 1) * 64],
                                 r=c4[:, hq * 128:(hq + 1) * 128]:
                                 t.matmul(o, l, r, start=False, stop=True),
                                 reads=[prevg[:, h * 64:(h + 1) * 64],
                                        c4[:, hq * 128:(hq + 1) * 128]], writes=[ctx.pA[:, co:co + 128]])
                    p.dma("sync", x8[:, :].rearrange("p (k t) -> p k t", t=128),
                          w["xss_d"][g * 1024:(g + 1) * 1024, gc * 128:(gc + 1) * 128]
                          .rearrange("(k p) t -> p k t", p=128))
                    p.dma("sync", g8[:, :].rearrange("p (k t) -> p k t", t=128),
                          w["gate_d"][g * 1024:(g + 1) * 1024, gc * 128:(gc + 1) * 128]
                          .rearrange("(k p) t -> p k t", p=128))
                    p.op("scalar", lambda s: s.activation(out=sg8, in_=g8, func=AF.Silu),
                         reads=[g8], writes=[sg8])
                    dview = vap(ctx.S, 1536 + g * 8, [[2048, 128], [1, 8], [0, 128]])
                    x8v = vap(ctx.B, 4096, [[16384, 128], [128, 8], [1, 128]])
                    t1v = vap(ctx.B, 7168, [[16384, 128], [128, 8], [0, 0]])
                    p.op("gpsimd", lambda e, x8v=x8v, dview=dview: e.tensor_tensor(
                         out=vap(ctx.B, 7168, [[16384, 128], [128, 8], [1, 128]]),
                         in0=x8v, in1=dview, op=OP.mult),
                         reads=[x8, ctx.S[:, 1536 + g * 8:1544 + g * 8]], writes=[t1s])
                    p.op("vector", lambda v: v.tensor_tensor(out=t1s, in0=t1s,
                         in1=ctx.pA[:, :], op=OP.add), reads=[t1s, ctx.pA[:, :]],
                         writes=[t1s])
                    p.op("vector", lambda v, g=g, gc=gc: v.tensor_tensor(
                         out=yn[:, g * 4096 + (gc % 4) * 128:g * 4096 + (gc % 4) * 128 + 1024]
                         .rearrange("p (k t) -> p k t", t=128) if False else
                         vap(ctx.LNX, g * 4096 + (gc % 4) * 128, [[32768, 128], [512, 8], [1, 128]]),
                         in0=t1s[:, :].rearrange("p (k t) -> p k t", t=128),
                         in1=sg8[:, :].rearrange("p (k t) -> p k t", t=128), op=OP.mult),
                         reads=[t1s, sg8], writes=[ctx.LNX[:, g * 4096:(g + 1) * 4096]])
            p.dma("sync", stc, w["std_d"][gc * 128:(gc + 1) * 128, :])
            cl = ctx.S[:, 4 + gc:5 + gc]
            p.mm(lambda t: t.transpose(ctx.pD[0:1, 0:128], cl, ctx.iden32[:, :]),
                 reads=[cl, ctx.iden32[:, :]], writes=[ctx.pD[0:1, 0:512]])
            p.op("scalar", lambda s: s.activation(out=ctx.S[0:1, 1136:1264],
                 in_=ctx.pD[0:1, 0:128], func=AF.Exp), reads=[ctx.pD[0:1, 0:512]],
                 writes=[ctx.S[0:1, 1136:1264]])
            p.op("vector", lambda v: v.tensor_copy(
                 out=vap(ctx.R, 0, [[8192, 1], [64, 128], [1, 64]]),
                 in_=vap(ctx.S, 1136, [[2048, 1], [1, 128], [0, 64]])),
                 reads=[ctx.S[0:1, 1136:1264]], writes=[cde])
            for bg in range(16):
                p.mm(lambda t, r_=cde[0:1, bg * 512:(bg + 1) * 512]: t.matmul(
                     ctx.pD[:, 0:512], ctx.ones_row32[:, :], r_, start=True, stop=True),
                     reads=[cde[0:1, bg * 512:(bg + 1) * 512], ctx.ones_row32[:, :]],
                     writes=[ctx.pD[:, 0:512]])
                p.op("vector", lambda v, bg=bg: v.tensor_tensor(
                     out=prevg[:, bg * 512:(bg + 1) * 512],
                     in0=prevg[:, bg * 512:(bg + 1) * 512], in1=ctx.pD[:, 0:512], op=OP.mult),
                     reads=[prevg[:, bg * 512:(bg + 1) * 512], ctx.pD[:, 0:512]],
                     writes=[prevg[:, bg * 512:(bg + 1) * 512]])
            p.op("vector", lambda v: v.tensor_tensor(out=prevg, in0=prevg, in1=stc, op=OP.add),
                 reads=[prevg, stc], writes=[prevg])
            p.cut()
        ysq = ctx.B[:, 4096:5120]
        vrow = ctx.R[0:1, 0:4096]
        for gg in range(8):
            for k in range(8):
                ksl = yn[:, gg * 4096 + k * 512:gg * 4096 + (k + 1) * 512]
                p.op("scalar", lambda s, ksl=ksl, k=k: s.activation(
                     out=ysq[:, (k % 2) * 512:(k % 2 + 1) * 512], in_=ksl, func=AF.Square),
                     reads=[ksl], writes=[ysq[:, (k % 2) * 512:(k % 2 + 1) * 512]])
                p.mm(lambda t, k=k, st_=(k == 0), sp=(k == 7): t.matmul(ctx.pD[0:1, 0:512],
                     ctx.ones_col_bf[:, :], ysq[:, (k % 2) * 512:(k % 2 + 1) * 512],
                     start=st_, stop=sp),
                     reads=[ysq[:, (k % 2) * 512:(k % 2 + 1) * 512], ctx.ones_col_bf[:, :]],
                     writes=[ctx.pD[0:1, 0:512]])
            p.op("vector", lambda v, gg=gg: v.tensor_scalar(
                 out=vrow[0:1, gg * 512:(gg + 1) * 512], in0=ctx.pD[0:1, 0:512],
                 scalar1=1.0 / 1024.0, scalar2=None, op0=OP.mult, op1=OP.bypass),
                 reads=[ctx.pD[0:1, 0:512]], writes=[vrow[0:1, gg * 512:(gg + 1) * 512]])
        p.op("scalar", lambda s: s.activation(out=vrow, in_=vrow, func=AF.Sqrt,
             bias=ctx.cst[0:1, 0:1], scale=1.0), reads=[vrow, ctx.cst[0:1, 0:1]],
             writes=[vrow])
        p.op("vector", lambda v: v.reciprocal(out=vrow, in_=vrow), reads=[vrow], writes=[vrow])
        for gg in range(8):
            p.mm(lambda t, gg=gg: t.matmul(ctx.pD[:, 0:512], ctx.ones_row32[:, :],
                 vrow[0:1, gg * 512:(gg + 1) * 512], start=True, stop=True),
                 reads=[vrow[0:1, gg * 512:(gg + 1) * 512], ctx.ones_row32[:, :]],
                 writes=[ctx.pD[:, 0:512]])
            p.op("vector", lambda v: v.tensor_copy(out=ysq[:, 0:512], in_=ctx.pD[:, 0:512]),
                 reads=[ctx.pD[:, 0:512]], writes=[ysq[:, 0:512]])
            for k in range(8):
                kk = gg * 8 + k
                ksl = yn[:, kk * 512:(kk + 1) * 512]
                p.op("vector", lambda v, ksl=ksl, kk=kk: v.scalar_tensor_tensor(
                     out=ksl, in0=ksl, scalar=nw_sb[:, kk:kk + 1], in1=ysq[:, 0:512],
                     op0=OP.mult, op1=OP.mult),
                     reads=[ksl, nw_sb[:, kk:kk + 1], ysq[:, 0:512]], writes=[ksl])
        p.cut()
        wos = ctx.C[:, 0:16384]
        for m in range(32):
            ws = wos[:, (m % 2) * 8192:(m % 2 + 1) * 8192]
            p.dma("gpsimd", ws, w["outw_t"][m])
            o = ctx.pC[:, 0:512] if m % 2 == 0 else ctx.pB[:, 0:512]
            xsl = ctx.A[:, (m % 2) * 512:(m % 2) * 512 + 512]
            nsl = ctx.A[:, 1024 + (m % 2) * 512:1024 + (m % 2) * 512 + 512]
            xdr = xd[m * 128:(m + 1) * 128, toff:toff + 512]
            p.dma("sync", xsl, xdr)
            for k in range(64):
                p.mm(lambda t, o=o, l=ws[:, k * 128:(k + 1) * 128],
                     r=yn[:, k * 512:(k + 1) * 512], st_=(k == 0), sp=(k == 63):
                     t.matmul(o, l, r, start=st_, stop=sp),
                     reads=[ws[:, k * 128:(k + 1) * 128], yn[:, k * 512:(k + 1) * 512]],
                     writes=[o])
            p.op("vector", lambda v, o=nsl, a=o, b=xsl: v.tensor_tensor(
                 out=o, in0=a, in1=b, op=OP.add), reads=[o, xsl], writes=[nsl])
            p.dma("sync", xdr, nsl)
            if m % 8 == 7:
                p.cut()
        p.cut()



D_ = 4096
NCORES = 8
STRIP = 512


def core_strips(c):
    q = c % 4
    return (q, 7 - q)


def shard_x(x):
    out = []
    for c in range(NCORES):
        b = c // 4
        sA, sB = core_strips(c)
        xa = x[b, sA * STRIP:(sA + 1) * STRIP]
        xb = x[b, sB * STRIP:(sB + 1) * STRIP]
        out.append(np.ascontiguousarray(np.concatenate([xa, xb], 0).T.astype(np.float32)))
    return out


def unshard_x(shards):
    x = np.zeros((2, 4096, 4096), dtype=np.float32)
    for c in range(NCORES):
        b = c // 4
        sA, sB = core_strips(c)
        t = np.asarray(shards[c]).T
        x[b, sA * STRIP:(sA + 1) * STRIP] = t[:STRIP]
        x[b, sB * STRIP:(sB + 1) * STRIP] = t[STRIP:]
    return x


def tileize(wT):
    IN, OUT = wT.shape
    K, M = IN // 128, OUT // 128
    a = wT.reshape(K, 128, M, 128).transpose(2, 1, 0, 3)
    return np.ascontiguousarray(a.reshape(M, 128, K * 128).astype(np.float32))


def l0ffn_host(inp):
    w = {}
    w["wt"] = tileize(np.asarray(inp["L0_ffn_weight"]).T)
    w["b_flat"] = np.ascontiguousarray(
        np.asarray(inp["L0_ffn_lora_B"]).reshape(128, 4096).astype(np.float32))
    gw1t = np.asarray(inp["L0_ffn_gw1"]).T.astype(np.float32)
    w["gw1t_t"] = np.ascontiguousarray(gw1t.reshape(32, 128, 32).transpose(1, 0, 2).reshape(128, 1024))
    aall = np.asarray(inp["L0_ffn_lora_A"]).transpose(1, 0, 2).reshape(4096, 128).astype(np.float32)
    w["a_all_t"] = np.ascontiguousarray(aall.reshape(32, 128, 128).transpose(1, 0, 2).reshape(128, 4096))
    w["gw2t"] = np.ascontiguousarray(np.asarray(inp["L0_ffn_gw2"]).T.astype(np.float32))
    w["gb1"] = np.asarray(inp["L0_ffn_gb1"]).reshape(32, 1).astype(np.float32)
    w["gb2"] = np.asarray(inp["L0_ffn_gb2"]).reshape(8, 1).astype(np.float32)
    w["lngamma"] = np.asarray(inp["L0_ffn_ln_gamma"]).reshape(32, 1).astype(np.float32)
    w["lnbeta"] = np.asarray(inp["L0_ffn_ln_beta"]).reshape(32, 1).astype(np.float32)
    esel = np.zeros((8, 128), dtype=np.float32)
    for e in range(8):
        esel[e, e * 16:(e + 1) * 16] = 1.0 / 16.0
    w["esel"] = esel
    return w


def l0attn_host(inp):
    w = {}
    for nm, key in [("wq_t", "L0_attn_wq"), ("wk_t", "L0_attn_wk"),
                    ("wv_t", "L0_attn_wv"), ("wo_t", "L0_attn_wo")]:
        w[nm] = tileize(np.asarray(inp[key]).T)
    T, h = 4096, 128
    pos = np.arange(T, dtype=np.float32)
    thetas = (1.0 / 10000.0 ** (np.arange(0, h, 2, dtype=np.float32) / h)).astype(np.float32)
    pt = np.concatenate([np.outer(pos, thetas)] * 2, axis=1).astype(np.float32)
    sin, cos = np.sin(pt).astype(np.float32), np.cos(pt).astype(np.float32)
    w["cos_pc"], w["sin_pc"] = [], []
    for c in range(NCORES):
        sa, sb = core_strips(c)
        idx = np.r_[sa * STRIP:(sa + 1) * STRIP, sb * STRIP:(sb + 1) * STRIP]
        w["cos_pc"].append(np.ascontiguousarray(cos[idx].T))
        w["sin_pc"].append(np.ascontiguousarray(sin[idx].T))
    return w


def attn_masks(c):
    sA, sB = core_strips(c)
    NEG = -1e30
    m = np.zeros((128, 8 * 1024), dtype=np.float32)
    causal = np.where(np.arange(1024)[None, :] >= np.arange(128)[:, None] + 384, 0.0, NEG)
    for mi, (qg, s) in enumerate([(0, 0), (0, 1), (0, 2), (0, 3),
                                  (1, 4), (1, 5), (1, 6), (1, 7)]):
        strip = sA if qg == 0 else sB
        if s == strip:
            m[:, mi * 1024:(mi + 1) * 1024] = causal
        elif s > strip:
            m[:, mi * 1024:(mi + 1) * 1024] = NEG
    return m


def l1ffn_host(inp):
    w = {}
    wqT = np.asarray(inp["L1_ffn_Wq"]).T.astype(np.float32)
    w["wq_t2"] = np.ascontiguousarray(wqT.reshape(32, 128, 64).transpose(1, 0, 2).reshape(128, 2048))
    w["wr_t"] = np.ascontiguousarray(np.asarray(inp["L1_ffn_Wr"]).T.astype(np.float32))
    w["keys_f"] = np.ascontiguousarray(np.asarray(inp["L1_ffn_aux_keys"]).T.astype(np.float32))
    w["values"] = np.ascontiguousarray(np.asarray(inp["L1_ffn_aux_values"]).astype(np.float32))
    w["wvd_t"] = tileize(np.asarray(inp["L1_ffn_Wvd"]).T)
    w["wvu_t"] = tileize(np.asarray(inp["L1_ffn_Wvu"]).T)
    w["wu1_t"] = tileize(np.asarray(inp["L1_ffn_Wu1"]).T)
    w["wu2_t"] = np.ascontiguousarray(
        np.asarray(inp["L1_ffn_Wu2"]).T.reshape(8, 128).T.astype(np.float32))
    w["bu1_t"] = np.ascontiguousarray(
        np.asarray(inp["L1_ffn_bu1"]).reshape(8, 128).T.astype(np.float32))
    w["log_rel"] = np.asarray(inp["L1_ffn_log_rel"]).reshape(1, 1024).astype(np.float32)
    w["bu2"] = np.asarray(inp["L1_ffn_bu2"]).reshape(1, 1).astype(np.float32)
    w["gate_w1"] = np.asarray(inp["L1_ffn_gate_w1"]).reshape(1, 1).astype(np.float32)
    w["gate_bias"] = np.asarray(inp["L1_ffn_gate_bias"]).reshape(1, 1).astype(np.float32)
    return w


def l1attn_host(inp):
    w = {}
    w["inproj_t"] = tileize(np.asarray(inp["L1_attn_in_proj_w"]).T)
    cw = np.asarray(inp["L1_attn_conv_w"]).reshape(10240, 4).astype(np.float32)
    cd = np.zeros((80, 128, 512), dtype=np.float32)
    for mc in range(80):
        for j in range(4):
            cd[mc, :, j * 128:(j + 1) * 128] = np.diag(cw[mc * 128:(mc + 1) * 128, j])
    w["convd_t"] = cd
    w["conv_b"] = np.ascontiguousarray(
        np.asarray(inp["L1_attn_conv_b"]).reshape(80, 128).T.astype(np.float32))
    w["dtb"] = np.asarray(inp["L1_attn_dt_bias"]).reshape(128, 1).astype(np.float32)
    w["alog"] = np.asarray(inp["L1_attn_A_log"]).reshape(128, 1).astype(np.float32)
    Dv = np.asarray(inp["L1_attn_D"]).astype(np.float32)
    de = np.zeros((128, 64), dtype=np.float32)
    for k in range(64):
        de[0:64, k] = Dv[2 * k]
        de[64:128, k] = Dv[2 * k + 1]
    w["d_exp"] = de
    w["nw_t"] = np.ascontiguousarray(
        np.asarray(inp["L1_attn_norm_w"]).reshape(64, 128).T.astype(np.float32))
    w["outw_t"] = tileize(np.asarray(inp["L1_attn_out_w"]).T)
    tri = np.where(np.arange(128)[:, None] <= np.tile(np.arange(128), 4)[None, :], 0.0, -1e30)
    w["trimask4"] = tri.astype(np.float32)
    w["selc_pc"], w["hsel_pc"] = [], []
    for c in range(NCORES):
        sA, sB = core_strips(c)
        sel = np.zeros((128, 16), dtype=np.float32)
        sel[:, sA] = 1.0
        sel[:, 8 + sB] = 1.0
        w["selc_pc"].append(sel)
        hs = np.zeros((128, 16), dtype=np.float32)
        for dsti, sd in ((0, sA), (1, sB)):
            if sd >= 1:
                src = sd - 1
                r = min(src, 7 - src)
                sl = 0 if src < 4 else 1
                hs[:, dsti * 8 + r * 2 + sl] = 1.0
        w["hsel_pc"].append(hs)
    return w



def build_l0ffn():
    nc = bass.Bass("TRN2", target_bir_lowering=False)
    ctx = Ctx(nc)
    p = ctx.p
    x_in = ctx.din("x_in", [4096, 1024])
    x_out = ctx.dout("x_out", [4096, 1024])
    xd = ctx.dtmp("x_sc", [4096, 1024])
    w = {
        "wt": ctx.din("w_wt", [32, 128, 4096]),
        "a_all_t": ctx.din("w_aall", [128, 4096]),
        "b_flat": ctx.din("w_bflat", [128, 4096]),
        "gw1t_t": ctx.din("w_gw1t", [128, 1024]),
        "gw2t": ctx.din("w_gw2t", [32, 8]),
        "gb1": ctx.din("w_gb1", [32, 1]),
        "gb2": ctx.din("w_gb2", [8, 1]),
        "lngamma": ctx.din("w_gamma", [32, 1]),
        "lnbeta": ctx.din("w_beta", [32, 1]),
        "esel": ctx.din("w_esel", [8, 128]),
        "cb_in": ctx.dtmp("cb_in", [1, 16]),
        "cb_out": ctx.dtmp("cb_out", [1, 16]),
    }
    ctx.pools()
    ctx.constants()
    p.dma("sync", xd, x_in)
    p.cut()
    emit_ln(ctx, xd, "A")
    emit_l0ffn(ctx, xd, w, "A")
    p.dma("sync", x_out, xd)
    p.finish()
    return p.build()


def build_l0attn():
    nc = bass.Bass("TRN2", target_bir_lowering=False)
    ctx = Ctx(nc)
    p = ctx.p
    x_in = ctx.din("x_in", [4096, 1024])
    x_out = ctx.dout("x_out", [4096, 1024])
    xd = ctx.dtmp("x_sc", [4096, 1024])
    w = {
        "wq_t": ctx.din("w_wq", [32, 128, 4096]),
        "wk_t": ctx.din("w_wk", [32, 128, 4096]),
        "wv_t": ctx.din("w_wv", [32, 128, 4096]),
        "wo_t": ctx.din("w_wo", [32, 128, 4096]),
        "cos": ctx.din("w_cos", [128, 1024]),
        "sin": ctx.din("w_sin", [128, 1024]),
        "mtab": ctx.din("w_mtab", [128, 8192], BF16),
        "q_loc": ctx.dtmp("q_loc", [4096, 1024], BF16),
        "k_bounce": ctx.dtmp("k_bounce", [4096, 1024], BF16),
        "v_loc": ctx.dtmp("v_loc", [1024, 4096], BF16),
        "k_all": ctx.dtmp("k_all", [16384, 1024], BF16),
        "v_all": ctx.dtmp("v_all", [4096, 4096], BF16),
    }
    ctx.pools()
    ctx.constants()
    p.dma("sync", xd, x_in)
    p.cut()
    emit_ln(ctx, xd, "A")
    emit_l0attn(ctx, xd, w)
    p.dma("sync", x_out, xd)
    p.finish()
    return p.build()


def build_l1ffn():
    nc = bass.Bass("TRN2", target_bir_lowering=False)
    ctx = Ctx(nc)
    p = ctx.p
    x_in = ctx.din("x_in", [4096, 1024])
    x_out = ctx.dout("x_out", [4096, 1024])
    xd = ctx.dtmp("x_sc", [4096, 1024])
    w = {
        "wq_t2": ctx.din("w_wq2", [128, 2048]),
        "wr_t": ctx.din("w_wr", [64, 48]),
        "keys_f": ctx.din("w_keys", [64, 1024]),
        "values": ctx.din("w_vals", [1024, 4096]),
        "wvd_t": ctx.din("w_wvd", [2, 128, 4096]),
        "wvu_t": ctx.din("w_wvu", [32, 128, 256]),
        "wu1_t": ctx.din("w_wu1", [8, 128, 4096]),
        "wu2_t": ctx.din("w_wu2", [128, 8]),
        "bu1_t": ctx.din("w_bu1", [128, 8]),
        "log_rel": ctx.din("w_lr", [1, 1024]),
        "bu2": ctx.din("w_bu2", [1, 1]),
        "gate_w1": ctx.din("w_gw1s", [1, 1]),
        "gate_bias": ctx.din("w_gbs", [1, 1]),
        "cb2_in": ctx.dtmp("cb2_in", [1, 1]),
        "cb2_out": ctx.dtmp("cb2_out", [1, 1]),
    }
    ctx.pools()
    ctx.constants()
    p.dma("sync", xd, x_in)
    p.cut()
    var_row = ctx.S[0:1, 1024:2048]
    emit_ln(ctx, xd, "A", var_row=var_row)
    emit_l1ffn(ctx, xd, w, var_row)
    p.dma("sync", x_out, xd)
    p.finish()
    return p.build()


def build_l1attn():
    nc = bass.Bass("TRN2", target_bir_lowering=False)
    ctx = Ctx(nc)
    p = ctx.p
    x_in = ctx.din("x_in", [4096, 1024])
    x_out = ctx.dout("x_out", [4096, 1024])
    xd = ctx.dtmp("x_sc", [4096, 1024])
    w = {
        "inproj_t": ctx.din("w_ip", [145, 128, 4096]),
        "convd_t": ctx.din("w_cd", [80, 128, 512]),
        "conv_b": ctx.din("w_cb", [128, 80]),
        "dtb": ctx.din("w_dtb", [128, 1]),
        "alog": ctx.din("w_alog", [128, 1]),
        "d_exp": ctx.din("w_dexp", [128, 64]),
        "nw_t": ctx.din("w_nw", [128, 64]),
        "outw_t": ctx.din("w_ow", [32, 128, 8192]),
        "trimask4": ctx.din("w_tri", [128, 512]),
        "selc": ctx.din("w_selc", [128, 16]),
        "hsel": ctx.din("w_hsel", [128, 16]),
        "gate_d": ctx.dtmp("gate_d", [8192, 1024], BF16),
        "xbc_d": ctx.dtmp("xbc_d", [10240, 1024], BF16),
        "xss_d": ctx.dtmp("xss_d", [8192, 1024], BF16),
        "bfc_d": ctx.dtmp("bfc_d", [128, 8192], BF16),
        "cfc_d": ctx.dtmp("cfc_d", [128, 8192], BF16),
        "std_d": ctx.dtmp("std_d", [1024, 8192], BF16),
        "cum_d": ctx.dtmp("cum_d", [128, 1024], F32),
        "halo_snd": ctx.dtmp("halo_snd", [128, 480], BF16),
        "halo_all": ctx.dtmp("halo_all", [512, 480], BF16),
        "st_snd": ctx.dtmp("st_snd", [256, 8192], BF16),
        "st_all": ctx.dtmp("st_all", [1024, 8192], BF16),
        "dp_snd": ctx.dtmp("dp_snd", [2, 128], F32),
        "dp_all": ctx.dtmp("dp_all", [8, 128], F32),
        "sin_d": ctx.dtmp("sin_d", [256, 8192], BF16),
    }
    ctx.pools()
    ctx.constants()
    p.dma("sync", xd, x_in)
    p.cut()
    emit_ln(ctx, xd, "A")
    emit_l1attn(ctx, xd, w)
    emit_l1attn_state(ctx, xd, w)
    emit_l1attn_y(ctx, xd, w)
    p.dma("sync", x_out, xd)
    p.finish()
    return p.build()


_CACHE = {}


def kernel_multi(**inputs):
    import ml_dtypes
    from concourse.bass_utils import run_bass_kernel_spmd
    if "progs" not in _CACHE:
        _CACHE["progs"] = {
            "l0ffn": build_l0ffn(), "l0attn": build_l0attn(),
            "l1ffn": build_l1ffn(), "l1attn": build_l1attn(),
        }
    progs = _CACHE["progs"]
    wf0 = l0ffn_host(inputs)
    wa0 = l0attn_host(inputs)
    wf1 = l1ffn_host(inputs)
    wa1 = l1attn_host(inputs)
    xs = shard_x(np.asarray(inputs["L0_attn_x"]).astype(np.float32))

    def launch(prog, permaps):
        res = run_bass_kernel_spmd(prog, permaps, core_ids=list(range(8)))
        return [r["x_out"] for r in res.results]

    def maps_f0(xs):
        return [{"x_in": xs[c], "w_wt": wf0["wt"], "w_aall": wf0["a_all_t"],
                 "w_bflat": wf0["b_flat"], "w_gw1t": wf0["gw1t_t"], "w_gw2t": wf0["gw2t"],
                 "w_gb1": wf0["gb1"], "w_gb2": wf0["gb2"], "w_gamma": wf0["lngamma"],
                 "w_beta": wf0["lnbeta"], "w_esel": wf0["esel"]} for c in range(8)]

    def maps_a0(xs):
        return [{"x_in": xs[c], "w_wq": wa0["wq_t"], "w_wk": wa0["wk_t"],
                 "w_wv": wa0["wv_t"], "w_wo": wa0["wo_t"], "w_cos": wa0["cos_pc"][c],
                 "w_sin": wa0["sin_pc"][c],
                 "w_mtab": attn_masks(c).astype(ml_dtypes.bfloat16)} for c in range(8)]

    def maps_f1(xs):
        return [{"x_in": xs[c], "w_wq2": wf1["wq_t2"], "w_wr": wf1["wr_t"],
                 "w_keys": wf1["keys_f"], "w_vals": wf1["values"], "w_wvd": wf1["wvd_t"],
                 "w_wvu": wf1["wvu_t"], "w_wu1": wf1["wu1_t"], "w_wu2": wf1["wu2_t"],
                 "w_bu1": wf1["bu1_t"], "w_lr": wf1["log_rel"], "w_bu2": wf1["bu2"],
                 "w_gw1s": wf1["gate_w1"], "w_gbs": wf1["gate_bias"]} for c in range(8)]

    def maps_a1(xs):
        return [{"x_in": xs[c], "w_ip": wa1["inproj_t"], "w_cd": wa1["convd_t"],
                 "w_cb": wa1["conv_b"], "w_dtb": wa1["dtb"], "w_alog": wa1["alog"],
                 "w_dexp": wa1["d_exp"], "w_nw": wa1["nw_t"], "w_ow": wa1["outw_t"],
                 "w_tri": wa1["trimask4"], "w_selc": wa1["selc_pc"][c],
                 "w_hsel": wa1["hsel_pc"][c]} for c in range(8)]

    xs = launch(progs["l0ffn"], maps_f0(xs))
    xs = launch(progs["l0attn"], maps_a0(xs))
    xs = launch(progs["l0ffn"], maps_f0(xs))
    xs = launch(progs["l1ffn"], maps_f1(xs))
    xs = launch(progs["l1attn"], maps_a1(xs))
    xs = launch(progs["l1ffn"], maps_f1(xs))
    return unshard_x(xs)


def build_fused():
    nc = bass.Bass("TRN2", target_bir_lowering=False)
    ctx = Ctx(nc)
    p = ctx.p
    x_in = ctx.din("x_in", [4096, 1024])
    x_out = ctx.dout("x_out", [4096, 1024])
    xd = ctx.dtmp("x_sc", [4096, 1024])
    wf0 = {
        "wt": ctx.din("w_wt", [32, 128, 4096]),
        "a_all_t": ctx.din("w_aall", [128, 4096]),
        "b_flat": ctx.din("w_bflat", [128, 4096]),
        "gw1t_t": ctx.din("w_gw1t", [128, 1024]),
        "gw2t": ctx.din("w_gw2t", [32, 8]),
        "gb1": ctx.din("w_gb1", [32, 1]),
        "gb2": ctx.din("w_gb2", [8, 1]),
        "lngamma": ctx.din("w_gamma", [32, 1]),
        "lnbeta": ctx.din("w_beta", [32, 1]),
        "esel": ctx.din("w_esel", [8, 128]),
        "cb_in": ctx.dtmp("cb_in", [1, 16]),
        "cb_out": ctx.dtmp("cb_out", [1, 16]),
    }
    wf0b = dict(wf0)
    wf0b["cb_in"] = ctx.dtmp("cb_in2", [1, 16])
    wf0b["cb_out"] = ctx.dtmp("cb_out2", [1, 16])
    wa0 = {
        "wq_t": ctx.din("w_wq", [32, 128, 4096]),
        "wk_t": ctx.din("w_wk", [32, 128, 4096]),
        "wv_t": ctx.din("w_wv", [32, 128, 4096]),
        "wo_t": ctx.din("w_wo", [32, 128, 4096]),
        "cos": ctx.din("w_cos", [128, 1024]),
        "sin": ctx.din("w_sin", [128, 1024]),
        "mtab": ctx.din("w_mtab", [128, 8192], BF16),
        "q_loc": ctx.dtmp("q_loc", [4096, 1024], BF16),
        "k_bounce": ctx.dtmp("k_bounce", [4096, 1024], BF16),
        "v_loc": ctx.dtmp("v_loc", [1024, 4096], BF16),
        "k_all": ctx.dtmp("k_all", [16384, 1024], BF16),
        "v_all": ctx.dtmp("v_all", [4096, 4096], BF16),
    }
    wf1 = {
        "wq_t2": ctx.din("w_wq2", [128, 2048]),
        "wr_t": ctx.din("w_wr", [64, 48]),
        "keys_f": ctx.din("w_keys", [64, 1024]),
        "values": ctx.din("w_vals", [1024, 4096]),
        "wvd_t": ctx.din("w_wvd", [2, 128, 4096]),
        "wvu_t": ctx.din("w_wvu", [32, 128, 256]),
        "wu1_t": ctx.din("w_wu1", [8, 128, 4096]),
        "wu2_t": ctx.din("w_wu2", [128, 8]),
        "bu1_t": ctx.din("w_bu1", [128, 8]),
        "log_rel": ctx.din("w_lr", [1, 1024]),
        "bu2": ctx.din("w_bu2", [1, 1]),
        "gate_w1": ctx.din("w_gw1s", [1, 1]),
        "gate_bias": ctx.din("w_gbs", [1, 1]),
        "cb2_in": ctx.dtmp("cb2_in", [1, 1]),
        "cb2_out": ctx.dtmp("cb2_out", [1, 1]),
    }
    wf1b = dict(wf1)
    wf1b["cb2_in"] = ctx.dtmp("cb2_in2", [1, 1])
    wf1b["cb2_out"] = ctx.dtmp("cb2_out2", [1, 1])
    wa1 = {
        "inproj_t": ctx.din("w_ip", [145, 128, 4096]),
        "convd_t": ctx.din("w_cd", [80, 128, 512]),
        "conv_b": ctx.din("w_cb", [128, 80]),
        "dtb": ctx.din("w_dtb", [128, 1]),
        "alog": ctx.din("w_alog", [128, 1]),
        "d_exp": ctx.din("w_dexp", [128, 64]),
        "nw_t": ctx.din("w_nw", [128, 64]),
        "outw_t": ctx.din("w_ow", [32, 128, 8192]),
        "trimask4": ctx.din("w_tri", [128, 512]),
        "selc": ctx.din("w_selc", [128, 16]),
        "hsel": ctx.din("w_hsel", [128, 16]),
        "gate_d": ctx.dtmp("gate_d", [8192, 1024], BF16),
        "xbc_d": ctx.dtmp("xbc_d", [10240, 1024], BF16),
        "xss_d": ctx.dtmp("xss_d", [8192, 1024], BF16),
        "bfc_d": ctx.dtmp("bfc_d", [128, 8192], BF16),
        "cfc_d": ctx.dtmp("cfc_d", [128, 8192], BF16),
        "std_d": ctx.dtmp("std_d", [1024, 8192], BF16),
        "cum_d": ctx.dtmp("cum_d", [128, 1024], F32),
        "halo_snd": ctx.dtmp("halo_snd", [128, 480], BF16),
        "halo_all": ctx.dtmp("halo_all", [512, 480], BF16),
        "st_snd": ctx.dtmp("st_snd", [256, 8192], BF16),
        "st_all": ctx.dtmp("st_all", [1024, 8192], BF16),
        "dp_snd": ctx.dtmp("dp_snd", [2, 128], F32),
        "dp_all": ctx.dtmp("dp_all", [8, 128], F32),
        "sin_d": ctx.dtmp("sin_d", [256, 8192], BF16),
    }
    ctx.pools()
    ctx.constants()
    p.dma("sync", xd, x_in)
    p.cut()
    emit_ln(ctx, xd, "A")
    emit_l0ffn(ctx, xd, wf0, "A")
    emit_ln(ctx, xd, "B")
    emit_l0attn(ctx, xd, wa0)
    emit_ln(ctx, xd, "C")
    emit_l0ffn(ctx, xd, wf0b, "C")
    var_row = ctx.S[0:1, 1024:2048]
    emit_ln(ctx, xd, "D", var_row=var_row)
    emit_l1ffn(ctx, xd, wf1, var_row)
    emit_ln(ctx, xd, "E")
    emit_l1attn(ctx, xd, wa1)
    emit_l1attn_state(ctx, xd, wa1)
    emit_l1attn_y(ctx, xd, wa1)
    emit_ln(ctx, xd, "F", var_row=var_row)
    emit_l1ffn(ctx, xd, wf1b, var_row)
    p.dma("sync", x_out, xd)
    p.finish()
    import sys
    print("FUSED ninstr:", p.ninstr, "waits:", p.nwaits, "segments:", len(p.segments),
          "dma_sems:", p.ndma_sems, file=sys.stderr)
    return p.build()


def kernel(**inputs):
    import ml_dtypes
    from concourse.bass_utils import run_bass_kernel_spmd
    if "fused" not in _CACHE:
        _CACHE["fused"] = build_fused()
    prog = _CACHE["fused"]
    wf0 = l0ffn_host(inputs)
    wa0 = l0attn_host(inputs)
    wf1 = l1ffn_host(inputs)
    wa1 = l1attn_host(inputs)
    xs = shard_x(np.asarray(inputs["L0_attn_x"]).astype(np.float32))
    maps = []
    for c in range(8):
        m = {"x_in": xs[c],
             "w_wt": wf0["wt"], "w_aall": wf0["a_all_t"], "w_bflat": wf0["b_flat"],
             "w_gw1t": wf0["gw1t_t"], "w_gw2t": wf0["gw2t"], "w_gb1": wf0["gb1"],
             "w_gb2": wf0["gb2"], "w_gamma": wf0["lngamma"], "w_beta": wf0["lnbeta"],
             "w_esel": wf0["esel"],
             "w_wq": wa0["wq_t"], "w_wk": wa0["wk_t"], "w_wv": wa0["wv_t"],
             "w_wo": wa0["wo_t"], "w_cos": wa0["cos_pc"][c], "w_sin": wa0["sin_pc"][c],
             "w_mtab": attn_masks(c).astype(ml_dtypes.bfloat16),
             "w_wq2": wf1["wq_t2"], "w_wr": wf1["wr_t"], "w_keys": wf1["keys_f"],
             "w_vals": wf1["values"], "w_wvd": wf1["wvd_t"], "w_wvu": wf1["wvu_t"],
             "w_wu1": wf1["wu1_t"], "w_wu2": wf1["wu2_t"], "w_bu1": wf1["bu1_t"],
             "w_lr": wf1["log_rel"], "w_bu2": wf1["bu2"], "w_gw1s": wf1["gate_w1"],
             "w_gbs": wf1["gate_bias"],
             "w_ip": wa1["inproj_t"], "w_cd": wa1["convd_t"], "w_cb": wa1["conv_b"],
             "w_dtb": wa1["dtb"], "w_alog": wa1["alog"], "w_dexp": wa1["d_exp"],
             "w_nw": wa1["nw_t"], "w_ow": wa1["outw_t"], "w_tri": wa1["trimask4"],
             "w_selc": wa1["selc_pc"][c], "w_hsel": wa1["hsel_pc"][c]}
        maps.append(m)
    res = run_bass_kernel_spmd(prog, maps, core_ids=list(range(8)))
    return unshard_x([r["x_out"] for r in res.results])


# version 04
# speedup vs baseline: 1.0185x; 1.0185x over previous
import numpy as np
import concourse.bass as bass
import concourse.mybir as mybir

F32 = mybir.dt.float32
BF16 = mybir.dt.bfloat16
AF = mybir.ActivationFunctionType
OP = mybir.AluOpType

ENGINES = ("tensor", "vector", "scalar", "gpsimd", "sync")
AUTO = {"tensor": "aT", "vector": "aV", "scalar": "aS", "gpsimd": "aG", "sync": "aY"}


def vap(handle, offset, pattern):
    return bass.AP(handle, offset, pattern)


def _rect(ap):
    t = ap.tensor
    name = t.name
    cls = type(t).__name__
    space = "sbuf" if "SB" in cls else ("psum" if "PSum" in cls else "dram")
    shape = list(t.shape)
    whole = (name, space, 0, 1 << 20, 0, 1 << 40)
    W = shape[-1] if len(shape) >= 2 else shape[0]
    apl = [list(x) for x in ap.ap]
    if len(apl) == 2 and apl[1][0] == 1 and apl[0][0] == W:
        p0 = ap.offset // W
        c0 = ap.offset % W
        p1 = p0 + apl[0][1]
        c1 = c0 + apl[1][1]
    elif len(apl) == 1:
        p0, p1 = 0, 1
        c0, c1 = ap.offset, ap.offset + apl[0][1]
    else:
        return whole
    if space == "psum":
        bank = 512 if t.dtype == F32 else 1024
        p0, p1 = 0, 128
        c0 = (c0 // bank) * bank
        c1 = ((c1 + bank - 1) // bank) * bank
    return (name, space, p0, p1, c0, c1)


def _overlap(a, b):
    return a[0] == b[0] and a[2] < b[3] and b[2] < a[3] and a[4] < b[5] and b[4] < a[5]


class Prog:
    def __init__(self, nc):
        self.nc = nc
        self.segments = [{e: [] for e in ENGINES}]
        self.semctr = {}
        self.sems = {}
        self.ninstr = 0
        self.nwaits = 0
        self.log = {}
        self.clocks = {e: {} for e in ENGINES}
        self.slot_sems = {}
        self.ndma_sems = 0

    def sem(self, name):
        if name not in self.sems:
            h = self.nc.semaphore(name).__enter__()
            self.sems[name] = h
            self.semctr[name] = 0
        return self.sems[name]

    def val(self, name):
        self.sem(name)
        return self.semctr[name]

    CUT_MIN = 1400

    def cut(self, force=False):
        n = sum(len(self.segments[-1][e]) for e in ENGINES)
        if n == 0:
            return
        if not force and n < self.CUT_MIN:
            return
        self.segments.append({e: [] for e in ENGINES})
        cover = {AUTO[e]: self.semctr.get(AUTO[e], 0) for e in ENGINES}
        for e in ENGINES:
            for s, v in cover.items():
                if self.clocks[e].get(s, 0) < v:
                    self.clocks[e][s] = v
        for nm in list(self.log):
            keep = [ent for ent in self.log[nm]
                    if not all(self.clocks[e].get(ent[3], 0) >= ent[4] for e in ENGINES)]
            if keep:
                self.log[nm] = keep
            else:
                del self.log[nm]

    def _emit_wait(self, engine, semname, v, prod_clock):
        ck = self.clocks[engine]
        if ck.get(semname, 0) >= v:
            if prod_clock:
                for s, pv in prod_clock.items():
                    if ck.get(s, 0) < pv:
                        ck[s] = pv
            return
        self.sem(semname)

        def run(eng, h=self.sems[semname], v=v):
            eng.wait_ge(h, v)
        self.segments[-1][engine].append(run)
        self.nwaits += 1
        ck[semname] = v
        if prod_clock:
            for s, pv in prod_clock.items():
                if ck.get(s, 0) < pv:
                    ck[s] = pv

    def _deps(self, engine, reads, writes, is_pe):
        rrects = [_rect(a) for a in reads]
        wrects = [_rect(a) for a in writes]
        needed = {}
        for nm in {r[0] for r in rrects + wrects}:
            for ent in self.log.get(nm, ()):
                k, rect, peng, sem, v, ck = ent
                if k == "w":
                    conflict = any(_overlap(rect, r) for r in rrects + wrects)
                else:
                    conflict = any(_overlap(rect, r) for r in wrects)
                if not conflict:
                    continue
                if is_pe and peng == "tensor" and sem == AUTO["tensor"]:
                    continue
                if sem not in needed or needed[sem][0] < v:
                    needed[sem] = (v, ck)
        for sem, (v, ck) in needed.items():
            self._emit_wait(engine, sem, v, ck)
        return rrects, wrects

    def _record(self, rrects, wrects, engine, sem, v):
        ck = dict(self.clocks[engine])
        for r in rrects:
            self.log.setdefault(r[0], []).append(("r", r, engine, sem, v, ck))
        for r in wrects:
            self.log.setdefault(r[0], []).append(("w", r, engine, sem, v, ck))

    def op(self, engine, fn, reads=(), writes=(), inc=None, is_pe=False):
        rr, wr = self._deps(engine, list(reads), list(writes), is_pe and engine == "tensor")
        auto = AUTO[engine]
        self.sem(auto)
        self.semctr[auto] += 1
        v = self.semctr[auto]
        handles = [(self.sems[auto], 1)]
        ret = v
        if inc is not None:
            nm, amt = inc
            self.sem(nm)
            self.semctr[nm] += amt
            handles.append((self.sems[nm], amt))
            ret = self.semctr[nm]

        def run(eng, fn=fn, handles=handles):
            inst = fn(eng)
            for h, amt in handles:
                inst = inst.then_inc(h, amt)
        self.segments[-1][engine].append(run)
        self.ninstr += 1
        self._record(rr, wr, engine, auto, v)
        return ret

    def mm(self, fn, reads, writes):
        return self.op("tensor", fn, reads, writes, is_pe=True)

    def _slot_sem(self, key):
        if key not in self.slot_sems:
            nm = f"d{self.ndma_sems}"
            self.ndma_sems += 1
            self.slot_sems[key] = nm
        return self.slot_sems[key]

    def dma(self, engine, out, in_, **kw):
        orect = _rect(out)
        irect = _rect(in_)
        key = orect if orect[1] != "dram" else (irect if irect[1] != "dram" else orect)
        semname = self._slot_sem((key[0], key[2], key[4]))
        rr, wr = self._deps(engine, [in_], [out], False)
        self.sem(semname)
        self.semctr[semname] += 16
        v = self.semctr[semname]

        def run(eng, out=out, in_=in_, h=self.sems[semname], kw=kw):
            eng.dma_start(out=out, in_=in_, **kw).then_inc(h, 16)
        self.segments[-1][engine].append(run)
        self.ninstr += 1
        self._record(rr, wr, engine, semname, v)
        return semname, v

    def collective(self, kind, op, groups, ins, outs):
        rr, wr = self._deps("gpsimd", ins, outs, False)
        self.sem("cc")
        self.semctr["cc"] += 1
        v = self.semctr["cc"]

        def run(eng, ins=ins, outs=outs):
            eng.collective_compute(kind, op, replica_groups=groups,
                                   ins=[a.opt() for a in ins],
                                   outs=[a.opt() for a in outs]).then_inc(self.sems["cc"], 1)
        self.segments[-1]["gpsimd"].append(run)
        self.ninstr += 1
        self._record(rr, wr, "gpsimd", "cc", v)
        return v

    def wait(self, engine, semname, val=None):
        self.sem(semname)
        v = self.semctr[semname] if val is None else val
        self._emit_wait(engine, semname, v, None)
        return v

    def finish(self):
        for nm in sorted(set(self.slot_sems.values())):
            self.wait("sync", nm)

    def build(self):
        for seg in self.segments:
            if not any(seg[e] for e in ENGINES):
                continue
            block = self.nc.Block().__enter__()
            for e in ENGINES:
                if not seg[e]:
                    continue
                seq = seg[e]

                def section(eng, seq=seq):
                    for f in seq:
                        f(eng)
                getattr(block, e)(section)
            block.__exit__(None, None, None)
        return self.nc


class Ctx:
    def __init__(self, nc):
        self.nc = nc
        self.p = Prog(nc)

    def sb(self, name, shape, dtype):
        return self.nc.sbuf_tensor(name, shape, dtype).__enter__()

    def ps(self, name, shape, dtype=F32):
        return self.nc.psum_tensor(name, shape, dtype).__enter__()

    def din(self, name, shape, dtype=F32):
        return self.nc.dram_tensor(name, shape, dtype, kind="ExternalInput").ap()

    def dout(self, name, shape, dtype=F32):
        return self.nc.dram_tensor(name, shape, dtype, kind="ExternalOutput").ap()

    def dtmp(self, name, shape, dtype=F32):
        return self.nc.dram_tensor(name, shape, dtype).ap()

    def pools(self):
        self.LNX = self.sb("pool_lnx", [128, 32768], BF16)
        self.A = self.sb("pool_a", [128, 8192], F32)
        self.B = self.sb("pool_b", [128, 16384], BF16)
        self.C = self.sb("pool_c", [128, 16384], BF16)
        self.S = self.sb("pool_s", [128, 2048], F32)
        self.R = self.sb("pool_r", [1, 8192], F32)
        self.pA = self.ps("pool_pa", [128, 1024])
        self.pB = self.ps("pool_pb", [128, 1024])
        self.pC = self.ps("pool_pc", [128, 1024])
        self.pD = self.ps("pool_pd", [128, 512])
        self.pE = self.ps("pool_pe", [128, 1024], BF16)

    def constants(self):
        p = self.p
        self.iden_bf = self.sb("c_iden_bf", [128, 128], BF16)
        self.iden32 = self.sb("c_iden32", [128, 128], F32)
        self.ones_col_bf = self.sb("c_ones_col_bf", [128, 1], BF16)
        self.ones_row32 = self.sb("c_ones_row32", [1, 128], F32)
        self.ones_row_bf = self.sb("c_ones_row_bf", [1, 128], BF16)
        self.cst = self.sb("c_cst", [128, 4], F32)
        for t in (self.iden_bf, self.iden32):
            p.op("gpsimd", lambda g, t=t: g.memset(t[:, :], 0.0), writes=[t[:, :]])
            p.op("gpsimd", lambda g, t=t: g.affine_select(
                out=t[:, :], in_=t[:, :], compare_op=OP.not_equal, fill=1.0,
                base=0, pattern=[[-1, 128]], channel_multiplier=1),
                reads=[t[:, :]], writes=[t[:, :]])
        p.op("gpsimd", lambda g: g.memset(self.ones_col_bf[:, :], 1.0),
             writes=[self.ones_col_bf[:, :]])
        p.op("gpsimd", lambda g: g.memset(self.ones_row32[:, :], 1.0),
             writes=[self.ones_row32[:, :]])
        p.op("gpsimd", lambda g: g.memset(self.ones_row_bf[:, :], 1.0),
             writes=[self.ones_row_bf[:, :]])
        p.op("gpsimd", lambda g: g.memset(self.cst[:, 0:1], 1e-5), writes=[self.cst[:, 0:1]])
        p.op("gpsimd", lambda g: g.memset(self.cst[:, 1:2], 1.0), writes=[self.cst[:, 1:2]])
        p.op("gpsimd", lambda g: g.memset(self.cst[:, 2:3], 0.0), writes=[self.cst[:, 2:3]])
        p.op("gpsimd", lambda g: g.memset(self.cst[:, 3:4], 1e-6), writes=[self.cst[:, 3:4]])


def emit_ln(ctx, xd, tag, var_row=None):
    p = ctx.p
    nK = 32
    xc = ctx.A[:, 0:4096]
    tt_ = ctx.A[:, 4096:6144]
    mu_b = ctx.A[:, 6144:7168]
    rs_b = ctx.A[:, 7168:8192]
    xb = ctx.B[:, 0:2048]
    sq = ctx.B[:, 2048:4096]
    mu = ctx.R[:, 0:1024]
    rstd = ctx.R[:, 1024:2048]
    tmpr = ctx.R[:, 2048:3072]
    lnx = ctx.LNX
    for k in range(nK):
        xck = xc[:, (k % 4) * 1024:(k % 4 + 1) * 1024]
        xbk = xb[:, (k % 2) * 1024:(k % 2 + 1) * 1024]
        sqk = sq[:, (k % 2) * 1024:(k % 2 + 1) * 1024]
        p.dma("sync", xck, xd[k * 128:(k + 1) * 128, :])
        p.op("vector", lambda v, a=xbk, b=xck: v.tensor_copy(out=a, in_=b),
             reads=[xck], writes=[xbk])
        p.op("scalar", lambda s, a=sqk, b=xck: s.activation(out=a, in_=b, func=AF.Square),
             reads=[xck], writes=[sqk])
        st, sp = (k == 0), (k == nK - 1)
        for g in range(2):
            xbg = xbk[:, g * 512:(g + 1) * 512]
            sqg = sqk[:, g * 512:(g + 1) * 512]
            oa = ctx.pA[0:1, g * 512:(g + 1) * 512]
            ob = ctx.pB[0:1, g * 512:(g + 1) * 512]
            p.mm(lambda t, o=oa, r=xbg, st=st, sp=sp: t.matmul(o, ctx.ones_col_bf[:, :], r,
                 start=st, stop=sp), reads=[xbg, ctx.ones_col_bf[:, :]], writes=[oa])
            p.mm(lambda t, o=ob, r=sqg, st=st, sp=sp: t.matmul(o, ctx.ones_col_bf[:, :], r,
                 start=st, stop=sp), reads=[sqg, ctx.ones_col_bf[:, :]], writes=[ob])
    p.op("vector", lambda v: v.tensor_scalar(out=mu, in0=ctx.pA[0:1, :], scalar1=1.0 / 4096,
         scalar2=None, op0=OP.mult, op1=OP.bypass), reads=[ctx.pA[0:1, :]], writes=[mu])
    p.op("vector", lambda v: v.tensor_scalar(out=tmpr, in0=ctx.pB[0:1, :], scalar1=1.0 / 4096,
         scalar2=None, op0=OP.mult, op1=OP.bypass), reads=[ctx.pB[0:1, :]], writes=[tmpr])
    p.op("vector", lambda v: v.tensor_tensor(out=rstd, in0=mu, in1=mu, op=OP.mult),
         reads=[mu], writes=[rstd])
    p.op("vector", lambda v: v.tensor_tensor(out=tmpr, in0=tmpr, in1=rstd, op=OP.subtract),
         reads=[tmpr, rstd], writes=[tmpr])
    if var_row is not None:
        p.op("vector", lambda v: v.tensor_copy(out=var_row, in_=tmpr), reads=[tmpr],
             writes=[var_row])
    p.op("scalar", lambda s: s.activation(out=rstd, in_=tmpr, func=AF.Sqrt,
         bias=ctx.cst[0:1, 0:1], scale=1.0), reads=[tmpr, ctx.cst[0:1, 0:1]], writes=[rstd])
    p.op("vector", lambda v: v.reciprocal(out=rstd, in_=rstd), reads=[rstd], writes=[rstd])
    p.cut()
    for g in range(2):
        p.mm(lambda t, g=g: t.matmul(ctx.pA[:, g * 512:(g + 1) * 512], ctx.ones_row32[:, :],
             mu[0:1, g * 512:(g + 1) * 512], start=True, stop=True),
             reads=[mu[0:1, g * 512:(g + 1) * 512], ctx.ones_row32[:, :]],
             writes=[ctx.pA[:, g * 512:(g + 1) * 512]])
        p.mm(lambda t, g=g: t.matmul(ctx.pB[:, g * 512:(g + 1) * 512], ctx.ones_row32[:, :],
             rstd[0:1, g * 512:(g + 1) * 512], start=True, stop=True),
             reads=[rstd[0:1, g * 512:(g + 1) * 512], ctx.ones_row32[:, :]],
             writes=[ctx.pB[:, g * 512:(g + 1) * 512]])
    p.op("vector", lambda v: v.tensor_copy(out=mu_b, in_=ctx.pA[:, :]),
         reads=[ctx.pA[:, :]], writes=[mu_b])
    p.op("vector", lambda v: v.tensor_copy(out=rs_b, in_=ctx.pB[:, :]),
         reads=[ctx.pB[:, :]], writes=[rs_b])
    for k in range(nK):
        xck = xc[:, (k % 4) * 1024:(k % 4 + 1) * 1024]
        ttk = tt_[:, (k % 2) * 1024:(k % 2 + 1) * 1024]
        lnk = lnx[:, k * 1024:(k + 1) * 1024]
        p.dma("sync", xck, xd[k * 128:(k + 1) * 128, :])
        eng = "vector" if k % 2 == 0 else "gpsimd"
        p.op(eng, lambda e, a=ttk, b=xck: e.tensor_tensor(out=a, in0=b, in1=mu_b, op=OP.subtract),
             reads=[xck, mu_b], writes=[ttk])
        p.op(eng, lambda e, a=lnk, b=ttk: e.tensor_tensor(out=a, in0=b, in1=rs_b, op=OP.mult),
             reads=[ttk, rs_b], writes=[lnk])
    p.cut()


def emit_l0ffn(ctx, xd, w, tag):
    p = ctx.p
    lnx = ctx.LNX
    gw1t = ctx.C[:, 0:1024]
    aall = ctx.C[:, 1024:5120]
    bflat = ctx.C[:, 5120:9216]
    tmps = ctx.C[:, 9216:10240]
    wexp = ctx.C[:, 10240:11264]
    gw2t = ctx.C[0:32, 11264:11272]
    esel = ctx.C[0:8, 11272:11400]
    wTbf = ctx.C[0:8, 11400:12424]
    gb1 = ctx.S[0:32, 0:1]
    gb2 = ctx.S[0:8, 1:2]
    gamma = ctx.S[0:32, 2:3]
    beta = ctx.S[0:32, 3:4]
    alc = ctx.S[0:8, 4:6]
    g12 = ctx.S[:, 8:12]
    gtm = ctx.S[:, 16:24]
    m8 = ctx.S[:, 24:32]
    wtm = ctx.S[:, 32:48]
    h32 = ctx.S[0:32, 48:1072]
    hbf = ctx.B[0:32, 4096:5120]
    hsq = ctx.B[0:32, 5120:6144]
    hrelu = ctx.B[0:32, 6144:7168]
    ohs = ctx.B[:, 7168:7184]
    gfm = ctx.A[0:8, 0:1024]
    wT1 = ctx.A[0:8, 1024:2048]
    wT2 = ctx.A[0:8, 2048:3072]
    xc2 = ctx.A[:, 3072:4096]
    xnew = ctx.A[:, 4096:5120]
    hmu = ctx.R[:, 3088:4112]
    hrstd = ctx.R[:, 4112:5136]
    htmp = ctx.R[:, 5136:6160]
    crow = ctx.R[:, 3072:3088]
    for dst, src, eng in [(gw1t, w["gw1t_t"], "gpsimd"), (gw2t, w["gw2t"], "gpsimd"),
                          (esel, w["esel"], "gpsimd"), (aall, w["a_all_t"], "gpsimd"),
                          (bflat, w["b_flat"], "gpsimd"), (gb1, w["gb1"], "sync"),
                          (gb2, w["gb2"], "sync"), (gamma, w["lngamma"], "sync"),
                          (beta, w["lnbeta"], "sync")]:
        p.dma(eng, dst, src)
    for k in range(32):
        for g in range(2):
            rhs = lnx[:, k * 1024 + g * 512:k * 1024 + (g + 1) * 512]
            o = ctx.pA[0:32, g * 512:(g + 1) * 512]
            l = gw1t[:, k * 32:(k + 1) * 32]
            p.mm(lambda t, o=o, l=l, r=rhs, st=(k == 0), sp=(k == 31):
                 t.matmul(o, l, r, start=st, stop=sp), reads=[l, rhs], writes=[o])
    p.op("vector", lambda v: v.tensor_scalar(out=h32, in0=ctx.pA[0:32, :], scalar1=gb1,
         scalar2=None, op0=OP.add, op1=OP.bypass), reads=[ctx.pA[0:32, :], gb1], writes=[h32])
    p.op("vector", lambda v: v.tensor_copy(out=hbf, in_=h32), reads=[h32], writes=[hbf])
    p.op("scalar", lambda s: s.activation(out=hsq, in_=h32, func=AF.Square),
         reads=[h32], writes=[hsq])
    p.cut()
    for g in range(2):
        hb = hbf[:, g * 512:(g + 1) * 512]
        hq = hsq[:, g * 512:(g + 1) * 512]
        p.mm(lambda t, o=ctx.pB[0:1, g * 512:(g + 1) * 512], r=hb:
             t.matmul(o, ctx.ones_col_bf[0:32, :], r, start=True, stop=True),
             reads=[hb, ctx.ones_col_bf[0:32, :]], writes=[ctx.pB[0:1, g * 512:(g + 1) * 512]])
        p.mm(lambda t, o=ctx.pC[0:1, g * 512:(g + 1) * 512], r=hq:
             t.matmul(o, ctx.ones_col_bf[0:32, :], r, start=True, stop=True),
             reads=[hq, ctx.ones_col_bf[0:32, :]], writes=[ctx.pC[0:1, g * 512:(g + 1) * 512]])
    p.op("vector", lambda v: v.tensor_scalar(out=hmu, in0=ctx.pB[0:1, :], scalar1=1.0 / 32,
         scalar2=None, op0=OP.mult, op1=OP.bypass), reads=[ctx.pB[0:1, :]], writes=[hmu])
    p.op("vector", lambda v: v.tensor_scalar(out=htmp, in0=ctx.pC[0:1, :], scalar1=1.0 / 32,
         scalar2=None, op0=OP.mult, op1=OP.bypass), reads=[ctx.pC[0:1, :]], writes=[htmp])
    p.op("vector", lambda v: v.tensor_tensor(out=hrstd, in0=hmu, in1=hmu, op=OP.mult),
         reads=[hmu], writes=[hrstd])
    p.op("vector", lambda v: v.tensor_tensor(out=htmp, in0=htmp, in1=hrstd, op=OP.subtract),
         reads=[htmp, hrstd], writes=[htmp])
    p.op("scalar", lambda s: s.activation(out=hrstd, in_=htmp, func=AF.Sqrt,
         bias=ctx.cst[0:1, 0:1], scale=1.0), reads=[htmp, ctx.cst[0:1, 0:1]], writes=[hrstd])
    p.op("vector", lambda v: v.reciprocal(out=hrstd, in_=hrstd), reads=[hrstd], writes=[hrstd])
    p.cut()
    for g in range(2):
        p.mm(lambda t, g=g: t.matmul(ctx.pB[0:32, g * 512:(g + 1) * 512],
             ctx.ones_row32[0:1, 0:32], hmu[0:1, g * 512:(g + 1) * 512], start=True, stop=True),
             reads=[hmu[0:1, g * 512:(g + 1) * 512], ctx.ones_row32[0:1, 0:32]],
             writes=[ctx.pB[0:32, g * 512:(g + 1) * 512]])
        p.mm(lambda t, g=g: t.matmul(ctx.pC[0:32, g * 512:(g + 1) * 512],
             ctx.ones_row32[0:1, 0:32], hrstd[0:1, g * 512:(g + 1) * 512], start=True, stop=True),
             reads=[hrstd[0:1, g * 512:(g + 1) * 512], ctx.ones_row32[0:1, 0:32]],
             writes=[ctx.pC[0:32, g * 512:(g + 1) * 512]])
    p.op("vector", lambda v: v.tensor_tensor(out=h32, in0=h32, in1=ctx.pB[0:32, :],
         op=OP.subtract), reads=[h32, ctx.pB[0:32, :]], writes=[h32])
    p.op("vector", lambda v: v.tensor_tensor(out=h32, in0=h32, in1=ctx.pC[0:32, :],
         op=OP.mult), reads=[h32, ctx.pC[0:32, :]], writes=[h32])
    p.op("vector", lambda v: v.tensor_scalar(out=h32, in0=h32, scalar1=gamma, scalar2=beta,
         op0=OP.mult, op1=OP.add), reads=[h32, gamma, beta], writes=[h32])
    p.op("vector", lambda v: v.tensor_scalar(out=hrelu, in0=h32, scalar1=0.0, scalar2=None,
         op0=OP.max, op1=OP.bypass), reads=[h32], writes=[hrelu])
    for g in range(2):
        hr = hrelu[:, g * 512:(g + 1) * 512]
        p.mm(lambda t, o=ctx.pC[0:8, g * 512:(g + 1) * 512], r=hr:
             t.matmul(o, gw2t, r, start=True, stop=True),
             reads=[gw2t, hr], writes=[ctx.pC[0:8, g * 512:(g + 1) * 512]])
    p.op("vector", lambda v: v.tensor_scalar(out=gfm, in0=ctx.pC[0:8, :], scalar1=gb2,
         scalar2=None, op0=OP.add, op1=OP.bypass), reads=[ctx.pC[0:8, :], gb2], writes=[gfm])
    p.cut()
    for tt in range(8):
        gsl = gfm[:, tt * 128:(tt + 1) * 128]
        p.mm(lambda t, g=gsl: t.transpose(ctx.pA[:, 0:8], g, ctx.iden32[0:8, 0:8]),
             reads=[gsl, ctx.iden32[0:8, 0:8]], writes=[ctx.pA[:, 0:512]])
        p.op("vector", lambda v: v.tensor_copy(out=gtm, in_=ctx.pA[:, 0:8]),
             reads=[ctx.pA[:, 0:512]], writes=[gtm])
        p.op("vector", lambda v: v.max(out=m8, in_=gtm), reads=[gtm], writes=[m8])
        p.op("vector", lambda v: v.tensor_tensor(out=g12[:, 0:1], in0=m8[:, 0:1], in1=m8[:, 1:2],
             op=OP.subtract), reads=[m8], writes=[g12[:, 0:1]])
        p.op("scalar", lambda s: s.activation(out=g12[:, 1:2], in_=g12[:, 0:1], func=AF.Sigmoid),
             reads=[g12[:, 0:1]], writes=[g12[:, 1:2]])
        p.op("vector", lambda v: v.tensor_scalar(out=g12[:, 2:3], in0=g12[:, 1:2], scalar1=-1.0,
             scalar2=1.0, op0=OP.mult, op1=OP.add), reads=[g12[:, 1:2]], writes=[g12[:, 2:3]])
        p.op("vector", lambda v: v.tensor_scalar(out=ohs[:, 0:8], in0=gtm, scalar1=m8[:, 0:1],
             scalar2=None, op0=OP.is_equal, op1=OP.bypass), reads=[gtm, m8], writes=[ohs[:, 0:8]])
        p.op("vector", lambda v: v.tensor_scalar(out=ohs[:, 8:16], in0=gtm, scalar1=m8[:, 1:2],
             scalar2=None, op0=OP.is_equal, op1=OP.bypass), reads=[gtm, m8], writes=[ohs[:, 8:16]])
        p.mm(lambda t, st=(tt == 0), sp=(tt == 7): t.matmul(ctx.pB[0:1, 0:16],
             ctx.ones_col_bf[:, :], ohs, start=st, stop=sp),
             reads=[ohs, ctx.ones_col_bf[:, :]], writes=[ctx.pB[0:1, 0:512]])
        p.op("vector", lambda v: v.tensor_scalar(out=wtm[:, 0:8], in0=ohs[:, 0:8],
             scalar1=g12[:, 1:2], scalar2=None, op0=OP.mult, op1=OP.bypass),
             reads=[ohs[:, 0:8], g12[:, 1:2]], writes=[wtm[:, 0:8]])
        p.op("vector", lambda v: v.tensor_scalar(out=wtm[:, 8:16], in0=ohs[:, 8:16],
             scalar1=g12[:, 2:3], scalar2=None, op0=OP.mult, op1=OP.bypass),
             reads=[ohs[:, 8:16], g12[:, 2:3]], writes=[wtm[:, 8:16]])
        p.mm(lambda t: t.transpose(ctx.pC[0:8, 0:128], wtm[:, 0:8], ctx.iden32[:, :]),
             reads=[wtm[:, 0:8], ctx.iden32[:, :]], writes=[ctx.pC[0:8, 0:512]])
        p.mm(lambda t: t.transpose(ctx.pC[0:8, 512:640], wtm[:, 8:16], ctx.iden32[:, :]),
             reads=[wtm[:, 8:16], ctx.iden32[:, :]], writes=[ctx.pC[0:8, 512:1024]])
        p.op("vector", lambda v, tt=tt: v.tensor_copy(out=wT1[:, tt * 128:(tt + 1) * 128],
             in_=ctx.pC[0:8, 0:128]), reads=[ctx.pC[0:8, 0:512]],
             writes=[wT1[:, tt * 128:(tt + 1) * 128]])
        p.op("vector", lambda v, tt=tt: v.tensor_copy(out=wT2[:, tt * 128:(tt + 1) * 128],
             in_=ctx.pC[0:8, 512:640]), reads=[ctx.pC[0:8, 512:1024]],
             writes=[wT2[:, tt * 128:(tt + 1) * 128]])
    p.op("vector", lambda v: v.tensor_copy(out=crow, in_=ctx.pB[0:1, 0:16]),
         reads=[ctx.pB[0:1, 0:512]], writes=[crow])
    p.dma("gpsimd", w["cb_in"], crow)
    p.collective("AllReduce", OP.add, [list(range(8))], [w["cb_in"]], [w["cb_out"]])
    p.dma("gpsimd", alc[:, 0:1], w["cb_out"][0:1, 0:8].rearrange("a e -> e a"))
    p.dma("gpsimd", alc[:, 1:2], w["cb_out"][0:1, 8:16].rearrange("a e -> e a"))
    p.op("vector", lambda v: v.tensor_scalar(out=alc, in0=alc, scalar1=3072.0, scalar2=None,
         op0=OP.is_le, op1=OP.bypass), reads=[alc], writes=[alc])
    p.op("vector", lambda v: v.tensor_scalar(out=wT1, in0=wT1, scalar1=alc[:, 0:1],
         scalar2=None, op0=OP.mult, op1=OP.bypass), reads=[wT1, alc], writes=[wT1])
    p.op("vector", lambda v: v.scalar_tensor_tensor(out=wTbf, in0=wT2, scalar=alc[:, 1:2],
         in1=wT1, op0=OP.mult, op1=OP.add), reads=[wT2, alc, wT1], writes=[wTbf])
    p.cut()
    for g in range(2):
        wsl = wTbf[:, g * 512:(g + 1) * 512]
        o = (ctx.pA if g == 0 else ctx.pB)[:, 0:512]
        p.mm(lambda t, o=o, r=wsl: t.matmul(o, esel, r, start=True, stop=True),
             reads=[esel, wsl], writes=[o])
    p.op("vector", lambda v: v.tensor_copy(out=wexp[:, 0:512], in_=ctx.pA[:, 0:512]),
         reads=[ctx.pA[:, 0:512]], writes=[wexp[:, 0:512]])
    p.op("vector", lambda v: v.tensor_copy(out=wexp[:, 512:1024], in_=ctx.pB[:, 0:512]),
         reads=[ctx.pB[:, 0:512]], writes=[wexp[:, 512:1024]])
    for g in range(2):
        o = (ctx.pA if g == 0 else ctx.pB)[:, 512:1024]
        for k in range(32):
            rhs = lnx[:, k * 1024 + g * 512:k * 1024 + (g + 1) * 512]
            l = aall[:, k * 128:(k + 1) * 128]
            p.mm(lambda t, o=o, l=l, r=rhs, st=(k == 0), sp=(k == 31):
                 t.matmul(o, l, r, start=st, stop=sp), reads=[l, rhs], writes=[o])
    p.op("vector", lambda v: v.tensor_tensor(out=tmps[:, 0:512], in0=ctx.pA[:, 512:1024],
         in1=wexp[:, 0:512], op=OP.mult), reads=[ctx.pA[:, 512:1024], wexp[:, 0:512]],
         writes=[tmps[:, 0:512]])
    p.op("vector", lambda v: v.tensor_tensor(out=tmps[:, 512:1024], in0=ctx.pB[:, 512:1024],
         in1=wexp[:, 512:1024], op=OP.mult), reads=[ctx.pB[:, 512:1024], wexp[:, 512:1024]],
         writes=[tmps[:, 512:1024]])
    p.cut()
    wtsb = ctx.B[:, 8192:16384]
    for m in range(32):
        wsl = wtsb[:, (m % 2) * 4096:(m % 2 + 1) * 4096]
        p.dma("gpsimd", wsl, w["wt"][m])
        for g in range(2):
            idx = m * 2 + g
            xsl = xc2[:, (idx % 2) * 512:(idx % 2 + 1) * 512]
            nsl = xnew[:, (idx % 2) * 512:(idx % 2 + 1) * 512]
            xdr = xd[m * 128:(m + 1) * 128, g * 512:(g + 1) * 512]
            mp = ctx.pC[:, 0:512] if idx % 2 == 0 else ctx.pD[:, 0:512]
            p.dma("sync", xsl, xdr)
            for k in range(32):
                rhs = lnx[:, k * 1024 + g * 512:k * 1024 + (g + 1) * 512]
                lhs = wsl[:, k * 128:(k + 1) * 128]
                p.mm(lambda t, o=mp, l=lhs, r=rhs, st=(k == 0): t.matmul(o, l, r, start=st,
                     stop=False), reads=[lhs, rhs], writes=[mp])
            p.mm(lambda t, o=mp, l=bflat[:, m * 128:(m + 1) * 128],
                 r=tmps[:, g * 512:(g + 1) * 512]: t.matmul(o, l, r, start=False, stop=True),
                 reads=[bflat[:, m * 128:(m + 1) * 128], tmps[:, g * 512:(g + 1) * 512]],
                 writes=[mp])
            p.op("vector", lambda v, o=nsl, a=mp, b=xsl: v.scalar_tensor_tensor(
                 out=o, in0=a, scalar=0.5, in1=b, op0=OP.mult, op1=OP.add),
                 reads=[mp, xsl], writes=[nsl])
            p.dma("sync", xdr, nsl)
        if m % 8 == 7:
            p.cut()


def emit_stream_proj(ctx, xd, src, wt, scale, nM=32, extra_mm=None):
    p = ctx.p
    wtsb = ctx.B[:, 8192:16384]
    xc2 = ctx.A[:, 3072:4096]
    xnew = ctx.A[:, 4096:5120]
    for m in range(nM):
        wsl = wtsb[:, (m % 2) * 4096:(m % 2 + 1) * 4096]
        p.dma("gpsimd", wsl, wt[m])
        for g in range(2):
            idx = m * 2 + g
            xsl = xc2[:, (idx % 2) * 512:(idx % 2 + 1) * 512]
            nsl = xnew[:, (idx % 2) * 512:(idx % 2 + 1) * 512]
            xdr = xd[m * 128:(m + 1) * 128, g * 512:(g + 1) * 512]
            mp = ctx.pC[:, 0:512] if idx % 2 == 0 else ctx.pD[:, 0:512]
            p.dma("sync", xsl, xdr)
            for k in range(32):
                rhs = src[:, k * 1024 + g * 512:k * 1024 + (g + 1) * 512]
                lhs = wsl[:, k * 128:(k + 1) * 128]
                p.mm(lambda t, o=mp, l=lhs, r=rhs, st=(k == 0),
                     sp=(k == 31 and extra_mm is None):
                     t.matmul(o, l, r, start=st, stop=sp), reads=[lhs, rhs], writes=[mp])
            if extra_mm is not None:
                extra_mm(p, mp, m, g)
            p.op("vector", lambda v, o=nsl, a=mp, b=xsl: v.scalar_tensor_tensor(
                 out=o, in0=a, scalar=scale, in1=b, op0=OP.mult, op1=OP.add),
                 reads=[mp, xsl], writes=[nsl])
            p.dma("sync", xdr, nsl)
        if m % 8 == 7:
            p.cut()


def emit_l0attn(ctx, xd, w):
    p = ctx.p
    lnx = ctx.LNX
    S8 = 1.0 / float(np.sqrt(128.0))
    KMAX = 32.0
    cosf = ctx.A[:, 0:1024]
    sinf = ctx.A[:, 1024:2048]
    p.dma("sync", cosf, w["cos"])
    p.dma("sync", sinf, w["sin"])
    wqs = ctx.B[:, 0:8192]
    wks = ctx.B[:, 8192:16384]
    wvs = ctx.C[:, 0:8192]
    qro = ctx.C[:, 8192:9216]
    kro = ctx.C[:, 9216:10240]
    vbf = ctx.C[:, 10240:11264]
    vtm = ctx.C[:, 12288:13312]
    qc = ctx.A[:, 2048:3072]
    qs = ctx.A[:, 3072:4096]
    kc = ctx.A[:, 4096:5120]
    ks_ = ctx.A[:, 5120:6144]
    for m in range(32):
        wq_m = wqs[:, (m % 2) * 4096:(m % 2 + 1) * 4096]
        wk_m = wks[:, (m % 2) * 4096:(m % 2 + 1) * 4096]
        wv_m = wvs[:, (m % 2) * 4096:(m % 2 + 1) * 4096]
        p.dma("gpsimd", wq_m, w["wq_t"][m])
        p.dma("gpsimd", wk_m, w["wk_t"][m])
        p.dma("gpsimd", wv_m, w["wv_t"][m])
        for g in range(2):
            for k in range(32):
                rhs = lnx[:, k * 1024 + g * 512:k * 1024 + (g + 1) * 512]
                for ws_, pp in ((wq_m, ctx.pA), (wk_m, ctx.pB), (wv_m, ctx.pC)):
                    o = pp[:, g * 512:(g + 1) * 512]
                    p.mm(lambda t, o=o, l=ws_[:, k * 128:(k + 1) * 128], r=rhs, st=(k == 0),
                         sp=(k == 31): t.matmul(o, l, r, start=st, stop=sp),
                         reads=[ws_[:, k * 128:(k + 1) * 128], rhs], writes=[o])
        p.op("vector", lambda v: v.tensor_copy(out=vbf, in_=ctx.pC[:, :]),
             reads=[ctx.pC[:, :]], writes=[vbf])
        for j in range(8):
            p.mm(lambda t, j=j: t.transpose(ctx.pE[:, j * 128:(j + 1) * 128],
                 vbf[:, j * 128:(j + 1) * 128], ctx.iden_bf[:, :]),
                 reads=[vbf[:, j * 128:(j + 1) * 128], ctx.iden_bf[:, :]],
                 writes=[ctx.pE[:, j * 128:(j + 1) * 128]])
        p.op("vector", lambda v: v.tensor_copy(out=vtm, in_=ctx.pE[:, :]),
             reads=[ctx.pE[:, :]], writes=[vtm])
        p.dma("sync", w["v_loc"][:, m * 128:(m + 1) * 128].rearrange("(j p) c -> p j c", p=128),
              vtm[:, :].rearrange("p (j c) -> p j c", c=128))
        for (pp, cc, ssn, ro) in ((ctx.pA, qc, qs, qro), (ctx.pB, kc, ks_, kro)):
            p.op("vector", lambda v, o=cc, a=pp: v.tensor_tensor(out=o, in0=a[:, :], in1=cosf,
                 op=OP.mult), reads=[pp[:, :], cosf], writes=[cc])
            p.op("vector", lambda v, o=ssn, a=pp: v.tensor_tensor(
                 out=o[0:64, :], in0=a[64:128, :], in1=sinf[64:128, :], op=OP.mult),
                 reads=[pp[:, :], sinf], writes=[ssn[0:64, :]])
            p.op("vector", lambda v, o=ssn, a=pp: v.tensor_tensor(
                 out=o[64:128, :], in0=a[0:64, :], in1=sinf[0:64, :], op=OP.mult),
                 reads=[pp[:, :], sinf], writes=[ssn[64:128, :]])
            p.op("gpsimd", lambda e, o=ro, a=cc, b=ssn: e.tensor_tensor(
                 out=o[0:64, :], in0=a[0:64, :], in1=b[0:64, :], op=OP.subtract),
                 reads=[cc[0:64, :], ssn[0:64, :]], writes=[ro[0:64, :]])
            p.op("gpsimd", lambda e, o=ro, a=cc, b=ssn: e.tensor_tensor(
                 out=o[64:128, :], in0=a[64:128, :], in1=b[64:128, :], op=OP.add),
                 reads=[cc[64:128, :], ssn[64:128, :]], writes=[ro[64:128, :]])
        p.dma("sync", w["q_loc"][m * 128:(m + 1) * 128, :], qro)
        p.dma("sync", w["k_bounce"][m * 128:(m + 1) * 128, :], kro)
        if m % 4 == 3:
            p.cut()
    p.cut()
    grp = [[0, 1, 2, 3], [4, 5, 6, 7]]
    p.collective("AllGather", OP.bypass, grp, [w["k_bounce"]], [w["k_all"]])
    p.collective("AllGather", OP.bypass, grp, [w["v_loc"]], [w["v_all"]])
    p.cut()
    mtab = ctx.C[:, 4096:12288]
    p.dma("gpsimd", mtab, w["mtab"])
    qh = ctx.B[:, 0:1024]
    ktile = ctx.B[:, 1024:2048]
    vtile = ctx.B[:, 2048:3072]
    kown = ctx.B[:, 3072:4096]
    qkp = ctx.C[:, 1536:2560]
    qsq2 = ctx.C[:, 2560:3584]
    negb = ctx.C[0:1, 3072:4096]
    rden = ctx.S[0:1, 0:512]
    d_row = ctx.A[0:1, 6144:7168]
    n_row = ctx.A[0:1, 7168:8192]
    t_row = ctx.A[0:1, 5120:6144]
    rbc = ctx.A[:, 2048:2560]
    stslots = [ctx.pA[:, 0:512], ctx.pA[:, 512:1024], ctx.pB[:, 0:512]]
    ptslots = [ctx.C[:, i * 512:(i + 1) * 512] for i in range(3)]
    sti = 0
    myrank = None
    for h in range(32):
        p.dma("sync", qh, w["q_loc"][h * 128:(h + 1) * 128, :])
        p.dma("sync", kown, w["k_bounce"][h * 128:(h + 1) * 128, :])
        p.op("vector", lambda v: v.tensor_tensor(out=qkp, in0=qh, in1=kown, op=OP.mult),
             reads=[qh, kown], writes=[qkp])
        p.op("scalar", lambda s: s.activation(out=qsq2, in_=qh, func=AF.Square),
             reads=[qh], writes=[qsq2])
        for g in range(2):
            p.mm(lambda t, r=qkp[:, g * 512:(g + 1) * 512], o=ctx.pD[0:1, 0:512]:
                 t.matmul(o, ctx.ones_col_bf[:, :], r, start=True, stop=True),
                 reads=[qkp[:, g * 512:(g + 1) * 512], ctx.ones_col_bf[:, :]],
                 writes=[ctx.pD[0:1, 0:512]])
            p.op("vector", lambda v, o=d_row[0:1, g * 512:(g + 1) * 512]: v.tensor_copy(
                 out=o, in_=ctx.pD[0:1, 0:512]), reads=[ctx.pD[0:1, 0:512]],
                 writes=[d_row[0:1, g * 512:(g + 1) * 512]])
            p.mm(lambda t, r=qsq2[:, g * 512:(g + 1) * 512], o=ctx.pD[0:1, 0:512]:
                 t.matmul(o, ctx.ones_col_bf[:, :], r, start=True, stop=True),
                 reads=[qsq2[:, g * 512:(g + 1) * 512], ctx.ones_col_bf[:, :]],
                 writes=[ctx.pD[0:1, 0:512]])
            p.op("vector", lambda v, o=n_row[0:1, g * 512:(g + 1) * 512]: v.tensor_copy(
                 out=o, in_=ctx.pD[0:1, 0:512]), reads=[ctx.pD[0:1, 0:512]],
                 writes=[n_row[0:1, g * 512:(g + 1) * 512]])
        p.op("scalar", lambda s: s.activation(out=t_row, in_=n_row, func=AF.Sqrt,
             bias=ctx.cst[0:1, 0:1], scale=1.0), reads=[n_row, ctx.cst[0:1, 0:1]],
             writes=[t_row])
        p.op("vector", lambda v: v.tensor_scalar(out=t_row, in0=t_row, scalar1=KMAX,
             scalar2=None, op0=OP.mult, op1=OP.bypass), reads=[t_row], writes=[t_row])
        p.op("vector", lambda v: v.tensor_scalar(out=d_row, in0=d_row, scalar1=80.0 / S8,
             scalar2=None, op0=OP.add, op1=OP.bypass), reads=[d_row], writes=[d_row])
        p.op("vector", lambda v: v.tensor_tensor(out=t_row, in0=t_row, in1=d_row, op=OP.min),
             reads=[t_row, d_row], writes=[t_row])
        p.op("vector", lambda v: v.tensor_scalar(out=negb, in0=t_row, scalar1=-1.0,
             scalar2=None, op0=OP.mult, op1=OP.bypass), reads=[t_row], writes=[negb])
        for qg in range(2):
            nstrips = 4 if qg == 0 else 8
            qsl = qh[:, qg * 512:(qg + 1) * 512]
            nbs = negb[0:1, qg * 512:(qg + 1) * 512]
            O = ctx.pC[:, 0:512]
            den = ctx.pB[0:1, 512:1024]
            npairs = nstrips * 4
            pi = 0
            for s_ in range(nstrips):
                r = min(s_, 7 - s_)
                off = 0 if s_ < 4 else 512
                kts = ktile[:, (s_ % 2) * 512:(s_ % 2 + 1) * 512]
                vts = vtile[:, (s_ % 2) * 512:(s_ % 2 + 1) * 512]
                p.dma("sync", kts, w["k_all"][r * 4096 + h * 128:r * 4096 + (h + 1) * 128,
                                              off:off + 512])
                p.dma("sync", vts[:, :].rearrange("p (j c) -> p j c", c=128),
                      w["v_all"][r * 1024 + off:r * 1024 + off + 512,
                                 h * 128:(h + 1) * 128].rearrange("(j p) c -> p j c", p=128))
                use_mask = (qg == 0) or (s_ >= 4)
                for kt in range(4):
                    st = stslots[sti % 3]
                    pt = ptslots[sti % 3]
                    sti += 1
                    p.mm(lambda t, o=st, l=kts[:, kt * 128:(kt + 1) * 128], r_=qsl:
                         t.matmul(o, l, r_, start=True, stop=False),
                         reads=[kts[:, kt * 128:(kt + 1) * 128], qsl], writes=[st])
                    p.mm(lambda t, o=st, r_=nbs: t.matmul(o, ctx.ones_row_bf[:, :], r_,
                         start=False, stop=not use_mask),
                         reads=[nbs, ctx.ones_row_bf[:, :]], writes=[st])
                    if use_mask:
                        mcol = s_ * 1024
                        msl = mtab[:, mcol + 384 - 128 * kt:mcol + 896 - 128 * kt]
                        p.mm(lambda t, o=st, m_=msl: t.matmul(o, ctx.iden_bf[:, :], m_,
                             start=False, stop=True), reads=[ctx.iden_bf[:, :], msl],
                             writes=[st])
                    p.op("scalar", lambda s, o=pt, i_=st: s.activation(
                         out=o, in_=i_, func=AF.Exp, scale=S8), reads=[st], writes=[pt])
                    p.mm(lambda t, o=O, l=vts[:, kt * 128:(kt + 1) * 128], r_=pt,
                         st_=(pi == 0), sp=(pi == npairs - 1):
                         t.matmul(o, l, r_, start=st_, stop=sp),
                         reads=[vts[:, kt * 128:(kt + 1) * 128], pt], writes=[O])
                    p.mm(lambda t, o=den, r_=pt, st_=(pi == 0), sp=(pi == npairs - 1):
                         t.matmul(o, ctx.ones_col_bf[:, :], r_, start=st_, stop=sp),
                         reads=[pt, ctx.ones_col_bf[:, :]], writes=[den])
                    pi += 1
            p.op("vector", lambda v: v.tensor_copy(out=rden, in_=den), reads=[den], writes=[rden])
            p.op("vector", lambda v: v.reciprocal(out=rden, in_=rden), reads=[rden], writes=[rden])
            p.mm(lambda t: t.matmul(ctx.pD[:, 0:512], ctx.ones_row32[:, :], rden,
                 start=True, stop=True), reads=[rden, ctx.ones_row32[:, :]],
                 writes=[ctx.pD[:, 0:512]])
            p.op("vector", lambda v: v.tensor_copy(out=rbc, in_=ctx.pD[:, 0:512]),
                 reads=[ctx.pD[:, 0:512]], writes=[rbc])
            p.op("vector", lambda v, o=lnx[:, h * 1024 + qg * 512:h * 1024 + (qg + 1) * 512],
                 O_=O: v.tensor_tensor(out=o, in0=O_, in1=rbc, op=OP.mult),
                 reads=[O, rbc], writes=[lnx[:, h * 1024 + qg * 512:h * 1024 + (qg + 1) * 512]])
        if h % 2 == 1:
            p.cut()
    p.cut()
    emit_stream_proj(ctx, xd, lnx, w["wo_t"], 1.0)
    p.cut()


def emit_l1ffn(ctx, xd, w, var_row):
    p = ctx.p
    lnx = ctx.LNX
    SR48 = 1.0 / float(np.sqrt(48.0))
    xck = ctx.A[:, 0:2048]
    xnk = ctx.A[:, 2048:4096]
    for k in range(32):
        a = xck[:, (k % 2) * 1024:(k % 2 + 1) * 1024]
        b = xnk[:, (k % 2) * 1024:(k % 2 + 1) * 1024]
        xdr = xd[k * 128:(k + 1) * 128, :]
        p.dma("sync", a, xdr)
        p.op("vector", lambda v, o=b, l=lnx[:, k * 1024:(k + 1) * 1024], a=a:
             v.scalar_tensor_tensor(out=o, in0=l, scalar=0.5, in1=a, op0=OP.mult, op1=OP.add),
             reads=[lnx[:, k * 1024:(k + 1) * 1024], a], writes=[b])
        p.dma("sync", xdr, b)
    p.cut()
    scl3 = ctx.S[0:1, 74:77]
    p.dma("sync", scl3[0:1, 0:1], w["bu2"])
    p.dma("sync", scl3[0:1, 1:2], w["gate_w1"])
    p.dma("sync", scl3[0:1, 2:3], w["gate_bias"])
    vrec = ctx.A[0:1, 4096:5120]
    vhr = ctx.A[0:1, 5120:6144]
    logr = ctx.A[0:1, 6144:7168]
    srow = ctx.A[0:1, 7168:8192]
    p.op("vector", lambda v: v.tensor_scalar(out=vrec, in0=var_row, scalar1=1e-5, scalar2=None,
         op0=OP.add, op1=OP.bypass), reads=[var_row], writes=[vrec])
    p.op("vector", lambda v: v.reciprocal(out=vrec, in_=vrec), reads=[vrec], writes=[vrec])
    p.op("vector", lambda v: v.tensor_tensor(out=vhr, in0=var_row, in1=vrec, op=OP.mult),
         reads=[var_row, vrec], writes=[vhr])
    p.op("scalar", lambda s: s.activation(out=logr, in_=vhr, func=AF.Ln,
         bias=ctx.cst[0:1, 1:2], scale=1.0), reads=[vhr, ctx.cst[0:1, 1:2]], writes=[logr])
    p.op("vector", lambda v: v.reduce_sum(out=srow[0:1, 0:1], in_=logr,
         axis=mybir.AxisListType.X), reads=[logr], writes=[srow[0:1, 0:1]])
    p.dma("gpsimd", w["cb2_in"], srow[0:1, 0:1])
    p.collective("AllReduce", OP.add, [list(range(8))], [w["cb2_in"]], [w["cb2_out"]])
    p.dma("gpsimd", srow[0:1, 1:2], w["cb2_out"])
    p.op("vector", lambda v: v.tensor_scalar(out=srow[0:1, 2:3], in0=srow[0:1, 1:2],
         scalar1=1.0 / 8192.0, scalar2=1e-6, op0=OP.mult, op1=OP.add),
         reads=[srow[0:1, 1:2]], writes=[srow[0:1, 2:3]])
    p.op("vector", lambda v: v.reciprocal(out=srow[0:1, 3:4], in_=srow[0:1, 2:3]),
         reads=[srow[0:1, 2:3]], writes=[srow[0:1, 3:4]])
    nv = ctx.A[0:1, 4096:5120]
    p.op("vector", lambda v: v.tensor_scalar(out=nv, in0=logr, scalar1=srow[0:1, 3:4],
         scalar2=0.5, op0=OP.mult, op1=OP.mult), reads=[logr, srow[0:1, 3:4]], writes=[nv])
    wu1s = ctx.B[:, 0:8192]
    bu1c = ctx.S[:, 8:16]
    wu2c = ctx.C[:, 15360:15368]
    gu = ctx.C[:, 0:8192]
    u32 = ctx.A[:, 0:1024]
    t1 = ctx.A[:, 1024:2048]
    t2 = ctx.A[:, 2048:3072]
    p.dma("sync", bu1c, w["bu1_t"])
    p.dma("gpsimd", wu2c, w["wu2_t"])
    for m in range(8):
        ws = wu1s[:, (m % 2) * 4096:(m % 2 + 1) * 4096]
        p.dma("gpsimd", ws, w["wu1_t"][m])
        for g in range(2):
            o = ctx.pA[:, g * 512:(g + 1) * 512]
            for k in range(32):
                rhs = lnx[:, k * 1024 + g * 512:k * 1024 + (g + 1) * 512]
                p.mm(lambda t, o=o, l=ws[:, k * 128:(k + 1) * 128], r=rhs, st=(k == 0),
                     sp=(k == 31): t.matmul(o, l, r, start=st, stop=sp),
                     reads=[ws[:, k * 128:(k + 1) * 128], rhs], writes=[o])
        p.op("vector", lambda v, m=m: v.tensor_scalar(out=u32, in0=ctx.pA[:, :],
             scalar1=bu1c[:, m:m + 1], scalar2=None, op0=OP.add, op1=OP.bypass),
             reads=[ctx.pA[:, :], bu1c[:, m:m + 1]], writes=[u32])
        p.op("scalar", lambda s: s.activation(out=t1, in_=u32, func=AF.Square),
             reads=[u32], writes=[t1])
        p.op("vector", lambda v: v.tensor_scalar(out=t2, in0=t1, scalar1=0.044715,
             scalar2=1.0, op0=OP.mult, op1=OP.add), reads=[t1], writes=[t2])
        p.op("vector", lambda v: v.tensor_tensor(out=t2, in0=u32, in1=t2, op=OP.mult),
             reads=[u32, t2], writes=[t2])
        p.op("scalar", lambda s: s.activation(out=t2, in_=t2, func=AF.Tanh,
             scale=0.7978845608028654), reads=[t2], writes=[t2])
        p.op("vector", lambda v: v.tensor_scalar(out=t2, in0=t2, scalar1=1.0, scalar2=None,
             op0=OP.add, op1=OP.bypass), reads=[t2], writes=[t2])
        p.op("vector", lambda v, m=m: v.scalar_tensor_tensor(out=gu[:, m * 1024:(m + 1) * 1024],
             in0=u32, scalar=0.5, in1=t2, op0=OP.mult, op1=OP.mult),
             reads=[u32, t2], writes=[gu[:, m * 1024:(m + 1) * 1024]])
    for m in range(8):
        for g in range(2):
            o = ctx.pB[0:1, g * 512:(g + 1) * 512]
            p.mm(lambda t, o=o, l=wu2c[:, m:m + 1], r=gu[:, m * 1024 + g * 512:m * 1024 + (g + 1) * 512],
                 st=(m == 0), sp=(m == 7): t.matmul(o, l, r, start=st, stop=sp),
                 reads=[wu2c[:, m:m + 1], gu[:, m * 1024 + g * 512:m * 1024 + (g + 1) * 512]],
                 writes=[o])
    sig = ctx.A[0:1, 5120:6144]
    p.op("scalar", lambda s: s.activation(out=sig, in_=ctx.pB[0:1, 0:1024], func=AF.Sigmoid,
         bias=scl3[0:1, 0:1], scale=1.0), reads=[ctx.pB[0:1, 0:1024], scl3[0:1, 0:1]],
         writes=[sig])
    gfin = ctx.A[0:1, 6144:7168]
    p.op("vector", lambda v: v.tensor_scalar(out=sig, in0=sig, scalar1=2.5, scalar2=None,
         op0=OP.mult, op1=OP.bypass), reads=[sig], writes=[sig])
    p.op("vector", lambda v: v.tensor_tensor(out=sig, in0=nv, in1=sig, op=OP.add),
         reads=[nv, sig], writes=[sig])
    p.op("vector", lambda v: v.tensor_scalar(out=sig, in0=sig, scalar1=0.0, scalar2=5.0,
         op0=OP.max, op1=OP.min), reads=[sig], writes=[sig])
    p.op("vector", lambda v: v.tensor_scalar(out=sig, in0=sig, scalar1=-0.5, scalar2=1.0 / 1.5,
         op0=OP.add, op1=OP.mult), reads=[sig], writes=[sig])
    p.op("vector", lambda v: v.tensor_scalar(out=sig, in0=sig, scalar1=0.0, scalar2=1.0,
         op0=OP.max, op1=OP.min), reads=[sig], writes=[sig])
    p.op("vector", lambda v: v.tensor_scalar(out=sig, in0=sig, scalar1=scl3[0:1, 1:2],
         scalar2=scl3[0:1, 2:3], op0=OP.mult, op1=OP.add),
         reads=[sig, scl3[0:1, 1:2], scl3[0:1, 2:3]], writes=[sig])
    p.op("scalar", lambda s: s.activation(out=sig, in_=sig, func=AF.Sigmoid),
         reads=[sig], writes=[sig])
    p.op("vector", lambda v: v.tensor_scalar(out=gfin, in0=sig, scalar1=0.05, scalar2=None,
         op0=OP.is_ge, op1=OP.bypass), reads=[sig], writes=[gfin])
    p.op("vector", lambda v: v.tensor_tensor(out=gfin, in0=gfin, in1=sig, op=OP.mult),
         reads=[gfin, sig], writes=[gfin])
    p.op("vector", lambda v: v.tensor_scalar(out=gfin, in0=gfin, scalar1=0.5, scalar2=None,
         op0=OP.mult, op1=OP.bypass), reads=[gfin], writes=[gfin])
    p.cut()
    wqt = ctx.C[:, 13312:15360]
    p.dma("gpsimd", wqt, w["wq_t2"])
    qbf = ctx.C[0:64, 8192:9216]
    q8a = ctx.C[0:65, 9216:10240]
    rqa = ctx.C[0:49, 10240:11264]
    rka = ctx.C[0:49, 11264:12288]
    k2a = ctx.C[0:65, 12288:13312]
    wrt = ctx.C[0:64, 15368:15416]
    keysf = ctx.A[0:64, 3072:4096]
    p.dma("gpsimd", wrt, w["wr_t"])
    p.dma("sync", keysf, w["keys_f"])
    p.dma("gpsimd", k2a[64:65, :], w["log_rel"])
    p.dma("gpsimd", rka[48:49, :], w["log_rel"])
    for g in range(2):
        o = ctx.pA[0:64, g * 512:(g + 1) * 512]
        for k in range(32):
            rhs = lnx[:, k * 1024 + g * 512:k * 1024 + (g + 1) * 512]
            p.mm(lambda t, o=o, l=wqt[:, k * 64:(k + 1) * 64], r=rhs, st=(k == 0), sp=(k == 31):
                 t.matmul(o, l, r, start=st, stop=sp),
                 reads=[wqt[:, k * 64:(k + 1) * 64], rhs], writes=[o])
    p.op("gpsimd", lambda g: g.memset(q8a[:, :], 1.0), writes=[q8a[:, :]])
    p.op("gpsimd", lambda g: g.memset(rqa[:, :], 1.0), writes=[rqa[:, :]])
    p.op("vector", lambda v: v.tensor_copy(out=qbf, in_=ctx.pA[0:64, :]),
         reads=[ctx.pA[0:64, :]], writes=[qbf])
    p.op("vector", lambda v: v.tensor_scalar(out=q8a[0:64, :], in0=ctx.pA[0:64, :],
         scalar1=0.125, scalar2=None, op0=OP.mult, op1=OP.bypass),
         reads=[ctx.pA[0:64, :]], writes=[q8a[0:64, :]])
    p.op("vector", lambda v: v.tensor_copy(out=k2a[0:64, :], in_=keysf),
         reads=[keysf], writes=[k2a[0:64, :]])
    for g in range(2):
        o = ctx.pB[0:48, g * 512:(g + 1) * 512]
        p.mm(lambda t, o=o, r=qbf[:, g * 512:(g + 1) * 512]: t.matmul(o, wrt, r,
             start=True, stop=True), reads=[wrt, qbf[:, g * 512:(g + 1) * 512]], writes=[o])
    p.op("vector", lambda v: v.tensor_scalar(out=rqa[0:48, :], in0=ctx.pB[0:48, :],
         scalar1=SR48, scalar2=None, op0=OP.mult, op1=OP.bypass),
         reads=[ctx.pB[0:48, :]], writes=[rqa[0:48, :]])
    for g in range(2):
        o = ctx.pB[0:48, g * 512:(g + 1) * 512]
        p.mm(lambda t, o=o, r=k2a[0:64, g * 512:(g + 1) * 512]: t.matmul(o, wrt, r,
             start=True, stop=True), reads=[wrt, k2a[0:64, g * 512:(g + 1) * 512]], writes=[o])
    p.op("vector", lambda v: v.tensor_copy(out=rka[0:48, :], in_=ctx.pB[0:48, :]),
         reads=[ctx.pB[0:48, :]], writes=[rka[0:48, :]])
    for sc in range(8):
        p.dma("gpsimd", lnx[:, sc * 4096:(sc + 1) * 4096],
              w["values"][sc * 128:(sc + 1) * 128, :])
    w_all = ctx.B[:, 0:8192]
    expS = ctx.B[:, 8192:9216]
    m8 = ctx.S[:, 0:8]
    mx = ctx.S[:, 72:74]
    sums = ctx.S[:, 80:88]
    rs = ctx.S[:, 88:96]
    Mscr = ctx.A[:, 1024:2048]
    Mt = ctx.A[:, 2048:3072]
    for tt in range(8):
        for g in range(2):
            o = ctx.pC[:, g * 512:(g + 1) * 512]
            p.mm(lambda t, o=o, l=rqa[:, tt * 128:(tt + 1) * 128],
                 r=rka[:, g * 512:(g + 1) * 512]: t.matmul(o, l, r, start=True, stop=True),
                 reads=[rqa[:, tt * 128:(tt + 1) * 128], rka[:, g * 512:(g + 1) * 512]],
                 writes=[o])
        p.op("vector", lambda v: v.max(out=m8, in_=ctx.pC[:, :]), reads=[ctx.pC[:, :]],
             writes=[m8])
        p.op("vector", lambda v: v.match_replace(out=Mscr, in_to_replace=m8,
             in_values=ctx.pC[:, :], imm_value=-1e30), reads=[m8, ctx.pC[:, :]], writes=[Mscr])
        p.op("vector", lambda v: v.tensor_tensor(out=Mt, in0=ctx.pC[:, :], in1=Mscr,
             op=OP.subtract), reads=[ctx.pC[:, :], Mscr], writes=[Mt])
        p.op("vector", lambda v: v.tensor_scalar(out=Mt, in0=Mt, scalar1=1.0, scalar2=None,
             op0=OP.min, op1=OP.bypass), reads=[Mt], writes=[Mt])
        for g in range(2):
            o = ctx.pB[:, g * 512:(g + 1) * 512]
            p.mm(lambda t, o=o, l=q8a[:, tt * 128:(tt + 1) * 128],
                 r=k2a[:, g * 512:(g + 1) * 512]: t.matmul(o, l, r, start=True, stop=True),
                 reads=[q8a[:, tt * 128:(tt + 1) * 128], k2a[:, g * 512:(g + 1) * 512]],
                 writes=[o])
        p.op("vector", lambda v: v.reduce_max(out=mx[:, 0:1], in_=ctx.pB[:, :],
             axis=mybir.AxisListType.X), reads=[ctx.pB[:, :]], writes=[mx[:, 0:1]])
        p.op("vector", lambda v: v.tensor_scalar(out=mx[:, 1:2], in0=mx[:, 0:1], scalar1=-1.0,
             scalar2=None, op0=OP.mult, op1=OP.bypass), reads=[mx[:, 0:1]], writes=[mx[:, 1:2]])
        p.op("scalar", lambda s: s.activation(out=expS, in_=ctx.pB[:, :], func=AF.Exp,
             bias=mx[:, 1:2], scale=1.0), reads=[ctx.pB[:, :], mx[:, 1:2]], writes=[expS])
        p.op("vector", lambda v, tt=tt: v.scalar_tensor_tensor(
             out=w_all[:, tt * 1024:(tt + 1) * 1024], in0=expS, scalar=1.0, in1=Mt,
             op0=OP.mult, op1=OP.mult, accum_out=sums[:, tt:tt + 1]),
             reads=[expS, Mt], writes=[w_all[:, tt * 1024:(tt + 1) * 1024], sums[:, tt:tt + 1]])
    p.op("vector", lambda v: v.reciprocal(out=rs, in_=sums), reads=[sums], writes=[rs])
    for tt in range(8):
        p.op("vector", lambda v, tt=tt: v.tensor_scalar(
             out=w_all[:, tt * 1024:(tt + 1) * 1024], in0=w_all[:, tt * 1024:(tt + 1) * 1024],
             scalar1=rs[:, tt:tt + 1], scalar2=None, op0=OP.mult, op1=OP.bypass),
             reads=[w_all[:, tt * 1024:(tt + 1) * 1024], rs[:, tt:tt + 1]],
             writes=[w_all[:, tt * 1024:(tt + 1) * 1024]])
    p.cut()
    wT = ctx.B[:, 8192:16384]
    for tt in range(8):
        for sc in range(8):
            src = w_all[:, tt * 1024 + sc * 128:tt * 1024 + (sc + 1) * 128]
            p.mm(lambda t, s_=src, o=ctx.pE[:, (sc % 8) * 128:(sc % 8 + 1) * 128]:
                 t.transpose(o, s_, ctx.iden_bf[:, :]),
                 reads=[src, ctx.iden_bf[:, :]],
                 writes=[ctx.pE[:, (sc % 8) * 128:(sc % 8 + 1) * 128]])
        for sc in range(8):
            p.op("vector", lambda v, tt=tt, sc=sc: v.tensor_copy(
                 out=wT[:, sc * 1024 + tt * 128:sc * 1024 + (tt + 1) * 128],
                 in_=ctx.pE[:, sc * 128:(sc + 1) * 128]),
                 reads=[ctx.pE[:, sc * 128:(sc + 1) * 128]],
                 writes=[wT[:, sc * 1024 + tt * 128:sc * 1024 + (tt + 1) * 128]])
    p.cut()
    wvds = ctx.C[:, 0:8192]
    aux1c = ctx.C[:, 8192:9216]
    p.dma("gpsimd", wvds[:, 0:4096], w["wvd_t"][0])
    p.dma("gpsimd", wvds[:, 4096:8192], w["wvd_t"][1])
    for m in range(32):
        for g in range(2):
            o = ctx.pC[:, g * 512:(g + 1) * 512]
            for sc in range(8):
                l = lnx[:, sc * 4096 + m * 128:sc * 4096 + (m + 1) * 128]
                r = wT[:, sc * 1024 + g * 512:sc * 1024 + (g + 1) * 512]
                p.mm(lambda t, o=o, l=l, r=r, st=(sc == 0), sp=(sc == 7):
                     t.matmul(o, l, r, start=st, stop=sp), reads=[l, r], writes=[o])
        p.op("vector", lambda v: v.tensor_copy(out=aux1c, in_=ctx.pC[:, :]),
             reads=[ctx.pC[:, :]], writes=[aux1c])
        for mv in range(2):
            pp = ctx.pA if mv == 0 else ctx.pB
            for g in range(2):
                o = pp[:, g * 512:(g + 1) * 512]
                l = wvds[:, mv * 4096 + m * 128:mv * 4096 + (m + 1) * 128]
                p.mm(lambda t, o=o, l=l, r=aux1c[:, g * 512:(g + 1) * 512], st=(m == 0),
                     sp=(m == 31): t.matmul(o, l, r, start=st, stop=sp),
                     reads=[l, aux1c[:, g * 512:(g + 1) * 512]], writes=[o])
        if m % 8 == 7:
            p.cut()
    aux2 = ctx.C[:, 8192:10240]
    p.op("vector", lambda v: v.tensor_copy(out=aux2[:, 0:1024], in_=ctx.pA[:, :]),
         reads=[ctx.pA[:, :]], writes=[aux2[:, 0:1024]])
    p.op("vector", lambda v: v.tensor_copy(out=aux2[:, 1024:2048], in_=ctx.pB[:, :]),
         reads=[ctx.pB[:, :]], writes=[aux2[:, 1024:2048]])
    gq_bc = ctx.A[:, 2048:3072]
    for g in range(2):
        p.mm(lambda t, g=g: t.matmul(ctx.pD[:, 0:512], ctx.ones_row32[:, :],
             gfin[0:1, g * 512:(g + 1) * 512], start=True, stop=True),
             reads=[gfin[0:1, g * 512:(g + 1) * 512], ctx.ones_row32[:, :]],
             writes=[ctx.pD[:, 0:512]])
        p.op("vector", lambda v, g=g: v.tensor_copy(out=gq_bc[:, g * 512:(g + 1) * 512],
             in_=ctx.pD[:, 0:512]), reads=[ctx.pD[:, 0:512]],
             writes=[gq_bc[:, g * 512:(g + 1) * 512]])
    p.cut()
    wvus = ctx.C[:, 0:8192]
    xc3 = ctx.A[:, 0:1024]
    xn3 = ctx.A[:, 1024:2048]
    for m in range(32):
        ws = wvus[:, (m % 2) * 4096:(m % 2) * 4096 + 256]
        p.dma("gpsimd", ws, w["wvu_t"][m])
        for g in range(2):
            idx = m * 2 + g
            o = ctx.pC[:, 0:512] if idx % 2 == 0 else ctx.pD[:, 0:512]
            xsl = xc3[:, (idx % 2) * 512:(idx % 2 + 1) * 512]
            nsl = xn3[:, (idx % 2) * 512:(idx % 2 + 1) * 512]
            xdr = xd[m * 128:(m + 1) * 128, g * 512:(g + 1) * 512]
            p.dma("sync", xsl, xdr)
            for k in range(2):
                l = ws[:, k * 128:(k + 1) * 128]
                r = aux2[:, k * 1024 + g * 512:k * 1024 + (g + 1) * 512]
                p.mm(lambda t, o=o, l=l, r=r, st=(k == 0), sp=(k == 1):
                     t.matmul(o, l, r, start=st, stop=sp), reads=[l, r], writes=[o])
            p.op("vector", lambda v, o=nsl, a=o, b=gq_bc[:, g * 512:(g + 1) * 512]:
                 v.tensor_tensor(out=o, in0=a, in1=b, op=OP.mult),
                 reads=[o, gq_bc[:, g * 512:(g + 1) * 512]], writes=[nsl])
            p.op("vector", lambda v, o=nsl, b=xsl: v.tensor_tensor(out=o, in0=o, in1=b,
                 op=OP.add), reads=[nsl, xsl], writes=[nsl])
            p.dma("sync", xdr, nsl)
        if m % 8 == 7:
            p.cut()
    p.cut()


def emit_l1attn(ctx, xd, w):
    p = ctx.p
    lnx = ctx.LNX
    NEG = -1e30
    wstr = ctx.B[:, 0:8192]
    evc = ctx.C[:, 0:2048]
    halo_sb = ctx.C[:, 2048:2528]
    dtraw = ctx.C[:, 2528:3552]
    for m in range(145):
        ws = wstr[:, (m % 2) * 4096:(m % 2 + 1) * 4096]
        p.dma("gpsimd", ws, w["inproj_t"][m])
        for g in range(2):
            o = ctx.pA[:, 0:512] if (m * 2 + g) % 2 == 0 else ctx.pB[:, 0:512]
            for k in range(32):
                rhs = lnx[:, k * 1024 + g * 512:k * 1024 + (g + 1) * 512]
                p.mm(lambda t, o=o, l=ws[:, k * 128:(k + 1) * 128], r=rhs, st=(k == 0),
                     sp=(k == 31): t.matmul(o, l, r, start=st, stop=sp),
                     reads=[ws[:, k * 128:(k + 1) * 128], rhs], writes=[o])
            ev = evc[:, ((m * 2 + g) % 2) * 1024:((m * 2 + g) % 2) * 1024 + 512]
            if m == 144:
                ev = dtraw[:, g * 512:(g + 1) * 512]
            p.op("vector", lambda v, o=ev, a=o: v.tensor_copy(out=o, in_=a),
                 reads=[o], writes=[ev])
            if m < 144:
                if m < 64:
                    dst = w["gate_d"][m * 128:(m + 1) * 128, g * 512:(g + 1) * 512]
                else:
                    dst = w["xbc_d"][(m - 64) * 128:(m - 63) * 128, g * 512:(g + 1) * 512]
                p.dma("sync", dst, ev)
                if m >= 64:
                    mc = m - 64
                    hs = halo_sb[:, mc * 6 + (g * 3):mc * 6 + (g * 3) + 3]
                    p.op("gpsimd", lambda e, o=hs, a=ev: e.tensor_copy(out=o, in_=a[:, 509:512]),
                         reads=[ev], writes=[hs])
        if m % 8 == 7:
            p.cut()
    p.dma("sync", w["halo_snd"], halo_sb)
    p.cut()
    grp = [[0, 1, 2, 3], [4, 5, 6, 7]]
    p.collective("AllGather", OP.bypass, grp, [w["halo_snd"]], [w["halo_all"]])
    dt_f = ctx.A[:, 0:1024]
    dA_f = ctx.A[:, 1024:2048]
    cum = ctx.A[:, 2048:3072]
    sc1 = ctx.A[:, 3072:4096]
    sc2 = ctx.A[:, 4096:5120]
    dtb = ctx.S[:, 0:1]
    alog = ctx.S[:, 1:2]
    acol = ctx.S[:, 2:3]
    p.dma("sync", dtb, w["dtb"])
    p.dma("sync", alog, w["alog"])
    p.op("vector", lambda v: v.tensor_scalar(out=sc1, in0=dtraw, scalar1=dtb, scalar2=None,
         op0=OP.add, op1=OP.bypass), reads=[dtraw, dtb], writes=[sc1])
    p.op("scalar", lambda s: s.activation(out=sc2, in_=sc1, func=AF.Abs),
         reads=[sc1], writes=[sc2])
    p.op("scalar", lambda s: s.activation(out=sc2, in_=sc2, func=AF.Exp, scale=-1.0),
         reads=[sc2], writes=[sc2])
    p.op("scalar", lambda s: s.activation(out=sc2, in_=sc2, func=AF.Ln,
         bias=ctx.cst[:, 1:2], scale=1.0), reads=[sc2, ctx.cst[:, 1:2]], writes=[sc2])
    p.op("vector", lambda v: v.tensor_scalar(out=sc1, in0=sc1, scalar1=0.0, scalar2=None,
         op0=OP.max, op1=OP.bypass), reads=[sc1], writes=[sc1])
    p.op("vector", lambda v: v.tensor_tensor(out=dt_f, in0=sc1, in1=sc2, op=OP.add),
         reads=[sc1, sc2], writes=[dt_f])
    p.op("scalar", lambda s: s.activation(out=acol, in_=alog, func=AF.Exp), reads=[alog],
         writes=[acol])
    p.op("vector", lambda v: v.tensor_scalar(out=acol, in0=acol, scalar1=-1.0, scalar2=None,
         op0=OP.mult, op1=OP.bypass), reads=[acol], writes=[acol])
    p.op("vector", lambda v: v.tensor_scalar(out=dA_f, in0=dt_f, scalar1=acol, scalar2=None,
         op0=OP.mult, op1=OP.bypass), reads=[dt_f, acol], writes=[dA_f])
    for c in range(8):
        p.op("vector", lambda v, c=c: v.tensor_tensor_scan(
             out=cum[:, c * 128:(c + 1) * 128], data0=dA_f[:, c * 128:(c + 1) * 128],
             data1=dA_f[:, c * 128:(c + 1) * 128], initial=0.0, op0=OP.add, op1=OP.bypass),
             reads=[dA_f[:, c * 128:(c + 1) * 128]], writes=[cum[:, c * 128:(c + 1) * 128]])
    p.dma("sync", w["cum_d"], cum)
    cum_tm = ctx.A[:, 5120:6144]
    dt_tm = ctx.A[:, 6144:7168]
    for c in range(8):
        p.mm(lambda t, c=c: t.transpose(ctx.pD[:, 0:128], cum[:, c * 128:(c + 1) * 128],
             ctx.iden32[:, :]), reads=[cum[:, c * 128:(c + 1) * 128], ctx.iden32[:, :]],
             writes=[ctx.pD[:, 0:512]])
        p.op("vector", lambda v, c=c: v.tensor_copy(out=cum_tm[:, c * 128:(c + 1) * 128],
             in_=ctx.pD[:, 0:128]), reads=[ctx.pD[:, 0:512]],
             writes=[cum_tm[:, c * 128:(c + 1) * 128]])
        p.mm(lambda t, c=c: t.transpose(ctx.pD[:, 0:128], dt_f[:, c * 128:(c + 1) * 128],
             ctx.iden32[:, :]), reads=[dt_f[:, c * 128:(c + 1) * 128], ctx.iden32[:, :]],
             writes=[ctx.pD[:, 0:512]])
        p.op("vector", lambda v, c=c: v.tensor_copy(out=dt_tm[:, c * 128:(c + 1) * 128],
             in_=ctx.pD[:, 0:128]), reads=[ctx.pD[:, 0:512]],
             writes=[dt_tm[:, c * 128:(c + 1) * 128]])
    coef_tm = ctx.A[:, 7168:8192]
    coefs = ctx.S[:, 16:1040]
    for c in range(8):
        cl = ctx.S[:, 4 + c:5 + c]
        p.op("vector", lambda v, c=c, cl=cl: v.tensor_copy(out=cl,
             in_=cum[:, c * 128 + 127:c * 128 + 128]),
             reads=[cum[:, c * 128 + 127:c * 128 + 128]], writes=[cl])
        p.op("scalar", lambda s, c=c, cl=cl: s.activation(out=coefs[:, c * 128:(c + 1) * 128],
             in_=cum[:, c * 128:(c + 1) * 128], func=AF.Exp, bias=cl, scale=-1.0),
             reads=[cum[:, c * 128:(c + 1) * 128], cl], writes=[coefs[:, c * 128:(c + 1) * 128]])
    p.op("vector", lambda v: v.tensor_tensor(out=coefs, in0=dt_f, in1=coefs, op=OP.mult),
         reads=[dt_f, coefs], writes=[coefs])
    for c in range(8):
        p.mm(lambda t, c=c: t.transpose(ctx.pD[:, 0:128], coefs[:, c * 128:(c + 1) * 128],
             ctx.iden32[:, :]), reads=[coefs[:, c * 128:(c + 1) * 128], ctx.iden32[:, :]],
             writes=[ctx.pD[:, 0:512]])
        p.op("vector", lambda v, c=c: v.tensor_copy(out=coef_tm[:, c * 128:(c + 1) * 128],
             in_=ctx.pD[:, 0:128]), reads=[ctx.pD[:, 0:512]],
             writes=[coef_tm[:, c * 128:(c + 1) * 128]])
    p.cut()
    ext = ctx.C[:, 3552:4582]
    cdiag = ctx.C[:, 4608:5632]
    cb = ctx.S[:, 1040:1120]
    hsel = ctx.S[:, 1120:1136]
    p.dma("sync", cb, w["conv_b"])
    p.dma("sync", hsel, w["hsel"])
    for mc in range(80):
        cds = cdiag[:, (mc % 2) * 512:(mc % 2 + 1) * 512]
        p.dma("gpsimd", cds, w["convd_t"][mc])
        p.dma("sync", ext[:, 3:515], w["xbc_d"][mc * 128:(mc + 1) * 128, 0:512])
        p.dma("sync", ext[:, 518:1030], w["xbc_d"][mc * 128:(mc + 1) * 128, 512:1024])
        p.op("gpsimd", lambda e: e.memset(ext[:, 0:3], 0.0), writes=[ext[:, 0:3]])
        p.op("gpsimd", lambda e: e.memset(ext[:, 515:518], 0.0), writes=[ext[:, 515:518]])
        for cand in range(8):
            r, sl = (cand // 2), (cand % 2)
            hsrc = ctx.B[:, cand * 3:cand * 3 + 3]
            p.dma("sync", hsrc, w["halo_all"][r * 128:(r + 1) * 128,
                                              mc * 6 + sl * 3:mc * 6 + sl * 3 + 3])
            for dsti, dcol in ((0, 0), (1, 515)):
                p.op("vector", lambda v, h=hsrc, d=ext[:, dcol:dcol + 3],
                     sc=hsel[:, dsti * 8 + cand:dsti * 8 + cand + 1]:
                     v.scalar_tensor_tensor(out=d, in0=h, scalar=sc, in1=d,
                     op0=OP.mult, op1=OP.add),
                     reads=[hsrc, hsel[:, dsti * 8 + cand:dsti * 8 + cand + 1],
                            ext[:, dcol:dcol + 3]],
                     writes=[ext[:, dcol:dcol + 3]])
        for st_ in range(2):
            o = ctx.pC[:, st_ * 512:(st_ + 1) * 512]
            for j in range(4):
                r_ = ext[:, st_ * 515 + j:st_ * 515 + j + 512]
                p.mm(lambda t, o=o, l=cds[:, j * 128:(j + 1) * 128], rr=r_, stt=(j == 0),
                     sp=(j == 3): t.matmul(o, l, rr, start=stt, stop=sp),
                     reads=[cds[:, j * 128:(j + 1) * 128], r_], writes=[o])
        po = ctx.C[:, 5632:6656]
        p.op("scalar", lambda s, mc=mc: s.activation(out=po, in_=ctx.pC[:, :], func=AF.Silu,
             bias=cb[:, mc:mc + 1], scale=1.0), reads=[ctx.pC[:, :], cb[:, mc:mc + 1]],
             writes=[po])
        if mc < 64:
            p.dma("sync", w["xss_d"][mc * 128:(mc + 1) * 128, :], po)
        elif mc < 72:
            g_ = mc - 64
            p.dma("sync", w["bfc_d"][:, g_ * 1024:(g_ + 1) * 1024], po)
        else:
            g_ = mc - 72
            p.dma("sync", w["cfc_d"][:, g_ * 1024:(g_ + 1) * 1024], po)
        if mc % 8 == 7:
            p.cut()
    p.cut()

def emit_l1attn_state(ctx, xd, w):
    p = ctx.p
    dt_tm = ctx.A[:, 6144:7168]
    coef_tm = ctx.A[:, 7168:8192]
    prev = ctx.B[:, 0:8192]
    bct = ctx.B[:, 8192:9216]
    btm = ctx.B[:, 9216:10240]
    x_tm = ctx.C[:, 0:8192]
    cdecb = ctx.LNX[:, 0:8192]
    b4 = ctx.LNX[:, 8192:8704]
    stsb = ctx.LNX[:, 8704:9216]
    t1s = ctx.LNX[:, 9216:9728]
    cde = ctx.R[0:1, 0:8192]
    p.op("gpsimd", lambda g: g.memset(prev[:, :], 0.0), writes=[prev[:, :]])
    grp = [[0, 1, 2, 3], [4, 5, 6, 7]]
    for gc in range(8):
        ss, c = gc // 4, gc % 4
        for kk in range(8):
            p.dma("sync", x_tm[:, kk * 1024:(kk + 1) * 1024].rearrange("p (k t) -> p k t", t=128),
                  w["xss_d"][kk * 1024:(kk + 1) * 1024, gc * 128:(gc + 1) * 128]
                  .rearrange("(k p) t -> p k t", p=128))
        for k8 in range(8):
            for k in range(k8 * 8, k8 * 8 + 8):
                src = x_tm[:, k * 128:(k + 1) * 128]
                p.mm(lambda t, s_=src, k=k: t.transpose(
                     ctx.pE[:, (k % 8) * 128:(k % 8 + 1) * 128], s_, ctx.iden_bf[:, :]),
                     reads=[src, ctx.iden_bf[:, :]], writes=[ctx.pE[:, 0:1024]])
            p.op("vector", lambda v, k8=k8: v.tensor_copy(
                 out=x_tm[:, k8 * 1024:(k8 + 1) * 1024], in_=ctx.pE[:, :]),
                 reads=[ctx.pE[:, 0:1024]], writes=[x_tm[:, k8 * 1024:(k8 + 1) * 1024]])
        for g in range(8):
            p.dma("sync", bct[:, g * 128:(g + 1) * 128],
                  w["bfc_d"][:, g * 1024 + gc * 128:g * 1024 + (gc + 1) * 128])
            src = bct[:, g * 128:(g + 1) * 128]
            p.mm(lambda t, s_=src: t.transpose(ctx.pE[:, 0:128], s_, ctx.iden_bf[:, :]),
                 reads=[src, ctx.iden_bf[:, :]], writes=[ctx.pE[:, 0:1024]])
            p.op("vector", lambda v, g=g: v.tensor_copy(out=btm[:, g * 128:(g + 1) * 128],
                 in_=ctx.pE[:, 0:128]), reads=[ctx.pE[:, 0:1024]],
                 writes=[btm[:, g * 128:(g + 1) * 128]])
        cl = ctx.S[:, 4 + gc:5 + gc]
        p.mm(lambda t: t.transpose(ctx.pD[0:1, 0:128], cl, ctx.iden32[:, :]),
             reads=[cl, ctx.iden32[:, :]], writes=[ctx.pD[0:1, 0:512]])
        p.op("scalar", lambda s: s.activation(out=ctx.S[0:1, 1136:1264], in_=ctx.pD[0:1, 0:128],
             func=AF.Exp), reads=[ctx.pD[0:1, 0:512]], writes=[ctx.S[0:1, 1136:1264]])
        p.op("vector", lambda v: v.tensor_copy(
             out=vap(ctx.R, 0, [[8192, 1], [64, 128], [1, 64]]),
             in_=vap(ctx.S, 1136, [[2048, 1], [1, 128], [0, 64]])),
             reads=[ctx.S[0:1, 1136:1264]], writes=[cde])
        for bg in range(16):
            p.mm(lambda t, r_=cde[0:1, bg * 512:(bg + 1) * 512]: t.matmul(
                 ctx.pD[:, 0:512], ctx.ones_row32[:, :], r_, start=True, stop=True),
                 reads=[cde[0:1, bg * 512:(bg + 1) * 512], ctx.ones_row32[:, :]],
                 writes=[ctx.pD[:, 0:512]])
            p.op("vector", lambda v, bg=bg: v.tensor_copy(
                 out=cdecb[:, bg * 512:(bg + 1) * 512], in_=ctx.pD[:, 0:512]),
                 reads=[ctx.pD[:, 0:512]], writes=[cdecb[:, bg * 512:(bg + 1) * 512]])
        for g in range(8):
            for q in range(4):
                h0 = g * 16 + q * 4
                bsrc = vap(ctx.B, 9216 + g * 128, [[16384, 128], [0, 4], [1, 128]])
                cview = vap(ctx.A, 7168 + gc * 128 + h0, [[8192, 128], [1, 4], [0, 128]])
                b4o = vap(ctx.LNX, 8192, [[32768, 128], [128, 4], [1, 128]])
                p.op("vector", lambda v, bs=bsrc, cv=cview, bo=b4o: v.tensor_tensor(
                     out=bo, in0=bs, in1=cv, op=OP.mult),
                     reads=[btm[:, g * 128:(g + 1) * 128],
                            coef_tm[:, gc * 128 + h0:gc * 128 + h0 + 4]], writes=[b4])
                for hq in range(4):
                    h = h0 + hq
                    bank = ctx.pA if (h % 16) < 8 else ctx.pB
                    o = bank[:, (h % 8) * 64:(h % 8) * 64 + 64]
                    p.mm(lambda t, o=o, l=b4[:, hq * 128:(hq + 1) * 128],
                         r=x_tm[:, h * 64:(h + 1) * 64]: t.matmul(o, l, r, start=True,
                         stop=True), reads=[b4[:, hq * 128:(hq + 1) * 128],
                         x_tm[:, h * 64:(h + 1) * 64]], writes=[o])
            for half in range(2):
                bank = ctx.pA if half == 0 else ctx.pB
                lo = g * 1024 + half * 512
                p.op("vector", lambda v, b_=bank: v.tensor_copy(out=stsb, in_=b_[:, 0:512]),
                     reads=[bank[:, 0:512]], writes=[stsb])
                p.dma("sync", w["std_d"][gc * 128:(gc + 1) * 128, lo:lo + 512], stsb)
                p.op("vector", lambda v, lo=lo: v.tensor_tensor(out=t1s,
                     in0=prev[:, lo:lo + 512], in1=cdecb[:, lo:lo + 512], op=OP.mult),
                     reads=[prev[:, lo:lo + 512], cdecb[:, lo:lo + 512]], writes=[t1s])
                p.op("vector", lambda v, lo=lo, b_=bank: v.tensor_tensor(
                     out=prev[:, lo:lo + 512], in0=t1s, in1=b_[:, 0:512], op=OP.add),
                     reads=[t1s, bank[:, 0:512]], writes=[prev[:, lo:lo + 512]])
        if c == 3:
            p.dma("sync", w["st_snd"][ss * 128:(ss + 1) * 128, :], prev)
            dcol = ctx.S[:, 12:13]
            p.op("vector", lambda v, ss=ss: v.tensor_tensor(out=dcol,
                 in0=ctx.S[:, 4 + ss * 4:5 + ss * 4], in1=ctx.S[:, 5 + ss * 4:6 + ss * 4],
                 op=OP.add), reads=[ctx.S[:, 4 + ss * 4:6 + ss * 4]], writes=[dcol])
            for i_ in (2, 3):
                p.op("vector", lambda v, ss=ss, i_=i_: v.tensor_tensor(out=dcol, in0=dcol,
                     in1=ctx.S[:, 4 + ss * 4 + i_:5 + ss * 4 + i_], op=OP.add),
                     reads=[dcol, ctx.S[:, 4 + ss * 4 + i_:5 + ss * 4 + i_]], writes=[dcol])
            p.mm(lambda t: t.transpose(ctx.pD[0:1, 0:128], dcol, ctx.iden32[:, :]),
                 reads=[dcol, ctx.iden32[:, :]], writes=[ctx.pD[0:1, 0:512]])
            p.op("scalar", lambda s: s.activation(out=ctx.S[0:1, 1264:1392],
                 in_=ctx.pD[0:1, 0:128], func=AF.Exp), reads=[ctx.pD[0:1, 0:512]],
                 writes=[ctx.S[0:1, 1264:1392]])
            p.dma("sync", w["dp_snd"][ss:ss + 1, :], ctx.S[0:1, 1264:1392])
            p.op("gpsimd", lambda g: g.memset(prev[:, :], 0.0), writes=[prev[:, :]])
        p.cut()
    p.collective("AllGather", OP.bypass, grp, [w["st_snd"]], [w["st_all"]])
    p.collective("AllGather", OP.bypass, grp, [w["dp_snd"]], [w["dp_all"]])
    run = ctx.LNX[:, 0:8192]
    accA = ctx.LNX[:, 8192:16384]
    accB = ctx.LNX[:, 16384:24576]
    sout = ctx.B[:, 0:8192]
    selc = ctx.S[:, 1392:1408]
    dpr = ctx.R[0:1, 0:8192]
    p.dma("sync", selc, w["selc"])
    p.op("gpsimd", lambda g: g.memset(run[:, :1] if False else run, 0.0), writes=[run])
    p.op("gpsimd", lambda g: g.memset(accA, 0.0), writes=[accA])
    p.op("gpsimd", lambda g: g.memset(accB, 0.0), writes=[accB])
    for s_ in range(8):
        r = min(s_, 7 - s_)
        sl = 0 if s_ < 4 else 1
        p.op("vector", lambda v, s_=s_: v.scalar_tensor_tensor(out=accA, in0=run,
             scalar=selc[:, s_:s_ + 1], in1=accA, op0=OP.mult, op1=OP.add),
             reads=[run, selc[:, s_:s_ + 1], accA], writes=[accA])
        p.op("vector", lambda v, s_=s_: v.scalar_tensor_tensor(out=accB, in0=run,
             scalar=selc[:, 8 + s_:9 + s_], in1=accB, op0=OP.mult, op1=OP.add),
             reads=[run, selc[:, 8 + s_:9 + s_], accB], writes=[accB])
        p.dma("sync", sout, w["st_all"][r * 256 + sl * 128:r * 256 + (sl + 1) * 128, :])
        p.dma("sync", ctx.S[0:1, 1264:1392], w["dp_all"][r * 2 + sl:r * 2 + sl + 1, :])
        p.op("vector", lambda v: v.tensor_copy(
             out=vap(ctx.R, 0, [[8192, 1], [64, 128], [1, 64]]),
             in_=vap(ctx.S, 1264, [[2048, 1], [1, 128], [0, 64]])),
             reads=[ctx.S[0:1, 1264:1392]], writes=[dpr])
        for bg in range(16):
            p.mm(lambda t, r_=dpr[0:1, bg * 512:(bg + 1) * 512]: t.matmul(ctx.pD[:, 0:512],
                 ctx.ones_row32[:, :], r_, start=True, stop=True),
                 reads=[dpr[0:1, bg * 512:(bg + 1) * 512], ctx.ones_row32[:, :]],
                 writes=[ctx.pD[:, 0:512]])
            p.op("vector", lambda v, bg=bg: v.tensor_tensor(
                 out=run[:, bg * 512:(bg + 1) * 512], in0=run[:, bg * 512:(bg + 1) * 512],
                 in1=ctx.pD[:, 0:512], op=OP.mult),
                 reads=[run[:, bg * 512:(bg + 1) * 512], ctx.pD[:, 0:512]],
                 writes=[run[:, bg * 512:(bg + 1) * 512]])
        p.op("vector", lambda v: v.tensor_tensor(out=run, in0=run, in1=sout, op=OP.add),
             reads=[run, sout], writes=[run])
        p.cut()
    p.dma("sync", w["sin_d"][0:128, :], accA)
    p.dma("sync", w["sin_d"][128:256, :], accB)
    p.cut()

def emit_l1attn_y(ctx, xd, w):
    p = ctx.p
    dt_tm = ctx.A[:, 6144:7168]
    x_tm = ctx.C[:, 0:8192]
    stc = ctx.C[:, 8192:16384]
    prevg = ctx.B[:, 8192:16384]
    bct = ctx.B[:, 0:1024]
    cct = ctx.B[:, 1024:2048]
    s0sb = ctx.B[:, 3584:3712]
    d4 = ctx.B[:, 2048:2560]
    s1q = ctx.B[:, 2560:3072]
    c4 = ctx.B[:, 3072:3584]
    x8 = ctx.B[:, 4096:5120]
    g8 = ctx.B[:, 5120:6144]
    sg8 = ctx.B[:, 6144:7168]
    t1s = ctx.B[:, 7168:8192]
    negones = ctx.S[0:1, 1408:1536]
    trimask_sb = ctx.S[:, 16:528]
    dexp_sb = ctx.S[:, 1536:1600]
    nw_sb = ctx.S[:, 1600:1664]
    yn = ctx.LNX
    cde = ctx.R[0:1, 0:8192]
    p.op("gpsimd", lambda g: g.memset(negones, -1.0), writes=[negones])
    p.dma("sync", trimask_sb, w["trimask4"])
    p.dma("sync", dexp_sb, w["d_exp"])
    p.dma("sync", nw_sb, w["nw_t"])
    for ss in range(2):
        toff = ss * 512
        p.dma("sync", prevg, w["sin_d"][ss * 128:(ss + 1) * 128, :])
        for c in range(4):
            gc = ss * 4 + c
            for kk in range(8):
                p.dma("sync", x_tm[:, kk * 1024:(kk + 1) * 1024]
                      .rearrange("p (k t) -> p k t", t=128),
                      w["xss_d"][kk * 1024:(kk + 1) * 1024, gc * 128:(gc + 1) * 128]
                      .rearrange("(k p) t -> p k t", p=128))
            for k8 in range(8):
                for k in range(k8 * 8, k8 * 8 + 8):
                    src = x_tm[:, k * 128:(k + 1) * 128]
                    p.mm(lambda t, s_=src, k=k: t.transpose(
                         ctx.pE[:, (k % 8) * 128:(k % 8 + 1) * 128], s_, ctx.iden_bf[:, :]),
                         reads=[src, ctx.iden_bf[:, :]], writes=[ctx.pE[:, 0:1024]])
                p.op("vector", lambda v, k8=k8: v.tensor_copy(
                     out=x_tm[:, k8 * 1024:(k8 + 1) * 1024], in_=ctx.pE[:, :]),
                     reads=[ctx.pE[:, 0:1024]], writes=[x_tm[:, k8 * 1024:(k8 + 1) * 1024]])
            for g in range(8):
                p.dma("sync", bct[:, g * 128:(g + 1) * 128],
                      w["bfc_d"][:, g * 1024 + gc * 128:g * 1024 + (gc + 1) * 128])
                p.dma("sync", cct[:, g * 128:(g + 1) * 128],
                      w["cfc_d"][:, g * 1024 + gc * 128:g * 1024 + (gc + 1) * 128])
            for hh in range(4):
                p.dma("sync", vap(ctx.R, 0, [[8192, 1], [128, 32], [1, 128]]),
                      w["cum_d"][32 * hh:32 * hh + 32, gc * 128:(gc + 1) * 128]
                      .rearrange("(o h) t -> o h t", o=1))
                p.op("vector", lambda v: v.tensor_scalar(out=cde[0:1, 0:4096],
                     in0=cde[0:1, 0:4096], scalar1=-1.0, scalar2=None, op0=OP.mult,
                     op1=OP.bypass), reads=[cde[0:1, 0:4096]], writes=[cde[0:1, 0:4096]])
                p.op("scalar", lambda s: s.activation(out=cde[0:1, 4096:8192],
                     in_=cde[0:1, 0:4096], func=AF.Exp, scale=-1.0),
                     reads=[cde[0:1, 0:4096]], writes=[cde[0:1, 4096:8192]])
                for gg in range(2):
                    g = hh * 2 + gg
                    p.mm(lambda t, l=bct[:, g * 128:(g + 1) * 128],
                         r=cct[:, g * 128:(g + 1) * 128]: t.matmul(ctx.pC[:, 0:128], l, r,
                         start=True, stop=True),
                         reads=[bct[:, g * 128:(g + 1) * 128], cct[:, g * 128:(g + 1) * 128]],
                         writes=[ctx.pC[:, 0:512]])
                    p.op("vector", lambda v: v.tensor_copy(out=s0sb, in_=ctx.pC[:, 0:128]),
                         reads=[ctx.pC[:, 0:512]], writes=[s0sb])
                    for q in range(4):
                        h0 = g * 16 + q * 4
                        hrel = h0 - 32 * hh
                        p.mm(lambda t, r_=cde[0:1, hrel * 128:hrel * 128 + 512]:
                             t.matmul(ctx.pB[:, 0:512], negones, r_, start=True, stop=False),
                             reads=[cde[0:1, hrel * 128:hrel * 128 + 512], negones],
                             writes=[ctx.pB[:, 0:512]])
                        for hq in range(4):
                            p.mm(lambda t, l=cde[0:1, (hrel + hq) * 128:(hrel + hq + 1) * 128],
                                 hq=hq: t.matmul(ctx.pB[:, hq * 128:(hq + 1) * 128], l,
                                 ctx.ones_row32[:, :], start=False, stop=False),
                                 reads=[cde[0:1, (hrel + hq) * 128:(hrel + hq + 1) * 128],
                                        ctx.ones_row32[:, :]], writes=[ctx.pB[:, 0:512]])
                        p.mm(lambda t: t.matmul(ctx.pB[:, 0:512], ctx.iden32[:, :],
                             trimask_sb, start=False, stop=True),
                             reads=[ctx.iden32[:, :], trimask_sb],
                             writes=[ctx.pB[:, 0:512]])
                        p.op("scalar", lambda s: s.activation(out=d4, in_=ctx.pB[:, 0:512],
                             func=AF.Exp), reads=[ctx.pB[:, 0:512]], writes=[d4])
                        dt4 = vap(ctx.A, 6144 + gc * 128 + h0, [[8192, 128], [1, 4], [0, 128]])
                        s0b = vap(ctx.B, 3584, [[16384, 128], [0, 4], [1, 128]])
                        d4v = vap(ctx.B, 2048, [[16384, 128], [128, 4], [1, 128]])
                        s1v = vap(ctx.B, 2560, [[16384, 128], [128, 4], [1, 128]])
                        p.op("vector", lambda v, d4v=d4v, dt4=dt4, s1v=s1v: v.tensor_tensor(
                             out=s1v, in0=d4v, in1=dt4, op=OP.mult),
                             reads=[d4, dt_tm[:, gc * 128 + h0:gc * 128 + h0 + 4]],
                             writes=[s1q])
                        p.op("vector", lambda v, s1v=s1v, s0b=s0b: v.tensor_tensor(
                             out=s1v, in0=s1v, in1=s0b, op=OP.mult),
                             reads=[s1q, s0sb], writes=[s1q])
                        p.mm(lambda t, r_=cde[0:1, 4096 + hrel * 128:4096 + hrel * 128 + 512]:
                             t.matmul(ctx.pB[:, 512:1024], ctx.ones_row32[:, :], r_,
                             start=True, stop=True),
                             reads=[cde[0:1, 4096 + hrel * 128:4096 + hrel * 128 + 512],
                                    ctx.ones_row32[:, :]], writes=[ctx.pB[:, 512:1024]])
                        cbv = vap(ctx.B, 1024 + g * 128, [[16384, 128], [0, 4], [1, 128]])
                        c4v = vap(ctx.B, 3072, [[16384, 128], [128, 4], [1, 128]])
                        ebv = vap(ctx.pB, 512, [[1024, 128], [128, 4], [1, 128]])
                        p.op("vector", lambda v, cbv=cbv, c4v=c4v, ebv=ebv: v.tensor_tensor(
                             out=c4v, in0=cbv, in1=ebv, op=OP.mult),
                             reads=[cct[:, g * 128:(g + 1) * 128], ctx.pB[:, 512:1024]],
                             writes=[c4])
                        for hq in range(4):
                            h = h0 + hq
                            k = h // 2
                            ro = (h % 2) * 64
                            co = (k - 8 * g) * 128 if False else (k % 8) * 128
                            o = ctx.pA[ro:ro + 64, co:co + 128]
                            p.mm(lambda t, o=o, l=x_tm[:, h * 64:(h + 1) * 64],
                                 r=s1q[:, hq * 128:(hq + 1) * 128]:
                                 t.matmul(o, l, r, start=True, stop=False),
                                 reads=[x_tm[:, h * 64:(h + 1) * 64],
                                        s1q[:, hq * 128:(hq + 1) * 128]], writes=[ctx.pA[:, co:co + 128]])
                            p.mm(lambda t, o=o, l=prevg[:, h * 64:(h + 1) * 64],
                                 r=c4[:, hq * 128:(hq + 1) * 128]:
                                 t.matmul(o, l, r, start=False, stop=True),
                                 reads=[prevg[:, h * 64:(h + 1) * 64],
                                        c4[:, hq * 128:(hq + 1) * 128]], writes=[ctx.pA[:, co:co + 128]])
                    p.dma("sync", x8[:, :].rearrange("p (k t) -> p k t", t=128),
                          w["xss_d"][g * 1024:(g + 1) * 1024, gc * 128:(gc + 1) * 128]
                          .rearrange("(k p) t -> p k t", p=128))
                    p.dma("sync", g8[:, :].rearrange("p (k t) -> p k t", t=128),
                          w["gate_d"][g * 1024:(g + 1) * 1024, gc * 128:(gc + 1) * 128]
                          .rearrange("(k p) t -> p k t", p=128))
                    p.op("scalar", lambda s: s.activation(out=sg8, in_=g8, func=AF.Silu),
                         reads=[g8], writes=[sg8])
                    dview = vap(ctx.S, 1536 + g * 8, [[2048, 128], [1, 8], [0, 128]])
                    x8v = vap(ctx.B, 4096, [[16384, 128], [128, 8], [1, 128]])
                    t1v = vap(ctx.B, 7168, [[16384, 128], [128, 8], [0, 0]])
                    p.op("gpsimd", lambda e, x8v=x8v, dview=dview: e.tensor_tensor(
                         out=vap(ctx.B, 7168, [[16384, 128], [128, 8], [1, 128]]),
                         in0=x8v, in1=dview, op=OP.mult),
                         reads=[x8, ctx.S[:, 1536 + g * 8:1544 + g * 8]], writes=[t1s])
                    p.op("vector", lambda v: v.tensor_tensor(out=t1s, in0=t1s,
                         in1=ctx.pA[:, :], op=OP.add), reads=[t1s, ctx.pA[:, :]],
                         writes=[t1s])
                    p.op("vector", lambda v, g=g, gc=gc: v.tensor_tensor(
                         out=yn[:, g * 4096 + (gc % 4) * 128:g * 4096 + (gc % 4) * 128 + 1024]
                         .rearrange("p (k t) -> p k t", t=128) if False else
                         vap(ctx.LNX, g * 4096 + (gc % 4) * 128, [[32768, 128], [512, 8], [1, 128]]),
                         in0=t1s[:, :].rearrange("p (k t) -> p k t", t=128),
                         in1=sg8[:, :].rearrange("p (k t) -> p k t", t=128), op=OP.mult),
                         reads=[t1s, sg8], writes=[ctx.LNX[:, g * 4096:(g + 1) * 4096]])
            p.dma("sync", stc, w["std_d"][gc * 128:(gc + 1) * 128, :])
            cl = ctx.S[:, 4 + gc:5 + gc]
            p.mm(lambda t: t.transpose(ctx.pD[0:1, 0:128], cl, ctx.iden32[:, :]),
                 reads=[cl, ctx.iden32[:, :]], writes=[ctx.pD[0:1, 0:512]])
            p.op("scalar", lambda s: s.activation(out=ctx.S[0:1, 1136:1264],
                 in_=ctx.pD[0:1, 0:128], func=AF.Exp), reads=[ctx.pD[0:1, 0:512]],
                 writes=[ctx.S[0:1, 1136:1264]])
            p.op("vector", lambda v: v.tensor_copy(
                 out=vap(ctx.R, 0, [[8192, 1], [64, 128], [1, 64]]),
                 in_=vap(ctx.S, 1136, [[2048, 1], [1, 128], [0, 64]])),
                 reads=[ctx.S[0:1, 1136:1264]], writes=[cde])
            for bg in range(16):
                p.mm(lambda t, r_=cde[0:1, bg * 512:(bg + 1) * 512]: t.matmul(
                     ctx.pD[:, 0:512], ctx.ones_row32[:, :], r_, start=True, stop=True),
                     reads=[cde[0:1, bg * 512:(bg + 1) * 512], ctx.ones_row32[:, :]],
                     writes=[ctx.pD[:, 0:512]])
                p.op("vector", lambda v, bg=bg: v.tensor_tensor(
                     out=prevg[:, bg * 512:(bg + 1) * 512],
                     in0=prevg[:, bg * 512:(bg + 1) * 512], in1=ctx.pD[:, 0:512], op=OP.mult),
                     reads=[prevg[:, bg * 512:(bg + 1) * 512], ctx.pD[:, 0:512]],
                     writes=[prevg[:, bg * 512:(bg + 1) * 512]])
            p.op("vector", lambda v: v.tensor_tensor(out=prevg, in0=prevg, in1=stc, op=OP.add),
                 reads=[prevg, stc], writes=[prevg])
            p.cut()
        ysq = ctx.B[:, 4096:5120]
        vrow = ctx.R[0:1, 0:4096]
        for gg in range(8):
            for k in range(8):
                ksl = yn[:, gg * 4096 + k * 512:gg * 4096 + (k + 1) * 512]
                p.op("scalar", lambda s, ksl=ksl, k=k: s.activation(
                     out=ysq[:, (k % 2) * 512:(k % 2 + 1) * 512], in_=ksl, func=AF.Square),
                     reads=[ksl], writes=[ysq[:, (k % 2) * 512:(k % 2 + 1) * 512]])
                p.mm(lambda t, k=k, st_=(k == 0), sp=(k == 7): t.matmul(ctx.pD[0:1, 0:512],
                     ctx.ones_col_bf[:, :], ysq[:, (k % 2) * 512:(k % 2 + 1) * 512],
                     start=st_, stop=sp),
                     reads=[ysq[:, (k % 2) * 512:(k % 2 + 1) * 512], ctx.ones_col_bf[:, :]],
                     writes=[ctx.pD[0:1, 0:512]])
            p.op("vector", lambda v, gg=gg: v.tensor_scalar(
                 out=vrow[0:1, gg * 512:(gg + 1) * 512], in0=ctx.pD[0:1, 0:512],
                 scalar1=1.0 / 1024.0, scalar2=None, op0=OP.mult, op1=OP.bypass),
                 reads=[ctx.pD[0:1, 0:512]], writes=[vrow[0:1, gg * 512:(gg + 1) * 512]])
        p.op("scalar", lambda s: s.activation(out=vrow, in_=vrow, func=AF.Sqrt,
             bias=ctx.cst[0:1, 0:1], scale=1.0), reads=[vrow, ctx.cst[0:1, 0:1]],
             writes=[vrow])
        p.op("vector", lambda v: v.reciprocal(out=vrow, in_=vrow), reads=[vrow], writes=[vrow])
        for gg in range(8):
            p.mm(lambda t, gg=gg: t.matmul(ctx.pD[:, 0:512], ctx.ones_row32[:, :],
                 vrow[0:1, gg * 512:(gg + 1) * 512], start=True, stop=True),
                 reads=[vrow[0:1, gg * 512:(gg + 1) * 512], ctx.ones_row32[:, :]],
                 writes=[ctx.pD[:, 0:512]])
            p.op("vector", lambda v: v.tensor_copy(out=ysq[:, 0:512], in_=ctx.pD[:, 0:512]),
                 reads=[ctx.pD[:, 0:512]], writes=[ysq[:, 0:512]])
            for k in range(8):
                kk = gg * 8 + k
                ksl = yn[:, kk * 512:(kk + 1) * 512]
                p.op("vector", lambda v, ksl=ksl, kk=kk: v.scalar_tensor_tensor(
                     out=ksl, in0=ksl, scalar=nw_sb[:, kk:kk + 1], in1=ysq[:, 0:512],
                     op0=OP.mult, op1=OP.mult),
                     reads=[ksl, nw_sb[:, kk:kk + 1], ysq[:, 0:512]], writes=[ksl])
        p.cut()
        wos = ctx.C[:, 0:16384]
        for m in range(32):
            ws = wos[:, (m % 2) * 8192:(m % 2 + 1) * 8192]
            p.dma("gpsimd", ws, w["outw_t"][m])
            o = ctx.pC[:, 0:512] if m % 2 == 0 else ctx.pB[:, 0:512]
            xsl = ctx.A[:, (m % 2) * 512:(m % 2) * 512 + 512]
            nsl = ctx.A[:, 1024 + (m % 2) * 512:1024 + (m % 2) * 512 + 512]
            xdr = xd[m * 128:(m + 1) * 128, toff:toff + 512]
            p.dma("sync", xsl, xdr)
            for k in range(64):
                p.mm(lambda t, o=o, l=ws[:, k * 128:(k + 1) * 128],
                     r=yn[:, k * 512:(k + 1) * 512], st_=(k == 0), sp=(k == 63):
                     t.matmul(o, l, r, start=st_, stop=sp),
                     reads=[ws[:, k * 128:(k + 1) * 128], yn[:, k * 512:(k + 1) * 512]],
                     writes=[o])
            p.op("vector", lambda v, o=nsl, a=o, b=xsl: v.tensor_tensor(
                 out=o, in0=a, in1=b, op=OP.add), reads=[o, xsl], writes=[nsl])
            p.dma("sync", xdr, nsl)
            if m % 8 == 7:
                p.cut()
        p.cut()



D_ = 4096
NCORES = 8
STRIP = 512


def core_strips(c):
    q = c % 4
    return (q, 7 - q)


def shard_x(x):
    out = []
    for c in range(NCORES):
        b = c // 4
        sA, sB = core_strips(c)
        xa = x[b, sA * STRIP:(sA + 1) * STRIP]
        xb = x[b, sB * STRIP:(sB + 1) * STRIP]
        out.append(np.ascontiguousarray(np.concatenate([xa, xb], 0).T.astype(np.float32)))
    return out


def unshard_x(shards):
    x = np.zeros((2, 4096, 4096), dtype=np.float32)
    for c in range(NCORES):
        b = c // 4
        sA, sB = core_strips(c)
        t = np.asarray(shards[c]).T
        x[b, sA * STRIP:(sA + 1) * STRIP] = t[:STRIP]
        x[b, sB * STRIP:(sB + 1) * STRIP] = t[STRIP:]
    return x


def tileize(wT):
    IN, OUT = wT.shape
    K, M = IN // 128, OUT // 128
    a = wT.reshape(K, 128, M, 128).transpose(2, 1, 0, 3)
    return np.ascontiguousarray(a.reshape(M, 128, K * 128).astype(np.float32))


def l0ffn_host(inp):
    w = {}
    w["wt"] = tileize(np.asarray(inp["L0_ffn_weight"]).T)
    w["b_flat"] = np.ascontiguousarray(
        np.asarray(inp["L0_ffn_lora_B"]).reshape(128, 4096).astype(np.float32))
    gw1t = np.asarray(inp["L0_ffn_gw1"]).T.astype(np.float32)
    w["gw1t_t"] = np.ascontiguousarray(gw1t.reshape(32, 128, 32).transpose(1, 0, 2).reshape(128, 1024))
    aall = np.asarray(inp["L0_ffn_lora_A"]).transpose(1, 0, 2).reshape(4096, 128).astype(np.float32)
    w["a_all_t"] = np.ascontiguousarray(aall.reshape(32, 128, 128).transpose(1, 0, 2).reshape(128, 4096))
    w["gw2t"] = np.ascontiguousarray(np.asarray(inp["L0_ffn_gw2"]).T.astype(np.float32))
    w["gb1"] = np.asarray(inp["L0_ffn_gb1"]).reshape(32, 1).astype(np.float32)
    w["gb2"] = np.asarray(inp["L0_ffn_gb2"]).reshape(8, 1).astype(np.float32)
    w["lngamma"] = np.asarray(inp["L0_ffn_ln_gamma"]).reshape(32, 1).astype(np.float32)
    w["lnbeta"] = np.asarray(inp["L0_ffn_ln_beta"]).reshape(32, 1).astype(np.float32)
    esel = np.zeros((8, 128), dtype=np.float32)
    for e in range(8):
        esel[e, e * 16:(e + 1) * 16] = 1.0 / 16.0
    w["esel"] = esel
    return w


def l0attn_host(inp):
    w = {}
    for nm, key in [("wq_t", "L0_attn_wq"), ("wk_t", "L0_attn_wk"),
                    ("wv_t", "L0_attn_wv"), ("wo_t", "L0_attn_wo")]:
        w[nm] = tileize(np.asarray(inp[key]).T)
    T, h = 4096, 128
    pos = np.arange(T, dtype=np.float32)
    thetas = (1.0 / 10000.0 ** (np.arange(0, h, 2, dtype=np.float32) / h)).astype(np.float32)
    pt = np.concatenate([np.outer(pos, thetas)] * 2, axis=1).astype(np.float32)
    sin, cos = np.sin(pt).astype(np.float32), np.cos(pt).astype(np.float32)
    w["cos_pc"], w["sin_pc"] = [], []
    for c in range(NCORES):
        sa, sb = core_strips(c)
        idx = np.r_[sa * STRIP:(sa + 1) * STRIP, sb * STRIP:(sb + 1) * STRIP]
        w["cos_pc"].append(np.ascontiguousarray(cos[idx].T))
        w["sin_pc"].append(np.ascontiguousarray(sin[idx].T))
    return w


def attn_masks(c):
    sA, sB = core_strips(c)
    NEG = -1e30
    m = np.zeros((128, 8 * 1024), dtype=np.float32)
    causal = np.where(np.arange(1024)[None, :] >= np.arange(128)[:, None] + 384, 0.0, NEG)
    for mi, (qg, s) in enumerate([(0, 0), (0, 1), (0, 2), (0, 3),
                                  (1, 4), (1, 5), (1, 6), (1, 7)]):
        strip = sA if qg == 0 else sB
        if s == strip:
            m[:, mi * 1024:(mi + 1) * 1024] = causal
        elif s > strip:
            m[:, mi * 1024:(mi + 1) * 1024] = NEG
    return m


def l1ffn_host(inp):
    w = {}
    wqT = np.asarray(inp["L1_ffn_Wq"]).T.astype(np.float32)
    w["wq_t2"] = np.ascontiguousarray(wqT.reshape(32, 128, 64).transpose(1, 0, 2).reshape(128, 2048))
    w["wr_t"] = np.ascontiguousarray(np.asarray(inp["L1_ffn_Wr"]).T.astype(np.float32))
    w["keys_f"] = np.ascontiguousarray(np.asarray(inp["L1_ffn_aux_keys"]).T.astype(np.float32))
    w["values"] = np.ascontiguousarray(np.asarray(inp["L1_ffn_aux_values"]).astype(np.float32))
    w["wvd_t"] = tileize(np.asarray(inp["L1_ffn_Wvd"]).T)
    w["wvu_t"] = tileize(np.asarray(inp["L1_ffn_Wvu"]).T)
    w["wu1_t"] = tileize(np.asarray(inp["L1_ffn_Wu1"]).T)
    w["wu2_t"] = np.ascontiguousarray(
        np.asarray(inp["L1_ffn_Wu2"]).T.reshape(8, 128).T.astype(np.float32))
    w["bu1_t"] = np.ascontiguousarray(
        np.asarray(inp["L1_ffn_bu1"]).reshape(8, 128).T.astype(np.float32))
    w["log_rel"] = np.asarray(inp["L1_ffn_log_rel"]).reshape(1, 1024).astype(np.float32)
    w["bu2"] = np.asarray(inp["L1_ffn_bu2"]).reshape(1, 1).astype(np.float32)
    w["gate_w1"] = np.asarray(inp["L1_ffn_gate_w1"]).reshape(1, 1).astype(np.float32)
    w["gate_bias"] = np.asarray(inp["L1_ffn_gate_bias"]).reshape(1, 1).astype(np.float32)
    return w


def l1attn_host(inp):
    w = {}
    w["inproj_t"] = tileize(np.asarray(inp["L1_attn_in_proj_w"]).T)
    cw = np.asarray(inp["L1_attn_conv_w"]).reshape(10240, 4).astype(np.float32)
    cd = np.zeros((80, 128, 512), dtype=np.float32)
    for mc in range(80):
        for j in range(4):
            cd[mc, :, j * 128:(j + 1) * 128] = np.diag(cw[mc * 128:(mc + 1) * 128, j])
    w["convd_t"] = cd
    w["conv_b"] = np.ascontiguousarray(
        np.asarray(inp["L1_attn_conv_b"]).reshape(80, 128).T.astype(np.float32))
    w["dtb"] = np.asarray(inp["L1_attn_dt_bias"]).reshape(128, 1).astype(np.float32)
    w["alog"] = np.asarray(inp["L1_attn_A_log"]).reshape(128, 1).astype(np.float32)
    Dv = np.asarray(inp["L1_attn_D"]).astype(np.float32)
    de = np.zeros((128, 64), dtype=np.float32)
    for k in range(64):
        de[0:64, k] = Dv[2 * k]
        de[64:128, k] = Dv[2 * k + 1]
    w["d_exp"] = de
    w["nw_t"] = np.ascontiguousarray(
        np.asarray(inp["L1_attn_norm_w"]).reshape(64, 128).T.astype(np.float32))
    w["outw_t"] = tileize(np.asarray(inp["L1_attn_out_w"]).T)
    tri = np.where(np.arange(128)[:, None] <= np.tile(np.arange(128), 4)[None, :], 0.0, -1e30)
    w["trimask4"] = tri.astype(np.float32)
    w["selc_pc"], w["hsel_pc"] = [], []
    for c in range(NCORES):
        sA, sB = core_strips(c)
        sel = np.zeros((128, 16), dtype=np.float32)
        sel[:, sA] = 1.0
        sel[:, 8 + sB] = 1.0
        w["selc_pc"].append(sel)
        hs = np.zeros((128, 16), dtype=np.float32)
        for dsti, sd in ((0, sA), (1, sB)):
            if sd >= 1:
                src = sd - 1
                r = min(src, 7 - src)
                sl = 0 if src < 4 else 1
                hs[:, dsti * 8 + r * 2 + sl] = 1.0
        w["hsel_pc"].append(hs)
    return w



def build_l0ffn():
    nc = bass.Bass("TRN2", target_bir_lowering=False)
    ctx = Ctx(nc)
    p = ctx.p
    x_in = ctx.din("x_in", [4096, 1024])
    x_out = ctx.dout("x_out", [4096, 1024])
    xd = ctx.dtmp("x_sc", [4096, 1024])
    w = {
        "wt": ctx.din("w_wt", [32, 128, 4096]),
        "a_all_t": ctx.din("w_aall", [128, 4096]),
        "b_flat": ctx.din("w_bflat", [128, 4096]),
        "gw1t_t": ctx.din("w_gw1t", [128, 1024]),
        "gw2t": ctx.din("w_gw2t", [32, 8]),
        "gb1": ctx.din("w_gb1", [32, 1]),
        "gb2": ctx.din("w_gb2", [8, 1]),
        "lngamma": ctx.din("w_gamma", [32, 1]),
        "lnbeta": ctx.din("w_beta", [32, 1]),
        "esel": ctx.din("w_esel", [8, 128]),
        "cb_in": ctx.dtmp("cb_in", [1, 16]),
        "cb_out": ctx.dtmp("cb_out", [1, 16]),
    }
    ctx.pools()
    ctx.constants()
    p.dma("sync", xd, x_in)
    p.cut()
    emit_ln(ctx, xd, "A")
    emit_l0ffn(ctx, xd, w, "A")
    p.dma("sync", x_out, xd)
    p.finish()
    return p.build()


def build_l0attn():
    nc = bass.Bass("TRN2", target_bir_lowering=False)
    ctx = Ctx(nc)
    p = ctx.p
    x_in = ctx.din("x_in", [4096, 1024])
    x_out = ctx.dout("x_out", [4096, 1024])
    xd = ctx.dtmp("x_sc", [4096, 1024])
    w = {
        "wq_t": ctx.din("w_wq", [32, 128, 4096]),
        "wk_t": ctx.din("w_wk", [32, 128, 4096]),
        "wv_t": ctx.din("w_wv", [32, 128, 4096]),
        "wo_t": ctx.din("w_wo", [32, 128, 4096]),
        "cos": ctx.din("w_cos", [128, 1024]),
        "sin": ctx.din("w_sin", [128, 1024]),
        "mtab": ctx.din("w_mtab", [128, 8192], BF16),
        "q_loc": ctx.dtmp("q_loc", [4096, 1024], BF16),
        "k_bounce": ctx.dtmp("k_bounce", [4096, 1024], BF16),
        "v_loc": ctx.dtmp("v_loc", [1024, 4096], BF16),
        "k_all": ctx.dtmp("k_all", [16384, 1024], BF16),
        "v_all": ctx.dtmp("v_all", [4096, 4096], BF16),
    }
    ctx.pools()
    ctx.constants()
    p.dma("sync", xd, x_in)
    p.cut()
    emit_ln(ctx, xd, "A")
    emit_l0attn(ctx, xd, w)
    p.dma("sync", x_out, xd)
    p.finish()
    return p.build()


def build_l1ffn():
    nc = bass.Bass("TRN2", target_bir_lowering=False)
    ctx = Ctx(nc)
    p = ctx.p
    x_in = ctx.din("x_in", [4096, 1024])
    x_out = ctx.dout("x_out", [4096, 1024])
    xd = ctx.dtmp("x_sc", [4096, 1024])
    w = {
        "wq_t2": ctx.din("w_wq2", [128, 2048]),
        "wr_t": ctx.din("w_wr", [64, 48]),
        "keys_f": ctx.din("w_keys", [64, 1024]),
        "values": ctx.din("w_vals", [1024, 4096]),
        "wvd_t": ctx.din("w_wvd", [2, 128, 4096]),
        "wvu_t": ctx.din("w_wvu", [32, 128, 256]),
        "wu1_t": ctx.din("w_wu1", [8, 128, 4096]),
        "wu2_t": ctx.din("w_wu2", [128, 8]),
        "bu1_t": ctx.din("w_bu1", [128, 8]),
        "log_rel": ctx.din("w_lr", [1, 1024]),
        "bu2": ctx.din("w_bu2", [1, 1]),
        "gate_w1": ctx.din("w_gw1s", [1, 1]),
        "gate_bias": ctx.din("w_gbs", [1, 1]),
        "cb2_in": ctx.dtmp("cb2_in", [1, 1]),
        "cb2_out": ctx.dtmp("cb2_out", [1, 1]),
    }
    ctx.pools()
    ctx.constants()
    p.dma("sync", xd, x_in)
    p.cut()
    var_row = ctx.S[0:1, 1024:2048]
    emit_ln(ctx, xd, "A", var_row=var_row)
    emit_l1ffn(ctx, xd, w, var_row)
    p.dma("sync", x_out, xd)
    p.finish()
    return p.build()


def build_l1attn():
    nc = bass.Bass("TRN2", target_bir_lowering=False)
    ctx = Ctx(nc)
    p = ctx.p
    x_in = ctx.din("x_in", [4096, 1024])
    x_out = ctx.dout("x_out", [4096, 1024])
    xd = ctx.dtmp("x_sc", [4096, 1024])
    w = {
        "inproj_t": ctx.din("w_ip", [145, 128, 4096]),
        "convd_t": ctx.din("w_cd", [80, 128, 512]),
        "conv_b": ctx.din("w_cb", [128, 80]),
        "dtb": ctx.din("w_dtb", [128, 1]),
        "alog": ctx.din("w_alog", [128, 1]),
        "d_exp": ctx.din("w_dexp", [128, 64]),
        "nw_t": ctx.din("w_nw", [128, 64]),
        "outw_t": ctx.din("w_ow", [32, 128, 8192]),
        "trimask4": ctx.din("w_tri", [128, 512]),
        "selc": ctx.din("w_selc", [128, 16]),
        "hsel": ctx.din("w_hsel", [128, 16]),
        "gate_d": ctx.dtmp("gate_d", [8192, 1024], BF16),
        "xbc_d": ctx.dtmp("xbc_d", [10240, 1024], BF16),
        "xss_d": ctx.dtmp("xss_d", [8192, 1024], BF16),
        "bfc_d": ctx.dtmp("bfc_d", [128, 8192], BF16),
        "cfc_d": ctx.dtmp("cfc_d", [128, 8192], BF16),
        "std_d": ctx.dtmp("std_d", [1024, 8192], BF16),
        "cum_d": ctx.dtmp("cum_d", [128, 1024], F32),
        "halo_snd": ctx.dtmp("halo_snd", [128, 480], BF16),
        "halo_all": ctx.dtmp("halo_all", [512, 480], BF16),
        "st_snd": ctx.dtmp("st_snd", [256, 8192], BF16),
        "st_all": ctx.dtmp("st_all", [1024, 8192], BF16),
        "dp_snd": ctx.dtmp("dp_snd", [2, 128], F32),
        "dp_all": ctx.dtmp("dp_all", [8, 128], F32),
        "sin_d": ctx.dtmp("sin_d", [256, 8192], BF16),
    }
    ctx.pools()
    ctx.constants()
    p.dma("sync", xd, x_in)
    p.cut()
    emit_ln(ctx, xd, "A")
    emit_l1attn(ctx, xd, w)
    emit_l1attn_state(ctx, xd, w)
    emit_l1attn_y(ctx, xd, w)
    p.dma("sync", x_out, xd)
    p.finish()
    return p.build()


_CACHE = {}


def kernel_multi(**inputs):
    import ml_dtypes
    from concourse.bass_utils import run_bass_kernel_spmd
    if "progs" not in _CACHE:
        _CACHE["progs"] = {
            "l0ffn": build_l0ffn(), "l0attn": build_l0attn(),
            "l1ffn": build_l1ffn(), "l1attn": build_l1attn(),
        }
    progs = _CACHE["progs"]
    wf0 = l0ffn_host(inputs)
    wa0 = l0attn_host(inputs)
    wf1 = l1ffn_host(inputs)
    wa1 = l1attn_host(inputs)
    xs = shard_x(np.asarray(inputs["L0_attn_x"]).astype(np.float32))

    def launch(prog, permaps):
        res = run_bass_kernel_spmd(prog, permaps, core_ids=list(range(8)))
        return [r["x_out"] for r in res.results]

    def maps_f0(xs):
        return [{"x_in": xs[c], "w_wt": wf0["wt"], "w_aall": wf0["a_all_t"],
                 "w_bflat": wf0["b_flat"], "w_gw1t": wf0["gw1t_t"], "w_gw2t": wf0["gw2t"],
                 "w_gb1": wf0["gb1"], "w_gb2": wf0["gb2"], "w_gamma": wf0["lngamma"],
                 "w_beta": wf0["lnbeta"], "w_esel": wf0["esel"]} for c in range(8)]

    def maps_a0(xs):
        return [{"x_in": xs[c], "w_wq": wa0["wq_t"], "w_wk": wa0["wk_t"],
                 "w_wv": wa0["wv_t"], "w_wo": wa0["wo_t"], "w_cos": wa0["cos_pc"][c],
                 "w_sin": wa0["sin_pc"][c],
                 "w_mtab": attn_masks(c).astype(ml_dtypes.bfloat16)} for c in range(8)]

    def maps_f1(xs):
        return [{"x_in": xs[c], "w_wq2": wf1["wq_t2"], "w_wr": wf1["wr_t"],
                 "w_keys": wf1["keys_f"], "w_vals": wf1["values"], "w_wvd": wf1["wvd_t"],
                 "w_wvu": wf1["wvu_t"], "w_wu1": wf1["wu1_t"], "w_wu2": wf1["wu2_t"],
                 "w_bu1": wf1["bu1_t"], "w_lr": wf1["log_rel"], "w_bu2": wf1["bu2"],
                 "w_gw1s": wf1["gate_w1"], "w_gbs": wf1["gate_bias"]} for c in range(8)]

    def maps_a1(xs):
        return [{"x_in": xs[c], "w_ip": wa1["inproj_t"], "w_cd": wa1["convd_t"],
                 "w_cb": wa1["conv_b"], "w_dtb": wa1["dtb"], "w_alog": wa1["alog"],
                 "w_dexp": wa1["d_exp"], "w_nw": wa1["nw_t"], "w_ow": wa1["outw_t"],
                 "w_tri": wa1["trimask4"], "w_selc": wa1["selc_pc"][c],
                 "w_hsel": wa1["hsel_pc"][c]} for c in range(8)]

    xs = launch(progs["l0ffn"], maps_f0(xs))
    xs = launch(progs["l0attn"], maps_a0(xs))
    xs = launch(progs["l0ffn"], maps_f0(xs))
    xs = launch(progs["l1ffn"], maps_f1(xs))
    xs = launch(progs["l1attn"], maps_a1(xs))
    xs = launch(progs["l1ffn"], maps_f1(xs))
    return unshard_x(xs)


def build_fused():
    nc = bass.Bass("TRN2", target_bir_lowering=False)
    ctx = Ctx(nc)
    p = ctx.p
    x_in = ctx.din("x_in", [4096, 1024])
    x_out = ctx.dout("x_out", [4096, 1024])
    xd = ctx.dtmp("x_sc", [4096, 1024])
    wf0 = {
        "wt": ctx.din("w_wt", [32, 128, 4096]),
        "a_all_t": ctx.din("w_aall", [128, 4096]),
        "b_flat": ctx.din("w_bflat", [128, 4096]),
        "gw1t_t": ctx.din("w_gw1t", [128, 1024]),
        "gw2t": ctx.din("w_gw2t", [32, 8]),
        "gb1": ctx.din("w_gb1", [32, 1]),
        "gb2": ctx.din("w_gb2", [8, 1]),
        "lngamma": ctx.din("w_gamma", [32, 1]),
        "lnbeta": ctx.din("w_beta", [32, 1]),
        "esel": ctx.din("w_esel", [8, 128]),
        "cb_in": ctx.dtmp("cb_in", [1, 16]),
        "cb_out": ctx.dtmp("cb_out", [1, 16]),
    }
    wf0b = dict(wf0)
    wf0b["cb_in"] = ctx.dtmp("cb_in2", [1, 16])
    wf0b["cb_out"] = ctx.dtmp("cb_out2", [1, 16])
    wa0 = {
        "wq_t": ctx.din("w_wq", [32, 128, 4096]),
        "wk_t": ctx.din("w_wk", [32, 128, 4096]),
        "wv_t": ctx.din("w_wv", [32, 128, 4096]),
        "wo_t": ctx.din("w_wo", [32, 128, 4096]),
        "cos": ctx.din("w_cos", [128, 1024]),
        "sin": ctx.din("w_sin", [128, 1024]),
        "mtab": ctx.din("w_mtab", [128, 8192], BF16),
        "q_loc": ctx.dtmp("q_loc", [4096, 1024], BF16),
        "k_bounce": ctx.dtmp("k_bounce", [4096, 1024], BF16),
        "v_loc": ctx.dtmp("v_loc", [1024, 4096], BF16),
        "k_all": ctx.dtmp("k_all", [16384, 1024], BF16),
        "v_all": ctx.dtmp("v_all", [4096, 4096], BF16),
    }
    wf1 = {
        "wq_t2": ctx.din("w_wq2", [128, 2048]),
        "wr_t": ctx.din("w_wr", [64, 48]),
        "keys_f": ctx.din("w_keys", [64, 1024]),
        "values": ctx.din("w_vals", [1024, 4096]),
        "wvd_t": ctx.din("w_wvd", [2, 128, 4096]),
        "wvu_t": ctx.din("w_wvu", [32, 128, 256]),
        "wu1_t": ctx.din("w_wu1", [8, 128, 4096]),
        "wu2_t": ctx.din("w_wu2", [128, 8]),
        "bu1_t": ctx.din("w_bu1", [128, 8]),
        "log_rel": ctx.din("w_lr", [1, 1024]),
        "bu2": ctx.din("w_bu2", [1, 1]),
        "gate_w1": ctx.din("w_gw1s", [1, 1]),
        "gate_bias": ctx.din("w_gbs", [1, 1]),
        "cb2_in": ctx.dtmp("cb2_in", [1, 1]),
        "cb2_out": ctx.dtmp("cb2_out", [1, 1]),
    }
    wf1b = dict(wf1)
    wf1b["cb2_in"] = ctx.dtmp("cb2_in2", [1, 1])
    wf1b["cb2_out"] = ctx.dtmp("cb2_out2", [1, 1])
    wa1 = {
        "inproj_t": ctx.din("w_ip", [145, 128, 4096]),
        "convd_t": ctx.din("w_cd", [80, 128, 512]),
        "conv_b": ctx.din("w_cb", [128, 80]),
        "dtb": ctx.din("w_dtb", [128, 1]),
        "alog": ctx.din("w_alog", [128, 1]),
        "d_exp": ctx.din("w_dexp", [128, 64]),
        "nw_t": ctx.din("w_nw", [128, 64]),
        "outw_t": ctx.din("w_ow", [32, 128, 8192]),
        "trimask4": ctx.din("w_tri", [128, 512]),
        "selc": ctx.din("w_selc", [128, 16]),
        "hsel": ctx.din("w_hsel", [128, 16]),
        "gate_d": ctx.dtmp("gate_d", [8192, 1024], BF16),
        "xbc_d": ctx.dtmp("xbc_d", [10240, 1024], BF16),
        "xss_d": ctx.dtmp("xss_d", [8192, 1024], BF16),
        "bfc_d": ctx.dtmp("bfc_d", [128, 8192], BF16),
        "cfc_d": ctx.dtmp("cfc_d", [128, 8192], BF16),
        "std_d": ctx.dtmp("std_d", [1024, 8192], BF16),
        "cum_d": ctx.dtmp("cum_d", [128, 1024], F32),
        "halo_snd": ctx.dtmp("halo_snd", [128, 480], BF16),
        "halo_all": ctx.dtmp("halo_all", [512, 480], BF16),
        "st_snd": ctx.dtmp("st_snd", [256, 8192], BF16),
        "st_all": ctx.dtmp("st_all", [1024, 8192], BF16),
        "dp_snd": ctx.dtmp("dp_snd", [2, 128], F32),
        "dp_all": ctx.dtmp("dp_all", [8, 128], F32),
        "sin_d": ctx.dtmp("sin_d", [256, 8192], BF16),
    }
    ctx.pools()
    ctx.constants()
    p.dma("sync", xd, x_in)
    p.cut()
    emit_ln(ctx, xd, "A")
    emit_l0ffn(ctx, xd, wf0, "A")
    emit_ln(ctx, xd, "B")
    emit_l0attn(ctx, xd, wa0)
    emit_ln(ctx, xd, "C")
    emit_l0ffn(ctx, xd, wf0b, "C")
    var_row = ctx.S[0:1, 1024:2048]
    emit_ln(ctx, xd, "D", var_row=var_row)
    emit_l1ffn(ctx, xd, wf1, var_row)
    emit_ln(ctx, xd, "E")
    emit_l1attn(ctx, xd, wa1)
    emit_l1attn_state(ctx, xd, wa1)
    emit_l1attn_y(ctx, xd, wa1)
    emit_ln(ctx, xd, "F", var_row=var_row)
    emit_l1ffn(ctx, xd, wf1b, var_row)
    p.dma("sync", x_out, xd)
    p.finish()
    import sys
    print("FUSED ninstr:", p.ninstr, "waits:", p.nwaits, "segments:", len(p.segments),
          "dma_sems:", p.ndma_sems, file=sys.stderr)
    return p.build()


def kernel(**inputs):
    import ml_dtypes
    from concourse.bass_utils import run_bass_kernel_spmd
    if "fused" not in _CACHE:
        _CACHE["fused"] = build_fused()
    prog = _CACHE["fused"]
    wf0 = l0ffn_host(inputs)
    wa0 = l0attn_host(inputs)
    wf1 = l1ffn_host(inputs)
    wa1 = l1attn_host(inputs)
    xs = shard_x(np.asarray(inputs["L0_attn_x"]).astype(np.float32))
    maps = []
    for c in range(8):
        m = {"x_in": xs[c],
             "w_wt": wf0["wt"], "w_aall": wf0["a_all_t"], "w_bflat": wf0["b_flat"],
             "w_gw1t": wf0["gw1t_t"], "w_gw2t": wf0["gw2t"], "w_gb1": wf0["gb1"],
             "w_gb2": wf0["gb2"], "w_gamma": wf0["lngamma"], "w_beta": wf0["lnbeta"],
             "w_esel": wf0["esel"],
             "w_wq": wa0["wq_t"], "w_wk": wa0["wk_t"], "w_wv": wa0["wv_t"],
             "w_wo": wa0["wo_t"], "w_cos": wa0["cos_pc"][c], "w_sin": wa0["sin_pc"][c],
             "w_mtab": attn_masks(c).astype(ml_dtypes.bfloat16),
             "w_wq2": wf1["wq_t2"], "w_wr": wf1["wr_t"], "w_keys": wf1["keys_f"],
             "w_vals": wf1["values"], "w_wvd": wf1["wvd_t"], "w_wvu": wf1["wvu_t"],
             "w_wu1": wf1["wu1_t"], "w_wu2": wf1["wu2_t"], "w_bu1": wf1["bu1_t"],
             "w_lr": wf1["log_rel"], "w_bu2": wf1["bu2"], "w_gw1s": wf1["gate_w1"],
             "w_gbs": wf1["gate_bias"],
             "w_ip": wa1["inproj_t"], "w_cd": wa1["convd_t"], "w_cb": wa1["conv_b"],
             "w_dtb": wa1["dtb"], "w_alog": wa1["alog"], "w_dexp": wa1["d_exp"],
             "w_nw": wa1["nw_t"], "w_ow": wa1["outw_t"], "w_tri": wa1["trimask4"],
             "w_selc": wa1["selc_pc"][c], "w_hsel": wa1["hsel_pc"][c]}
        maps.append(m)
    res = run_bass_kernel_spmd(prog, maps, core_ids=list(range(8)))
    return unshard_x([r["x_out"] for r in res.results])


# version 05 (final)
# speedup vs baseline: 1.1370x; 1.1164x over previous
import numpy as np
import concourse.bass as bass
import concourse.mybir as mybir

F32 = mybir.dt.float32
BF16 = mybir.dt.bfloat16
AF = mybir.ActivationFunctionType
OP = mybir.AluOpType

ENGINES = ("tensor", "vector", "scalar", "gpsimd", "sync")
AUTO = {"tensor": "aT", "vector": "aV", "scalar": "aS", "gpsimd": "aG", "sync": "aY"}


def vap(handle, offset, pattern):
    return bass.AP(handle, offset, pattern)


def _rect(ap):
    t = ap.tensor
    name = t.name
    cls = type(t).__name__
    space = "sbuf" if "SB" in cls else ("psum" if "PSum" in cls else "dram")
    shape = list(t.shape)
    whole = (name, space, 0, 1 << 20, 0, 1 << 40)
    W = shape[-1] if len(shape) >= 2 else shape[0]
    apl = [list(x) for x in ap.ap]
    if len(apl) == 2 and apl[1][0] == 1 and apl[0][0] == W:
        p0 = ap.offset // W
        c0 = ap.offset % W
        p1 = p0 + apl[0][1]
        c1 = c0 + apl[1][1]
    elif len(apl) == 1:
        p0, p1 = 0, 1
        c0, c1 = ap.offset, ap.offset + apl[0][1]
    else:
        return whole
    if space == "psum":
        bank = 512 if t.dtype == F32 else 1024
        p0, p1 = 0, 128
        c0 = (c0 // bank) * bank
        c1 = ((c1 + bank - 1) // bank) * bank
    return (name, space, p0, p1, c0, c1)


def _overlap(a, b):
    return a[0] == b[0] and a[2] < b[3] and b[2] < a[3] and a[4] < b[5] and b[4] < a[5]


class Prog:
    def __init__(self, nc):
        self.nc = nc
        self.segments = [{e: [] for e in ENGINES}]
        self.semctr = {}
        self.sems = {}
        self.ninstr = 0
        self.nwaits = 0
        self.log = {}
        self.clocks = {e: {} for e in ENGINES}
        self.slot_sems = {}
        self.ndma_sems = 0

    def sem(self, name):
        if name not in self.sems:
            h = self.nc.semaphore(name).__enter__()
            self.sems[name] = h
            self.semctr[name] = 0
        return self.sems[name]

    def val(self, name):
        self.sem(name)
        return self.semctr[name]

    CUT_MIN = 1400

    def cut(self, force=False):
        n = sum(len(self.segments[-1][e]) for e in ENGINES)
        if n == 0:
            return
        if not force and n < self.CUT_MIN:
            return
        self.segments.append({e: [] for e in ENGINES})
        cover = {AUTO[e]: self.semctr.get(AUTO[e], 0) for e in ENGINES}
        for e in ENGINES:
            for s, v in cover.items():
                if self.clocks[e].get(s, 0) < v:
                    self.clocks[e][s] = v
        for nm in list(self.log):
            keep = [ent for ent in self.log[nm]
                    if not all(self.clocks[e].get(ent[3], 0) >= ent[4] for e in ENGINES)]
            if keep:
                self.log[nm] = keep
            else:
                del self.log[nm]

    def _emit_wait(self, engine, semname, v, prod_clock):
        ck = self.clocks[engine]
        if ck.get(semname, 0) >= v:
            if prod_clock:
                for s, pv in prod_clock.items():
                    if ck.get(s, 0) < pv:
                        ck[s] = pv
            return
        self.sem(semname)

        def run(eng, h=self.sems[semname], v=v):
            eng.wait_ge(h, v)
        self.segments[-1][engine].append(run)
        self.nwaits += 1
        ck[semname] = v
        if prod_clock:
            for s, pv in prod_clock.items():
                if ck.get(s, 0) < pv:
                    ck[s] = pv

    def _deps(self, engine, reads, writes, is_pe):
        rrects = [_rect(a) for a in reads]
        wrects = [_rect(a) for a in writes]
        needed = {}
        for nm in {r[0] for r in rrects + wrects}:
            for ent in self.log.get(nm, ()):
                k, rect, peng, sem, v, ck = ent
                if k == "w":
                    conflict = any(_overlap(rect, r) for r in rrects + wrects)
                else:
                    conflict = any(_overlap(rect, r) for r in wrects)
                if not conflict:
                    continue
                if is_pe and peng == "tensor" and sem == AUTO["tensor"]:
                    continue
                if sem not in needed or needed[sem][0] < v:
                    needed[sem] = (v, ck)
        for sem, (v, ck) in needed.items():
            self._emit_wait(engine, sem, v, ck)
        return rrects, wrects

    def _record(self, rrects, wrects, engine, sem, v):
        ck = dict(self.clocks[engine])
        for r in rrects:
            self.log.setdefault(r[0], []).append(("r", r, engine, sem, v, ck))
        for r in wrects:
            self.log.setdefault(r[0], []).append(("w", r, engine, sem, v, ck))

    def op(self, engine, fn, reads=(), writes=(), inc=None, is_pe=False):
        rr, wr = self._deps(engine, list(reads), list(writes), is_pe and engine == "tensor")
        auto = AUTO[engine]
        self.sem(auto)
        self.semctr[auto] += 1
        v = self.semctr[auto]
        handles = [(self.sems[auto], 1)]
        ret = v
        if inc is not None:
            nm, amt = inc
            self.sem(nm)
            self.semctr[nm] += amt
            handles.append((self.sems[nm], amt))
            ret = self.semctr[nm]

        def run(eng, fn=fn, handles=handles):
            inst = fn(eng)
            for h, amt in handles:
                inst = inst.then_inc(h, amt)
        self.segments[-1][engine].append(run)
        self.ninstr += 1
        self._record(rr, wr, engine, auto, v)
        return ret

    def mm(self, fn, reads, writes):
        return self.op("tensor", fn, reads, writes, is_pe=True)

    def _slot_sem(self, key):
        if key not in self.slot_sems:
            nm = f"d{self.ndma_sems}"
            self.ndma_sems += 1
            self.slot_sems[key] = nm
        return self.slot_sems[key]

    def dma(self, engine, out, in_, **kw):
        orect = _rect(out)
        irect = _rect(in_)
        key = orect if orect[1] != "dram" else (irect if irect[1] != "dram" else orect)
        semname = self._slot_sem((key[0], key[2], key[4]))
        rr, wr = self._deps(engine, [in_], [out], False)
        self.sem(semname)
        self.semctr[semname] += 16
        v = self.semctr[semname]

        def run(eng, out=out, in_=in_, h=self.sems[semname], kw=kw):
            eng.dma_start(out=out, in_=in_, **kw).then_inc(h, 16)
        self.segments[-1][engine].append(run)
        self.ninstr += 1
        self._record(rr, wr, engine, semname, v)
        return semname, v

    def collective(self, kind, op, groups, ins, outs):
        rr, wr = self._deps("gpsimd", ins, outs, False)
        self.sem("cc")
        self.semctr["cc"] += 1
        v = self.semctr["cc"]

        def run(eng, ins=ins, outs=outs):
            eng.collective_compute(kind, op, replica_groups=groups,
                                   ins=[a.opt() for a in ins],
                                   outs=[a.opt() for a in outs]).then_inc(self.sems["cc"], 1)
        self.segments[-1]["gpsimd"].append(run)
        self.ninstr += 1
        self._record(rr, wr, "gpsimd", "cc", v)
        return v

    def wait(self, engine, semname, val=None):
        self.sem(semname)
        v = self.semctr[semname] if val is None else val
        self._emit_wait(engine, semname, v, None)
        return v

    def finish(self):
        for nm in sorted(set(self.slot_sems.values())):
            self.wait("sync", nm)

    def build(self):
        for seg in self.segments:
            if not any(seg[e] for e in ENGINES):
                continue
            block = self.nc.Block().__enter__()
            for e in ENGINES:
                if not seg[e]:
                    continue
                seq = seg[e]

                def section(eng, seq=seq):
                    for f in seq:
                        f(eng)
                getattr(block, e)(section)
            block.__exit__(None, None, None)
        return self.nc


class Ctx:
    def __init__(self, nc):
        self.nc = nc
        self.p = Prog(nc)

    def sb(self, name, shape, dtype):
        return self.nc.sbuf_tensor(name, shape, dtype).__enter__()

    def ps(self, name, shape, dtype=F32):
        return self.nc.psum_tensor(name, shape, dtype).__enter__()

    def din(self, name, shape, dtype=F32):
        return self.nc.dram_tensor(name, shape, dtype, kind="ExternalInput").ap()

    def dout(self, name, shape, dtype=F32):
        return self.nc.dram_tensor(name, shape, dtype, kind="ExternalOutput").ap()

    def dtmp(self, name, shape, dtype=F32):
        return self.nc.dram_tensor(name, shape, dtype).ap()

    def pools(self):
        self.LNX = self.sb("pool_lnx", [128, 32768], BF16)
        self.A = self.sb("pool_a", [128, 8192], F32)
        self.B = self.sb("pool_b", [128, 16384], BF16)
        self.C = self.sb("pool_c", [128, 16384], BF16)
        self.S = self.sb("pool_s", [128, 2048], F32)
        self.R = self.sb("pool_r", [1, 8192], F32)
        self.pA = self.ps("pool_pa", [128, 1024])
        self.pB = self.ps("pool_pb", [128, 1024])
        self.pC = self.ps("pool_pc", [128, 1024])
        self.pD = self.ps("pool_pd", [128, 512])
        self.pE = self.ps("pool_pe", [128, 1024], BF16)

    def constants(self):
        p = self.p
        self.iden_bf = self.sb("c_iden_bf", [128, 128], BF16)
        self.iden32 = self.sb("c_iden32", [128, 128], F32)
        self.ones_col_bf = self.sb("c_ones_col_bf", [128, 1], BF16)
        self.ones_row32 = self.sb("c_ones_row32", [1, 128], F32)
        self.ones_row_bf = self.sb("c_ones_row_bf", [1, 128], BF16)
        self.cst = self.sb("c_cst", [128, 4], F32)
        for t in (self.iden_bf, self.iden32):
            p.op("gpsimd", lambda g, t=t: g.memset(t[:, :], 0.0), writes=[t[:, :]])
            p.op("gpsimd", lambda g, t=t: g.affine_select(
                out=t[:, :], in_=t[:, :], compare_op=OP.not_equal, fill=1.0,
                base=0, pattern=[[-1, 128]], channel_multiplier=1),
                reads=[t[:, :]], writes=[t[:, :]])
        p.op("gpsimd", lambda g: g.memset(self.ones_col_bf[:, :], 1.0),
             writes=[self.ones_col_bf[:, :]])
        p.op("gpsimd", lambda g: g.memset(self.ones_row32[:, :], 1.0),
             writes=[self.ones_row32[:, :]])
        p.op("gpsimd", lambda g: g.memset(self.ones_row_bf[:, :], 1.0),
             writes=[self.ones_row_bf[:, :]])
        p.op("gpsimd", lambda g: g.memset(self.cst[:, 0:1], 1e-5), writes=[self.cst[:, 0:1]])
        p.op("gpsimd", lambda g: g.memset(self.cst[:, 1:2], 1.0), writes=[self.cst[:, 1:2]])
        p.op("gpsimd", lambda g: g.memset(self.cst[:, 2:3], 0.0), writes=[self.cst[:, 2:3]])
        p.op("gpsimd", lambda g: g.memset(self.cst[:, 3:4], 1e-6), writes=[self.cst[:, 3:4]])


def emit_ln(ctx, xd, tag, var_row=None):
    p = ctx.p
    nK = 32
    xc = ctx.A[:, 0:4096]
    tt_ = ctx.A[:, 4096:6144]
    mu_b = ctx.A[:, 6144:7168]
    rs_b = ctx.A[:, 7168:8192]
    xb = ctx.B[:, 0:2048]
    sq = ctx.B[:, 2048:4096]
    mu = ctx.R[:, 0:1024]
    rstd = ctx.R[:, 1024:2048]
    tmpr = ctx.R[:, 2048:3072]
    lnx = ctx.LNX
    for k in range(nK):
        xck = xc[:, (k % 4) * 1024:(k % 4 + 1) * 1024]
        xbk = xb[:, (k % 2) * 1024:(k % 2 + 1) * 1024]
        sqk = sq[:, (k % 2) * 1024:(k % 2 + 1) * 1024]
        p.dma("sync", xck, xd[k * 128:(k + 1) * 128, :])
        p.op("vector", lambda v, a=xbk, b=xck: v.tensor_copy(out=a, in_=b),
             reads=[xck], writes=[xbk])
        p.op("scalar", lambda s, a=sqk, b=xck: s.activation(out=a, in_=b, func=AF.Square),
             reads=[xck], writes=[sqk])
        st, sp = (k == 0), (k == nK - 1)
        for g in range(2):
            xbg = xbk[:, g * 512:(g + 1) * 512]
            sqg = sqk[:, g * 512:(g + 1) * 512]
            oa = ctx.pA[0:1, g * 512:(g + 1) * 512]
            ob = ctx.pB[0:1, g * 512:(g + 1) * 512]
            p.mm(lambda t, o=oa, r=xbg, st=st, sp=sp: t.matmul(o, ctx.ones_col_bf[:, :], r,
                 start=st, stop=sp), reads=[xbg, ctx.ones_col_bf[:, :]], writes=[oa])
            p.mm(lambda t, o=ob, r=sqg, st=st, sp=sp: t.matmul(o, ctx.ones_col_bf[:, :], r,
                 start=st, stop=sp), reads=[sqg, ctx.ones_col_bf[:, :]], writes=[ob])
    p.op("vector", lambda v: v.tensor_scalar(out=mu, in0=ctx.pA[0:1, :], scalar1=1.0 / 4096,
         scalar2=None, op0=OP.mult, op1=OP.bypass), reads=[ctx.pA[0:1, :]], writes=[mu])
    p.op("vector", lambda v: v.tensor_scalar(out=tmpr, in0=ctx.pB[0:1, :], scalar1=1.0 / 4096,
         scalar2=None, op0=OP.mult, op1=OP.bypass), reads=[ctx.pB[0:1, :]], writes=[tmpr])
    p.op("vector", lambda v: v.tensor_tensor(out=rstd, in0=mu, in1=mu, op=OP.mult),
         reads=[mu], writes=[rstd])
    p.op("vector", lambda v: v.tensor_tensor(out=tmpr, in0=tmpr, in1=rstd, op=OP.subtract),
         reads=[tmpr, rstd], writes=[tmpr])
    if var_row is not None:
        p.op("vector", lambda v: v.tensor_copy(out=var_row, in_=tmpr), reads=[tmpr],
             writes=[var_row])
    p.op("scalar", lambda s: s.activation(out=rstd, in_=tmpr, func=AF.Sqrt,
         bias=ctx.cst[0:1, 0:1], scale=1.0), reads=[tmpr, ctx.cst[0:1, 0:1]], writes=[rstd])
    p.op("vector", lambda v: v.reciprocal(out=rstd, in_=rstd), reads=[rstd], writes=[rstd])
    p.cut()
    for g in range(2):
        p.mm(lambda t, g=g: t.matmul(ctx.pA[:, g * 512:(g + 1) * 512], ctx.ones_row32[:, :],
             mu[0:1, g * 512:(g + 1) * 512], start=True, stop=True),
             reads=[mu[0:1, g * 512:(g + 1) * 512], ctx.ones_row32[:, :]],
             writes=[ctx.pA[:, g * 512:(g + 1) * 512]])
        p.mm(lambda t, g=g: t.matmul(ctx.pB[:, g * 512:(g + 1) * 512], ctx.ones_row32[:, :],
             rstd[0:1, g * 512:(g + 1) * 512], start=True, stop=True),
             reads=[rstd[0:1, g * 512:(g + 1) * 512], ctx.ones_row32[:, :]],
             writes=[ctx.pB[:, g * 512:(g + 1) * 512]])
    p.op("vector", lambda v: v.tensor_copy(out=mu_b, in_=ctx.pA[:, :]),
         reads=[ctx.pA[:, :]], writes=[mu_b])
    p.op("vector", lambda v: v.tensor_copy(out=rs_b, in_=ctx.pB[:, :]),
         reads=[ctx.pB[:, :]], writes=[rs_b])
    for k in range(nK):
        xck = xc[:, (k % 4) * 1024:(k % 4 + 1) * 1024]
        ttk = tt_[:, (k % 2) * 1024:(k % 2 + 1) * 1024]
        lnk = lnx[:, k * 1024:(k + 1) * 1024]
        p.dma("sync", xck, xd[k * 128:(k + 1) * 128, :])
        eng = "vector" if k % 2 == 0 else "gpsimd"
        p.op(eng, lambda e, a=ttk, b=xck: e.tensor_tensor(out=a, in0=b, in1=mu_b, op=OP.subtract),
             reads=[xck, mu_b], writes=[ttk])
        p.op(eng, lambda e, a=lnk, b=ttk: e.tensor_tensor(out=a, in0=b, in1=rs_b, op=OP.mult),
             reads=[ttk, rs_b], writes=[lnk])
    p.cut()


def emit_l0ffn(ctx, xd, w, tag):
    p = ctx.p
    lnx = ctx.LNX
    gw1t = ctx.C[:, 0:1024]
    aall = ctx.C[:, 1024:5120]
    bflat = ctx.C[:, 5120:9216]
    tmps = ctx.C[:, 9216:10240]
    wexp = ctx.C[:, 10240:11264]
    gw2t = ctx.C[0:32, 11264:11272]
    esel = ctx.C[0:8, 11272:11400]
    wTbf = ctx.C[0:8, 11400:12424]
    gb1 = ctx.S[0:32, 0:1]
    gb2 = ctx.S[0:8, 1:2]
    gamma = ctx.S[0:32, 2:3]
    beta = ctx.S[0:32, 3:4]
    alc = ctx.S[0:8, 4:6]
    g12 = ctx.S[:, 8:12]
    gtm = ctx.S[:, 16:24]
    m8 = ctx.S[:, 24:32]
    wtm = ctx.S[:, 32:48]
    h32 = ctx.S[0:32, 48:1072]
    hbf = ctx.B[0:32, 4096:5120]
    hsq = ctx.B[0:32, 5120:6144]
    hrelu = ctx.B[0:32, 6144:7168]
    ohs = ctx.B[:, 7168:7184]
    gfm = ctx.A[0:8, 0:1024]
    wT1 = ctx.A[0:8, 1024:2048]
    wT2 = ctx.A[0:8, 2048:3072]
    xc2 = ctx.A[:, 3072:4096]
    xnew = ctx.A[:, 4096:5120]
    hmu = ctx.R[:, 3088:4112]
    hrstd = ctx.R[:, 4112:5136]
    htmp = ctx.R[:, 5136:6160]
    crow = ctx.R[:, 3072:3088]
    for dst, src, eng in [(gw1t, w["gw1t_t"], "gpsimd"), (gw2t, w["gw2t"], "gpsimd"),
                          (esel, w["esel"], "gpsimd"), (aall, w["a_all_t"], "gpsimd"),
                          (bflat, w["b_flat"], "gpsimd"), (gb1, w["gb1"], "sync"),
                          (gb2, w["gb2"], "sync"), (gamma, w["lngamma"], "sync"),
                          (beta, w["lnbeta"], "sync")]:
        p.dma(eng, dst, src)
    for k in range(32):
        for g in range(2):
            rhs = lnx[:, k * 1024 + g * 512:k * 1024 + (g + 1) * 512]
            o = ctx.pA[0:32, g * 512:(g + 1) * 512]
            l = gw1t[:, k * 32:(k + 1) * 32]
            p.mm(lambda t, o=o, l=l, r=rhs, st=(k == 0), sp=(k == 31):
                 t.matmul(o, l, r, start=st, stop=sp), reads=[l, rhs], writes=[o])
    p.op("vector", lambda v: v.tensor_scalar(out=h32, in0=ctx.pA[0:32, :], scalar1=gb1,
         scalar2=None, op0=OP.add, op1=OP.bypass), reads=[ctx.pA[0:32, :], gb1], writes=[h32])
    p.op("vector", lambda v: v.tensor_copy(out=hbf, in_=h32), reads=[h32], writes=[hbf])
    p.op("scalar", lambda s: s.activation(out=hsq, in_=h32, func=AF.Square),
         reads=[h32], writes=[hsq])
    p.cut()
    for g in range(2):
        hb = hbf[:, g * 512:(g + 1) * 512]
        hq = hsq[:, g * 512:(g + 1) * 512]
        p.mm(lambda t, o=ctx.pB[0:1, g * 512:(g + 1) * 512], r=hb:
             t.matmul(o, ctx.ones_col_bf[0:32, :], r, start=True, stop=True),
             reads=[hb, ctx.ones_col_bf[0:32, :]], writes=[ctx.pB[0:1, g * 512:(g + 1) * 512]])
        p.mm(lambda t, o=ctx.pC[0:1, g * 512:(g + 1) * 512], r=hq:
             t.matmul(o, ctx.ones_col_bf[0:32, :], r, start=True, stop=True),
             reads=[hq, ctx.ones_col_bf[0:32, :]], writes=[ctx.pC[0:1, g * 512:(g + 1) * 512]])
    p.op("vector", lambda v: v.tensor_scalar(out=hmu, in0=ctx.pB[0:1, :], scalar1=1.0 / 32,
         scalar2=None, op0=OP.mult, op1=OP.bypass), reads=[ctx.pB[0:1, :]], writes=[hmu])
    p.op("vector", lambda v: v.tensor_scalar(out=htmp, in0=ctx.pC[0:1, :], scalar1=1.0 / 32,
         scalar2=None, op0=OP.mult, op1=OP.bypass), reads=[ctx.pC[0:1, :]], writes=[htmp])
    p.op("vector", lambda v: v.tensor_tensor(out=hrstd, in0=hmu, in1=hmu, op=OP.mult),
         reads=[hmu], writes=[hrstd])
    p.op("vector", lambda v: v.tensor_tensor(out=htmp, in0=htmp, in1=hrstd, op=OP.subtract),
         reads=[htmp, hrstd], writes=[htmp])
    p.op("scalar", lambda s: s.activation(out=hrstd, in_=htmp, func=AF.Sqrt,
         bias=ctx.cst[0:1, 0:1], scale=1.0), reads=[htmp, ctx.cst[0:1, 0:1]], writes=[hrstd])
    p.op("vector", lambda v: v.reciprocal(out=hrstd, in_=hrstd), reads=[hrstd], writes=[hrstd])
    p.cut()
    for g in range(2):
        p.mm(lambda t, g=g: t.matmul(ctx.pB[0:32, g * 512:(g + 1) * 512],
             ctx.ones_row32[0:1, 0:32], hmu[0:1, g * 512:(g + 1) * 512], start=True, stop=True),
             reads=[hmu[0:1, g * 512:(g + 1) * 512], ctx.ones_row32[0:1, 0:32]],
             writes=[ctx.pB[0:32, g * 512:(g + 1) * 512]])
        p.mm(lambda t, g=g: t.matmul(ctx.pC[0:32, g * 512:(g + 1) * 512],
             ctx.ones_row32[0:1, 0:32], hrstd[0:1, g * 512:(g + 1) * 512], start=True, stop=True),
             reads=[hrstd[0:1, g * 512:(g + 1) * 512], ctx.ones_row32[0:1, 0:32]],
             writes=[ctx.pC[0:32, g * 512:(g + 1) * 512]])
    p.op("vector", lambda v: v.tensor_tensor(out=h32, in0=h32, in1=ctx.pB[0:32, :],
         op=OP.subtract), reads=[h32, ctx.pB[0:32, :]], writes=[h32])
    p.op("vector", lambda v: v.tensor_tensor(out=h32, in0=h32, in1=ctx.pC[0:32, :],
         op=OP.mult), reads=[h32, ctx.pC[0:32, :]], writes=[h32])
    p.op("vector", lambda v: v.tensor_scalar(out=h32, in0=h32, scalar1=gamma, scalar2=beta,
         op0=OP.mult, op1=OP.add), reads=[h32, gamma, beta], writes=[h32])
    p.op("vector", lambda v: v.tensor_scalar(out=hrelu, in0=h32, scalar1=0.0, scalar2=None,
         op0=OP.max, op1=OP.bypass), reads=[h32], writes=[hrelu])
    for g in range(2):
        hr = hrelu[:, g * 512:(g + 1) * 512]
        p.mm(lambda t, o=ctx.pC[0:8, g * 512:(g + 1) * 512], r=hr:
             t.matmul(o, gw2t, r, start=True, stop=True),
             reads=[gw2t, hr], writes=[ctx.pC[0:8, g * 512:(g + 1) * 512]])
    p.op("vector", lambda v: v.tensor_scalar(out=gfm, in0=ctx.pC[0:8, :], scalar1=gb2,
         scalar2=None, op0=OP.add, op1=OP.bypass), reads=[ctx.pC[0:8, :], gb2], writes=[gfm])
    p.cut()
    for tt in range(8):
        gsl = gfm[:, tt * 128:(tt + 1) * 128]
        p.mm(lambda t, g=gsl: t.transpose(ctx.pA[:, 0:8], g, ctx.iden32[0:8, 0:8]),
             reads=[gsl, ctx.iden32[0:8, 0:8]], writes=[ctx.pA[:, 0:512]])
        p.op("vector", lambda v: v.tensor_copy(out=gtm, in_=ctx.pA[:, 0:8]),
             reads=[ctx.pA[:, 0:512]], writes=[gtm])
        p.op("vector", lambda v: v.max(out=m8, in_=gtm), reads=[gtm], writes=[m8])
        p.op("vector", lambda v: v.tensor_tensor(out=g12[:, 0:1], in0=m8[:, 0:1], in1=m8[:, 1:2],
             op=OP.subtract), reads=[m8], writes=[g12[:, 0:1]])
        p.op("scalar", lambda s: s.activation(out=g12[:, 1:2], in_=g12[:, 0:1], func=AF.Sigmoid),
             reads=[g12[:, 0:1]], writes=[g12[:, 1:2]])
        p.op("vector", lambda v: v.tensor_scalar(out=g12[:, 2:3], in0=g12[:, 1:2], scalar1=-1.0,
             scalar2=1.0, op0=OP.mult, op1=OP.add), reads=[g12[:, 1:2]], writes=[g12[:, 2:3]])
        p.op("vector", lambda v: v.tensor_scalar(out=ohs[:, 0:8], in0=gtm, scalar1=m8[:, 0:1],
             scalar2=None, op0=OP.is_equal, op1=OP.bypass), reads=[gtm, m8], writes=[ohs[:, 0:8]])
        p.op("vector", lambda v: v.tensor_scalar(out=ohs[:, 8:16], in0=gtm, scalar1=m8[:, 1:2],
             scalar2=None, op0=OP.is_equal, op1=OP.bypass), reads=[gtm, m8], writes=[ohs[:, 8:16]])
        p.mm(lambda t, st=(tt == 0), sp=(tt == 7): t.matmul(ctx.pB[0:1, 0:16],
             ctx.ones_col_bf[:, :], ohs, start=st, stop=sp),
             reads=[ohs, ctx.ones_col_bf[:, :]], writes=[ctx.pB[0:1, 0:512]])
        p.op("vector", lambda v: v.tensor_scalar(out=wtm[:, 0:8], in0=ohs[:, 0:8],
             scalar1=g12[:, 1:2], scalar2=None, op0=OP.mult, op1=OP.bypass),
             reads=[ohs[:, 0:8], g12[:, 1:2]], writes=[wtm[:, 0:8]])
        p.op("vector", lambda v: v.tensor_scalar(out=wtm[:, 8:16], in0=ohs[:, 8:16],
             scalar1=g12[:, 2:3], scalar2=None, op0=OP.mult, op1=OP.bypass),
             reads=[ohs[:, 8:16], g12[:, 2:3]], writes=[wtm[:, 8:16]])
        p.mm(lambda t: t.transpose(ctx.pC[0:8, 0:128], wtm[:, 0:8], ctx.iden32[:, :]),
             reads=[wtm[:, 0:8], ctx.iden32[:, :]], writes=[ctx.pC[0:8, 0:512]])
        p.mm(lambda t: t.transpose(ctx.pC[0:8, 512:640], wtm[:, 8:16], ctx.iden32[:, :]),
             reads=[wtm[:, 8:16], ctx.iden32[:, :]], writes=[ctx.pC[0:8, 512:1024]])
        p.op("vector", lambda v, tt=tt: v.tensor_copy(out=wT1[:, tt * 128:(tt + 1) * 128],
             in_=ctx.pC[0:8, 0:128]), reads=[ctx.pC[0:8, 0:512]],
             writes=[wT1[:, tt * 128:(tt + 1) * 128]])
        p.op("vector", lambda v, tt=tt: v.tensor_copy(out=wT2[:, tt * 128:(tt + 1) * 128],
             in_=ctx.pC[0:8, 512:640]), reads=[ctx.pC[0:8, 512:1024]],
             writes=[wT2[:, tt * 128:(tt + 1) * 128]])
    p.op("vector", lambda v: v.tensor_copy(out=crow, in_=ctx.pB[0:1, 0:16]),
         reads=[ctx.pB[0:1, 0:512]], writes=[crow])
    p.dma("gpsimd", w["cb_in"], crow)
    p.collective("AllReduce", OP.add, [list(range(8))], [w["cb_in"]], [w["cb_out"]])
    p.dma("gpsimd", alc[:, 0:1], w["cb_out"][0:1, 0:8].rearrange("a e -> e a"))
    p.dma("gpsimd", alc[:, 1:2], w["cb_out"][0:1, 8:16].rearrange("a e -> e a"))
    p.op("vector", lambda v: v.tensor_scalar(out=alc, in0=alc, scalar1=3072.0, scalar2=None,
         op0=OP.is_le, op1=OP.bypass), reads=[alc], writes=[alc])
    p.op("vector", lambda v: v.tensor_scalar(out=wT1, in0=wT1, scalar1=alc[:, 0:1],
         scalar2=None, op0=OP.mult, op1=OP.bypass), reads=[wT1, alc], writes=[wT1])
    p.op("vector", lambda v: v.scalar_tensor_tensor(out=wTbf, in0=wT2, scalar=alc[:, 1:2],
         in1=wT1, op0=OP.mult, op1=OP.add), reads=[wT2, alc, wT1], writes=[wTbf])
    p.cut()
    for g in range(2):
        wsl = wTbf[:, g * 512:(g + 1) * 512]
        o = (ctx.pA if g == 0 else ctx.pB)[:, 0:512]
        p.mm(lambda t, o=o, r=wsl: t.matmul(o, esel, r, start=True, stop=True),
             reads=[esel, wsl], writes=[o])
    p.op("vector", lambda v: v.tensor_copy(out=wexp[:, 0:512], in_=ctx.pA[:, 0:512]),
         reads=[ctx.pA[:, 0:512]], writes=[wexp[:, 0:512]])
    p.op("vector", lambda v: v.tensor_copy(out=wexp[:, 512:1024], in_=ctx.pB[:, 0:512]),
         reads=[ctx.pB[:, 0:512]], writes=[wexp[:, 512:1024]])
    for g in range(2):
        o = (ctx.pA if g == 0 else ctx.pB)[:, 512:1024]
        for k in range(32):
            rhs = lnx[:, k * 1024 + g * 512:k * 1024 + (g + 1) * 512]
            l = aall[:, k * 128:(k + 1) * 128]
            p.mm(lambda t, o=o, l=l, r=rhs, st=(k == 0), sp=(k == 31):
                 t.matmul(o, l, r, start=st, stop=sp), reads=[l, rhs], writes=[o])
    p.op("vector", lambda v: v.tensor_tensor(out=tmps[:, 0:512], in0=ctx.pA[:, 512:1024],
         in1=wexp[:, 0:512], op=OP.mult), reads=[ctx.pA[:, 512:1024], wexp[:, 0:512]],
         writes=[tmps[:, 0:512]])
    p.op("vector", lambda v: v.tensor_tensor(out=tmps[:, 512:1024], in0=ctx.pB[:, 512:1024],
         in1=wexp[:, 512:1024], op=OP.mult), reads=[ctx.pB[:, 512:1024], wexp[:, 512:1024]],
         writes=[tmps[:, 512:1024]])
    p.cut()
    wtsb = ctx.B[:, 8192:16384]
    for m in range(32):
        wsl = wtsb[:, (m % 2) * 4096:(m % 2 + 1) * 4096]
        p.dma("gpsimd", wsl, w["wt"][m])
        for g in range(2):
            idx = m * 2 + g
            xsl = xc2[:, (idx % 2) * 512:(idx % 2 + 1) * 512]
            nsl = xnew[:, (idx % 2) * 512:(idx % 2 + 1) * 512]
            xdr = xd[m * 128:(m + 1) * 128, g * 512:(g + 1) * 512]
            mp = ctx.pC[:, 0:512] if idx % 2 == 0 else ctx.pD[:, 0:512]
            p.dma("sync", xsl, xdr)
            for k in range(32):
                rhs = lnx[:, k * 1024 + g * 512:k * 1024 + (g + 1) * 512]
                lhs = wsl[:, k * 128:(k + 1) * 128]
                p.mm(lambda t, o=mp, l=lhs, r=rhs, st=(k == 0): t.matmul(o, l, r, start=st,
                     stop=False), reads=[lhs, rhs], writes=[mp])
            p.mm(lambda t, o=mp, l=bflat[:, m * 128:(m + 1) * 128],
                 r=tmps[:, g * 512:(g + 1) * 512]: t.matmul(o, l, r, start=False, stop=True),
                 reads=[bflat[:, m * 128:(m + 1) * 128], tmps[:, g * 512:(g + 1) * 512]],
                 writes=[mp])
            p.op("vector", lambda v, o=nsl, a=mp, b=xsl: v.scalar_tensor_tensor(
                 out=o, in0=a, scalar=0.5, in1=b, op0=OP.mult, op1=OP.add),
                 reads=[mp, xsl], writes=[nsl])
            p.dma("sync", xdr, nsl)
        if m % 8 == 7:
            p.cut()


def emit_stream_proj(ctx, xd, src, wt, scale, nM=32, extra_mm=None):
    p = ctx.p
    wtsb = ctx.B[:, 8192:16384]
    xc2 = ctx.A[:, 3072:4096]
    xnew = ctx.A[:, 4096:5120]
    for m in range(nM):
        wsl = wtsb[:, (m % 2) * 4096:(m % 2 + 1) * 4096]
        p.dma("gpsimd", wsl, wt[m])
        for g in range(2):
            idx = m * 2 + g
            xsl = xc2[:, (idx % 2) * 512:(idx % 2 + 1) * 512]
            nsl = xnew[:, (idx % 2) * 512:(idx % 2 + 1) * 512]
            xdr = xd[m * 128:(m + 1) * 128, g * 512:(g + 1) * 512]
            mp = ctx.pC[:, 0:512] if idx % 2 == 0 else ctx.pD[:, 0:512]
            p.dma("sync", xsl, xdr)
            for k in range(32):
                rhs = src[:, k * 1024 + g * 512:k * 1024 + (g + 1) * 512]
                lhs = wsl[:, k * 128:(k + 1) * 128]
                p.mm(lambda t, o=mp, l=lhs, r=rhs, st=(k == 0),
                     sp=(k == 31 and extra_mm is None):
                     t.matmul(o, l, r, start=st, stop=sp), reads=[lhs, rhs], writes=[mp])
            if extra_mm is not None:
                extra_mm(p, mp, m, g)
            p.op("vector", lambda v, o=nsl, a=mp, b=xsl: v.scalar_tensor_tensor(
                 out=o, in0=a, scalar=scale, in1=b, op0=OP.mult, op1=OP.add),
                 reads=[mp, xsl], writes=[nsl])
            p.dma("sync", xdr, nsl)
        if m % 8 == 7:
            p.cut()


def emit_l0attn(ctx, xd, w):
    p = ctx.p
    lnx = ctx.LNX
    S8 = 1.0 / float(np.sqrt(128.0))
    KMAX = 32.0
    cosf = ctx.A[:, 0:1024]
    sinf = ctx.A[:, 1024:2048]
    p.dma("sync", cosf, w["cos"])
    p.dma("sync", sinf, w["sin"])
    wqs = ctx.B[:, 0:8192]
    wks = ctx.B[:, 8192:16384]
    wvs = ctx.C[:, 0:8192]
    qro = ctx.C[:, 8192:9216]
    kro = ctx.C[:, 9216:10240]
    vbf = ctx.C[:, 10240:11264]
    vtm = ctx.C[:, 12288:13312]
    qc = ctx.A[:, 2048:3072]
    qs = ctx.A[:, 3072:4096]
    kc = ctx.A[:, 4096:5120]
    ks_ = ctx.A[:, 5120:6144]
    for m in range(32):
        wq_m = wqs[:, (m % 2) * 4096:(m % 2 + 1) * 4096]
        wk_m = wks[:, (m % 2) * 4096:(m % 2 + 1) * 4096]
        wv_m = wvs[:, (m % 2) * 4096:(m % 2 + 1) * 4096]
        p.dma("gpsimd", wq_m, w["wq_t"][m])
        p.dma("gpsimd", wk_m, w["wk_t"][m])
        p.dma("gpsimd", wv_m, w["wv_t"][m])
        for g in range(2):
            for k in range(32):
                rhs = lnx[:, k * 1024 + g * 512:k * 1024 + (g + 1) * 512]
                for ws_, pp in ((wq_m, ctx.pA), (wk_m, ctx.pB), (wv_m, ctx.pC)):
                    o = pp[:, g * 512:(g + 1) * 512]
                    p.mm(lambda t, o=o, l=ws_[:, k * 128:(k + 1) * 128], r=rhs, st=(k == 0),
                         sp=(k == 31): t.matmul(o, l, r, start=st, stop=sp),
                         reads=[ws_[:, k * 128:(k + 1) * 128], rhs], writes=[o])
        p.op("vector", lambda v: v.tensor_copy(out=vbf, in_=ctx.pC[:, :]),
             reads=[ctx.pC[:, :]], writes=[vbf])
        for j in range(8):
            p.mm(lambda t, j=j: t.transpose(ctx.pE[:, j * 128:(j + 1) * 128],
                 vbf[:, j * 128:(j + 1) * 128], ctx.iden_bf[:, :]),
                 reads=[vbf[:, j * 128:(j + 1) * 128], ctx.iden_bf[:, :]],
                 writes=[ctx.pE[:, j * 128:(j + 1) * 128]])
        p.op("vector", lambda v: v.tensor_copy(out=vtm, in_=ctx.pE[:, :]),
             reads=[ctx.pE[:, :]], writes=[vtm])
        p.dma("sync", w["v_loc"][:, m * 128:(m + 1) * 128].rearrange("(j p) c -> p j c", p=128),
              vtm[:, :].rearrange("p (j c) -> p j c", c=128))
        for (pp, cc, ssn, ro) in ((ctx.pA, qc, qs, qro), (ctx.pB, kc, ks_, kro)):
            p.op("vector", lambda v, o=cc, a=pp: v.tensor_tensor(out=o, in0=a[:, :], in1=cosf,
                 op=OP.mult), reads=[pp[:, :], cosf], writes=[cc])
            p.op("vector", lambda v, o=ssn, a=pp: v.tensor_tensor(
                 out=o[0:64, :], in0=a[64:128, :], in1=sinf[64:128, :], op=OP.mult),
                 reads=[pp[:, :], sinf], writes=[ssn[0:64, :]])
            p.op("vector", lambda v, o=ssn, a=pp: v.tensor_tensor(
                 out=o[64:128, :], in0=a[0:64, :], in1=sinf[0:64, :], op=OP.mult),
                 reads=[pp[:, :], sinf], writes=[ssn[64:128, :]])
            p.op("vector", lambda e, o=ro, a=cc, b=ssn: e.tensor_tensor(
                 out=o[0:64, :], in0=a[0:64, :], in1=b[0:64, :], op=OP.subtract),
                 reads=[cc[0:64, :], ssn[0:64, :]], writes=[ro[0:64, :]])
            p.op("vector", lambda e, o=ro, a=cc, b=ssn: e.tensor_tensor(
                 out=o[64:128, :], in0=a[64:128, :], in1=b[64:128, :], op=OP.add),
                 reads=[cc[64:128, :], ssn[64:128, :]], writes=[ro[64:128, :]])
        p.dma("sync", w["q_loc"][m * 128:(m + 1) * 128, :], qro)
        p.dma("sync", w["k_bounce"][m * 128:(m + 1) * 128, :], kro)
        if m % 4 == 3:
            p.cut()
    p.cut()
    grp = [[0, 1, 2, 3], [4, 5, 6, 7]]
    p.collective("AllGather", OP.bypass, grp, [w["k_bounce"]], [w["k_all"]])
    p.collective("AllGather", OP.bypass, grp, [w["v_loc"]], [w["v_all"]])
    p.cut()
    mtab = ctx.C[:, 4096:12288]
    p.dma("gpsimd", mtab, w["mtab"])
    qh = ctx.B[:, 0:1024]
    ktile = ctx.B[:, 1024:3072]
    vtile = ctx.B[:, 3072:5120]
    kown = ctx.B[:, 5120:6144]
    qkp = ctx.C[:, 1536:2560]
    qsq2 = ctx.C[:, 2560:3584]
    negb = ctx.C[0:1, 3072:4096]
    rden = ctx.S[0:1, 0:512]
    d_row = ctx.A[0:1, 6144:7168]
    n_row = ctx.A[0:1, 7168:8192]
    t_row = ctx.A[0:1, 5120:6144]
    rbc = ctx.A[:, 2048:2560]
    stslots = [ctx.pA[:, 0:512], ctx.pA[:, 512:1024], ctx.pB[:, 0:512]]
    ptslots = [ctx.C[:, i * 512:(i + 1) * 512] for i in range(3)]
    AGM = 160.0
    for h in range(32):
        p.dma("sync", qh, w["q_loc"][h * 128:(h + 1) * 128, :])
        p.dma("sync", kown, w["k_bounce"][h * 128:(h + 1) * 128, :])
        p.op("vector", lambda v: v.tensor_tensor(out=qkp, in0=qh, in1=kown, op=OP.mult),
             reads=[qh, kown], writes=[qkp])
        p.op("vector", lambda v: v.tensor_tensor(out=qsq2, in0=qh, in1=qh, op=OP.mult),
             reads=[qh], writes=[qsq2])
        for g in range(2):
            p.mm(lambda t, r=qkp[:, g * 512:(g + 1) * 512]: t.matmul(ctx.pD[0:1, 0:512],
                 ctx.ones_col_bf[:, :], r, start=True, stop=True),
                 reads=[qkp[:, g * 512:(g + 1) * 512], ctx.ones_col_bf[:, :]],
                 writes=[ctx.pD[0:1, 0:512]])
            p.op("vector", lambda v, o=d_row[0:1, g * 512:(g + 1) * 512]: v.tensor_copy(
                 out=o, in_=ctx.pD[0:1, 0:512]), reads=[ctx.pD[0:1, 0:512]],
                 writes=[d_row[0:1, g * 512:(g + 1) * 512]])
            p.mm(lambda t, r=qsq2[:, g * 512:(g + 1) * 512]: t.matmul(ctx.pD[0:1, 0:512],
                 ctx.ones_col_bf[:, :], r, start=True, stop=True),
                 reads=[qsq2[:, g * 512:(g + 1) * 512], ctx.ones_col_bf[:, :]],
                 writes=[ctx.pD[0:1, 0:512]])
            p.op("vector", lambda v, o=n_row[0:1, g * 512:(g + 1) * 512]: v.tensor_copy(
                 out=o, in_=ctx.pD[0:1, 0:512]), reads=[ctx.pD[0:1, 0:512]],
                 writes=[n_row[0:1, g * 512:(g + 1) * 512]])
        cc_ = KMAX / (2.0 * float(np.sqrt(AGM)))
        p.op("vector", lambda v: v.tensor_scalar(out=t_row, in0=n_row, scalar1=cc_,
             scalar2=AGM * cc_, op0=OP.mult, op1=OP.add), reads=[n_row], writes=[t_row])
        p.op("vector", lambda v: v.tensor_scalar(out=d_row, in0=d_row, scalar1=80.0 / S8,
             scalar2=None, op0=OP.add, op1=OP.bypass), reads=[d_row], writes=[d_row])
        p.op("vector", lambda v: v.tensor_tensor(out=t_row, in0=t_row, in1=d_row, op=OP.min),
             reads=[t_row, d_row], writes=[t_row])
        p.op("vector", lambda v: v.tensor_scalar(out=negb, in0=t_row, scalar1=-1.0,
             scalar2=None, op0=OP.mult, op1=OP.bypass), reads=[t_row], writes=[negb])
        for qg in range(2):
            nstrips = 4 if qg == 0 else 8
            qsl = qh[:, qg * 512:(qg + 1) * 512]
            nbs = negb[0:1, qg * 512:(qg + 1) * 512]
            O = ctx.pC[:, 0:512]
            den = ctx.pB[0:1, 512:1024]
            npairs = nstrips * 4
            def issue_kv(s_):
                r = min(s_, 7 - s_)
                off = 0 if s_ < 4 else 512
                kts = ktile[:, (s_ % 4) * 512:(s_ % 4 + 1) * 512]
                vts = vtile[:, (s_ % 4) * 512:(s_ % 4 + 1) * 512]
                p.dma("sync", kts, w["k_all"][r * 4096 + h * 128:r * 4096 + (h + 1) * 128,
                                              off:off + 512])
                p.dma("sync", vts[:, :].rearrange("p (j c) -> p j c", c=128),
                      w["v_all"][r * 1024 + off:r * 1024 + off + 512,
                                 h * 128:(h + 1) * 128].rearrange("(j p) c -> p j c", p=128))
                return kts, vts
            tiles = {}
            for s_ in range(min(2, nstrips)):
                tiles[s_] = issue_kv(s_)
            pend = []
            pi = 0
            for s_ in range(nstrips):
                if s_ + 2 < nstrips:
                    tiles[s_ + 2] = issue_kv(s_ + 2)
                kts, vts = tiles[s_]
                use_mask = (qg == 0) or (s_ >= 4)
                for kt in range(4):
                    st = stslots[pi % 3]
                    pt = ptslots[pi % 3]
                    p.mm(lambda t, o=st, l=kts[:, kt * 128:(kt + 1) * 128], r_=qsl:
                         t.matmul(o, l, r_, start=True, stop=False),
                         reads=[kts[:, kt * 128:(kt + 1) * 128], qsl], writes=[st])
                    p.mm(lambda t, o=st, r_=nbs: t.matmul(o, ctx.ones_row_bf[:, :], r_,
                         start=False, stop=not use_mask),
                         reads=[nbs, ctx.ones_row_bf[:, :]], writes=[st])
                    if use_mask:
                        mcol = s_ * 1024
                        msl = mtab[:, mcol + 384 - 128 * kt:mcol + 896 - 128 * kt]
                        p.mm(lambda t, o=st, m_=msl: t.matmul(o, ctx.iden_bf[:, :], m_,
                             start=False, stop=True), reads=[ctx.iden_bf[:, :], msl],
                             writes=[st])
                    p.op("scalar", lambda s, o=pt, i_=st: s.activation(
                         out=o, in_=i_, func=AF.Exp, scale=S8), reads=[st], writes=[pt])
                    pend.append((pi, pt, vts))
                    if len(pend) > 2:
                        pj, ptj, vtsj = pend.pop(0)
                        p.mm(lambda t, o=O, l=vtsj[:, (pj % 4) * 128:(pj % 4) * 128 + 128],
                             r_=ptj, st_=(pj == 0), sp=(pj == npairs - 1):
                             t.matmul(o, l, r_, start=st_, stop=sp),
                             reads=[vtsj[:, (pj % 4) * 128:(pj % 4) * 128 + 128], ptj],
                             writes=[O])
                        p.mm(lambda t, o=den, r_=ptj, st_=(pj == 0), sp=(pj == npairs - 1):
                             t.matmul(o, ctx.ones_col_bf[:, :], r_, start=st_, stop=sp),
                             reads=[ptj, ctx.ones_col_bf[:, :]], writes=[den])
                    pi += 1
            for pj, ptj, vtsj in pend:
                p.mm(lambda t, o=O, l=vtsj[:, (pj % 4) * 128:(pj % 4) * 128 + 128],
                     r_=ptj, st_=(pj == 0), sp=(pj == npairs - 1):
                     t.matmul(o, l, r_, start=st_, stop=sp),
                     reads=[vtsj[:, (pj % 4) * 128:(pj % 4) * 128 + 128], ptj], writes=[O])
                p.mm(lambda t, o=den, r_=ptj, st_=(pj == 0), sp=(pj == npairs - 1):
                     t.matmul(o, ctx.ones_col_bf[:, :], r_, start=st_, stop=sp),
                     reads=[ptj, ctx.ones_col_bf[:, :]], writes=[den])
            p.op("vector", lambda v: v.tensor_copy(out=rden, in_=den), reads=[den], writes=[rden])
            p.op("vector", lambda v: v.reciprocal(out=rden, in_=rden), reads=[rden], writes=[rden])
            p.mm(lambda t: t.matmul(ctx.pD[:, 0:512], ctx.ones_row32[:, :], rden,
                 start=True, stop=True), reads=[rden, ctx.ones_row32[:, :]],
                 writes=[ctx.pD[:, 0:512]])
            p.op("vector", lambda v: v.tensor_copy(out=rbc, in_=ctx.pD[:, 0:512]),
                 reads=[ctx.pD[:, 0:512]], writes=[rbc])
            p.op("vector", lambda v, o=lnx[:, h * 1024 + qg * 512:h * 1024 + (qg + 1) * 512],
                 O_=O: v.tensor_tensor(out=o, in0=O_, in1=rbc, op=OP.mult),
                 reads=[O, rbc], writes=[lnx[:, h * 1024 + qg * 512:h * 1024 + (qg + 1) * 512]])
        p.cut()
    p.cut()
    emit_stream_proj(ctx, xd, lnx, w["wo_t"], 1.0)
    p.cut()


def emit_l1ffn(ctx, xd, w, var_row):
    p = ctx.p
    lnx = ctx.LNX
    SR48 = 1.0 / float(np.sqrt(48.0))
    xck = ctx.A[:, 0:2048]
    xnk = ctx.A[:, 2048:4096]
    for k in range(32):
        a = xck[:, (k % 2) * 1024:(k % 2 + 1) * 1024]
        b = xnk[:, (k % 2) * 1024:(k % 2 + 1) * 1024]
        xdr = xd[k * 128:(k + 1) * 128, :]
        p.dma("sync", a, xdr)
        p.op("vector", lambda v, o=b, l=lnx[:, k * 1024:(k + 1) * 1024], a=a:
             v.scalar_tensor_tensor(out=o, in0=l, scalar=0.5, in1=a, op0=OP.mult, op1=OP.add),
             reads=[lnx[:, k * 1024:(k + 1) * 1024], a], writes=[b])
        p.dma("sync", xdr, b)
    p.cut()
    scl3 = ctx.S[0:1, 74:77]
    p.dma("sync", scl3[0:1, 0:1], w["bu2"])
    p.dma("sync", scl3[0:1, 1:2], w["gate_w1"])
    p.dma("sync", scl3[0:1, 2:3], w["gate_bias"])
    vrec = ctx.A[0:1, 4096:5120]
    vhr = ctx.A[0:1, 5120:6144]
    logr = ctx.A[0:1, 6144:7168]
    srow = ctx.A[0:1, 7168:8192]
    p.op("vector", lambda v: v.tensor_scalar(out=vrec, in0=var_row, scalar1=1e-5, scalar2=None,
         op0=OP.add, op1=OP.bypass), reads=[var_row], writes=[vrec])
    p.op("vector", lambda v: v.reciprocal(out=vrec, in_=vrec), reads=[vrec], writes=[vrec])
    p.op("vector", lambda v: v.tensor_tensor(out=vhr, in0=var_row, in1=vrec, op=OP.mult),
         reads=[var_row, vrec], writes=[vhr])
    p.op("scalar", lambda s: s.activation(out=logr, in_=vhr, func=AF.Ln,
         bias=ctx.cst[0:1, 1:2], scale=1.0), reads=[vhr, ctx.cst[0:1, 1:2]], writes=[logr])
    p.op("vector", lambda v: v.reduce_sum(out=srow[0:1, 0:1], in_=logr,
         axis=mybir.AxisListType.X), reads=[logr], writes=[srow[0:1, 0:1]])
    p.dma("gpsimd", w["cb2_in"], srow[0:1, 0:1])
    p.collective("AllReduce", OP.add, [list(range(8))], [w["cb2_in"]], [w["cb2_out"]])
    p.dma("gpsimd", srow[0:1, 1:2], w["cb2_out"])
    p.op("vector", lambda v: v.tensor_scalar(out=srow[0:1, 2:3], in0=srow[0:1, 1:2],
         scalar1=1.0 / 8192.0, scalar2=1e-6, op0=OP.mult, op1=OP.add),
         reads=[srow[0:1, 1:2]], writes=[srow[0:1, 2:3]])
    p.op("vector", lambda v: v.reciprocal(out=srow[0:1, 3:4], in_=srow[0:1, 2:3]),
         reads=[srow[0:1, 2:3]], writes=[srow[0:1, 3:4]])
    nv = ctx.A[0:1, 4096:5120]
    p.op("vector", lambda v: v.tensor_scalar(out=nv, in0=logr, scalar1=srow[0:1, 3:4],
         scalar2=0.5, op0=OP.mult, op1=OP.mult), reads=[logr, srow[0:1, 3:4]], writes=[nv])
    wu1s = ctx.B[:, 0:8192]
    bu1c = ctx.S[:, 8:16]
    wu2c = ctx.C[:, 15360:15368]
    gu = ctx.C[:, 0:8192]
    u32 = ctx.A[:, 0:1024]
    t1 = ctx.A[:, 1024:2048]
    t2 = ctx.A[:, 2048:3072]
    p.dma("sync", bu1c, w["bu1_t"])
    p.dma("gpsimd", wu2c, w["wu2_t"])
    for m in range(8):
        ws = wu1s[:, (m % 2) * 4096:(m % 2 + 1) * 4096]
        p.dma("gpsimd", ws, w["wu1_t"][m])
        for g in range(2):
            o = ctx.pA[:, g * 512:(g + 1) * 512]
            for k in range(32):
                rhs = lnx[:, k * 1024 + g * 512:k * 1024 + (g + 1) * 512]
                p.mm(lambda t, o=o, l=ws[:, k * 128:(k + 1) * 128], r=rhs, st=(k == 0),
                     sp=(k == 31): t.matmul(o, l, r, start=st, stop=sp),
                     reads=[ws[:, k * 128:(k + 1) * 128], rhs], writes=[o])
        p.op("vector", lambda v, m=m: v.tensor_scalar(out=u32, in0=ctx.pA[:, :],
             scalar1=bu1c[:, m:m + 1], scalar2=None, op0=OP.add, op1=OP.bypass),
             reads=[ctx.pA[:, :], bu1c[:, m:m + 1]], writes=[u32])
        p.op("scalar", lambda s: s.activation(out=t1, in_=u32, func=AF.Square),
             reads=[u32], writes=[t1])
        p.op("vector", lambda v: v.tensor_scalar(out=t2, in0=t1, scalar1=0.044715,
             scalar2=1.0, op0=OP.mult, op1=OP.add), reads=[t1], writes=[t2])
        p.op("vector", lambda v: v.tensor_tensor(out=t2, in0=u32, in1=t2, op=OP.mult),
             reads=[u32, t2], writes=[t2])
        p.op("scalar", lambda s: s.activation(out=t2, in_=t2, func=AF.Tanh,
             scale=0.7978845608028654), reads=[t2], writes=[t2])
        p.op("vector", lambda v: v.tensor_scalar(out=t2, in0=t2, scalar1=1.0, scalar2=None,
             op0=OP.add, op1=OP.bypass), reads=[t2], writes=[t2])
        p.op("vector", lambda v, m=m: v.scalar_tensor_tensor(out=gu[:, m * 1024:(m + 1) * 1024],
             in0=u32, scalar=0.5, in1=t2, op0=OP.mult, op1=OP.mult),
             reads=[u32, t2], writes=[gu[:, m * 1024:(m + 1) * 1024]])
    for m in range(8):
        for g in range(2):
            o = ctx.pB[0:1, g * 512:(g + 1) * 512]
            p.mm(lambda t, o=o, l=wu2c[:, m:m + 1], r=gu[:, m * 1024 + g * 512:m * 1024 + (g + 1) * 512],
                 st=(m == 0), sp=(m == 7): t.matmul(o, l, r, start=st, stop=sp),
                 reads=[wu2c[:, m:m + 1], gu[:, m * 1024 + g * 512:m * 1024 + (g + 1) * 512]],
                 writes=[o])
    sig = ctx.A[0:1, 5120:6144]
    p.op("scalar", lambda s: s.activation(out=sig, in_=ctx.pB[0:1, 0:1024], func=AF.Sigmoid,
         bias=scl3[0:1, 0:1], scale=1.0), reads=[ctx.pB[0:1, 0:1024], scl3[0:1, 0:1]],
         writes=[sig])
    gfin = ctx.A[0:1, 6144:7168]
    p.op("vector", lambda v: v.tensor_scalar(out=sig, in0=sig, scalar1=2.5, scalar2=None,
         op0=OP.mult, op1=OP.bypass), reads=[sig], writes=[sig])
    p.op("vector", lambda v: v.tensor_tensor(out=sig, in0=nv, in1=sig, op=OP.add),
         reads=[nv, sig], writes=[sig])
    p.op("vector", lambda v: v.tensor_scalar(out=sig, in0=sig, scalar1=0.0, scalar2=5.0,
         op0=OP.max, op1=OP.min), reads=[sig], writes=[sig])
    p.op("vector", lambda v: v.tensor_scalar(out=sig, in0=sig, scalar1=-0.5, scalar2=1.0 / 1.5,
         op0=OP.add, op1=OP.mult), reads=[sig], writes=[sig])
    p.op("vector", lambda v: v.tensor_scalar(out=sig, in0=sig, scalar1=0.0, scalar2=1.0,
         op0=OP.max, op1=OP.min), reads=[sig], writes=[sig])
    p.op("vector", lambda v: v.tensor_scalar(out=sig, in0=sig, scalar1=scl3[0:1, 1:2],
         scalar2=scl3[0:1, 2:3], op0=OP.mult, op1=OP.add),
         reads=[sig, scl3[0:1, 1:2], scl3[0:1, 2:3]], writes=[sig])
    p.op("scalar", lambda s: s.activation(out=sig, in_=sig, func=AF.Sigmoid),
         reads=[sig], writes=[sig])
    p.op("vector", lambda v: v.tensor_scalar(out=gfin, in0=sig, scalar1=0.05, scalar2=None,
         op0=OP.is_ge, op1=OP.bypass), reads=[sig], writes=[gfin])
    p.op("vector", lambda v: v.tensor_tensor(out=gfin, in0=gfin, in1=sig, op=OP.mult),
         reads=[gfin, sig], writes=[gfin])
    p.op("vector", lambda v: v.tensor_scalar(out=gfin, in0=gfin, scalar1=0.5, scalar2=None,
         op0=OP.mult, op1=OP.bypass), reads=[gfin], writes=[gfin])
    p.cut()
    wqt = ctx.C[:, 13312:15360]
    p.dma("gpsimd", wqt, w["wq_t2"])
    qbf = ctx.C[0:64, 8192:9216]
    q8a = ctx.C[0:65, 9216:10240]
    rqa = ctx.C[0:49, 10240:11264]
    rka = ctx.C[0:49, 11264:12288]
    k2a = ctx.C[0:65, 12288:13312]
    wrt = ctx.C[0:64, 15368:15416]
    keysf = ctx.A[0:64, 3072:4096]
    p.dma("gpsimd", wrt, w["wr_t"])
    p.dma("sync", keysf, w["keys_f"])
    p.dma("gpsimd", k2a[64:65, :], w["log_rel"])
    p.dma("gpsimd", rka[48:49, :], w["log_rel"])
    for g in range(2):
        o = ctx.pA[0:64, g * 512:(g + 1) * 512]
        for k in range(32):
            rhs = lnx[:, k * 1024 + g * 512:k * 1024 + (g + 1) * 512]
            p.mm(lambda t, o=o, l=wqt[:, k * 64:(k + 1) * 64], r=rhs, st=(k == 0), sp=(k == 31):
                 t.matmul(o, l, r, start=st, stop=sp),
                 reads=[wqt[:, k * 64:(k + 1) * 64], rhs], writes=[o])
    p.op("gpsimd", lambda g: g.memset(q8a[:, :], 1.0), writes=[q8a[:, :]])
    p.op("gpsimd", lambda g: g.memset(rqa[:, :], 1.0), writes=[rqa[:, :]])
    p.op("vector", lambda v: v.tensor_copy(out=qbf, in_=ctx.pA[0:64, :]),
         reads=[ctx.pA[0:64, :]], writes=[qbf])
    p.op("vector", lambda v: v.tensor_scalar(out=q8a[0:64, :], in0=ctx.pA[0:64, :],
         scalar1=0.125, scalar2=None, op0=OP.mult, op1=OP.bypass),
         reads=[ctx.pA[0:64, :]], writes=[q8a[0:64, :]])
    p.op("vector", lambda v: v.tensor_copy(out=k2a[0:64, :], in_=keysf),
         reads=[keysf], writes=[k2a[0:64, :]])
    for g in range(2):
        o = ctx.pB[0:48, g * 512:(g + 1) * 512]
        p.mm(lambda t, o=o, r=qbf[:, g * 512:(g + 1) * 512]: t.matmul(o, wrt, r,
             start=True, stop=True), reads=[wrt, qbf[:, g * 512:(g + 1) * 512]], writes=[o])
    p.op("vector", lambda v: v.tensor_scalar(out=rqa[0:48, :], in0=ctx.pB[0:48, :],
         scalar1=SR48, scalar2=None, op0=OP.mult, op1=OP.bypass),
         reads=[ctx.pB[0:48, :]], writes=[rqa[0:48, :]])
    for g in range(2):
        o = ctx.pB[0:48, g * 512:(g + 1) * 512]
        p.mm(lambda t, o=o, r=k2a[0:64, g * 512:(g + 1) * 512]: t.matmul(o, wrt, r,
             start=True, stop=True), reads=[wrt, k2a[0:64, g * 512:(g + 1) * 512]], writes=[o])
    p.op("vector", lambda v: v.tensor_copy(out=rka[0:48, :], in_=ctx.pB[0:48, :]),
         reads=[ctx.pB[0:48, :]], writes=[rka[0:48, :]])
    for sc in range(8):
        p.dma("gpsimd", lnx[:, sc * 4096:(sc + 1) * 4096],
              w["values"][sc * 128:(sc + 1) * 128, :])
    w_all = ctx.B[:, 0:8192]
    expS = ctx.B[:, 8192:9216]
    m8 = ctx.S[:, 0:8]
    mx = ctx.S[:, 72:74]
    sums = ctx.S[:, 80:88]
    rs = ctx.S[:, 88:96]
    Mscr = ctx.A[:, 1024:2048]
    Mt = ctx.A[:, 2048:3072]
    for tt in range(8):
        for g in range(2):
            o = ctx.pC[:, g * 512:(g + 1) * 512]
            p.mm(lambda t, o=o, l=rqa[:, tt * 128:(tt + 1) * 128],
                 r=rka[:, g * 512:(g + 1) * 512]: t.matmul(o, l, r, start=True, stop=True),
                 reads=[rqa[:, tt * 128:(tt + 1) * 128], rka[:, g * 512:(g + 1) * 512]],
                 writes=[o])
        p.op("vector", lambda v: v.max(out=m8, in_=ctx.pC[:, :]), reads=[ctx.pC[:, :]],
             writes=[m8])
        p.op("vector", lambda v: v.match_replace(out=Mscr, in_to_replace=m8,
             in_values=ctx.pC[:, :], imm_value=-1e30), reads=[m8, ctx.pC[:, :]], writes=[Mscr])
        p.op("vector", lambda v: v.tensor_tensor(out=Mt, in0=ctx.pC[:, :], in1=Mscr,
             op=OP.subtract), reads=[ctx.pC[:, :], Mscr], writes=[Mt])
        p.op("vector", lambda v: v.tensor_scalar(out=Mt, in0=Mt, scalar1=1.0, scalar2=None,
             op0=OP.min, op1=OP.bypass), reads=[Mt], writes=[Mt])
        for g in range(2):
            o = ctx.pB[:, g * 512:(g + 1) * 512]
            p.mm(lambda t, o=o, l=q8a[:, tt * 128:(tt + 1) * 128],
                 r=k2a[:, g * 512:(g + 1) * 512]: t.matmul(o, l, r, start=True, stop=True),
                 reads=[q8a[:, tt * 128:(tt + 1) * 128], k2a[:, g * 512:(g + 1) * 512]],
                 writes=[o])
        p.op("vector", lambda v: v.reduce_max(out=mx[:, 0:1], in_=ctx.pB[:, :],
             axis=mybir.AxisListType.X), reads=[ctx.pB[:, :]], writes=[mx[:, 0:1]])
        p.op("vector", lambda v: v.tensor_scalar(out=mx[:, 1:2], in0=mx[:, 0:1], scalar1=-1.0,
             scalar2=None, op0=OP.mult, op1=OP.bypass), reads=[mx[:, 0:1]], writes=[mx[:, 1:2]])
        p.op("scalar", lambda s: s.activation(out=expS, in_=ctx.pB[:, :], func=AF.Exp,
             bias=mx[:, 1:2], scale=1.0), reads=[ctx.pB[:, :], mx[:, 1:2]], writes=[expS])
        p.op("vector", lambda v, tt=tt: v.scalar_tensor_tensor(
             out=w_all[:, tt * 1024:(tt + 1) * 1024], in0=expS, scalar=1.0, in1=Mt,
             op0=OP.mult, op1=OP.mult, accum_out=sums[:, tt:tt + 1]),
             reads=[expS, Mt], writes=[w_all[:, tt * 1024:(tt + 1) * 1024], sums[:, tt:tt + 1]])
    p.op("vector", lambda v: v.reciprocal(out=rs, in_=sums), reads=[sums], writes=[rs])
    for tt in range(8):
        p.op("vector", lambda v, tt=tt: v.tensor_scalar(
             out=w_all[:, tt * 1024:(tt + 1) * 1024], in0=w_all[:, tt * 1024:(tt + 1) * 1024],
             scalar1=rs[:, tt:tt + 1], scalar2=None, op0=OP.mult, op1=OP.bypass),
             reads=[w_all[:, tt * 1024:(tt + 1) * 1024], rs[:, tt:tt + 1]],
             writes=[w_all[:, tt * 1024:(tt + 1) * 1024]])
    p.cut()
    wT = ctx.B[:, 8192:16384]
    for tt in range(8):
        for sc in range(8):
            src = w_all[:, tt * 1024 + sc * 128:tt * 1024 + (sc + 1) * 128]
            p.mm(lambda t, s_=src, o=ctx.pE[:, (sc % 8) * 128:(sc % 8 + 1) * 128]:
                 t.transpose(o, s_, ctx.iden_bf[:, :]),
                 reads=[src, ctx.iden_bf[:, :]],
                 writes=[ctx.pE[:, (sc % 8) * 128:(sc % 8 + 1) * 128]])
        for sc in range(8):
            p.op("vector", lambda v, tt=tt, sc=sc: v.tensor_copy(
                 out=wT[:, sc * 1024 + tt * 128:sc * 1024 + (tt + 1) * 128],
                 in_=ctx.pE[:, sc * 128:(sc + 1) * 128]),
                 reads=[ctx.pE[:, sc * 128:(sc + 1) * 128]],
                 writes=[wT[:, sc * 1024 + tt * 128:sc * 1024 + (tt + 1) * 128]])
    p.cut()
    wvds = ctx.C[:, 0:8192]
    aux1c = ctx.C[:, 8192:9216]
    p.dma("gpsimd", wvds[:, 0:4096], w["wvd_t"][0])
    p.dma("gpsimd", wvds[:, 4096:8192], w["wvd_t"][1])
    for m in range(32):
        for g in range(2):
            o = ctx.pC[:, g * 512:(g + 1) * 512]
            for sc in range(8):
                l = lnx[:, sc * 4096 + m * 128:sc * 4096 + (m + 1) * 128]
                r = wT[:, sc * 1024 + g * 512:sc * 1024 + (g + 1) * 512]
                p.mm(lambda t, o=o, l=l, r=r, st=(sc == 0), sp=(sc == 7):
                     t.matmul(o, l, r, start=st, stop=sp), reads=[l, r], writes=[o])
        p.op("vector", lambda v: v.tensor_copy(out=aux1c, in_=ctx.pC[:, :]),
             reads=[ctx.pC[:, :]], writes=[aux1c])
        for mv in range(2):
            pp = ctx.pA if mv == 0 else ctx.pB
            for g in range(2):
                o = pp[:, g * 512:(g + 1) * 512]
                l = wvds[:, mv * 4096 + m * 128:mv * 4096 + (m + 1) * 128]
                p.mm(lambda t, o=o, l=l, r=aux1c[:, g * 512:(g + 1) * 512], st=(m == 0),
                     sp=(m == 31): t.matmul(o, l, r, start=st, stop=sp),
                     reads=[l, aux1c[:, g * 512:(g + 1) * 512]], writes=[o])
        if m % 8 == 7:
            p.cut()
    aux2 = ctx.C[:, 8192:10240]
    p.op("vector", lambda v: v.tensor_copy(out=aux2[:, 0:1024], in_=ctx.pA[:, :]),
         reads=[ctx.pA[:, :]], writes=[aux2[:, 0:1024]])
    p.op("vector", lambda v: v.tensor_copy(out=aux2[:, 1024:2048], in_=ctx.pB[:, :]),
         reads=[ctx.pB[:, :]], writes=[aux2[:, 1024:2048]])
    gq_bc = ctx.A[:, 2048:3072]
    for g in range(2):
        p.mm(lambda t, g=g: t.matmul(ctx.pD[:, 0:512], ctx.ones_row32[:, :],
             gfin[0:1, g * 512:(g + 1) * 512], start=True, stop=True),
             reads=[gfin[0:1, g * 512:(g + 1) * 512], ctx.ones_row32[:, :]],
             writes=[ctx.pD[:, 0:512]])
        p.op("vector", lambda v, g=g: v.tensor_copy(out=gq_bc[:, g * 512:(g + 1) * 512],
             in_=ctx.pD[:, 0:512]), reads=[ctx.pD[:, 0:512]],
             writes=[gq_bc[:, g * 512:(g + 1) * 512]])
    p.cut()
    wvus = ctx.C[:, 0:8192]
    xc3 = ctx.A[:, 0:1024]
    xn3 = ctx.A[:, 1024:2048]
    for m in range(32):
        ws = wvus[:, (m % 2) * 4096:(m % 2) * 4096 + 256]
        p.dma("gpsimd", ws, w["wvu_t"][m])
        for g in range(2):
            idx = m * 2 + g
            o = ctx.pC[:, 0:512] if idx % 2 == 0 else ctx.pD[:, 0:512]
            xsl = xc3[:, (idx % 2) * 512:(idx % 2 + 1) * 512]
            nsl = xn3[:, (idx % 2) * 512:(idx % 2 + 1) * 512]
            xdr = xd[m * 128:(m + 1) * 128, g * 512:(g + 1) * 512]
            p.dma("sync", xsl, xdr)
            for k in range(2):
                l = ws[:, k * 128:(k + 1) * 128]
                r = aux2[:, k * 1024 + g * 512:k * 1024 + (g + 1) * 512]
                p.mm(lambda t, o=o, l=l, r=r, st=(k == 0), sp=(k == 1):
                     t.matmul(o, l, r, start=st, stop=sp), reads=[l, r], writes=[o])
            p.op("vector", lambda v, o=nsl, a=o, b=gq_bc[:, g * 512:(g + 1) * 512]:
                 v.tensor_tensor(out=o, in0=a, in1=b, op=OP.mult),
                 reads=[o, gq_bc[:, g * 512:(g + 1) * 512]], writes=[nsl])
            p.op("vector", lambda v, o=nsl, b=xsl: v.tensor_tensor(out=o, in0=o, in1=b,
                 op=OP.add), reads=[nsl, xsl], writes=[nsl])
            p.dma("sync", xdr, nsl)
        if m % 8 == 7:
            p.cut()
    p.cut()


def emit_l1attn(ctx, xd, w):
    p = ctx.p
    lnx = ctx.LNX
    NEG = -1e30
    wstr = ctx.B[:, 0:12288]
    evc = ctx.C[:, 0:2048]
    halo_sb = ctx.C[:, 2048:2528]
    dtraw = ctx.C[:, 2528:3552]
    for m in range(145):
        ws = wstr[:, (m % 3) * 4096:(m % 3 + 1) * 4096]
        p.dma("gpsimd", ws, w["inproj_t"][m])
        for g in range(2):
            o = ctx.pA[:, 0:512] if (m * 2 + g) % 2 == 0 else ctx.pB[:, 0:512]
            for k in range(32):
                rhs = lnx[:, k * 1024 + g * 512:k * 1024 + (g + 1) * 512]
                p.mm(lambda t, o=o, l=ws[:, k * 128:(k + 1) * 128], r=rhs, st=(k == 0),
                     sp=(k == 31): t.matmul(o, l, r, start=st, stop=sp),
                     reads=[ws[:, k * 128:(k + 1) * 128], rhs], writes=[o])
            ev = evc[:, ((m * 2 + g) % 2) * 1024:((m * 2 + g) % 2) * 1024 + 512]
            if m == 144:
                ev = dtraw[:, g * 512:(g + 1) * 512]
            p.op("vector", lambda v, o=ev, a=o: v.tensor_copy(out=o, in_=a),
                 reads=[o], writes=[ev])
            if m < 144:
                if m < 64:
                    dst = w["gate_d"][m * 128:(m + 1) * 128, g * 512:(g + 1) * 512]
                else:
                    dst = w["xbc_d"][(m - 64) * 128:(m - 63) * 128, g * 512:(g + 1) * 512]
                p.dma("sync", dst, ev)
                if m >= 64:
                    mc = m - 64
                    hs = halo_sb[:, mc * 6 + (g * 3):mc * 6 + (g * 3) + 3]
                    p.op("vector", lambda e, o=hs, a=ev: e.tensor_copy(out=o, in_=a[:, 509:512]),
                         reads=[ev], writes=[hs])
        if m % 8 == 7:
            p.cut()
    p.dma("sync", w["halo_snd"], halo_sb)
    p.cut()
    grp = [[0, 1, 2, 3], [4, 5, 6, 7]]
    p.collective("AllGather", OP.bypass, grp, [w["halo_snd"]], [w["halo_all"]])
    dt_f = ctx.A[:, 0:1024]
    dA_f = ctx.A[:, 1024:2048]
    cum = ctx.A[:, 2048:3072]
    sc1 = ctx.A[:, 3072:4096]
    sc2 = ctx.A[:, 4096:5120]
    dtb = ctx.S[:, 0:1]
    alog = ctx.S[:, 1:2]
    acol = ctx.S[:, 2:3]
    p.dma("sync", dtb, w["dtb"])
    p.dma("sync", alog, w["alog"])
    p.op("vector", lambda v: v.tensor_scalar(out=sc1, in0=dtraw, scalar1=dtb, scalar2=None,
         op0=OP.add, op1=OP.bypass), reads=[dtraw, dtb], writes=[sc1])
    p.op("scalar", lambda s: s.activation(out=sc2, in_=sc1, func=AF.Abs),
         reads=[sc1], writes=[sc2])
    p.op("scalar", lambda s: s.activation(out=sc2, in_=sc2, func=AF.Exp, scale=-1.0),
         reads=[sc2], writes=[sc2])
    p.op("scalar", lambda s: s.activation(out=sc2, in_=sc2, func=AF.Ln,
         bias=ctx.cst[:, 1:2], scale=1.0), reads=[sc2, ctx.cst[:, 1:2]], writes=[sc2])
    p.op("vector", lambda v: v.tensor_scalar(out=sc1, in0=sc1, scalar1=0.0, scalar2=None,
         op0=OP.max, op1=OP.bypass), reads=[sc1], writes=[sc1])
    p.op("vector", lambda v: v.tensor_tensor(out=dt_f, in0=sc1, in1=sc2, op=OP.add),
         reads=[sc1, sc2], writes=[dt_f])
    p.op("scalar", lambda s: s.activation(out=acol, in_=alog, func=AF.Exp), reads=[alog],
         writes=[acol])
    p.op("vector", lambda v: v.tensor_scalar(out=acol, in0=acol, scalar1=-1.0, scalar2=None,
         op0=OP.mult, op1=OP.bypass), reads=[acol], writes=[acol])
    p.op("vector", lambda v: v.tensor_scalar(out=dA_f, in0=dt_f, scalar1=acol, scalar2=None,
         op0=OP.mult, op1=OP.bypass), reads=[dt_f, acol], writes=[dA_f])
    for c in range(8):
        p.op("vector", lambda v, c=c: v.tensor_tensor_scan(
             out=cum[:, c * 128:(c + 1) * 128], data0=dA_f[:, c * 128:(c + 1) * 128],
             data1=dA_f[:, c * 128:(c + 1) * 128], initial=0.0, op0=OP.add, op1=OP.bypass),
             reads=[dA_f[:, c * 128:(c + 1) * 128]], writes=[cum[:, c * 128:(c + 1) * 128]])
    p.dma("sync", w["cum_d"], cum)
    cum_tm = ctx.A[:, 5120:6144]
    dt_tm = ctx.A[:, 6144:7168]
    for c in range(8):
        p.mm(lambda t, c=c: t.transpose(ctx.pD[:, 0:128], cum[:, c * 128:(c + 1) * 128],
             ctx.iden32[:, :]), reads=[cum[:, c * 128:(c + 1) * 128], ctx.iden32[:, :]],
             writes=[ctx.pD[:, 0:512]])
        p.op("vector", lambda v, c=c: v.tensor_copy(out=cum_tm[:, c * 128:(c + 1) * 128],
             in_=ctx.pD[:, 0:128]), reads=[ctx.pD[:, 0:512]],
             writes=[cum_tm[:, c * 128:(c + 1) * 128]])
        p.mm(lambda t, c=c: t.transpose(ctx.pD[:, 0:128], dt_f[:, c * 128:(c + 1) * 128],
             ctx.iden32[:, :]), reads=[dt_f[:, c * 128:(c + 1) * 128], ctx.iden32[:, :]],
             writes=[ctx.pD[:, 0:512]])
        p.op("vector", lambda v, c=c: v.tensor_copy(out=dt_tm[:, c * 128:(c + 1) * 128],
             in_=ctx.pD[:, 0:128]), reads=[ctx.pD[:, 0:512]],
             writes=[dt_tm[:, c * 128:(c + 1) * 128]])
    coef_tm = ctx.A[:, 7168:8192]
    coefs = ctx.S[:, 16:1040]
    for c in range(8):
        cl = ctx.S[:, 4 + c:5 + c]
        p.op("vector", lambda v, c=c, cl=cl: v.tensor_copy(out=cl,
             in_=cum[:, c * 128 + 127:c * 128 + 128]),
             reads=[cum[:, c * 128 + 127:c * 128 + 128]], writes=[cl])
        p.op("scalar", lambda s, c=c, cl=cl: s.activation(out=coefs[:, c * 128:(c + 1) * 128],
             in_=cum[:, c * 128:(c + 1) * 128], func=AF.Exp, bias=cl, scale=-1.0),
             reads=[cum[:, c * 128:(c + 1) * 128], cl], writes=[coefs[:, c * 128:(c + 1) * 128]])
    p.op("vector", lambda v: v.tensor_tensor(out=coefs, in0=dt_f, in1=coefs, op=OP.mult),
         reads=[dt_f, coefs], writes=[coefs])
    for c in range(8):
        p.mm(lambda t, c=c: t.transpose(ctx.pD[:, 0:128], coefs[:, c * 128:(c + 1) * 128],
             ctx.iden32[:, :]), reads=[coefs[:, c * 128:(c + 1) * 128], ctx.iden32[:, :]],
             writes=[ctx.pD[:, 0:512]])
        p.op("vector", lambda v, c=c: v.tensor_copy(out=coef_tm[:, c * 128:(c + 1) * 128],
             in_=ctx.pD[:, 0:128]), reads=[ctx.pD[:, 0:512]],
             writes=[coef_tm[:, c * 128:(c + 1) * 128]])
    p.cut()
    ext = ctx.C[:, 3552:4582]
    cdiag = ctx.C[:, 4608:5632]
    cb = ctx.S[:, 1040:1120]
    hsel = ctx.S[:, 1120:1136]
    p.dma("sync", cb, w["conv_b"])
    p.dma("sync", hsel, w["hsel"])
    for mc in range(80):
        cds = cdiag[:, (mc % 2) * 512:(mc % 2 + 1) * 512]
        p.dma("gpsimd", cds, w["convd_t"][mc])
        p.dma("sync", ext[:, 3:515], w["xbc_d"][mc * 128:(mc + 1) * 128, 0:512])
        p.dma("sync", ext[:, 518:1030], w["xbc_d"][mc * 128:(mc + 1) * 128, 512:1024])
        p.op("gpsimd", lambda e: e.memset(ext[:, 0:3], 0.0), writes=[ext[:, 0:3]])
        p.op("gpsimd", lambda e: e.memset(ext[:, 515:518], 0.0), writes=[ext[:, 515:518]])
        for cand in range(8):
            r, sl = (cand // 2), (cand % 2)
            hsrc = ctx.B[:, cand * 3:cand * 3 + 3]
            p.dma("sync", hsrc, w["halo_all"][r * 128:(r + 1) * 128,
                                              mc * 6 + sl * 3:mc * 6 + sl * 3 + 3])
            for dsti, dcol in ((0, 0), (1, 515)):
                p.op("vector", lambda v, h=hsrc, d=ext[:, dcol:dcol + 3],
                     sc=hsel[:, dsti * 8 + cand:dsti * 8 + cand + 1]:
                     v.scalar_tensor_tensor(out=d, in0=h, scalar=sc, in1=d,
                     op0=OP.mult, op1=OP.add),
                     reads=[hsrc, hsel[:, dsti * 8 + cand:dsti * 8 + cand + 1],
                            ext[:, dcol:dcol + 3]],
                     writes=[ext[:, dcol:dcol + 3]])
        for st_ in range(2):
            o = ctx.pC[:, st_ * 512:(st_ + 1) * 512]
            for j in range(4):
                r_ = ext[:, st_ * 515 + j:st_ * 515 + j + 512]
                p.mm(lambda t, o=o, l=cds[:, j * 128:(j + 1) * 128], rr=r_, stt=(j == 0),
                     sp=(j == 3): t.matmul(o, l, rr, start=stt, stop=sp),
                     reads=[cds[:, j * 128:(j + 1) * 128], r_], writes=[o])
        po = ctx.C[:, 5632:6656]
        p.op("scalar", lambda s, mc=mc: s.activation(out=po, in_=ctx.pC[:, :], func=AF.Silu,
             bias=cb[:, mc:mc + 1], scale=1.0), reads=[ctx.pC[:, :], cb[:, mc:mc + 1]],
             writes=[po])
        if mc < 64:
            p.dma("sync", w["xss_d"][mc * 128:(mc + 1) * 128, :], po)
        elif mc < 72:
            g_ = mc - 64
            p.dma("sync", w["bfc_d"][:, g_ * 1024:(g_ + 1) * 1024], po)
        else:
            g_ = mc - 72
            p.dma("sync", w["cfc_d"][:, g_ * 1024:(g_ + 1) * 1024], po)
        if mc % 8 == 7:
            p.cut()
    p.cut()

def emit_l1attn_state(ctx, xd, w):
    p = ctx.p
    dt_tm = ctx.A[:, 6144:7168]
    coef_tm = ctx.A[:, 7168:8192]
    prev = ctx.B[:, 0:8192]
    bct = ctx.B[:, 8192:9216]
    btm = ctx.B[:, 9216:10240]
    x_tm = ctx.C[:, 0:8192]
    cdecb = ctx.LNX[:, 0:8192]
    b4 = ctx.LNX[:, 8192:8704]
    stsb = ctx.LNX[:, 8704:9216]
    t1s = ctx.LNX[:, 9216:9728]
    cde = ctx.R[0:1, 0:8192]
    p.op("gpsimd", lambda g: g.memset(prev[:, :], 0.0), writes=[prev[:, :]])
    grp = [[0, 1, 2, 3], [4, 5, 6, 7]]
    for gc in range(8):
        ss, c = gc // 4, gc % 4
        for kk in range(8):
            p.dma("sync", x_tm[:, kk * 1024:(kk + 1) * 1024].rearrange("p (k t) -> p k t", t=128),
                  w["xss_d"][kk * 1024:(kk + 1) * 1024, gc * 128:(gc + 1) * 128]
                  .rearrange("(k p) t -> p k t", p=128))
        for k8 in range(8):
            for k in range(k8 * 8, k8 * 8 + 8):
                src = x_tm[:, k * 128:(k + 1) * 128]
                p.mm(lambda t, s_=src, k=k: t.transpose(
                     ctx.pE[:, (k % 8) * 128:(k % 8 + 1) * 128], s_, ctx.iden_bf[:, :]),
                     reads=[src, ctx.iden_bf[:, :]], writes=[ctx.pE[:, 0:1024]])
            p.op("vector", lambda v, k8=k8: v.tensor_copy(
                 out=x_tm[:, k8 * 1024:(k8 + 1) * 1024], in_=ctx.pE[:, :]),
                 reads=[ctx.pE[:, 0:1024]], writes=[x_tm[:, k8 * 1024:(k8 + 1) * 1024]])
        for g in range(8):
            p.dma("sync", bct[:, g * 128:(g + 1) * 128],
                  w["bfc_d"][:, g * 1024 + gc * 128:g * 1024 + (gc + 1) * 128])
            src = bct[:, g * 128:(g + 1) * 128]
            p.mm(lambda t, s_=src: t.transpose(ctx.pE[:, 0:128], s_, ctx.iden_bf[:, :]),
                 reads=[src, ctx.iden_bf[:, :]], writes=[ctx.pE[:, 0:1024]])
            p.op("vector", lambda v, g=g: v.tensor_copy(out=btm[:, g * 128:(g + 1) * 128],
                 in_=ctx.pE[:, 0:128]), reads=[ctx.pE[:, 0:1024]],
                 writes=[btm[:, g * 128:(g + 1) * 128]])
        cl = ctx.S[:, 4 + gc:5 + gc]
        p.mm(lambda t: t.transpose(ctx.pD[0:1, 0:128], cl, ctx.iden32[:, :]),
             reads=[cl, ctx.iden32[:, :]], writes=[ctx.pD[0:1, 0:512]])
        p.op("scalar", lambda s: s.activation(out=ctx.S[0:1, 1136:1264], in_=ctx.pD[0:1, 0:128],
             func=AF.Exp), reads=[ctx.pD[0:1, 0:512]], writes=[ctx.S[0:1, 1136:1264]])
        p.op("vector", lambda v: v.tensor_copy(
             out=vap(ctx.R, 0, [[8192, 1], [64, 128], [1, 64]]),
             in_=vap(ctx.S, 1136, [[2048, 1], [1, 128], [0, 64]])),
             reads=[ctx.S[0:1, 1136:1264]], writes=[cde])
        for bg in range(16):
            p.mm(lambda t, r_=cde[0:1, bg * 512:(bg + 1) * 512]: t.matmul(
                 ctx.pD[:, 0:512], ctx.ones_row32[:, :], r_, start=True, stop=True),
                 reads=[cde[0:1, bg * 512:(bg + 1) * 512], ctx.ones_row32[:, :]],
                 writes=[ctx.pD[:, 0:512]])
            p.op("vector", lambda v, bg=bg: v.tensor_copy(
                 out=cdecb[:, bg * 512:(bg + 1) * 512], in_=ctx.pD[:, 0:512]),
                 reads=[ctx.pD[:, 0:512]], writes=[cdecb[:, bg * 512:(bg + 1) * 512]])
        for g in range(8):
            for q in range(4):
                h0 = g * 16 + q * 4
                bsrc = vap(ctx.B, 9216 + g * 128, [[16384, 128], [0, 4], [1, 128]])
                cview = vap(ctx.A, 7168 + gc * 128 + h0, [[8192, 128], [1, 4], [0, 128]])
                b4o = vap(ctx.LNX, 8192, [[32768, 128], [128, 4], [1, 128]])
                p.op("vector", lambda v, bs=bsrc, cv=cview, bo=b4o: v.tensor_tensor(
                     out=bo, in0=bs, in1=cv, op=OP.mult),
                     reads=[btm[:, g * 128:(g + 1) * 128],
                            coef_tm[:, gc * 128 + h0:gc * 128 + h0 + 4]], writes=[b4])
                for hq in range(4):
                    h = h0 + hq
                    bank = ctx.pA if (h % 16) < 8 else ctx.pB
                    o = bank[:, (h % 8) * 64:(h % 8) * 64 + 64]
                    p.mm(lambda t, o=o, l=b4[:, hq * 128:(hq + 1) * 128],
                         r=x_tm[:, h * 64:(h + 1) * 64]: t.matmul(o, l, r, start=True,
                         stop=True), reads=[b4[:, hq * 128:(hq + 1) * 128],
                         x_tm[:, h * 64:(h + 1) * 64]], writes=[o])
            for half in range(2):
                bank = ctx.pA if half == 0 else ctx.pB
                lo = g * 1024 + half * 512
                p.op("vector", lambda v, b_=bank: v.tensor_copy(out=stsb, in_=b_[:, 0:512]),
                     reads=[bank[:, 0:512]], writes=[stsb])
                p.dma("sync", w["std_d"][gc * 128:(gc + 1) * 128, lo:lo + 512], stsb)
                p.op("vector", lambda v, lo=lo: v.tensor_tensor(out=t1s,
                     in0=prev[:, lo:lo + 512], in1=cdecb[:, lo:lo + 512], op=OP.mult),
                     reads=[prev[:, lo:lo + 512], cdecb[:, lo:lo + 512]], writes=[t1s])
                p.op("vector", lambda v, lo=lo, b_=bank: v.tensor_tensor(
                     out=prev[:, lo:lo + 512], in0=t1s, in1=b_[:, 0:512], op=OP.add),
                     reads=[t1s, bank[:, 0:512]], writes=[prev[:, lo:lo + 512]])
        if c == 3:
            p.dma("sync", w["st_snd"][ss * 128:(ss + 1) * 128, :], prev)
            dcol = ctx.S[:, 12:13]
            p.op("vector", lambda v, ss=ss: v.tensor_tensor(out=dcol,
                 in0=ctx.S[:, 4 + ss * 4:5 + ss * 4], in1=ctx.S[:, 5 + ss * 4:6 + ss * 4],
                 op=OP.add), reads=[ctx.S[:, 4 + ss * 4:6 + ss * 4]], writes=[dcol])
            for i_ in (2, 3):
                p.op("vector", lambda v, ss=ss, i_=i_: v.tensor_tensor(out=dcol, in0=dcol,
                     in1=ctx.S[:, 4 + ss * 4 + i_:5 + ss * 4 + i_], op=OP.add),
                     reads=[dcol, ctx.S[:, 4 + ss * 4 + i_:5 + ss * 4 + i_]], writes=[dcol])
            p.mm(lambda t: t.transpose(ctx.pD[0:1, 0:128], dcol, ctx.iden32[:, :]),
                 reads=[dcol, ctx.iden32[:, :]], writes=[ctx.pD[0:1, 0:512]])
            p.op("scalar", lambda s: s.activation(out=ctx.S[0:1, 1264:1392],
                 in_=ctx.pD[0:1, 0:128], func=AF.Exp), reads=[ctx.pD[0:1, 0:512]],
                 writes=[ctx.S[0:1, 1264:1392]])
            p.dma("sync", w["dp_snd"][ss:ss + 1, :], ctx.S[0:1, 1264:1392])
            p.op("gpsimd", lambda g: g.memset(prev[:, :], 0.0), writes=[prev[:, :]])
        p.cut()
    p.collective("AllGather", OP.bypass, grp, [w["st_snd"]], [w["st_all"]])
    p.collective("AllGather", OP.bypass, grp, [w["dp_snd"]], [w["dp_all"]])
    run = ctx.LNX[:, 0:8192]
    accA = ctx.LNX[:, 8192:16384]
    accB = ctx.LNX[:, 16384:24576]
    sout = ctx.B[:, 0:8192]
    selc = ctx.S[:, 1392:1408]
    dpr = ctx.R[0:1, 0:8192]
    p.dma("sync", selc, w["selc"])
    p.op("gpsimd", lambda g: g.memset(run[:, :1] if False else run, 0.0), writes=[run])
    p.op("gpsimd", lambda g: g.memset(accA, 0.0), writes=[accA])
    p.op("gpsimd", lambda g: g.memset(accB, 0.0), writes=[accB])
    for s_ in range(8):
        r = min(s_, 7 - s_)
        sl = 0 if s_ < 4 else 1
        p.op("vector", lambda v, s_=s_: v.scalar_tensor_tensor(out=accA, in0=run,
             scalar=selc[:, s_:s_ + 1], in1=accA, op0=OP.mult, op1=OP.add),
             reads=[run, selc[:, s_:s_ + 1], accA], writes=[accA])
        p.op("vector", lambda v, s_=s_: v.scalar_tensor_tensor(out=accB, in0=run,
             scalar=selc[:, 8 + s_:9 + s_], in1=accB, op0=OP.mult, op1=OP.add),
             reads=[run, selc[:, 8 + s_:9 + s_], accB], writes=[accB])
        p.dma("sync", sout, w["st_all"][r * 256 + sl * 128:r * 256 + (sl + 1) * 128, :])
        p.dma("sync", ctx.S[0:1, 1264:1392], w["dp_all"][r * 2 + sl:r * 2 + sl + 1, :])
        p.op("vector", lambda v: v.tensor_copy(
             out=vap(ctx.R, 0, [[8192, 1], [64, 128], [1, 64]]),
             in_=vap(ctx.S, 1264, [[2048, 1], [1, 128], [0, 64]])),
             reads=[ctx.S[0:1, 1264:1392]], writes=[dpr])
        for bg in range(16):
            p.mm(lambda t, r_=dpr[0:1, bg * 512:(bg + 1) * 512]: t.matmul(ctx.pD[:, 0:512],
                 ctx.ones_row32[:, :], r_, start=True, stop=True),
                 reads=[dpr[0:1, bg * 512:(bg + 1) * 512], ctx.ones_row32[:, :]],
                 writes=[ctx.pD[:, 0:512]])
            p.op("vector", lambda v, bg=bg: v.tensor_tensor(
                 out=run[:, bg * 512:(bg + 1) * 512], in0=run[:, bg * 512:(bg + 1) * 512],
                 in1=ctx.pD[:, 0:512], op=OP.mult),
                 reads=[run[:, bg * 512:(bg + 1) * 512], ctx.pD[:, 0:512]],
                 writes=[run[:, bg * 512:(bg + 1) * 512]])
        p.op("vector", lambda v: v.tensor_tensor(out=run, in0=run, in1=sout, op=OP.add),
             reads=[run, sout], writes=[run])
        p.cut()
    p.dma("sync", w["sin_d"][0:128, :], accA)
    p.dma("sync", w["sin_d"][128:256, :], accB)
    p.cut()

def emit_l1attn_y(ctx, xd, w):
    p = ctx.p
    dt_tm = ctx.A[:, 6144:7168]
    x_tm = ctx.C[:, 0:8192]
    stc = ctx.C[:, 8192:16384]
    prevg = ctx.B[:, 8192:16384]
    bct = ctx.B[:, 0:1024]
    cct = ctx.B[:, 1024:2048]
    s0sb = ctx.B[:, 3584:3712]
    d4 = ctx.B[:, 2048:2560]
    s1q = ctx.B[:, 2560:3072]
    c4 = ctx.B[:, 3072:3584]
    x8 = ctx.B[:, 4096:5120]
    g8 = ctx.B[:, 5120:6144]
    sg8 = ctx.B[:, 6144:7168]
    t1s = ctx.B[:, 7168:8192]
    negones = ctx.S[0:1, 1408:1536]
    trimask_sb = ctx.S[:, 16:528]
    dexp_sb = ctx.S[:, 1536:1600]
    nw_sb = ctx.S[:, 1600:1664]
    yn = ctx.LNX
    cde = ctx.R[0:1, 0:8192]
    p.op("gpsimd", lambda g: g.memset(negones, -1.0), writes=[negones])
    p.dma("sync", trimask_sb, w["trimask4"])
    p.dma("sync", dexp_sb, w["d_exp"])
    p.dma("sync", nw_sb, w["nw_t"])
    for ss in range(2):
        toff = ss * 512
        p.dma("sync", prevg, w["sin_d"][ss * 128:(ss + 1) * 128, :])
        for c in range(4):
            gc = ss * 4 + c
            for kk in range(8):
                p.dma("sync", x_tm[:, kk * 1024:(kk + 1) * 1024]
                      .rearrange("p (k t) -> p k t", t=128),
                      w["xss_d"][kk * 1024:(kk + 1) * 1024, gc * 128:(gc + 1) * 128]
                      .rearrange("(k p) t -> p k t", p=128))
            for k8 in range(8):
                for k in range(k8 * 8, k8 * 8 + 8):
                    src = x_tm[:, k * 128:(k + 1) * 128]
                    p.mm(lambda t, s_=src, k=k: t.transpose(
                         ctx.pE[:, (k % 8) * 128:(k % 8 + 1) * 128], s_, ctx.iden_bf[:, :]),
                         reads=[src, ctx.iden_bf[:, :]], writes=[ctx.pE[:, 0:1024]])
                p.op("vector", lambda v, k8=k8: v.tensor_copy(
                     out=x_tm[:, k8 * 1024:(k8 + 1) * 1024], in_=ctx.pE[:, :]),
                     reads=[ctx.pE[:, 0:1024]], writes=[x_tm[:, k8 * 1024:(k8 + 1) * 1024]])
            for g in range(8):
                p.dma("sync", bct[:, g * 128:(g + 1) * 128],
                      w["bfc_d"][:, g * 1024 + gc * 128:g * 1024 + (gc + 1) * 128])
                p.dma("sync", cct[:, g * 128:(g + 1) * 128],
                      w["cfc_d"][:, g * 1024 + gc * 128:g * 1024 + (gc + 1) * 128])
            for hh in range(4):
                p.dma("sync", vap(ctx.R, 0, [[8192, 1], [128, 32], [1, 128]]),
                      w["cum_d"][32 * hh:32 * hh + 32, gc * 128:(gc + 1) * 128]
                      .rearrange("(o h) t -> o h t", o=1))
                p.op("vector", lambda v: v.tensor_scalar(out=cde[0:1, 0:4096],
                     in0=cde[0:1, 0:4096], scalar1=-1.0, scalar2=None, op0=OP.mult,
                     op1=OP.bypass), reads=[cde[0:1, 0:4096]], writes=[cde[0:1, 0:4096]])
                p.op("scalar", lambda s: s.activation(out=cde[0:1, 4096:8192],
                     in_=cde[0:1, 0:4096], func=AF.Exp, scale=-1.0),
                     reads=[cde[0:1, 0:4096]], writes=[cde[0:1, 4096:8192]])
                for gg in range(2):
                    g = hh * 2 + gg
                    p.mm(lambda t, l=bct[:, g * 128:(g + 1) * 128],
                         r=cct[:, g * 128:(g + 1) * 128]: t.matmul(ctx.pC[:, 0:128], l, r,
                         start=True, stop=True),
                         reads=[bct[:, g * 128:(g + 1) * 128], cct[:, g * 128:(g + 1) * 128]],
                         writes=[ctx.pC[:, 0:512]])
                    p.op("vector", lambda v: v.tensor_copy(out=s0sb, in_=ctx.pC[:, 0:128]),
                         reads=[ctx.pC[:, 0:512]], writes=[s0sb])
                    for q in range(4):
                        h0 = g * 16 + q * 4
                        hrel = h0 - 32 * hh
                        p.mm(lambda t, r_=cde[0:1, hrel * 128:hrel * 128 + 512]:
                             t.matmul(ctx.pB[:, 0:512], negones, r_, start=True, stop=False),
                             reads=[cde[0:1, hrel * 128:hrel * 128 + 512], negones],
                             writes=[ctx.pB[:, 0:512]])
                        for hq in range(4):
                            p.mm(lambda t, l=cde[0:1, (hrel + hq) * 128:(hrel + hq + 1) * 128],
                                 hq=hq: t.matmul(ctx.pB[:, hq * 128:(hq + 1) * 128], l,
                                 ctx.ones_row32[:, :], start=False, stop=False),
                                 reads=[cde[0:1, (hrel + hq) * 128:(hrel + hq + 1) * 128],
                                        ctx.ones_row32[:, :]], writes=[ctx.pB[:, 0:512]])
                        p.mm(lambda t: t.matmul(ctx.pB[:, 0:512], ctx.iden32[:, :],
                             trimask_sb, start=False, stop=True),
                             reads=[ctx.iden32[:, :], trimask_sb],
                             writes=[ctx.pB[:, 0:512]])
                        p.op("scalar", lambda s: s.activation(out=d4, in_=ctx.pB[:, 0:512],
                             func=AF.Exp), reads=[ctx.pB[:, 0:512]], writes=[d4])
                        dt4 = vap(ctx.A, 6144 + gc * 128 + h0, [[8192, 128], [1, 4], [0, 128]])
                        s0b = vap(ctx.B, 3584, [[16384, 128], [0, 4], [1, 128]])
                        d4v = vap(ctx.B, 2048, [[16384, 128], [128, 4], [1, 128]])
                        s1v = vap(ctx.B, 2560, [[16384, 128], [128, 4], [1, 128]])
                        p.op("vector", lambda v, d4v=d4v, dt4=dt4, s1v=s1v: v.tensor_tensor(
                             out=s1v, in0=d4v, in1=dt4, op=OP.mult),
                             reads=[d4, dt_tm[:, gc * 128 + h0:gc * 128 + h0 + 4]],
                             writes=[s1q])
                        p.op("vector", lambda v, s1v=s1v, s0b=s0b: v.tensor_tensor(
                             out=s1v, in0=s1v, in1=s0b, op=OP.mult),
                             reads=[s1q, s0sb], writes=[s1q])
                        p.mm(lambda t, r_=cde[0:1, 4096 + hrel * 128:4096 + hrel * 128 + 512]:
                             t.matmul(ctx.pB[:, 512:1024], ctx.ones_row32[:, :], r_,
                             start=True, stop=True),
                             reads=[cde[0:1, 4096 + hrel * 128:4096 + hrel * 128 + 512],
                                    ctx.ones_row32[:, :]], writes=[ctx.pB[:, 512:1024]])
                        cbv = vap(ctx.B, 1024 + g * 128, [[16384, 128], [0, 4], [1, 128]])
                        c4v = vap(ctx.B, 3072, [[16384, 128], [128, 4], [1, 128]])
                        ebv = vap(ctx.pB, 512, [[1024, 128], [128, 4], [1, 128]])
                        p.op("vector", lambda v, cbv=cbv, c4v=c4v, ebv=ebv: v.tensor_tensor(
                             out=c4v, in0=cbv, in1=ebv, op=OP.mult),
                             reads=[cct[:, g * 128:(g + 1) * 128], ctx.pB[:, 512:1024]],
                             writes=[c4])
                        for hq in range(4):
                            h = h0 + hq
                            k = h // 2
                            ro = (h % 2) * 64
                            co = (k - 8 * g) * 128 if False else (k % 8) * 128
                            o = ctx.pA[ro:ro + 64, co:co + 128]
                            p.mm(lambda t, o=o, l=x_tm[:, h * 64:(h + 1) * 64],
                                 r=s1q[:, hq * 128:(hq + 1) * 128]:
                                 t.matmul(o, l, r, start=True, stop=False),
                                 reads=[x_tm[:, h * 64:(h + 1) * 64],
                                        s1q[:, hq * 128:(hq + 1) * 128]], writes=[ctx.pA[:, co:co + 128]])
                            p.mm(lambda t, o=o, l=prevg[:, h * 64:(h + 1) * 64],
                                 r=c4[:, hq * 128:(hq + 1) * 128]:
                                 t.matmul(o, l, r, start=False, stop=True),
                                 reads=[prevg[:, h * 64:(h + 1) * 64],
                                        c4[:, hq * 128:(hq + 1) * 128]], writes=[ctx.pA[:, co:co + 128]])
                    p.dma("sync", x8[:, :].rearrange("p (k t) -> p k t", t=128),
                          w["xss_d"][g * 1024:(g + 1) * 1024, gc * 128:(gc + 1) * 128]
                          .rearrange("(k p) t -> p k t", p=128))
                    p.dma("sync", g8[:, :].rearrange("p (k t) -> p k t", t=128),
                          w["gate_d"][g * 1024:(g + 1) * 1024, gc * 128:(gc + 1) * 128]
                          .rearrange("(k p) t -> p k t", p=128))
                    p.op("scalar", lambda s: s.activation(out=sg8, in_=g8, func=AF.Silu),
                         reads=[g8], writes=[sg8])
                    dview = vap(ctx.S, 1536 + g * 8, [[2048, 128], [1, 8], [0, 128]])
                    x8v = vap(ctx.B, 4096, [[16384, 128], [128, 8], [1, 128]])
                    t1v = vap(ctx.B, 7168, [[16384, 128], [128, 8], [0, 0]])
                    p.op("gpsimd", lambda e, x8v=x8v, dview=dview: e.tensor_tensor(
                         out=vap(ctx.B, 7168, [[16384, 128], [128, 8], [1, 128]]),
                         in0=x8v, in1=dview, op=OP.mult),
                         reads=[x8, ctx.S[:, 1536 + g * 8:1544 + g * 8]], writes=[t1s])
                    p.op("vector", lambda v: v.tensor_tensor(out=t1s, in0=t1s,
                         in1=ctx.pA[:, :], op=OP.add), reads=[t1s, ctx.pA[:, :]],
                         writes=[t1s])
                    p.op("vector", lambda v, g=g, gc=gc: v.tensor_tensor(
                         out=yn[:, g * 4096 + (gc % 4) * 128:g * 4096 + (gc % 4) * 128 + 1024]
                         .rearrange("p (k t) -> p k t", t=128) if False else
                         vap(ctx.LNX, g * 4096 + (gc % 4) * 128, [[32768, 128], [512, 8], [1, 128]]),
                         in0=t1s[:, :].rearrange("p (k t) -> p k t", t=128),
                         in1=sg8[:, :].rearrange("p (k t) -> p k t", t=128), op=OP.mult),
                         reads=[t1s, sg8], writes=[ctx.LNX[:, g * 4096:(g + 1) * 4096]])
            p.dma("sync", stc, w["std_d"][gc * 128:(gc + 1) * 128, :])
            cl = ctx.S[:, 4 + gc:5 + gc]
            p.mm(lambda t: t.transpose(ctx.pD[0:1, 0:128], cl, ctx.iden32[:, :]),
                 reads=[cl, ctx.iden32[:, :]], writes=[ctx.pD[0:1, 0:512]])
            p.op("scalar", lambda s: s.activation(out=ctx.S[0:1, 1136:1264],
                 in_=ctx.pD[0:1, 0:128], func=AF.Exp), reads=[ctx.pD[0:1, 0:512]],
                 writes=[ctx.S[0:1, 1136:1264]])
            p.op("vector", lambda v: v.tensor_copy(
                 out=vap(ctx.R, 0, [[8192, 1], [64, 128], [1, 64]]),
                 in_=vap(ctx.S, 1136, [[2048, 1], [1, 128], [0, 64]])),
                 reads=[ctx.S[0:1, 1136:1264]], writes=[cde])
            for bg in range(16):
                p.mm(lambda t, r_=cde[0:1, bg * 512:(bg + 1) * 512]: t.matmul(
                     ctx.pD[:, 0:512], ctx.ones_row32[:, :], r_, start=True, stop=True),
                     reads=[cde[0:1, bg * 512:(bg + 1) * 512], ctx.ones_row32[:, :]],
                     writes=[ctx.pD[:, 0:512]])
                p.op("vector", lambda v, bg=bg: v.tensor_tensor(
                     out=prevg[:, bg * 512:(bg + 1) * 512],
                     in0=prevg[:, bg * 512:(bg + 1) * 512], in1=ctx.pD[:, 0:512], op=OP.mult),
                     reads=[prevg[:, bg * 512:(bg + 1) * 512], ctx.pD[:, 0:512]],
                     writes=[prevg[:, bg * 512:(bg + 1) * 512]])
            p.op("vector", lambda v: v.tensor_tensor(out=prevg, in0=prevg, in1=stc, op=OP.add),
                 reads=[prevg, stc], writes=[prevg])
            p.cut()
        ysq = ctx.B[:, 4096:5120]
        vrow = ctx.R[0:1, 0:4096]
        for gg in range(8):
            for k in range(8):
                ksl = yn[:, gg * 4096 + k * 512:gg * 4096 + (k + 1) * 512]
                p.op("scalar", lambda s, ksl=ksl, k=k: s.activation(
                     out=ysq[:, (k % 2) * 512:(k % 2 + 1) * 512], in_=ksl, func=AF.Square),
                     reads=[ksl], writes=[ysq[:, (k % 2) * 512:(k % 2 + 1) * 512]])
                p.mm(lambda t, k=k, st_=(k == 0), sp=(k == 7): t.matmul(ctx.pD[0:1, 0:512],
                     ctx.ones_col_bf[:, :], ysq[:, (k % 2) * 512:(k % 2 + 1) * 512],
                     start=st_, stop=sp),
                     reads=[ysq[:, (k % 2) * 512:(k % 2 + 1) * 512], ctx.ones_col_bf[:, :]],
                     writes=[ctx.pD[0:1, 0:512]])
            p.op("vector", lambda v, gg=gg: v.tensor_scalar(
                 out=vrow[0:1, gg * 512:(gg + 1) * 512], in0=ctx.pD[0:1, 0:512],
                 scalar1=1.0 / 1024.0, scalar2=None, op0=OP.mult, op1=OP.bypass),
                 reads=[ctx.pD[0:1, 0:512]], writes=[vrow[0:1, gg * 512:(gg + 1) * 512]])
        p.op("scalar", lambda s: s.activation(out=vrow, in_=vrow, func=AF.Sqrt,
             bias=ctx.cst[0:1, 0:1], scale=1.0), reads=[vrow, ctx.cst[0:1, 0:1]],
             writes=[vrow])
        p.op("vector", lambda v: v.reciprocal(out=vrow, in_=vrow), reads=[vrow], writes=[vrow])
        for gg in range(8):
            p.mm(lambda t, gg=gg: t.matmul(ctx.pD[:, 0:512], ctx.ones_row32[:, :],
                 vrow[0:1, gg * 512:(gg + 1) * 512], start=True, stop=True),
                 reads=[vrow[0:1, gg * 512:(gg + 1) * 512], ctx.ones_row32[:, :]],
                 writes=[ctx.pD[:, 0:512]])
            p.op("vector", lambda v: v.tensor_copy(out=ysq[:, 0:512], in_=ctx.pD[:, 0:512]),
                 reads=[ctx.pD[:, 0:512]], writes=[ysq[:, 0:512]])
            for k in range(8):
                kk = gg * 8 + k
                ksl = yn[:, kk * 512:(kk + 1) * 512]
                p.op("vector", lambda v, ksl=ksl, kk=kk: v.scalar_tensor_tensor(
                     out=ksl, in0=ksl, scalar=nw_sb[:, kk:kk + 1], in1=ysq[:, 0:512],
                     op0=OP.mult, op1=OP.mult),
                     reads=[ksl, nw_sb[:, kk:kk + 1], ysq[:, 0:512]], writes=[ksl])
        p.cut()
        wos = ctx.C[:, 0:16384]
        for m in range(32):
            ws = wos[:, (m % 2) * 8192:(m % 2 + 1) * 8192]
            p.dma("gpsimd", ws, w["outw_t"][m])
            o = ctx.pC[:, 0:512] if m % 2 == 0 else ctx.pB[:, 0:512]
            xsl = ctx.A[:, (m % 2) * 512:(m % 2) * 512 + 512]
            nsl = ctx.A[:, 1024 + (m % 2) * 512:1024 + (m % 2) * 512 + 512]
            xdr = xd[m * 128:(m + 1) * 128, toff:toff + 512]
            p.dma("sync", xsl, xdr)
            for k in range(64):
                p.mm(lambda t, o=o, l=ws[:, k * 128:(k + 1) * 128],
                     r=yn[:, k * 512:(k + 1) * 512], st_=(k == 0), sp=(k == 63):
                     t.matmul(o, l, r, start=st_, stop=sp),
                     reads=[ws[:, k * 128:(k + 1) * 128], yn[:, k * 512:(k + 1) * 512]],
                     writes=[o])
            p.op("vector", lambda v, o=nsl, a=o, b=xsl: v.tensor_tensor(
                 out=o, in0=a, in1=b, op=OP.add), reads=[o, xsl], writes=[nsl])
            p.dma("sync", xdr, nsl)
            if m % 8 == 7:
                p.cut()
        p.cut()



D_ = 4096
NCORES = 8
STRIP = 512


def core_strips(c):
    q = c % 4
    return (q, 7 - q)


def shard_x(x):
    out = []
    for c in range(NCORES):
        b = c // 4
        sA, sB = core_strips(c)
        xa = x[b, sA * STRIP:(sA + 1) * STRIP]
        xb = x[b, sB * STRIP:(sB + 1) * STRIP]
        out.append(np.ascontiguousarray(np.concatenate([xa, xb], 0).T.astype(np.float32)))
    return out


def unshard_x(shards):
    x = np.zeros((2, 4096, 4096), dtype=np.float32)
    for c in range(NCORES):
        b = c // 4
        sA, sB = core_strips(c)
        t = np.asarray(shards[c]).T
        x[b, sA * STRIP:(sA + 1) * STRIP] = t[:STRIP]
        x[b, sB * STRIP:(sB + 1) * STRIP] = t[STRIP:]
    return x


def tileize(wT):
    IN, OUT = wT.shape
    K, M = IN // 128, OUT // 128
    a = wT.reshape(K, 128, M, 128).transpose(2, 1, 0, 3)
    return np.ascontiguousarray(a.reshape(M, 128, K * 128).astype(np.float32))


def l0ffn_host(inp):
    w = {}
    w["wt"] = tileize(np.asarray(inp["L0_ffn_weight"]).T)
    w["b_flat"] = np.ascontiguousarray(
        np.asarray(inp["L0_ffn_lora_B"]).reshape(128, 4096).astype(np.float32))
    gw1t = np.asarray(inp["L0_ffn_gw1"]).T.astype(np.float32)
    w["gw1t_t"] = np.ascontiguousarray(gw1t.reshape(32, 128, 32).transpose(1, 0, 2).reshape(128, 1024))
    aall = np.asarray(inp["L0_ffn_lora_A"]).transpose(1, 0, 2).reshape(4096, 128).astype(np.float32)
    w["a_all_t"] = np.ascontiguousarray(aall.reshape(32, 128, 128).transpose(1, 0, 2).reshape(128, 4096))
    w["gw2t"] = np.ascontiguousarray(np.asarray(inp["L0_ffn_gw2"]).T.astype(np.float32))
    w["gb1"] = np.asarray(inp["L0_ffn_gb1"]).reshape(32, 1).astype(np.float32)
    w["gb2"] = np.asarray(inp["L0_ffn_gb2"]).reshape(8, 1).astype(np.float32)
    w["lngamma"] = np.asarray(inp["L0_ffn_ln_gamma"]).reshape(32, 1).astype(np.float32)
    w["lnbeta"] = np.asarray(inp["L0_ffn_ln_beta"]).reshape(32, 1).astype(np.float32)
    esel = np.zeros((8, 128), dtype=np.float32)
    for e in range(8):
        esel[e, e * 16:(e + 1) * 16] = 1.0 / 16.0
    w["esel"] = esel
    return w


def l0attn_host(inp):
    w = {}
    for nm, key in [("wq_t", "L0_attn_wq"), ("wk_t", "L0_attn_wk"),
                    ("wv_t", "L0_attn_wv"), ("wo_t", "L0_attn_wo")]:
        w[nm] = tileize(np.asarray(inp[key]).T)
    T, h = 4096, 128
    pos = np.arange(T, dtype=np.float32)
    thetas = (1.0 / 10000.0 ** (np.arange(0, h, 2, dtype=np.float32) / h)).astype(np.float32)
    pt = np.concatenate([np.outer(pos, thetas)] * 2, axis=1).astype(np.float32)
    sin, cos = np.sin(pt).astype(np.float32), np.cos(pt).astype(np.float32)
    w["cos_pc"], w["sin_pc"] = [], []
    for c in range(NCORES):
        sa, sb = core_strips(c)
        idx = np.r_[sa * STRIP:(sa + 1) * STRIP, sb * STRIP:(sb + 1) * STRIP]
        w["cos_pc"].append(np.ascontiguousarray(cos[idx].T))
        w["sin_pc"].append(np.ascontiguousarray(sin[idx].T))
    return w


def attn_masks(c):
    sA, sB = core_strips(c)
    NEG = -1e30
    m = np.zeros((128, 8 * 1024), dtype=np.float32)
    causal = np.where(np.arange(1024)[None, :] >= np.arange(128)[:, None] + 384, 0.0, NEG)
    for mi, (qg, s) in enumerate([(0, 0), (0, 1), (0, 2), (0, 3),
                                  (1, 4), (1, 5), (1, 6), (1, 7)]):
        strip = sA if qg == 0 else sB
        if s == strip:
            m[:, mi * 1024:(mi + 1) * 1024] = causal
        elif s > strip:
            m[:, mi * 1024:(mi + 1) * 1024] = NEG
    return m


def l1ffn_host(inp):
    w = {}
    wqT = np.asarray(inp["L1_ffn_Wq"]).T.astype(np.float32)
    w["wq_t2"] = np.ascontiguousarray(wqT.reshape(32, 128, 64).transpose(1, 0, 2).reshape(128, 2048))
    w["wr_t"] = np.ascontiguousarray(np.asarray(inp["L1_ffn_Wr"]).T.astype(np.float32))
    w["keys_f"] = np.ascontiguousarray(np.asarray(inp["L1_ffn_aux_keys"]).T.astype(np.float32))
    w["values"] = np.ascontiguousarray(np.asarray(inp["L1_ffn_aux_values"]).astype(np.float32))
    w["wvd_t"] = tileize(np.asarray(inp["L1_ffn_Wvd"]).T)
    w["wvu_t"] = tileize(np.asarray(inp["L1_ffn_Wvu"]).T)
    w["wu1_t"] = tileize(np.asarray(inp["L1_ffn_Wu1"]).T)
    w["wu2_t"] = np.ascontiguousarray(
        np.asarray(inp["L1_ffn_Wu2"]).T.reshape(8, 128).T.astype(np.float32))
    w["bu1_t"] = np.ascontiguousarray(
        np.asarray(inp["L1_ffn_bu1"]).reshape(8, 128).T.astype(np.float32))
    w["log_rel"] = np.asarray(inp["L1_ffn_log_rel"]).reshape(1, 1024).astype(np.float32)
    w["bu2"] = np.asarray(inp["L1_ffn_bu2"]).reshape(1, 1).astype(np.float32)
    w["gate_w1"] = np.asarray(inp["L1_ffn_gate_w1"]).reshape(1, 1).astype(np.float32)
    w["gate_bias"] = np.asarray(inp["L1_ffn_gate_bias"]).reshape(1, 1).astype(np.float32)
    return w


def l1attn_host(inp):
    w = {}
    w["inproj_t"] = tileize(np.asarray(inp["L1_attn_in_proj_w"]).T)
    cw = np.asarray(inp["L1_attn_conv_w"]).reshape(10240, 4).astype(np.float32)
    cd = np.zeros((80, 128, 512), dtype=np.float32)
    for mc in range(80):
        for j in range(4):
            cd[mc, :, j * 128:(j + 1) * 128] = np.diag(cw[mc * 128:(mc + 1) * 128, j])
    w["convd_t"] = cd
    w["conv_b"] = np.ascontiguousarray(
        np.asarray(inp["L1_attn_conv_b"]).reshape(80, 128).T.astype(np.float32))
    w["dtb"] = np.asarray(inp["L1_attn_dt_bias"]).reshape(128, 1).astype(np.float32)
    w["alog"] = np.asarray(inp["L1_attn_A_log"]).reshape(128, 1).astype(np.float32)
    Dv = np.asarray(inp["L1_attn_D"]).astype(np.float32)
    de = np.zeros((128, 64), dtype=np.float32)
    for k in range(64):
        de[0:64, k] = Dv[2 * k]
        de[64:128, k] = Dv[2 * k + 1]
    w["d_exp"] = de
    w["nw_t"] = np.ascontiguousarray(
        np.asarray(inp["L1_attn_norm_w"]).reshape(64, 128).T.astype(np.float32))
    w["outw_t"] = tileize(np.asarray(inp["L1_attn_out_w"]).T)
    tri = np.where(np.arange(128)[:, None] <= np.tile(np.arange(128), 4)[None, :], 0.0, -1e30)
    w["trimask4"] = tri.astype(np.float32)
    w["selc_pc"], w["hsel_pc"] = [], []
    for c in range(NCORES):
        sA, sB = core_strips(c)
        sel = np.zeros((128, 16), dtype=np.float32)
        sel[:, sA] = 1.0
        sel[:, 8 + sB] = 1.0
        w["selc_pc"].append(sel)
        hs = np.zeros((128, 16), dtype=np.float32)
        for dsti, sd in ((0, sA), (1, sB)):
            if sd >= 1:
                src = sd - 1
                r = min(src, 7 - src)
                sl = 0 if src < 4 else 1
                hs[:, dsti * 8 + r * 2 + sl] = 1.0
        w["hsel_pc"].append(hs)
    return w



def build_l0ffn():
    nc = bass.Bass("TRN2", target_bir_lowering=False)
    ctx = Ctx(nc)
    p = ctx.p
    x_in = ctx.din("x_in", [4096, 1024])
    x_out = ctx.dout("x_out", [4096, 1024])
    xd = ctx.dtmp("x_sc", [4096, 1024])
    w = {
        "wt": ctx.din("w_wt", [32, 128, 4096]),
        "a_all_t": ctx.din("w_aall", [128, 4096]),
        "b_flat": ctx.din("w_bflat", [128, 4096]),
        "gw1t_t": ctx.din("w_gw1t", [128, 1024]),
        "gw2t": ctx.din("w_gw2t", [32, 8]),
        "gb1": ctx.din("w_gb1", [32, 1]),
        "gb2": ctx.din("w_gb2", [8, 1]),
        "lngamma": ctx.din("w_gamma", [32, 1]),
        "lnbeta": ctx.din("w_beta", [32, 1]),
        "esel": ctx.din("w_esel", [8, 128]),
        "cb_in": ctx.dtmp("cb_in", [1, 16]),
        "cb_out": ctx.dtmp("cb_out", [1, 16]),
    }
    ctx.pools()
    ctx.constants()
    p.dma("sync", xd, x_in)
    p.cut()
    emit_ln(ctx, xd, "A")
    emit_l0ffn(ctx, xd, w, "A")
    p.dma("sync", x_out, xd)
    p.finish()
    return p.build()


def build_l0attn():
    nc = bass.Bass("TRN2", target_bir_lowering=False)
    ctx = Ctx(nc)
    p = ctx.p
    x_in = ctx.din("x_in", [4096, 1024])
    x_out = ctx.dout("x_out", [4096, 1024])
    xd = ctx.dtmp("x_sc", [4096, 1024])
    w = {
        "wq_t": ctx.din("w_wq", [32, 128, 4096]),
        "wk_t": ctx.din("w_wk", [32, 128, 4096]),
        "wv_t": ctx.din("w_wv", [32, 128, 4096]),
        "wo_t": ctx.din("w_wo", [32, 128, 4096]),
        "cos": ctx.din("w_cos", [128, 1024]),
        "sin": ctx.din("w_sin", [128, 1024]),
        "mtab": ctx.din("w_mtab", [128, 8192], BF16),
        "q_loc": ctx.dtmp("q_loc", [4096, 1024], BF16),
        "k_bounce": ctx.dtmp("k_bounce", [4096, 1024], BF16),
        "v_loc": ctx.dtmp("v_loc", [1024, 4096], BF16),
        "k_all": ctx.dtmp("k_all", [16384, 1024], BF16),
        "v_all": ctx.dtmp("v_all", [4096, 4096], BF16),
    }
    ctx.pools()
    ctx.constants()
    p.dma("sync", xd, x_in)
    p.cut()
    emit_ln(ctx, xd, "A")
    emit_l0attn(ctx, xd, w)
    p.dma("sync", x_out, xd)
    p.finish()
    return p.build()


def build_l1ffn():
    nc = bass.Bass("TRN2", target_bir_lowering=False)
    ctx = Ctx(nc)
    p = ctx.p
    x_in = ctx.din("x_in", [4096, 1024])
    x_out = ctx.dout("x_out", [4096, 1024])
    xd = ctx.dtmp("x_sc", [4096, 1024])
    w = {
        "wq_t2": ctx.din("w_wq2", [128, 2048]),
        "wr_t": ctx.din("w_wr", [64, 48]),
        "keys_f": ctx.din("w_keys", [64, 1024]),
        "values": ctx.din("w_vals", [1024, 4096]),
        "wvd_t": ctx.din("w_wvd", [2, 128, 4096]),
        "wvu_t": ctx.din("w_wvu", [32, 128, 256]),
        "wu1_t": ctx.din("w_wu1", [8, 128, 4096]),
        "wu2_t": ctx.din("w_wu2", [128, 8]),
        "bu1_t": ctx.din("w_bu1", [128, 8]),
        "log_rel": ctx.din("w_lr", [1, 1024]),
        "bu2": ctx.din("w_bu2", [1, 1]),
        "gate_w1": ctx.din("w_gw1s", [1, 1]),
        "gate_bias": ctx.din("w_gbs", [1, 1]),
        "cb2_in": ctx.dtmp("cb2_in", [1, 1]),
        "cb2_out": ctx.dtmp("cb2_out", [1, 1]),
    }
    ctx.pools()
    ctx.constants()
    p.dma("sync", xd, x_in)
    p.cut()
    var_row = ctx.S[0:1, 1024:2048]
    emit_ln(ctx, xd, "A", var_row=var_row)
    emit_l1ffn(ctx, xd, w, var_row)
    p.dma("sync", x_out, xd)
    p.finish()
    return p.build()


def build_l1attn():
    nc = bass.Bass("TRN2", target_bir_lowering=False)
    ctx = Ctx(nc)
    p = ctx.p
    x_in = ctx.din("x_in", [4096, 1024])
    x_out = ctx.dout("x_out", [4096, 1024])
    xd = ctx.dtmp("x_sc", [4096, 1024])
    w = {
        "inproj_t": ctx.din("w_ip", [145, 128, 4096]),
        "convd_t": ctx.din("w_cd", [80, 128, 512]),
        "conv_b": ctx.din("w_cb", [128, 80]),
        "dtb": ctx.din("w_dtb", [128, 1]),
        "alog": ctx.din("w_alog", [128, 1]),
        "d_exp": ctx.din("w_dexp", [128, 64]),
        "nw_t": ctx.din("w_nw", [128, 64]),
        "outw_t": ctx.din("w_ow", [32, 128, 8192]),
        "trimask4": ctx.din("w_tri", [128, 512]),
        "selc": ctx.din("w_selc", [128, 16]),
        "hsel": ctx.din("w_hsel", [128, 16]),
        "gate_d": ctx.dtmp("gate_d", [8192, 1024], BF16),
        "xbc_d": ctx.dtmp("xbc_d", [10240, 1024], BF16),
        "xss_d": ctx.dtmp("xss_d", [8192, 1024], BF16),
        "bfc_d": ctx.dtmp("bfc_d", [128, 8192], BF16),
        "cfc_d": ctx.dtmp("cfc_d", [128, 8192], BF16),
        "std_d": ctx.dtmp("std_d", [1024, 8192], BF16),
        "cum_d": ctx.dtmp("cum_d", [128, 1024], F32),
        "halo_snd": ctx.dtmp("halo_snd", [128, 480], BF16),
        "halo_all": ctx.dtmp("halo_all", [512, 480], BF16),
        "st_snd": ctx.dtmp("st_snd", [256, 8192], BF16),
        "st_all": ctx.dtmp("st_all", [1024, 8192], BF16),
        "dp_snd": ctx.dtmp("dp_snd", [2, 128], F32),
        "dp_all": ctx.dtmp("dp_all", [8, 128], F32),
        "sin_d": ctx.dtmp("sin_d", [256, 8192], BF16),
    }
    ctx.pools()
    ctx.constants()
    p.dma("sync", xd, x_in)
    p.cut()
    emit_ln(ctx, xd, "A")
    emit_l1attn(ctx, xd, w)
    emit_l1attn_state(ctx, xd, w)
    emit_l1attn_y(ctx, xd, w)
    p.dma("sync", x_out, xd)
    p.finish()
    return p.build()


_CACHE = {}


def kernel_multi(**inputs):
    import ml_dtypes
    from concourse.bass_utils import run_bass_kernel_spmd
    if "progs" not in _CACHE:
        _CACHE["progs"] = {
            "l0ffn": build_l0ffn(), "l0attn": build_l0attn(),
            "l1ffn": build_l1ffn(), "l1attn": build_l1attn(),
        }
    progs = _CACHE["progs"]
    wf0 = l0ffn_host(inputs)
    wa0 = l0attn_host(inputs)
    wf1 = l1ffn_host(inputs)
    wa1 = l1attn_host(inputs)
    xs = shard_x(np.asarray(inputs["L0_attn_x"]).astype(np.float32))

    def launch(prog, permaps):
        res = run_bass_kernel_spmd(prog, permaps, core_ids=list(range(8)))
        return [r["x_out"] for r in res.results]

    def maps_f0(xs):
        return [{"x_in": xs[c], "w_wt": wf0["wt"], "w_aall": wf0["a_all_t"],
                 "w_bflat": wf0["b_flat"], "w_gw1t": wf0["gw1t_t"], "w_gw2t": wf0["gw2t"],
                 "w_gb1": wf0["gb1"], "w_gb2": wf0["gb2"], "w_gamma": wf0["lngamma"],
                 "w_beta": wf0["lnbeta"], "w_esel": wf0["esel"]} for c in range(8)]

    def maps_a0(xs):
        return [{"x_in": xs[c], "w_wq": wa0["wq_t"], "w_wk": wa0["wk_t"],
                 "w_wv": wa0["wv_t"], "w_wo": wa0["wo_t"], "w_cos": wa0["cos_pc"][c],
                 "w_sin": wa0["sin_pc"][c],
                 "w_mtab": attn_masks(c).astype(ml_dtypes.bfloat16)} for c in range(8)]

    def maps_f1(xs):
        return [{"x_in": xs[c], "w_wq2": wf1["wq_t2"], "w_wr": wf1["wr_t"],
                 "w_keys": wf1["keys_f"], "w_vals": wf1["values"], "w_wvd": wf1["wvd_t"],
                 "w_wvu": wf1["wvu_t"], "w_wu1": wf1["wu1_t"], "w_wu2": wf1["wu2_t"],
                 "w_bu1": wf1["bu1_t"], "w_lr": wf1["log_rel"], "w_bu2": wf1["bu2"],
                 "w_gw1s": wf1["gate_w1"], "w_gbs": wf1["gate_bias"]} for c in range(8)]

    def maps_a1(xs):
        return [{"x_in": xs[c], "w_ip": wa1["inproj_t"], "w_cd": wa1["convd_t"],
                 "w_cb": wa1["conv_b"], "w_dtb": wa1["dtb"], "w_alog": wa1["alog"],
                 "w_dexp": wa1["d_exp"], "w_nw": wa1["nw_t"], "w_ow": wa1["outw_t"],
                 "w_tri": wa1["trimask4"], "w_selc": wa1["selc_pc"][c],
                 "w_hsel": wa1["hsel_pc"][c]} for c in range(8)]

    xs = launch(progs["l0ffn"], maps_f0(xs))
    xs = launch(progs["l0attn"], maps_a0(xs))
    xs = launch(progs["l0ffn"], maps_f0(xs))
    xs = launch(progs["l1ffn"], maps_f1(xs))
    xs = launch(progs["l1attn"], maps_a1(xs))
    xs = launch(progs["l1ffn"], maps_f1(xs))
    return unshard_x(xs)


def build_fused():
    nc = bass.Bass("TRN2", target_bir_lowering=False)
    ctx = Ctx(nc)
    p = ctx.p
    x_in = ctx.din("x_in", [4096, 1024])
    x_out = ctx.dout("x_out", [4096, 1024])
    xd = ctx.dtmp("x_sc", [4096, 1024])
    wf0 = {
        "wt": ctx.din("w_wt", [32, 128, 4096]),
        "a_all_t": ctx.din("w_aall", [128, 4096]),
        "b_flat": ctx.din("w_bflat", [128, 4096]),
        "gw1t_t": ctx.din("w_gw1t", [128, 1024]),
        "gw2t": ctx.din("w_gw2t", [32, 8]),
        "gb1": ctx.din("w_gb1", [32, 1]),
        "gb2": ctx.din("w_gb2", [8, 1]),
        "lngamma": ctx.din("w_gamma", [32, 1]),
        "lnbeta": ctx.din("w_beta", [32, 1]),
        "esel": ctx.din("w_esel", [8, 128]),
        "cb_in": ctx.dtmp("cb_in", [1, 16]),
        "cb_out": ctx.dtmp("cb_out", [1, 16]),
    }
    wf0b = dict(wf0)
    wf0b["cb_in"] = ctx.dtmp("cb_in2", [1, 16])
    wf0b["cb_out"] = ctx.dtmp("cb_out2", [1, 16])
    wa0 = {
        "wq_t": ctx.din("w_wq", [32, 128, 4096]),
        "wk_t": ctx.din("w_wk", [32, 128, 4096]),
        "wv_t": ctx.din("w_wv", [32, 128, 4096]),
        "wo_t": ctx.din("w_wo", [32, 128, 4096]),
        "cos": ctx.din("w_cos", [128, 1024]),
        "sin": ctx.din("w_sin", [128, 1024]),
        "mtab": ctx.din("w_mtab", [128, 8192], BF16),
        "q_loc": ctx.dtmp("q_loc", [4096, 1024], BF16),
        "k_bounce": ctx.dtmp("k_bounce", [4096, 1024], BF16),
        "v_loc": ctx.dtmp("v_loc", [1024, 4096], BF16),
        "k_all": ctx.dtmp("k_all", [16384, 1024], BF16),
        "v_all": ctx.dtmp("v_all", [4096, 4096], BF16),
    }
    wf1 = {
        "wq_t2": ctx.din("w_wq2", [128, 2048]),
        "wr_t": ctx.din("w_wr", [64, 48]),
        "keys_f": ctx.din("w_keys", [64, 1024]),
        "values": ctx.din("w_vals", [1024, 4096]),
        "wvd_t": ctx.din("w_wvd", [2, 128, 4096]),
        "wvu_t": ctx.din("w_wvu", [32, 128, 256]),
        "wu1_t": ctx.din("w_wu1", [8, 128, 4096]),
        "wu2_t": ctx.din("w_wu2", [128, 8]),
        "bu1_t": ctx.din("w_bu1", [128, 8]),
        "log_rel": ctx.din("w_lr", [1, 1024]),
        "bu2": ctx.din("w_bu2", [1, 1]),
        "gate_w1": ctx.din("w_gw1s", [1, 1]),
        "gate_bias": ctx.din("w_gbs", [1, 1]),
        "cb2_in": ctx.dtmp("cb2_in", [1, 1]),
        "cb2_out": ctx.dtmp("cb2_out", [1, 1]),
    }
    wf1b = dict(wf1)
    wf1b["cb2_in"] = ctx.dtmp("cb2_in2", [1, 1])
    wf1b["cb2_out"] = ctx.dtmp("cb2_out2", [1, 1])
    wa1 = {
        "inproj_t": ctx.din("w_ip", [145, 128, 4096]),
        "convd_t": ctx.din("w_cd", [80, 128, 512]),
        "conv_b": ctx.din("w_cb", [128, 80]),
        "dtb": ctx.din("w_dtb", [128, 1]),
        "alog": ctx.din("w_alog", [128, 1]),
        "d_exp": ctx.din("w_dexp", [128, 64]),
        "nw_t": ctx.din("w_nw", [128, 64]),
        "outw_t": ctx.din("w_ow", [32, 128, 8192]),
        "trimask4": ctx.din("w_tri", [128, 512]),
        "selc": ctx.din("w_selc", [128, 16]),
        "hsel": ctx.din("w_hsel", [128, 16]),
        "gate_d": ctx.dtmp("gate_d", [8192, 1024], BF16),
        "xbc_d": ctx.dtmp("xbc_d", [10240, 1024], BF16),
        "xss_d": ctx.dtmp("xss_d", [8192, 1024], BF16),
        "bfc_d": ctx.dtmp("bfc_d", [128, 8192], BF16),
        "cfc_d": ctx.dtmp("cfc_d", [128, 8192], BF16),
        "std_d": ctx.dtmp("std_d", [1024, 8192], BF16),
        "cum_d": ctx.dtmp("cum_d", [128, 1024], F32),
        "halo_snd": ctx.dtmp("halo_snd", [128, 480], BF16),
        "halo_all": ctx.dtmp("halo_all", [512, 480], BF16),
        "st_snd": ctx.dtmp("st_snd", [256, 8192], BF16),
        "st_all": ctx.dtmp("st_all", [1024, 8192], BF16),
        "dp_snd": ctx.dtmp("dp_snd", [2, 128], F32),
        "dp_all": ctx.dtmp("dp_all", [8, 128], F32),
        "sin_d": ctx.dtmp("sin_d", [256, 8192], BF16),
    }
    ctx.pools()
    ctx.constants()
    p.dma("sync", xd, x_in)
    p.cut()
    emit_ln(ctx, xd, "A")
    emit_l0ffn(ctx, xd, wf0, "A")
    emit_ln(ctx, xd, "B")
    emit_l0attn(ctx, xd, wa0)
    emit_ln(ctx, xd, "C")
    emit_l0ffn(ctx, xd, wf0b, "C")
    var_row = ctx.S[0:1, 1024:2048]
    emit_ln(ctx, xd, "D", var_row=var_row)
    emit_l1ffn(ctx, xd, wf1, var_row)
    emit_ln(ctx, xd, "E")
    emit_l1attn(ctx, xd, wa1)
    emit_l1attn_state(ctx, xd, wa1)
    emit_l1attn_y(ctx, xd, wa1)
    emit_ln(ctx, xd, "F", var_row=var_row)
    emit_l1ffn(ctx, xd, wf1b, var_row)
    p.dma("sync", x_out, xd)
    p.finish()
    import sys
    print("FUSED ninstr:", p.ninstr, "waits:", p.nwaits, "segments:", len(p.segments),
          "dma_sems:", p.ndma_sems, file=sys.stderr)
    return p.build()


def kernel(**inputs):
    import ml_dtypes
    from concourse.bass_utils import run_bass_kernel_spmd
    if "fused" not in _CACHE:
        _CACHE["fused"] = build_fused()
    prog = _CACHE["fused"]
    wf0 = l0ffn_host(inputs)
    wa0 = l0attn_host(inputs)
    wf1 = l1ffn_host(inputs)
    wa1 = l1attn_host(inputs)
    xs = shard_x(np.asarray(inputs["L0_attn_x"]).astype(np.float32))
    maps = []
    for c in range(8):
        m = {"x_in": xs[c],
             "w_wt": wf0["wt"], "w_aall": wf0["a_all_t"], "w_bflat": wf0["b_flat"],
             "w_gw1t": wf0["gw1t_t"], "w_gw2t": wf0["gw2t"], "w_gb1": wf0["gb1"],
             "w_gb2": wf0["gb2"], "w_gamma": wf0["lngamma"], "w_beta": wf0["lnbeta"],
             "w_esel": wf0["esel"],
             "w_wq": wa0["wq_t"], "w_wk": wa0["wk_t"], "w_wv": wa0["wv_t"],
             "w_wo": wa0["wo_t"], "w_cos": wa0["cos_pc"][c], "w_sin": wa0["sin_pc"][c],
             "w_mtab": attn_masks(c).astype(ml_dtypes.bfloat16),
             "w_wq2": wf1["wq_t2"], "w_wr": wf1["wr_t"], "w_keys": wf1["keys_f"],
             "w_vals": wf1["values"], "w_wvd": wf1["wvd_t"], "w_wvu": wf1["wvu_t"],
             "w_wu1": wf1["wu1_t"], "w_wu2": wf1["wu2_t"], "w_bu1": wf1["bu1_t"],
             "w_lr": wf1["log_rel"], "w_bu2": wf1["bu2"], "w_gw1s": wf1["gate_w1"],
             "w_gbs": wf1["gate_bias"],
             "w_ip": wa1["inproj_t"], "w_cd": wa1["convd_t"], "w_cb": wa1["conv_b"],
             "w_dtb": wa1["dtb"], "w_alog": wa1["alog"], "w_dexp": wa1["d_exp"],
             "w_nw": wa1["nw_t"], "w_ow": wa1["outw_t"], "w_tri": wa1["trimask4"],
             "w_selc": wa1["selc_pc"][c], "w_hsel": wa1["hsel_pc"][c]}
        maps.append(m)
    res = run_bass_kernel_spmd(prog, maps, core_ids=list(range(8)))
    return unshard_x([r["x_out"] for r in res.results])
